# version 28
# baseline (speedup 1.0000x reference)
import math
import numpy as np

import concourse.bass as bass
import concourse.mybir as mybir
from concourse.bass_utils import run_bass_kernel_spmd

F32 = mybir.dt.float32
BF16 = mybir.dt.bfloat16
AF = mybir.ActivationFunctionType
ALU = mybir.AluOpType
AX = mybir.AxisListType

D = 1024
H = 16
HD = 64
NCORES = 8
EPS = 1e-5

ENGS = ("pe", "act", "dve", "pool", "sp")


class Buf:
    __slots__ = ("name", "last_write", "readers")

    def __init__(self, name):
        self.name = name
        self.last_write = None
        self.readers = []

    def __repr__(self):
        return f"Buf({self.name})"


class Arena:

    def __init__(self, t, ncols, slot_cols, name):
        self.t = t
        self.ncols = ncols
        self.slot_cols = slot_cols
        self.nslots = ncols // slot_cols
        self.bufs = [Buf(f"{name}[{i}]") for i in range(self.nslots)]
        self.name = name

    def bufs_for(self, lo, hi):
        s0 = lo // self.slot_cols
        s1 = (hi - 1) // self.slot_cols
        return self.bufs[s0:s1 + 1]


class RingAlloc:

    def __init__(self, arena):
        self.a = arena
        self.pos = 0
        self.top = arena.nslots

    def alloc(self, cols):
        nsl = (cols + self.a.slot_cols - 1) // self.a.slot_cols
        assert nsl <= self.top, f"ring alloc {cols} too big for {self.a.name}"
        if self.pos + nsl > self.top:
            self.pos = 0
        s0 = self.pos
        self.pos += nsl
        lo = s0 * self.a.slot_cols
        return lo, self.a.bufs[s0:s0 + nsl]

    def persist(self, cols):
        nsl = (cols + self.a.slot_cols - 1) // self.a.slot_cols
        self.top -= nsl
        assert self.top >= 0, "persist overflow"
        lo = self.top * self.a.slot_cols
        return lo, self.a.bufs[self.top:self.top + nsl]

    def scope(self):
        return self.top

    def unscope(self, saved):
        self.top = saved


class Ctx:
    def __init__(self, nc, name="mk"):
        self.nc = nc
        self.eng = {
            "pe": nc.tensor, "act": nc.scalar, "dve": nc.vector,
            "pool": nc.gpsimd, "sp": nc.sync,
        }
        self.esem = {}
        self.dram_bufs = {}
        self.ecount = {e: 0 for e in ENGS}
        self.waited = {}
        self.queues = {}
        self._closers = []
        self.uid = 0

    def init_sems(self, es):
        for e in ENGS:
            self.esem[e] = es.enter_context(self.nc.semaphore(f"S_{e}"))
        for q, n in (("sp", 12), ("pool", 8)):
            sems = [es.enter_context(self.nc.semaphore(f"Q_{q}{i}")) for i in range(n)]
            self.queues[q] = {"sems": sems, "exp": [0] * n, "i": 0}

    def nid(self):
        self.uid += 1
        return self.uid

    def _resolve(self, ek, tag):
        if tag is None:
            return None
        kind = tag[0]
        if kind == "e":
            _, pe, val = tag
            if pe == "pe" and ek == "pe":
                return None
            if pe == ek:
                assert val <= self.ecount[ek], f"wait on future own tag {tag}"
            sem = self.esem[pe]
        else:
            _, sem, val = tag
        key = (ek, id(sem))
        cur = self.waited.get(key, 0)
        if val <= cur:
            return None
        self.waited[key] = val
        return (sem, val)

    def _wait(self, ek, tag):
        r = self._resolve(ek, tag)
        if r is not None:
            self.eng[ek].wait_ge(r[0], r[1])

    def _collect(self, reads, writes):
        tags = []
        for b in reads:
            if b.last_write is not None:
                tags.append(b.last_write)
        for b in writes:
            if b.last_write is not None:
                tags.append(b.last_write)
            tags.extend(b.readers)
        best = {}
        for t in tags:
            if t is None:
                continue
            key = id(t[1]) if t[0] == "d" else t[1]
            if key not in best or best[key][2] < t[2]:
                best[key] = t
        return best.values()

    def _update(self, reads, writes, tag):
        for b in reads:
            b.readers.append(tag)
        for b in writes:
            b.last_write = tag
            b.readers = []

    def emit(self, ek, fn, reads=(), writes=(), inc=True):
        ws = [r for r in (self._resolve(ek, t) for t in self._collect(reads, writes))
              if r is not None]
        for (sem, val) in ws[1:]:
            self.eng[ek].wait_ge(sem, val)
        inst = fn(self.eng[ek])
        if ws:
            inst._wait_ge(ws[0][0], ws[0][1])
        if inc:
            self.ecount[ek] += 1
            inst.then_inc(self.esem[ek], 1)
        tag = ("e", ek, self.ecount[ek])
        self._update(reads, writes, tag)
        return tag

    def dma(self, q, dst_ap, src_ap, reads=(), writes=(), transpose=False,
            issue_ek=None, slow=False):
        qd = self.queues[q]
        i = qd["i"]
        qd["i"] = (i + 1) % len(qd["sems"])
        sem = qd["sems"][i]
        ek = issue_ek or ("sp" if q == "sp" else "pool")
        eng = self.eng[ek]
        ws = []
        if qd["exp"][i] > 0:
            key = (ek, id(sem))
            if self.waited.get(key, 0) < qd["exp"][i]:
                self.waited[key] = qd["exp"][i]
                ws.append((sem, qd["exp"][i]))
        reads = list(reads)
        writes = list(writes)
        try:
            nm = src_ap.tensor.name
            if nm in self.dram_bufs:
                reads.append(self.dram_bufs[nm])
            nm = dst_ap.tensor.name
            if nm in self.dram_bufs:
                writes.append(self.dram_bufs[nm])
        except AttributeError:
            pass
        ws += [r for r in (self._resolve(ek, t) for t in self._collect(reads, writes))
               if r is not None]
        for (s_, v_) in ws[1:]:
            eng.wait_ge(s_, v_)
        if transpose:
            inst = eng.dma_start_transpose(dst_ap, src_ap)
        elif slow:
            inst = eng.dma_start(dst_ap, src_ap, allow_slow_non_contiguous=True)
        else:
            inst = eng.dma_start(dst_ap, src_ap)
        if ws:
            inst._wait_ge(ws[0][0], ws[0][1])
        qd["exp"][i] += 16
        inst.then_inc(sem, 16)
        tag = ("d", sem, qd["exp"][i])
        self._update(reads, writes, tag)
        return tag

    def mm(self, psum_ap, psum_buf, steps, extra_reads=()):
        n = len(steps)
        for j, (lap, rap, rbufs) in enumerate(steps):
            first, last = j == 0, j == n - 1
            reads = list(rbufs) + (list(extra_reads) if last else [])
            wr = [psum_buf] if (first or last) else []
            if last:
                def fn(eng, lap=lap, rap=rap, first=first):
                    return eng.matmul(psum_ap, lap, rap, start=first, stop=True)
                self.emit("pe", fn, reads=reads,
                          writes=[psum_buf] if first else [psum_buf], inc=True)
            else:
                wb = [psum_buf] if first else []
                ws = [r for r in (self._resolve("pe", t)
                                  for t in self._collect(reads, wb))
                      if r is not None]
                for (s_, v_) in ws[1:]:
                    self.eng["pe"].wait_ge(s_, v_)
                inst = self.eng["pe"].matmul(psum_ap, lap, rap, start=first, stop=False)
                if ws:
                    inst._wait_ge(ws[0][0], ws[0][1])
                for b in reads:
                    b.readers.append(("e", "pe", self.ecount["pe"] + 1))

    def act(self, out_ap, in_ap, func, reads, writes, bias=0.0, scale=1.0,
            accum_ap=None):
        reads = list(reads)
        if isinstance(bias, tuple):
            bias, bb_ = bias
            reads += list(bb_)
        if isinstance(scale, tuple):
            scale, sb_ = scale
            reads += list(sb_)

        def fn(eng):
            return eng.activation(out_ap, in_ap, func, bias=bias, scale=scale,
                                  accum_out=accum_ap)
        return self.emit("act", fn, reads=reads, writes=writes)

    def tt(self, ek, out_ap, a_ap, b_ap, op, reads, writes):
        def fn(eng):
            return eng.tensor_tensor(out_ap, a_ap, b_ap, op)
        return self.emit(ek, fn, reads=reads, writes=writes)

    def ts(self, ek, out_ap, in_ap, s1, s2, op0, op1, reads, writes):
        def fn(eng):
            if s2 is None:
                return eng.tensor_single_scalar(out_ap, in_ap, s1, op0)
            return eng.tensor_scalar(out_ap, in_ap, s1, s2, op0, op1)
        return self.emit(ek, fn, reads=reads, writes=writes)

    def stt(self, ek, out_ap, in0_ap, scalar, in1_ap, op0, op1, reads, writes):
        def fn(eng):
            return eng.scalar_tensor_tensor(out_ap, in0_ap, scalar, in1_ap, op0, op1)
        return self.emit(ek, fn, reads=reads, writes=writes)

    def copy(self, ek, out_ap, in_ap, reads, writes):
        def fn(eng):
            if ek == "act":
                return eng.copy(out_ap, in_ap)
            return eng.tensor_copy(out_ap, in_ap)
        return self.emit(ek, fn, reads=reads, writes=writes)

    def red(self, ek, out_ap, in_ap, op, reads, writes, absval=False, negate=False):
        def fn(eng):
            return eng.tensor_reduce(out_ap, in_ap, AX.X, op,
                                     apply_absolute_value=absval, negate=negate)
        return self.emit(ek, fn, reads=reads, writes=writes)

    def memset(self, ek, ap, val, writes):
        def fn(eng):
            return eng.memset(ap, val)
        return self.emit(ek, fn, reads=(), writes=writes)

    def transpose(self, out_psum_ap, in_ap, ident_ap, reads, writes, ident_buf):
        def fn(eng):
            return eng.transpose(out_psum_ap, in_ap, ident_ap)
        return self.emit("pe", fn, reads=list(reads) + [ident_buf], writes=writes)

    def barrier(self):
        for e in ENGS:
            for f in ENGS:
                if e != f and self.ecount[f] > 0:
                    self._wait(e, ("e", f, self.ecount[f]))



class Prog:

    def __init__(self, nc, es, S=2048, n_elems=2):
        self.nc = nc
        self.es = es
        self.S = S
        self.NT = S // 128
        self.n_elems = n_elems
        self.WQ = "pool"
        self.ctx = Ctx(nc)
        self.ctx.init_sems(es)
        c = self.ctx

        def sbuf(name, shape, dt):
            return es.enter_context(nc.sbuf_tensor(name, shape, dt))

        def psum(name, shape, dt):
            return es.enter_context(nc.psum_tensor(name, shape, dt))

        self.RES = Arena(sbuf("RES", [128, self.NT * 1024], F32), self.NT * 1024, 1024, "RES")
        self.AA = Arena(sbuf("AA", [128, 8 * S], BF16), 8 * S, 512, "AA")
        abcols = max(8 * S, 16384)
        self.AB = Arena(sbuf("AB", [128, abcols], BF16), abcols, 512, "AB")
        self.WR = Arena(sbuf("WR", [128, 12288], BF16), 12288, 512, "WR")
        self.SM = Arena(sbuf("SM", [128, 7680], F32), 7680, 128, "SM")
        self.BB = Arena(sbuf("BB", [128, 12288], BF16), 12288, 256, "BB")
        self.wr_ring = RingAlloc(self.WR)
        self.sm_ring = RingAlloc(self.SM)
        self.bb_ring = RingAlloc(self.BB)

        self.PB = [psum(f"pb{i}", [128, 512], F32) for i in range(4)]
        self.PBbuf = [Buf(f"pb{i}") for i in range(4)]
        self.PA = [psum(f"pa{i}", [128, 1024], F32) for i in range(2)]
        self.PAbuf = [Buf(f"pa{i}") for i in range(2)]
        self._pb_i = 0

        self.ID_BF = sbuf("ID_BF", [128, 128], BF16)
        self.ID_F32 = sbuf("ID_F32", [128, 128], F32)
        self.ID_buf = Buf("ID_BF")
        self.IDF_buf = Buf("ID_F32")
        self.CONST = sbuf("CONST", [128, 8], F32)
        self.CONST_buf = Buf("CONST")
        self.ONE_ROW = sbuf("ONE_ROW", [1, 128], F32)
        self.ONE_ROW_buf = Buf("ONE_ROW")
        self.ONE_ROW_BF = sbuf("ONE_ROW_BF", [1, 128], BF16)
        self.ONE_ROW_BF_buf = Buf("ONE_ROW_BF")
        self.MASKHALF = sbuf("MASKHALF", [128, 2], BF16)
        self.MASKHALF_buf = Buf("MASKHALF")
        self.ONE_COL_BF = sbuf("ONE_COL_BF", [128, 1], BF16)
        self.ONE_COL_BF_buf = Buf("ONE_COL_BF")

        self._const_cache = {}
        self._const_slot = self.sm_ring.persist(128)
        self._const_used = 0
        self._emit_consts()

    def pb(self):
        i = self._pb_i
        self._pb_i = (i + 1) % len(self.PB)
        return self.PB[i], self.PBbuf[i]

    def _emit_consts(self):
        c = self.ctx
        for t, b in ((self.ID_BF, self.ID_buf), (self.ID_F32, self.IDF_buf)):
            c.memset("pool", t[:, :], 1.0, writes=[b])

            def fn(eng, t=t):
                return eng.affine_select(t[:, :], t[:, :], [[-1, 128]],
                                         ALU.is_equal, 0.0, channel_multiplier=1)
            c.emit("pool", fn, reads=[b], writes=[b])
        def fn_iota(eng):
            return eng.iota(self.CONST[:, 0:1], [[0, 1]], channel_multiplier=1,
                            allow_small_or_imprecise_dtypes=True)
        c.emit("pool", fn_iota, writes=[self.CONST_buf])
        c.memset("pool", self.CONST[:, 1:2], 1.0, writes=[self.CONST_buf])
        c.memset("pool", self.ONE_ROW[:, :], 1.0, writes=[self.ONE_ROW_buf])
        c.memset("pool", self.ONE_ROW_BF[:, :], 1.0, writes=[self.ONE_ROW_BF_buf])
        c.memset("pool", self.MASKHALF[:, :], 1.0, writes=[self.MASKHALF_buf])
        c.memset("pool", self.ONE_COL_BF[:, :], 1.0, writes=[self.ONE_COL_BF_buf])

        def fn_mh0(eng):
            return eng.affine_select(self.MASKHALF[:, 0:1], self.MASKHALF[:, 0:1],
                                     [[0, 1]], ALU.is_ge, 0.0, base=63,
                                     channel_multiplier=-1)
        c.emit("pool", fn_mh0, reads=[self.MASKHALF_buf], writes=[self.MASKHALF_buf])

        def fn_mh1(eng):
            return eng.affine_select(self.MASKHALF[:, 1:2], self.MASKHALF[:, 1:2],
                                     [[0, 1]], ALU.is_ge, 0.0, base=-64,
                                     channel_multiplier=1)
        c.emit("pool", fn_mh1, reads=[self.MASKHALF_buf], writes=[self.MASKHALF_buf])

    def rsqrt(self, out_ap, in_ap, eps, reads, writes, power=-0.5):
        c = self.ctx
        e_ap, e_b = self.const_col(float(eps))
        pw_ap, pw_b = self.const_col(float(power))
        pr = in_ap.shape[0]
        c.act(out_ap, in_ap, AF.Ln, bias=(e_ap[0:pr, :], e_b), reads=reads, writes=writes)
        c.act(out_ap, out_ap, AF.Exp, scale=(pw_ap[0:pr, :], pw_b), reads=writes, writes=writes)
        return writes

    def const_col(self, value):
        if value in self._const_cache:
            return self._const_cache[value]
        assert self._const_used < 128
        lo, bufs = self._const_slot
        col = lo + self._const_used
        self._const_used += 1
        ap = self.SM.t[:, col:col + 1]
        self.ctx.memset("pool", ap, float(value), writes=bufs)
        self._const_cache[value] = (ap, bufs)
        return ap, bufs

    def to_bf16_dram(self, src_fp32_ap, dst_bf16_ap, rows, cols, tag=""):
        c = self.ctx
        eidx = 0
        for r0 in range(0, rows, 128):
            pr = min(128, rows - r0)
            for c0 in range(0, cols, 2048):
                w = min(2048, cols - c0)
                slo, sbufs = self.sm_ring.alloc(w)
                dlo, dbufs = self.wr_ring.alloc(w)
                st = self.SM.t
                dt_ = self.WR.t
                c.dma("pool", st[0:pr, slo:slo + w], src_fp32_ap[r0:r0 + pr, c0:c0 + w],
                      writes=sbufs)
                ek = ("dve", "act", "pool")[eidx % 3]
                eidx += 1
                c.copy(ek, dt_[0:pr, dlo:dlo + w], st[0:pr, slo:slo + w],
                       reads=sbufs, writes=dbufs)
                c.dma("pool", dst_bf16_ap[r0:r0 + pr, c0:c0 + w], dt_[0:pr, dlo:dlo + w],
                      reads=dbufs)

    def load_bias_col(self, dram_1d_ap, n, name):
        c = self.ctx
        m = (n + 127) // 128
        slo, sbufs = self.sm_ring.alloc(128)
        st = self.SM.t
        if n % 128 != 0:
            c.memset("dve", st[0:m, slo:slo + 128], 0.0, writes=sbufs)
        full = (n // 128) * 128
        if full > 0:
            c.dma("pool", st[0:n // 128, slo:slo + 128].rearrange("p f -> p f"),
                  dram_1d_ap[0:full].rearrange("(p f) -> p f", f=128), reads=(), writes=sbufs)
        if n % 128 != 0:
            c.dma("pool", st[n // 128:n // 128 + 1, slo:slo + n % 128],
                  dram_1d_ap[full:n].rearrange("(a f) -> a f", a=1), reads=(), writes=sbufs)
        pbk, pbuf = self.pb()
        c.transpose(pbk[:, 0:m].rearrange("p f -> p f"), st[0:m, slo:slo + 128],
                    self.ID_F32[0:m, 0:m], reads=sbufs, writes=[pbuf], ident_buf=self.IDF_buf)
        dlo, dbufs = self.sm_ring.persist(m)
        c.copy("dve", st[:, dlo:dlo + m], pbk[:, 0:m], reads=[pbuf], writes=dbufs)
        return st, dlo, m, dbufs

    def wload(self, dram_bf16_ap, rows, cols, q=None, persist=False):
        c = self.ctx
        if q is None:
            q = self.WQ
        nch = (rows + 127) // 128
        cpad = ((cols + 511) // 512) * 512
        if persist:
            lo0, _ = self.wr_ring.persist(cpad * nch)
        else:
            lo0, _ = self.wr_ring.alloc(cpad * nch)
        out = []
        for i, r0 in enumerate(range(0, rows, 128)):
            pr = min(128, rows - r0)
            lo = lo0 + i * cpad
            bufs = self.WR.bufs_for(lo, lo + cols)
            ap = self.WR.t[0:pr, lo:lo + cols]
            c.dma(q, ap, dram_bf16_ap[r0:r0 + pr, 0:cols], writes=bufs)
            out.append((self.WR.t, lo, pr, cols, bufs))
        return out

    def ln_xn(self, elem_tag="", spill_fp32=None, spill_bufs=None, want_xnT=True,
              want_xn_tok=True):
        c = self.ctx
        S, NT = self.S, self.NT
        st = self.SM.t

        def stage1(t):
            slo, sbufs = self.sm_ring.alloc(16)
            r = self.RES.t[:, t * 1024:(t + 1) * 1024]
            rb = self.RES.bufs[t]
            def f1(eng, r=r):
                return eng.bn_stats(st[:, slo:slo + 6], r[:, 0:512])
            def f2(eng, r=r):
                return eng.bn_stats(st[:, slo + 6:slo + 12], r[:, 512:1024])
            c.emit("dve", f1, reads=[rb], writes=sbufs)
            c.emit("dve", f2, reads=[rb], writes=sbufs)
            def f3(eng):
                return eng.bn_aggr(st[:, slo + 12:slo + 14], st[:, slo:slo + 12])
            c.emit("dve", f3, reads=sbufs, writes=sbufs)
            self.rsqrt(st[:, slo + 14:slo + 15], st[:, slo + 13:slo + 14], EPS,
                       reads=sbufs, writes=sbufs)
            return slo, sbufs

        st1 = {0: stage1(0)}
        for t in range(NT):
            if t + 1 < NT:
                st1[t + 1] = stage1(t + 1)
            slo, sbufs = st1.pop(t)
            r = self.RES.t[:, t * 1024:(t + 1) * 1024]
            rb = self.RES.bufs[t]
            c.ts("dve", st[:, slo + 15:slo + 16], st[:, slo + 12:slo + 13],
                 st[:, slo + 14:slo + 15], -1.0, ALU.mult, ALU.mult,
                 reads=sbufs, writes=sbufs)
            if want_xn_tok or want_xnT:
                xn = self.AB.t[:, t * 1024:(t + 1) * 1024]
                xnb = self.AB.bufs_for(t * 1024, (t + 1) * 1024)
                c.ts("dve", xn, r, st[:, slo + 14:slo + 15], st[:, slo + 15:slo + 16],
                     ALU.mult, ALU.add, reads=[rb] + sbufs, writes=xnb)
                if spill_fp32 is not None:
                    sap, sb = spill_fp32
                    c.dma("pool", sap[t * 128:(t + 1) * 128, :], xn, reads=xnb,
                          writes=sb)
                if want_xnT:
                    dst = self.AA.t[:, t * 128:t * 128 + 128]
                    dst = self.AA.t[:, :].rearrange("p (c s) -> p c s", c=8)[:, :, t * 128:(t + 1) * 128]
                    dstb = []
                    for ch in range(8):
                        dstb += self.AA.bufs_for(ch * self.S + t * 128, ch * self.S + (t + 1) * 128)
                    c.dma("sp", dst, xn, reads=xnb, writes=dstb, transpose=True)

    def store_res(self, out_ap):
        c = self.ctx
        for t in range(self.NT):
            c.dma("sp", out_ap[t * 128:(t + 1) * 128, :],
                  self.RES.t[:, t * 1024:(t + 1) * 1024], reads=[self.RES.bufs[t]])

    def load_res(self, in_ap):
        c = self.ctx
        for t in range(self.NT):
            c.dma("sp", self.RES.t[:, t * 1024:(t + 1) * 1024],
                  in_ap[t * 128:(t + 1) * 128, :], writes=[self.RES.bufs[t]])

    def finish(self):
        c = self.ctx
        for q, qd in c.queues.items():
            ek = "sp" if q == "sp" else "pool"
            for i, sem in enumerate(qd["sems"]):
                if qd["exp"][i] > 0:
                    key = (ek, id(sem))
                    if c.waited.get(key, 0) < qd["exp"][i]:
                        c.waited[key] = qd["exp"][i]
                        c.eng[ek].wait_ge(sem, qd["exp"][i])

    def build_diag_masks(self, col):
        c = self.ctx
        self._mk_col = col
        ap = self.BB.t[:, col:col + 896]
        bufs = self.BB.bufs_for(col, col + 896)
        c.memset("pool", ap, 1.0, writes=bufs)

        def fn(eng, ap=ap):
            return eng.affine_select(ap, ap, [[1, 896]], ALU.is_ge, 0.0,
                                     base=-384, channel_multiplier=-1)
        c.emit("pool", fn, reads=bufs, writes=bufs)

    def mk(self, mi, width):
        col = self._mk_col + 384 - mi * 128
        return self.BB.t[:, col:col + width], self.BB.bufs_for(col, col + width)

    def head_tables(self, scale_row, bias_row, row_reads):
        c = self.ctx
        st = self.SM.t
        out = []
        for (rap, rbufs) in (scale_row, bias_row):
            n = rap.shape[-1]
            pbk, pbuf = self.PB[3], self.PBbuf[3]
            def fn(eng, rap=rap, n=n):
                return eng.matmul(pbk[:, 0:n], self.ONE_ROW[0:1, 0:128], rap,
                                  start=True, stop=True)
            c.emit("pe", fn, reads=[self.ONE_ROW_buf] + list(rbufs) + list(row_reads),
                   writes=[pbuf])
            lo, bufs = self.sm_ring.persist(n)
            c.copy("dve", st[:, lo:lo + n], pbk[:, 0:n], reads=[pbuf], writes=bufs)
            out.append((lo, bufs))
        return out


def emit_l0_attn_a(p, W, dbg=None):
    c = p.ctx
    S, NT = p.S, p.NT
    assert S % 512 == 0
    NQ = S // 512
    NKC = S // 128
    st = p.SM.t
    bb = p.BB.t
    _scope = p.sm_ring.scope()

    p.ln_xn()
    p.build_diag_masks(10016)

    lo_nk, b_nk = p.sm_ring.alloc(128)
    c.dma("pool", st[0:16, lo_nk:lo_nk + 64], W["null_k"], writes=b_nk)
    c.copy("dve", st[0:16, lo_nk + 64:lo_nk + 128], st[0:16, lo_nk:lo_nk + 64],
           reads=b_nk, writes=b_nk)
    pbk, pbuf = p.PB[2], p.PBbuf[2]
    c.transpose(pbk[:, 0:16], st[0:16, lo_nk:lo_nk + 128],
                p.ID_F32[0:16, 0:16], reads=b_nk, writes=[pbuf], ident_buf=p.IDF_buf)
    NKT = 8960
    b_nkT = p.BB.bufs_for(NKT, NKT + 16)
    c.copy("dve", bb[:, NKT:NKT + 16], pbk[:, 0:16], reads=[pbuf], writes=b_nkT)
    lo_n2, b_n2 = p.sm_ring.alloc(128)
    c.tt("dve", st[0:16, lo_n2:lo_n2 + 64], st[0:16, lo_nk:lo_nk + 64],
         st[0:16, lo_nk:lo_nk + 64], ALU.mult, reads=b_nk, writes=b_n2)
    c.red("dve", st[0:16, lo_n2 + 64:lo_n2 + 65], st[0:16, lo_n2:lo_n2 + 64], ALU.add,
          reads=b_n2, writes=b_n2)
    c.act(st[0:16, lo_n2 + 65:lo_n2 + 66], st[0:16, lo_n2 + 64:lo_n2 + 65], AF.Sqrt,
          reads=b_n2, writes=b_n2)
    c.transpose(pbk[0:1, 16:32], st[0:16, lo_n2 + 65:lo_n2 + 66],
                p.ID_F32[0:16, 0:16], reads=b_n2, writes=[pbuf], ident_buf=p.IDF_buf)
    lo_nn, b_nn = p.sm_ring.persist(16)
    c.copy("dve", st[0:1, lo_nn:lo_nn + 16], pbk[0:1, 16:32], reads=[pbuf], writes=b_nn)

    NV = 8976
    b_nv = p.BB.bufs_for(NV, NV + 1040)
    lo_v, b_v = p.sm_ring.alloc(1024)
    c.dma("pool", st[0:1, lo_v:lo_v + 1024], W["null_v"].rearrange("(a h) d -> a (h d)", a=1),
          writes=b_v)
    c.copy("dve", bb[0:1, NV:NV + 1040].rearrange("p (h d) -> p h d", h=16)[:, :, 0:64],
           st[0:1, lo_v:lo_v + 1024].rearrange("p (h d) -> p h d", h=16), reads=b_v,
           writes=b_nv)
    c.memset("dve", bb[0:1, NV:NV + 1040].rearrange("p (h d) -> p h d", h=16)[:, :, 64:65],
             1.0, writes=b_nv)

    lo_ls, b_ls = p.sm_ring.alloc(16)
    c.dma("pool", st[0:1, lo_ls:lo_ls + 16], W["logit_scale"], writes=b_ls)
    lo_c, b_c = p.sm_ring.persist(16)
    bconst = p.const_col(-math.log(8.0))
    c.act(st[0:1, lo_c:lo_c + 16], st[0:1, lo_ls:lo_ls + 16], AF.Exp,
          bias=(bconst[0][0:1, :], bconst[1]), reads=b_ls, writes=b_c)

    lo_qm, b_qm = p.sm_ring.persist(16)
    lo_km, b_km = p.sm_ring.persist(16)
    c.memset("dve", st[0:1, lo_qm:lo_qm + 16], 0.0, writes=b_qm)
    c.memset("dve", st[0:1, lo_km:lo_km + 16], 0.0, writes=b_km)

    QT, KT, VA, EXL, SQ = 0, 2048, 4096, 6400, 8448

    xnT_bufs = p.AA.bufs_for(0, 8 * S)

    def qkv_pair(pair):
        wq = p.wload(W["wq"][:, pair * 128:(pair + 1) * 128], 1024, 128)
        wk = p.wload(W["wk"][:, pair * 128:(pair + 1) * 128], 1024, 128)
        wv = p.wload(W["wv"][:, pair * 128:(pair + 1) * 128], 1024, 128)
        b_qT = p.BB.bufs_for(QT, QT + S)
        b_kT = p.BB.bufs_for(KT, KT + S)
        b_va = p.BB.bufs_for(VA, VA + 16 * 130)
        for (wtiles, dstcol, dstbufs, which) in ((wq, QT, b_qT, "q"), (wk, KT, b_kT, "k")):
            for tc in range(NQ):
                pbk, pbuf = (p.PB[0], p.PBbuf[0]) if tc % 2 == 0 else (p.PB[1], p.PBbuf[1])
                steps = []
                for kc in range(8):
                    (wt, wlo, wpr, wcols, wbufs) = wtiles[kc]
                    lap = wt[0:128, wlo:wlo + 128]
                    rap = p.AA.t[:, kc * S + tc * 512: kc * S + tc * 512 + 512]
                    steps.append((lap, rap, list(wbufs) + p.AA.bufs_for(kc * S + tc * 512, kc * S + tc * 512 + 512)))
                c.mm(pbk[:, :], pbuf, steps)
                c.copy("act", bb[:, dstcol + tc * 512: dstcol + tc * 512 + 512],
                       pbk[:, :], reads=[pbuf], writes=dstbufs)
                sqb = p.BB.bufs_for(SQ, SQ + 512)
                c.tt("pool", bb[:, SQ:SQ + 512],
                     bb[:, dstcol + tc * 512: dstcol + tc * 512 + 512],
                     bb[:, dstcol + tc * 512: dstcol + tc * 512 + 512], ALU.mult,
                     reads=dstbufs, writes=sqb)
                pn, pnb = p.PB[2], p.PBbuf[2]
                def fn(eng):
                    return eng.matmul(pn[0:2, 0:512], p.MASKHALF[:, 0:2],
                                      bb[:, SQ:SQ + 512], start=True, stop=True)
                c.emit("pe", fn, reads=sqb + [p.MASKHALF_buf], writes=[pnb])
                lo_t, b_t = p.sm_ring.alloc(2)
                c.red("dve", st[0:2, lo_t:lo_t + 1], pn[0:2, 0:512], ALU.max,
                      reads=[pnb], writes=b_t)
                c.transpose(pn[0:1, 508:510], st[0:2, lo_t:lo_t + 1],
                            p.ID_F32[0:2, 0:2], reads=b_t, writes=[pnb],
                            ident_buf=p.IDF_buf)
                accap = st[0:1, (lo_qm if which == "q" else lo_km) + 2 * pair:
                           (lo_qm if which == "q" else lo_km) + 2 * pair + 2]
                accb = b_qm if which == "q" else b_km
                c.tt("dve", accap, accap, pn[0:1, 508:510], ALU.max,
                     reads=[pnb] + accb, writes=accb)
        for tt_ in range(NT):
            pbk, pbuf = (p.PB[0], p.PBbuf[0]) if tt_ % 2 == 0 else (p.PB[1], p.PBbuf[1])
            steps = []
            for kc in range(8):
                (wt, wlo, wpr, wcols, wbufs) = wv[kc]
                lap = p.AA.t[:, kc * S + tt_ * 128: kc * S + tt_ * 128 + 128]
                rap = wt[0:128, wlo:wlo + 128]
                steps.append((lap, rap,
                              list(wbufs) + p.AA.bufs_for(kc * S + tt_ * 128, kc * S + tt_ * 128 + 128)))
            c.mm(pbk[:, 0:128], pbuf, steps)
            for hh in range(2):
                c.copy("act", bb[:, VA + tt_ * 130 + hh * 65: VA + tt_ * 130 + hh * 65 + 64],
                       pbk[:, hh * 64:hh * 64 + 64], reads=[pbuf], writes=b_va)
        ones_ap = bb[:, VA:VA + NT * 130].rearrange("p (t h d) -> p t h d", t=NT, h=2)[:, :, :, 64:65]
        c.memset("dve", ones_ap, 1.0, writes=b_va)
        return b_qT, b_kT, b_va

    def pair_tables(pair, b_reads):
        lo_r, b_r = p.sm_ring.persist(8)
        sl = slice(lo_r, lo_r + 2)
        c.act(st[0:1, lo_r:lo_r + 2], st[0:1, lo_qm + 2 * pair:lo_qm + 2 * pair + 2],
              AF.Sqrt, reads=b_qm + b_reads, writes=b_r)
        c.act(st[0:1, lo_r + 2:lo_r + 4], st[0:1, lo_km + 2 * pair:lo_km + 2 * pair + 2],
              AF.Sqrt, reads=b_km + b_reads, writes=b_r)
        c.tt("dve", st[0:1, lo_r + 2:lo_r + 4], st[0:1, lo_r + 2:lo_r + 4],
             st[0:1, lo_nn + 2 * pair:lo_nn + 2 * pair + 2], ALU.max,
             reads=b_r + b_nn, writes=b_r)
        c.tt("dve", st[0:1, lo_r + 4:lo_r + 6], st[0:1, lo_r:lo_r + 2],
             st[0:1, lo_r + 2:lo_r + 4], ALU.mult, reads=b_r, writes=b_r)
        c.stt("dve", st[0:1, lo_r + 6:lo_r + 8], st[0:1, lo_r + 4:lo_r + 6], -1.0,
              st[0:1, lo_c + 2 * pair:lo_c + 2 * pair + 2], ALU.mult, ALU.mult,
              reads=b_r + b_c, writes=b_r)
        tabs = p.head_tables((st[0:1, lo_c + 2 * pair:lo_c + 2 * pair + 2], b_c),
                             (st[0:1, lo_r + 6:lo_r + 8], b_r), b_r)
        return tabs

    for pair in range(8):
        _pscope = p.sm_ring.scope()
        b_qT, b_kT, b_va = qkv_pair(pair)
        (lo_sc, b_sc), (lo_bi, b_bi) = pair_tables(pair, [])
        if dbg is not None and pair == 0:
            dbg("pair0", dict(b_qT=b_qT, b_kT=b_kT, b_va=b_va, lo_sc=lo_sc,
                              b_sc=b_sc, lo_bi=lo_bi, b_bi=b_bi, QT=QT, KT=KT,
                              VA=VA, lo_qm=lo_qm, lo_km=lo_km, b_qm=b_qm, b_km=b_km))
        pending = []
        for hh in range(2):
            h = 2 * pair + hh
            qrow0 = hh * 64
            for qc in range(NQ):
                par = (qc + hh) % 2
                ob, obuf = (p.PB[2], p.PBbuf[2]) if par == 0 else (p.PB[3], p.PBbuf[3])
                alt, altbuf = (p.PB[3], p.PBbuf[3]) if par == 0 else (p.PB[2], p.PBbuf[2])
                nb, nbuf = p.PB[0], p.PBbuf[0]
                def fns(eng, nb=nb, qrow0=qrow0, qc=qc):
                    return eng.matmul(nb[0:1, :], bb[qrow0:qrow0 + 64, NKT + h:NKT + h + 1],
                                      bb[qrow0:qrow0 + 64, QT + qc * 512:QT + qc * 512 + 512],
                                      start=True, stop=True)
                c.emit("pe", fns, reads=b_nkT + b_qT, writes=[nbuf])
                exn = 10912
                exnb = p.BB.bufs_for(exn, exn + 512)
                c.act(bb[0:1, exn:exn + 512], nb[0:1, :], AF.Exp,
                      bias=st[0:1, lo_bi:lo_bi + 2][:, hh:hh + 1],
                      scale=st[0:1, lo_sc:lo_sc + 2][:, hh:hh + 1],
                      reads=[nbuf] + b_sc + b_bi, writes=exnb)
                kc_hi = min(NKC, 4 * qc + 4)
                exi = 0
                pend_pv = None
                for j0 in range(0, kc_hi, 2):
                    pk = list(range(j0, min(j0 + 2, kc_hi)))
                    sa, sabuf = (p.PA[0], p.PAbuf[0]) if (j0 // 2) % 2 == 0 else (p.PA[1], p.PAbuf[1])
                    for i2, kc in enumerate(pk):
                        def fsc(eng, kc=kc, i2=i2, sa=sa, qrow0=qrow0, qc=qc):
                            return eng.matmul(
                                sa[:, i2 * 512:i2 * 512 + 512],
                                bb[qrow0:qrow0 + 64, KT + kc * 128:KT + kc * 128 + 128],
                                bb[qrow0:qrow0 + 64, QT + qc * 512:QT + qc * 512 + 512],
                                start=True, stop=True)
                        c.emit("pe", fsc, reads=b_kT + b_qT, writes=[sabuf])
                    if j0 == 2 and pending:
                        pending.pop(0)()
                    w2 = 512 * len(pk)
                    ex = EXL + (exi % 4) * 512
                    exi += 2
                    exb = p.BB.bufs_for(ex, ex + w2)
                    c.act(bb[:, ex:ex + w2], sa[:, 0:w2], AF.Exp,
                          bias=st[:, lo_bi:lo_bi + 2][:, hh:hh + 1],
                          scale=st[:, lo_sc:lo_sc + 2][:, hh:hh + 1],
                          reads=[sabuf] + b_sc + b_bi, writes=exb)
                    for i2, kc in enumerate(pk):
                        o = kc * 128 - qc * 512
                        if o >= 0:
                            mkap, mkb = p.mk(o // 128, 512)
                            c.tt("pool", bb[:, ex + i2 * 512:ex + i2 * 512 + 512],
                                 bb[:, ex + i2 * 512:ex + i2 * 512 + 512],
                                 mkap, ALU.mult, reads=exb + mkb, writes=exb)
                    if pend_pv is not None:
                        pend_pv()

                    def mk_pv(pk=pk, ex=ex, exb=exb, ob=ob, obuf=obuf, hh=hh):
                        def emit_pv():
                            for i2, kc in enumerate(pk):
                                def fpv(eng, kc=kc, i2=i2, first=(kc == 0)):
                                    return eng.matmul(ob[0:65, :],
                                                      bb[:, VA + kc * 130 + hh * 65:VA + kc * 130 + hh * 65 + 65],
                                                      bb[:, ex + i2 * 512:ex + i2 * 512 + 512],
                                                      start=first, stop=False)
                                c.emit("pe", fpv, reads=b_va + exb, writes=[obuf])
                        return emit_pv

                    pend_pv = mk_pv()
                pend_pv()
                def fpvn(eng, exn=exn, ob=ob, h=h):
                    return eng.matmul(ob[0:65, :], bb[0:1, NV + h * 65:NV + h * 65 + 65],
                                      bb[0:1, exn:exn + 512], start=False, stop=True)
                c.emit("pe", fpvn, reads=b_nv + exnb, writes=[obuf])

                def tail(ob=ob, obuf=obuf, h=h, qc=qc):
                    lo_rr, b_rr = p.sm_ring.alloc(512)
                    p.rsqrt(st[0:1, lo_rr:lo_rr + 512], ob[64:65, :], 0.0,
                            reads=[obuf], writes=b_rr, power=-1.0)
                    och = h // 2
                    orow = (h % 2) * 64
                    ocol = och * S + qc * 512
                    ob_bufs = p.AB.bufs_for(ocol, ocol + 512)
                    dstab = p.AB.t[orow:orow + 64, ocol:ocol + 512]
                    c.copy("act", dstab, ob[0:64, :], reads=[obuf], writes=ob_bufs)
                    bcb, bcbuf = p.PB[1], p.PBbuf[1]
                    def fbc(eng):
                        return eng.matmul(bcb[0:64, :], p.ONE_ROW[0:1, 0:64],
                                          st[0:1, lo_rr:lo_rr + 512], start=True, stop=True)
                    c.emit("pe", fbc, reads=[p.ONE_ROW_buf] + b_rr, writes=[bcbuf])
                    c.tt("dve", dstab, dstab, bcb[0:64, :], ALU.mult,
                         reads=ob_bufs + [bcbuf], writes=ob_bufs)
                pending.append(tail)
        while pending:
            pending.pop(0)()

    wo = p.wload(W["wo"], 1024, 1024)
    for t in range(NT):
        for half in range(2):
            pbk, pbuf = (p.PB[0], p.PBbuf[0]) if half == 0 else (p.PB[1], p.PBbuf[1])
            steps = []
            for kc in range(8):
                (wt, wlo, wpr, wcols, wbufs) = wo[kc]
                lap = p.AB.t[:, kc * S + t * 128: kc * S + t * 128 + 128]
                rap = wt[0:128, wlo + half * 512: wlo + half * 512 + 512]
                steps.append((lap, rap,
                              list(wbufs) + p.AB.bufs_for(kc * S + t * 128, kc * S + t * 128 + 128)))
            c.mm(pbk[:, :], pbuf, steps)
            rb = p.RES.bufs[t]
            rap = p.RES.t[:, t * 1024 + half * 512: t * 1024 + half * 512 + 512]
            c.tt("dve", rap, rap, pbk[:, :], ALU.add, reads=[pbuf, rb], writes=[rb])
    p.sm_ring.unscope(_scope)


def emit_l1_attn_a(p, W):
    c = p.ctx
    S, NT = p.S, p.NT
    NKC = S // 128
    st = p.SM.t
    bb = p.BB.t
    _scope = p.sm_ring.scope()

    p.ln_xn()
    p.build_diag_masks(9984)

    t_, lo_ba, nba, b_ba = p.load_bias_col(W["b_attn"].rearrange("a n -> (a n)"), 3072, "ba")
    lo_bp, b_bp = p.sm_ring.persist(1024)
    c.dma("pool", st[0:1, lo_bp:lo_bp + 1024], W["b_proj"], writes=b_bp)
    VBROW = 8960
    b_vbrow = p.BB.bufs_for(VBROW, VBROW + 1024)
    lo_tmp, b_tmp = p.sm_ring.alloc(1024)
    c.dma("pool", st[0:1, lo_tmp:lo_tmp + 1024], W["b_attn"][0:1, 2048:3072], writes=b_tmp)
    c.copy("dve", bb[0:1, VBROW:VBROW + 1024], st[0:1, lo_tmp:lo_tmp + 1024],
           reads=b_tmp, writes=b_vbrow)

    lo_qm, b_qm = p.sm_ring.persist(16)
    lo_km, b_km = p.sm_ring.persist(16)
    c.memset("dve", st[0:1, lo_qm:lo_qm + 16], 0.0, writes=b_qm)
    c.memset("dve", st[0:1, lo_km:lo_km + 16], 0.0, writes=b_km)

    QT, KT, VA, EXL, SQ = 0, 2048, 4096, 6400, 8448
    NTI = NT + 3
    inv8_ap, inv8_b = p.const_col(0.125)

    def qkv_pair(pair):
        wq = p.wload(W["w_attn"][:, pair * 128:(pair + 1) * 128], 1024, 128)
        wk = p.wload(W["w_attn"][:, 1024 + pair * 128:1024 + (pair + 1) * 128], 1024, 128)
        wv = p.wload(W["w_attn"][:, 2048 + pair * 128:2048 + (pair + 1) * 128], 1024, 128)
        b_qT = p.BB.bufs_for(QT, QT + S)
        b_kT = p.BB.bufs_for(KT, KT + S)
        b_va = p.BB.bufs_for(VA, VA + NT * 130)
        for (wtiles, dstcol, dstbufs, which) in ((wq, QT, b_qT, "q"), (wk, KT, b_kT, "k")):
            bcol = (0 if which == "q" else 8) + pair
            for tc in range(S // 512):
                pbk, pbuf = (p.PB[0], p.PBbuf[0]) if tc % 2 == 0 else (p.PB[1], p.PBbuf[1])
                steps = []
                for kc in range(8):
                    (wt, wlo, wpr, wcols, wbufs) = wtiles[kc]
                    steps.append((wt[0:128, wlo:wlo + 128],
                                  p.AA.t[:, kc * S + tc * 512: kc * S + tc * 512 + 512],
                                  list(wbufs) + p.AA.bufs_for(kc * S + tc * 512, kc * S + tc * 512 + 512)))
                c.mm(pbk[:, :], pbuf, steps)
                def fev(eng, pbk=pbk, tc=tc, dstcol=dstcol, bcol=bcol):
                    return eng.activation(bb[:, dstcol + tc * 512: dstcol + tc * 512 + 512],
                                          pbk[:, :], AF.Identity,
                                          bias=st[:, lo_ba + bcol:lo_ba + bcol + 1])
                c.emit("act", fev, reads=[pbuf] + b_ba, writes=dstbufs)
                sqb = p.BB.bufs_for(SQ, SQ + 512)
                c.tt("pool", bb[:, SQ:SQ + 512],
                     bb[:, dstcol + tc * 512: dstcol + tc * 512 + 512],
                     bb[:, dstcol + tc * 512: dstcol + tc * 512 + 512], ALU.mult,
                     reads=dstbufs, writes=sqb)
                pn, pnb = p.PB[2], p.PBbuf[2]
                def fn(eng):
                    return eng.matmul(pn[0:2, 0:512], p.MASKHALF[:, 0:2],
                                      bb[:, SQ:SQ + 512], start=True, stop=True)
                c.emit("pe", fn, reads=sqb + [p.MASKHALF_buf], writes=[pnb])
                lo_t, b_t = p.sm_ring.alloc(2)
                c.red("dve", st[0:2, lo_t:lo_t + 1], pn[0:2, 0:512], ALU.max,
                      reads=[pnb], writes=b_t)
                c.transpose(pn[0:1, 508:510], st[0:2, lo_t:lo_t + 1],
                            p.ID_F32[0:2, 0:2], reads=b_t, writes=[pnb],
                            ident_buf=p.IDF_buf)
                accap = st[0:1, (lo_qm if which == "q" else lo_km) + 2 * pair:
                           (lo_qm if which == "q" else lo_km) + 2 * pair + 2]
                accb = b_qm if which == "q" else b_km
                c.tt("dve", accap, accap, pn[0:1, 508:510], ALU.max,
                     reads=[pnb] + accb, writes=accb)
        for tt_ in range(NT):
            pbk, pbuf = (p.PB[0], p.PBbuf[0]) if tt_ % 2 == 0 else (p.PB[1], p.PBbuf[1])
            steps = []
            for kc in range(8):
                (wt, wlo, wpr, wcols, wbufs) = wv[kc]
                steps.append((p.AA.t[:, kc * S + tt_ * 128: kc * S + tt_ * 128 + 128],
                              wt[0:128, wlo:wlo + 128],
                              list(wbufs) + p.AA.bufs_for(kc * S + tt_ * 128, kc * S + tt_ * 128 + 128)))
            steps.append((p.ONE_ROW_BF[0:1, 0:128],
                          bb[0:1, VBROW + pair * 128:VBROW + (pair + 1) * 128],
                          [p.ONE_ROW_BF_buf] + b_vbrow))
            c.mm(pbk[:, 0:128], pbuf, steps)
            for hh in range(2):
                c.copy("act", bb[:, VA + tt_ * 130 + hh * 65: VA + tt_ * 130 + hh * 65 + 64],
                       pbk[:, hh * 64:hh * 64 + 64], reads=[pbuf], writes=b_va)
        ones_ap = bb[:, VA:VA + NT * 130].rearrange("p (t h d) -> p t h d", t=NT, h=2)[:, :, :, 64:65]
        c.memset("dve", ones_ap, 1.0, writes=b_va)
        return b_qT, b_kT, b_va

    def pair_bias_tab(pair):
        lo_r, b_r = p.sm_ring.persist(8)
        c.act(st[0:1, lo_r:lo_r + 2], st[0:1, lo_qm + 2 * pair:lo_qm + 2 * pair + 2],
              AF.Sqrt, reads=b_qm, writes=b_r)
        c.act(st[0:1, lo_r + 2:lo_r + 4], st[0:1, lo_km + 2 * pair:lo_km + 2 * pair + 2],
              AF.Sqrt, reads=b_km, writes=b_r)
        c.ts("dve", st[0:1, lo_r + 4:lo_r + 6], st[0:1, lo_r:lo_r + 2], 0.125, None,
             ALU.mult, None, reads=b_r, writes=b_r)
        c.tt("dve", st[0:1, lo_r + 4:lo_r + 6], st[0:1, lo_r + 4:lo_r + 6],
             st[0:1, lo_r + 2:lo_r + 4], ALU.mult, reads=b_r, writes=b_r)
        lo_w, b_w = p.sm_ring.alloc(4 * NTI)
        lo_s2 = lo_w + 2 * NTI
        for hh in range(2):
            h = 2 * pair + hh
            slope = 2.0 ** (-(h + 1))
            cst = st[0:1, lo_w + hh * NTI:lo_w + (hh + 1) * NTI]
            def fio(eng, cst=cst):
                return eng.iota(cst, [[1, NTI]], channel_multiplier=0,
                                allow_small_or_imprecise_dtypes=True)
            c.emit("pool", fio, writes=b_w)
            c.ts("dve", cst, cst, 128.0 * slope, -128.0 * slope * (NT - 1),
                 ALU.mult, ALU.add, reads=b_w, writes=b_w)
            def fsub(eng, cst=cst, hh=hh):
                return eng.tensor_scalar(cst, cst,
                                         st[0:1, lo_r + 4 + hh:lo_r + 5 + hh],
                                         None, ALU.subtract)
            c.emit("dve", fsub, reads=b_w + b_r, writes=b_w)
            c.memset("dve", st[0:1, lo_s2 + hh * NTI:lo_s2 + (hh + 1) * NTI], slope,
                     writes=b_w)
        lo_l, b_l = p.sm_ring.alloc(128)
        def fi2(eng):
            return eng.iota(st[0:1, lo_l:lo_l + 128], [[1, 128]], channel_multiplier=0,
                            allow_small_or_imprecise_dtypes=True)
        c.emit("pool", fi2, writes=b_l)
        pbk, pbuf = p.PB[3], p.PBbuf[3]
        def fm1(eng):
            return eng.matmul(pbk[:, 0:2 * NTI], st[0:1, lo_l:lo_l + 128],
                              st[0:1, lo_s2:lo_s2 + 2 * NTI], start=True, stop=False)
        c.emit("pe", fm1, reads=b_l + b_w, writes=[pbuf])
        def fm2(eng):
            return eng.matmul(pbk[:, 0:2 * NTI], p.ONE_ROW[0:1, 0:128],
                              st[0:1, lo_w:lo_w + 2 * NTI], start=False, stop=True)
        c.emit("pe", fm2, reads=[p.ONE_ROW_buf] + b_w, writes=[pbuf])
        lo_tab, b_tab = p.sm_ring.persist(2 * NTI)
        c.copy("dve", st[:, lo_tab:lo_tab + 2 * NTI], pbk[:, 0:2 * NTI],
               reads=[pbuf], writes=b_tab)
        return lo_tab, b_tab

    for pair in range(8):
        _pscope = p.sm_ring.scope()
        b_qT, b_kT, b_va = qkv_pair(pair)
        lo_tab, b_tab = pair_bias_tab(pair)
        pending = []

        def l1a_tail(ob, obuf, h, qbase, NQh):
            lo_rr, b_rr = p.sm_ring.alloc(512)
            def frc(eng, ob=ob):
                return eng.reciprocal(st[0:1, lo_rr:lo_rr + NQh], ob[64:65, 0:NQh])
            c.emit("dve", frc, reads=[obuf], writes=b_rr)
            och = h // 2
            orow = (h % 2) * 64
            ocol = och * S + qbase
            ob_bufs = p.AB.bufs_for(ocol, ocol + NQh)
            dstab = p.AB.t[orow:orow + 64, ocol:ocol + NQh]
            c.copy("act", dstab, ob[0:64, 0:NQh], reads=[obuf], writes=ob_bufs)
            bcb, bcbuf = p.PB[1], p.PBbuf[1]
            def fbc(eng):
                return eng.matmul(bcb[0:64, 0:NQh], p.ONE_ROW[0:1, 0:64],
                                  st[0:1, lo_rr:lo_rr + NQh], start=True, stop=True)
            c.emit("pe", fbc, reads=[p.ONE_ROW_buf] + b_rr, writes=[bcbuf])
            c.tt("dve", dstab, dstab, bcb[0:64, 0:NQh], ALU.mult,
                 reads=ob_bufs + [bcbuf], writes=ob_bufs)

        def l1a_sc(hh, kc, sa, soff, qbase, NQh, sabuf):
            qrow0 = hh * 64
            def fsc(eng):
                return eng.matmul(
                    sa[:, soff:soff + NQh],
                    bb[qrow0:qrow0 + 64, KT + kc * 128:KT + kc * 128 + 128],
                    bb[qrow0:qrow0 + 64, QT + qbase:QT + qbase + NQh],
                    start=True, stop=True)
            c.emit("pe", fsc, reads=b_kT + b_qT, writes=[sabuf])

        def l1a_exp(hh, kc, sa, soff, sabuf, ex, qbase, NQh):
            qb128 = qbase // 128
            ti = kc - qb128 + (NT - 1)
            bias_ap = st[:, lo_tab + hh * NTI + ti:lo_tab + hh * NTI + ti + 1]
            exb = p.BB.bufs_for(ex, ex + 512)
            c.act(bb[:, ex:ex + NQh], sa[:, soff:soff + NQh], AF.Exp,
                  bias=(bias_ap, b_tab), scale=(inv8_ap, inv8_b),
                  reads=[sabuf], writes=exb)
            o = kc * 128 - qbase
            if o >= 0:
                mkap, mkb = p.mk(o // 128, NQh)
                c.tt("pool", bb[:, ex:ex + NQh], bb[:, ex:ex + NQh],
                     mkap, ALU.mult, reads=exb + mkb, writes=exb)
            return exb

        def l1a_pv(hh, kc, j, nj, ex, exb, NQh, ob, obuf):
            def fpv(eng):
                return eng.matmul(ob[0:65, 0:NQh],
                                  bb[:, VA + kc * 130 + hh * 65:VA + kc * 130 + hh * 65 + 65],
                                  bb[:, ex:ex + NQh],
                                  start=(j == 0), stop=(j == nj - 1))
            c.emit("pe", fpv, reads=b_va + exb, writes=[obuf])

        def kcs_for(h, qbase, NQh):
            slope = 2.0 ** (-(h + 1))
            kc_hi = (qbase + NQh - 1) // 128 + 1
            return [kc for kc in range(kc_hi)
                    if slope * ((qbase // 128 - kc) * 128 - 127) <= 200.0]

        for hh in range(2):
            h = 2 * pair + hh
            if h >= 2:
                continue
            NQh = 128 if h == 0 else 256
            for qc in range(S // NQh):
                qbase = qc * NQh
                ob, obuf = (p.PB[2], p.PBbuf[2]) if (qc + hh) % 2 == 0 else (p.PB[3], p.PBbuf[3])
                kcs = kcs_for(h, qbase, NQh)
                pend_pv = None
                for j, kc in enumerate(kcs):
                    sa, sabuf = (p.PA[0], p.PAbuf[0]) if (j // 2) % 2 == 0 else (p.PA[1], p.PAbuf[1])
                    soff = (j % 2) * 512
                    l1a_sc(hh, kc, sa, soff, qbase, NQh, sabuf)
                    if j == 1 and pending:
                        pending.pop(0)()
                    ex = EXL + (j % 4) * 512
                    exb = l1a_exp(hh, kc, sa, soff, sabuf, ex, qbase, NQh)
                    if pend_pv is not None:
                        pend_pv()
                    pend_pv = (lambda hh=hh, kc=kc, j=j, nj=len(kcs), ex=ex, exb=exb,
                               NQh=NQh, ob=ob, obuf=obuf:
                               l1a_pv(hh, kc, j, nj, ex, exb, NQh, ob, obuf))
                pend_pv()
                pending.append(lambda ob=ob, obuf=obuf, h=h, qbase=qbase, NQh=NQh:
                               l1a_tail(ob, obuf, h, qbase, NQh))
        if pair >= 1:
            for qc in range(S // 512):
                qbase = qc * 512
                obs = [(p.PB[2], p.PBbuf[2]), (p.PB[3], p.PBbuf[3])]
                kcs2 = [kcs_for(2 * pair, qbase, 512), kcs_for(2 * pair + 1, qbase, 512)]
                nmax = max(len(kcs2[0]), len(kcs2[1]))
                pend_pv = None
                for j in range(nmax):
                    for hh in range(2):
                        if j < len(kcs2[hh]):
                            sa, sabuf = p.PA[hh], p.PAbuf[hh]
                            soff = (j % 2) * 512
                            l1a_sc(hh, kcs2[hh][j], sa, soff, qbase, 512, sabuf)
                    if j == 1:
                        while pending:
                            pending.pop(0)()
                    pvs = []
                    for hh in range(2):
                        if j < len(kcs2[hh]):
                            sa, sabuf = p.PA[hh], p.PAbuf[hh]
                            soff = (j % 2) * 512
                            ex = EXL + hh * 1024 + soff
                            exb = l1a_exp(hh, kcs2[hh][j], sa, soff, sabuf, ex, qbase, 512)
                            pvs.append((hh, kcs2[hh][j], j, len(kcs2[hh]), ex, exb,
                                        obs[hh][0], obs[hh][1]))
                    if pend_pv is not None:
                        pend_pv()
                    def mk_pv(pvs=pvs):
                        def emit_pv():
                            for (hh, kc, j_, nj, ex, exb, ob, obuf) in pvs:
                                l1a_pv(hh, kc, j_, nj, ex, exb, 512, ob, obuf)
                        return emit_pv
                    pend_pv = mk_pv()
                pend_pv()
                for hh in range(2):
                    pending.append(lambda ob=obs[hh][0], obuf=obs[hh][1],
                                   h=2 * pair + hh, qbase=qbase:
                                   l1a_tail(ob, obuf, h, qbase, 512))
        while pending:
            pending.pop(0)()
        p.sm_ring.unscope(_pscope)

    wo = p.wload(W["w_proj"], 1024, 1024)
    for t in range(NT):
        for half in range(2):
            pbk, pbuf = (p.PB[0], p.PBbuf[0]) if half == 0 else (p.PB[1], p.PBbuf[1])
            steps = []
            for kc in range(8):
                (wt, wlo, wpr, wcols, wbufs) = wo[kc]
                steps.append((p.AB.t[:, kc * S + t * 128: kc * S + t * 128 + 128],
                              wt[0:128, wlo + half * 512: wlo + half * 512 + 512],
                              list(wbufs) + p.AB.bufs_for(kc * S + t * 128, kc * S + t * 128 + 128)))
            steps.append((p.ONE_ROW[0:1, 0:128],
                          st[0:1, lo_bp + half * 512:lo_bp + half * 512 + 512],
                          [p.ONE_ROW_buf] + b_bp))
            c.mm(pbk[:, :], pbuf, steps)
            rb = p.RES.bufs[t]
            rap = p.RES.t[:, t * 1024 + half * 512: t * 1024 + half * 512 + 512]
            c.tt("dve", rap, rap, pbk[:, :], ALU.add, reads=[pbuf, rb], writes=[rb])
    p.sm_ring.unscope(_scope)


def emit_l0_attn_b(p, W):
    c = p.ctx
    S, NT = p.S, p.NT
    NKC = S // 128
    NQ = S // 512
    st = p.SM.t
    bb = p.BB.t
    _scope = p.sm_ring.scope()

    p.ln_xn()

    lo_s, b_s = p.sm_ring.persist(48)
    c.dma("pool", st[0:1, lo_s:lo_s + 16], W["scale"], writes=b_s)
    c.act(st[0:1, lo_s + 16:lo_s + 32], st[0:1, lo_s:lo_s + 16], AF.Exp,
          reads=b_s, writes=b_s)
    c.ts("dve", st[0:1, lo_s + 32:lo_s + 48], st[0:1, lo_s + 16:lo_s + 32], -1.0, None,
         ALU.mult, None, reads=b_s, writes=b_s)
    (lo_sc, b_sc), (lo_bi, b_bi) = p.head_tables(
        (st[0:1, lo_s + 16:lo_s + 32], b_s), (st[0:1, lo_s + 32:lo_s + 48], b_s), [])

    lo_bo, b_bo = p.sm_ring.persist(1024)
    c.dma("pool", st[0:1, lo_bo:lo_bo + 1024], W["b_out"], writes=b_bo)

    QT, KT, VA, EXL, SQ = 0, 2048, 4096, 6400, 10496

    def qkv_pair(pair):
        rinv = {}
        wq = p.wload(W["w_q"][:, pair * 128:(pair + 1) * 128], 1024, 128)
        wk = p.wload(W["w_kv"][:, pair * 128:(pair + 1) * 128], 1024, 128)
        wv = p.wload(W["w_kv"][:, 1024 + pair * 128:1024 + (pair + 1) * 128], 1024, 128)
        b_qT = p.BB.bufs_for(QT, QT + S)
        b_kT = p.BB.bufs_for(KT, KT + S)
        b_va = p.BB.bufs_for(VA, VA + NT * 130)
        for wi, (wtiles, dstcol, dstbufs) in enumerate(((wq, QT, b_qT), (wk, KT, b_kT))):
            for tc in range(NQ):
                pbk, pbuf = (p.PB[0], p.PBbuf[0]) if tc % 2 == 0 else (p.PB[1], p.PBbuf[1])
                steps = []
                for kc in range(8):
                    (wt, wlo, wpr, wcols, wbufs) = wtiles[kc]
                    steps.append((wt[0:128, wlo:wlo + 128],
                                  p.AA.t[:, kc * S + tc * 512: kc * S + tc * 512 + 512],
                                  list(wbufs) + p.AA.bufs_for(kc * S + tc * 512, kc * S + tc * 512 + 512)))
                c.mm(pbk[:, :], pbuf, steps)
                dst = bb[:, dstcol + tc * 512: dstcol + tc * 512 + 512]
                c.copy("act", dst, pbk[:, :], reads=[pbuf], writes=dstbufs)
                sq_col = SQ + (wi * NQ + tc) % 3 * 512
                sqb = p.BB.bufs_for(sq_col, sq_col + 512)
                c.tt("pool", bb[:, sq_col:sq_col + 512], dst, dst, ALU.mult,
                     reads=dstbufs, writes=sqb)
                for hh in range(2):
                    pn, pnb = (p.PB[2], p.PBbuf[2]) if (tc + hh) % 2 == 0 else (p.PB[3], p.PBbuf[3])
                    def fn(eng, hh=hh, pn=pn, sq_col=sq_col):
                        return eng.matmul(pn[0:1, 0:512], p.MASKHALF[:, hh:hh + 1],
                                          bb[:, sq_col:sq_col + 512], start=True, stop=True)
                    c.emit("pe", fn, reads=sqb + [p.MASKHALF_buf], writes=[pnb])
                    lo_r, b_r = p.sm_ring.alloc(512)
                    p.rsqrt(st[0:1, lo_r:lo_r + 512], pn[0:1, 0:512], 1e-24,
                            reads=[pnb], writes=b_r)
                    rinv[(wi, tc, hh)] = (lo_r, b_r)
        bi = 0
        for wi, (dstcol, dstbufs) in enumerate(((QT, b_qT), (KT, b_kT))):
            for tc in range(NQ):
                for hh in range(2):
                    lo_r, b_r = rinv[(wi, tc, hh)]
                    pa_, pab_ = (p.PA[0], p.PAbuf[0]) if bi % 2 == 0 else (p.PA[1], p.PAbuf[1])
                    boff = (bi // 2 % 2) * 512
                    bi += 1
                    def fbc(eng, lo_r=lo_r, pa_=pa_, boff=boff):
                        return eng.matmul(pa_[0:64, boff:boff + 512], p.ONE_ROW[0:1, 0:64],
                                          st[0:1, lo_r:lo_r + 512], start=True, stop=True)
                    c.emit("pe", fbc, reads=[p.ONE_ROW_buf] + b_r, writes=[pab_])
                    sub = bb[hh * 64:hh * 64 + 64, dstcol + tc * 512: dstcol + tc * 512 + 512]
                    def fmu(eng, sub=sub, pa_=pa_, boff=boff):
                        return eng.tensor_tensor(sub, sub, pa_[0:64, boff:boff + 512], ALU.mult)
                    c.emit("dve", fmu, reads=dstbufs + [pab_], writes=dstbufs)
        for tt_ in range(NT):
            pbk, pbuf = (p.PB[0], p.PBbuf[0]) if tt_ % 2 == 0 else (p.PB[1], p.PBbuf[1])
            steps = []
            for kc in range(8):
                (wt, wlo, wpr, wcols, wbufs) = wv[kc]
                steps.append((p.AA.t[:, kc * S + tt_ * 128: kc * S + tt_ * 128 + 128],
                              wt[0:128, wlo:wlo + 128],
                              list(wbufs) + p.AA.bufs_for(kc * S + tt_ * 128, kc * S + tt_ * 128 + 128)))
            c.mm(pbk[:, 0:128], pbuf, steps)
            for hh in range(2):
                c.copy("act", bb[:, VA + tt_ * 130 + hh * 65: VA + tt_ * 130 + hh * 65 + 64],
                       pbk[:, hh * 64:hh * 64 + 64], reads=[pbuf], writes=b_va)
        ones_ap = bb[:, VA:VA + NT * 130].rearrange("p (t h d) -> p t h d", t=NT, h=2)[:, :, :, 64:65]
        c.memset("dve", ones_ap, 1.0, writes=b_va)
        return b_qT, b_kT, b_va

    for pair in range(8):
        _pscope = p.sm_ring.scope()
        b_qT, b_kT, b_va = qkv_pair(pair)
        pending = []
        for qc in range(NQ):
            obs = [(p.PB[2], p.PBbuf[2]), (p.PB[3], p.PBbuf[3])]
            exi = 0
            pend_pv = None
            for j0 in range(0, NKC, 2):
                pk = list(range(j0, min(j0 + 2, NKC)))
                exoff = [EXL + (exi % 2) * 1024, EXL + 2048 + (exi % 2) * 1024]
                exi += 1
                for hh in range(2):
                    qrow0 = hh * 64
                    sa, sabuf = p.PA[hh], p.PAbuf[hh]
                    for i2, kc in enumerate(pk):
                        def fsc(eng, kc=kc, i2=i2, sa=sa, qrow0=qrow0, qc=qc):
                            return eng.matmul(
                                sa[:, i2 * 512:i2 * 512 + 512],
                                bb[qrow0:qrow0 + 64, KT + kc * 128:KT + kc * 128 + 128],
                                bb[qrow0:qrow0 + 64, QT + qc * 512:QT + qc * 512 + 512],
                                start=True, stop=True)
                        c.emit("pe", fsc, reads=b_kT + b_qT, writes=[sabuf])
                if j0 == 2:
                    while pending:
                        pending.pop(0)()
                w2 = 512 * len(pk)
                for hh in range(2):
                    h = 2 * pair + hh
                    sa, sabuf = p.PA[hh], p.PAbuf[hh]
                    ex = exoff[hh]
                    exb = p.BB.bufs_for(ex, ex + w2)
                    c.act(bb[:, ex:ex + w2], sa[:, 0:w2], AF.Exp,
                          bias=st[:, lo_bi + h:lo_bi + h + 1],
                          scale=st[:, lo_sc + h:lo_sc + h + 1],
                          reads=[sabuf] + b_sc + b_bi, writes=exb)
                if pend_pv is not None:
                    pend_pv()

                def mk_pv(pk=pk, exoff=exoff, w2=w2, obs=obs):
                    def emit_pv():
                        for hh in range(2):
                            ob, obuf = obs[hh]
                            ex = exoff[hh]
                            exb = p.BB.bufs_for(ex, ex + w2)
                            for i2, kc in enumerate(pk):
                                def fpv(eng, kc=kc, i2=i2, ex=ex, ob=ob, hh=hh,
                                        first=(kc == 0), last=(kc == NKC - 1)):
                                    return eng.matmul(ob[0:65, :],
                                                      bb[:, VA + kc * 130 + hh * 65:VA + kc * 130 + hh * 65 + 65],
                                                      bb[:, ex + i2 * 512:ex + i2 * 512 + 512],
                                                      start=first, stop=last)
                                c.emit("pe", fpv, reads=b_va + exb, writes=[obuf])
                    return emit_pv

                pend_pv = mk_pv()
            pend_pv()
            for hh in range(2):
                ob, obuf = obs[hh]
                h = 2 * pair + hh

                def tail(ob=ob, obuf=obuf, h=h, qc=qc):
                    lo_rr, b_rr = p.sm_ring.alloc(512)
                    p.rsqrt(st[0:1, lo_rr:lo_rr + 512], ob[64:65, :], 0.0,
                            reads=[obuf], writes=b_rr, power=-1.0)
                    och = h // 2
                    orow = (h % 2) * 64
                    ocol = och * S + qc * 512
                    ob_bufs = p.AB.bufs_for(ocol, ocol + 512)
                    dstab = p.AB.t[orow:orow + 64, ocol:ocol + 512]
                    c.copy("act", dstab, ob[0:64, :], reads=[obuf], writes=ob_bufs)
                    bcb, bcbuf = p.PB[1], p.PBbuf[1]
                    def fbc(eng):
                        return eng.matmul(bcb[0:64, :], p.ONE_ROW[0:1, 0:64],
                                          st[0:1, lo_rr:lo_rr + 512], start=True, stop=True)
                    c.emit("pe", fbc, reads=[p.ONE_ROW_buf] + b_rr, writes=[bcbuf])
                    c.tt("dve", dstab, dstab, bcb[0:64, :], ALU.mult,
                         reads=ob_bufs + [bcbuf], writes=ob_bufs)
                pending.append(tail)
        while pending:
            pending.pop(0)()
        p.sm_ring.unscope(_pscope)

    wo = p.wload(W["w_out"], 1024, 1024)
    for t in range(NT):
        for half in range(2):
            pbk, pbuf = (p.PB[0], p.PBbuf[0]) if half == 0 else (p.PB[1], p.PBbuf[1])
            steps = []
            for kc in range(8):
                (wt, wlo, wpr, wcols, wbufs) = wo[kc]
                steps.append((p.AB.t[:, kc * S + t * 128: kc * S + t * 128 + 128],
                              wt[0:128, wlo + half * 512: wlo + half * 512 + 512],
                              list(wbufs) + p.AB.bufs_for(kc * S + t * 128, kc * S + t * 128 + 128)))
            steps.append((p.ONE_ROW[0:1, 0:128],
                          st[0:1, lo_bo + half * 512:lo_bo + half * 512 + 512],
                          [p.ONE_ROW_buf] + b_bo))
            c.mm(pbk[:, :], pbuf, steps)
            rb = p.RES.bufs[t]
            rap = p.RES.t[:, t * 1024 + half * 512: t * 1024 + half * 512 + 512]
            c.tt("dve", rap, rap, pbk[:, :], ALU.add, reads=[pbuf, rb], writes=[rb])
    p.sm_ring.unscope(_scope)


def emit_l0_ffn(p, W, dbg=None):
    c = p.ctx
    S, NT = p.S, p.NT
    NQ = S // 512
    st = p.SM.t
    bb = p.BB.t
    _scope = p.sm_ring.scope()
    _wscope = p.wr_ring.scope()

    p.ln_xn()

    _, lo_encb, _, b_encb = p.load_bias_col(W["enc_b"].rearrange("a n -> (a n)"), 2048, "encb")
    _, lo_s2cb, _, b_s2cb = p.load_bias_col(W["s2c_b"].rearrange("a n -> (a n)"), 64, "s2cb")
    _, lo_rb1, _, b_rb1 = p.load_bias_col(W["rb1"].rearrange("a n -> (a n)"), 64, "rb1")
    _, lo_c2sb, _, b_c2sb = p.load_bias_col(W["c2s_b"].rearrange("a n -> (a n)"), 2048, "c2sb")
    be1T = []
    for e in range(8):
        _, lo_e, _, b_e = p.load_bias_col(W["be1"][e:e + 1, :].rearrange("a n -> (a n)"), 1024, f"be1{e}")
        be1T.append((lo_e, b_e))
    lo_rows, b_rows = p.sm_ring.persist(8 + 8 * 64 + 1024)
    lo_rb2 = lo_rows
    lo_be2 = lo_rows + 8
    lo_decb = lo_be2 + 8 * 64
    c.dma("pool", st[0:1, lo_rb2:lo_rb2 + 8], W["rb2"], writes=b_rows)
    c.dma("pool", st[0:1, lo_be2:lo_be2 + 8 * 64], W["be2"].rearrange("e n -> (e n)").rearrange("(a n) -> a n", a=1), writes=b_rows)
    c.dma("pool", st[0:1, lo_decb:lo_decb + 1024], W["dec_b"], writes=b_rows)

    CONT, R1, SMT = 0, 2048, 4096
    b_cont = p.BB.bufs_for(CONT, CONT + S)
    b_r1 = p.BB.bufs_for(R1, R1 + S)
    b_smt = p.BB.bufs_for(SMT, SMT + 8192)

    assert NQ <= 4
    pending_ct = []
    for m in range(16):
        par = m % 2
        SROW = SMT + par * 2048
        b_srow_m = [p.BB.bufs_for(SROW + tc * 512, SROW + tc * 512 + 512)
                    for tc in range(NQ)]
        encw = p.wload(W["enc_W"][:, m * 128:(m + 1) * 128], 1024, 128)
        s2c_m = p.wload(W["s2c_W"][m * 128:(m + 1) * 128, :], 128, 64)
        for tc in range(NQ):
            pbk, pbuf = (p.PB[0], p.PBbuf[0]) if tc % 2 == 0 else (p.PB[1], p.PBbuf[1])
            steps = []
            for kc in range(8):
                (wt, wlo, wpr, wcols, wbufs) = encw[kc]
                steps.append((wt[0:128, wlo:wlo + 128],
                              p.AA.t[:, kc * S + tc * 512: kc * S + tc * 512 + 512],
                              list(wbufs) + p.AA.bufs_for(kc * S + tc * 512, kc * S + tc * 512 + 512)))
            c.mm(pbk[:, :], pbuf, steps)
            if pending_ct:
                pending_ct.pop(0)()
            def fsg(eng, pbk=pbk, tc=tc, m=m, SROW=SROW):
                return eng.activation(bb[:, SROW + tc * 512:SROW + tc * 512 + 512],
                                      pbk[:, :], AF.Sigmoid,
                                      bias=st[:, lo_encb + m:lo_encb + m + 1])
            c.emit("act", fsg, reads=[pbuf] + b_encb, writes=b_srow_m[tc])

            def fct_tail(m=m, tc=tc, SROW=SROW, s2c_m=s2c_m, srb=b_srow_m[tc]):
                (wt, wlo, wpr, wcols, wbufs) = s2c_m[0]
                cb, cbuf = p.PA[tc // 2], p.PAbuf[tc // 2]
                coff = (tc % 2) * 512
                def fct(eng):
                    return eng.matmul(cb[0:64, coff:coff + 512], wt[0:128, wlo:wlo + 64],
                                      bb[:, SROW + tc * 512:SROW + tc * 512 + 512],
                                      start=(m == 0), stop=(m == 15))
                c.emit("pe", fct, reads=list(wbufs) + srb, writes=[cbuf])
            pending_ct.append(fct_tail)
    while pending_ct:
        pending_ct.pop(0)()
    for tc2 in range(0, NQ, 2):
        cb, cbuf = p.PA[tc2 // 2], p.PAbuf[tc2 // 2]
        w2 = min(1024, (NQ - tc2) * 512)
        def fev(eng, cb=cb, tc2=tc2, w2=w2):
            return eng.activation(bb[0:64, CONT + tc2 * 512:CONT + tc2 * 512 + w2],
                                  cb[0:64, 0:w2], AF.Identity,
                                  bias=st[0:64, lo_s2cb:lo_s2cb + 1])
        c.emit("act", fev, reads=[cbuf] + b_s2cb, writes=b_cont)

    if dbg is not None:
        dbg("cont", dict(CONT=CONT, b_cont=b_cont, SROW=SROW, b_srow=b_srow,
                         lo_encb=lo_encb, b_encb=b_encb))
    _rscope = p.wr_ring.scope()
    rw1 = p.wload(W["rW1"], 64, 64, persist=True)
    rw2 = p.wload(W["rW2"], 64, 8, persist=True)
    for tc in range(NQ):
        pbk, pbuf = (p.PB[0], p.PBbuf[0]) if tc % 2 == 0 else (p.PB[1], p.PBbuf[1])
        (wt, wlo, wpr, wcols, wbufs) = rw1[0]
        def fr1(eng, wt=wt, wlo=wlo, tc=tc, pbk=pbk):
            return eng.matmul(pbk[0:64, :], wt[0:64, wlo:wlo + 64],
                              bb[0:64, CONT + tc * 512:CONT + tc * 512 + 512],
                              start=True, stop=True)
        c.emit("pe", fr1, reads=list(wbufs) + b_cont, writes=[pbuf])
        def ft1(eng, pbk=pbk, tc=tc):
            return eng.activation(bb[0:64, R1 + tc * 512:R1 + tc * 512 + 512],
                                  pbk[0:64, :], AF.Tanh,
                                  bias=st[0:64, lo_rb1:lo_rb1 + 1])
        c.emit("act", ft1, reads=[pbuf] + b_rb1, writes=b_r1)
    lo_lg, b_lg = p.sm_ring.persist(NT * 8)
    for t in range(NT):
        pbk, pbuf = (p.PB[0], p.PBbuf[0]) if t % 2 == 0 else (p.PB[1], p.PBbuf[1])
        (wt, wlo, wpr, wcols, wbufs) = rw2[0]
        steps = [(bb[0:64, R1 + t * 128:R1 + t * 128 + 128], wt[0:64, wlo:wlo + 8],
                  list(wbufs) + b_r1),
                 (p.ONE_ROW[0:1, 0:128], st[0:1, lo_rb2:lo_rb2 + 8],
                  [p.ONE_ROW_buf] + b_rows)]
        c.mm(pbk[:, 0:8], pbuf, steps)
        c.copy("dve", st[:, lo_lg + t * 8:lo_lg + t * 8 + 8], pbk[:, 0:8],
               reads=[pbuf], writes=b_lg)

    p.wr_ring.unscope(_rscope)

    lo_gt, b_gt = p.sm_ring.persist(NT * 8)
    lo_w1, b_w1 = p.sm_ring.alloc(NT * 8 * 3)
    lo_m, b_m = p.sm_ring.alloc(NT * 4)
    L = st[:, lo_lg:lo_lg + NT * 8].rearrange("p (t e) -> p t e", e=8)
    A = st[:, lo_w1:lo_w1 + NT * 8].rearrange("p (t e) -> p t e", e=8)
    B = st[:, lo_w1 + NT * 8:lo_w1 + NT * 16].rearrange("p (t e) -> p t e", e=8)
    Z = st[:, lo_w1 + NT * 16:lo_w1 + NT * 24].rearrange("p (t e) -> p t e", e=8)
    M1 = st[:, lo_m:lo_m + NT]
    M2 = st[:, lo_m + NT:lo_m + 2 * NT]
    EM2 = st[:, lo_m + 2 * NT:lo_m + 3 * NT]
    DI = st[:, lo_m + 3 * NT:lo_m + 4 * NT]
    M1b = M1.rearrange("p (t o) -> p t o", o=1)
    M2b = M2.rearrange("p (t o) -> p t o", o=1)

    def bcast8(ap_3d):
        import dataclasses as _dc
        new = [list(d) for d in ap_3d.ap]
        new[-1] = [0, 8]
        return _dc.replace(ap_3d, ap=new)

    c.red("dve", M1b, L, ALU.max, reads=b_lg, writes=b_m)
    c.tt("dve", A, L, bcast8(M1b), ALU.is_ge, reads=b_lg + b_m, writes=b_w1)

    def top_from(selA, selbufs, outap, outbufs):
        c.tt("dve", B[:, :, 1:8], selA[:, :, 1:8], selA[:, :, 0:7], ALU.add,
             reads=selbufs, writes=b_w1)
        c.copy("dve", B[:, :, 0:1], selA[:, :, 0:1], reads=selbufs, writes=b_w1)
        c.tt("dve", Z[:, :, 2:8], B[:, :, 2:8], B[:, :, 0:6], ALU.add,
             reads=b_w1, writes=b_w1)
        c.copy("dve", Z[:, :, 0:2], B[:, :, 0:2], reads=b_w1, writes=b_w1)
        c.tt("dve", B[:, :, 4:8], Z[:, :, 4:8], Z[:, :, 0:4], ALU.add,
             reads=b_w1, writes=b_w1)
        c.copy("dve", B[:, :, 0:4], Z[:, :, 0:4], reads=b_w1, writes=b_w1)
        c.tt("dve", Z, B, selA, ALU.subtract, reads=b_w1 + selbufs, writes=b_w1)
        c.ts("dve", Z, Z, 0.0, None, ALU.is_equal, None, reads=b_w1, writes=b_w1)
        c.tt("dve", outap, Z, selA, ALU.mult, reads=b_w1 + selbufs, writes=outbufs)

    lo_f, b_f = p.sm_ring.alloc(NT * 16)
    F1 = st[:, lo_f:lo_f + NT * 8].rearrange("p (t e) -> p t e", e=8)
    F2 = st[:, lo_f + NT * 8:lo_f + NT * 16].rearrange("p (t e) -> p t e", e=8)
    top_from(A, b_w1 + b_lg, F1, b_f)
    def fmk(eng):
        return eng.scalar_tensor_tensor(A, F1, -1e30, L, ALU.mult, ALU.add)
    c.emit("dve", fmk, reads=b_f + b_lg, writes=b_w1)
    c.red("dve", M2b, A, ALU.max, reads=b_w1, writes=b_m)
    lo_a2, b_a2 = p.sm_ring.alloc(NT * 8)
    A2 = st[:, lo_a2:lo_a2 + NT * 8].rearrange("p (t e) -> p t e", e=8)
    c.tt("dve", A2, A, bcast8(M2b), ALU.is_ge, reads=b_w1 + b_m, writes=b_a2)
    top_from(A2, b_a2, F2, b_f)
    c.tt("dve", EM2, M2, M1, ALU.subtract, reads=b_m, writes=b_m)
    c.act(EM2, EM2, AF.Exp, reads=b_m, writes=b_m)
    c.ts("dve", DI, EM2, 1.0, None, ALU.add, None, reads=b_m, writes=b_m)
    def fre(eng):
        return eng.reciprocal(DI, DI)
    c.emit("dve", fre, reads=b_m, writes=b_m)
    G = st[:, lo_gt:lo_gt + NT * 8].rearrange("p (t e) -> p t e", e=8)
    EM2b = EM2.rearrange("p (t o) -> p t o", o=1)
    DIb = DI.rearrange("p (t o) -> p t o", o=1)
    c.tt("dve", G, F2, bcast8(EM2b), ALU.mult, reads=b_f + b_m, writes=b_gt)
    c.tt("dve", G, G, F1, ALU.add, reads=b_gt + b_f, writes=b_gt)
    c.tt("dve", G, G, bcast8(DIb), ALU.mult, reads=b_gt + b_m, writes=b_gt)

    if dbg is not None:
        dbg("gates", dict(lo_lg=lo_lg, b_lg=b_lg, lo_gt=lo_gt, b_gt=b_gt))
    lo_of, b_of = p.sm_ring.persist(NT * 64)
    for e in range(8):
        we1 = p.wload(W["We1"][e:e + 1].rearrange("a b c -> (a b) c"), 64, 1024)
        we2 = p.wload(W["We2"][e:e + 1].rearrange("a b c -> (a b) c"), 1024, 64)
        (w1t, w1lo, _, _, w1bufs) = we1[0]
        lo_be1, b_be1 = be1T[e]
        for mh in range(8):
            for tc in range(NQ):
                pbk, pbuf = (p.PB[0], p.PBbuf[0]) if (mh + tc) % 2 == 0 else (p.PB[1], p.PBbuf[1])
                def fh(eng, mh=mh, tc=tc, pbk=pbk):
                    return eng.matmul(pbk[:, :],
                                      w1t[0:64, w1lo + mh * 128:w1lo + mh * 128 + 128],
                                      bb[0:64, CONT + tc * 512:CONT + tc * 512 + 512],
                                      start=True, stop=True)
                c.emit("pe", fh, reads=list(w1bufs) + b_cont, writes=[pbuf])
                dst = p.AB.t[:, mh * S + tc * 512: mh * S + tc * 512 + 512]
                dstb = p.AB.bufs_for(mh * S + tc * 512, mh * S + tc * 512 + 512)
                def fre_(eng, pbk=pbk, dst=dst, mh=mh):
                    return eng.activation(dst, pbk[:, :], AF.Relu,
                                          bias=st[:, lo_be1 + mh:lo_be1 + mh + 1])
                c.emit("act", fre_, reads=[pbuf] + b_be1, writes=dstb)
        for t in range(NT):
            pbk, pbuf = (p.PB[2], p.PBbuf[2]) if t % 2 == 0 else (p.PB[3], p.PBbuf[3])
            steps = []
            for mh in range(8):
                (w2t, w2lo, _, _, w2bufs) = we2[mh]
                steps.append((p.AB.t[:, mh * S + t * 128: mh * S + t * 128 + 128],
                              w2t[0:128, w2lo:w2lo + 64],
                              list(w2bufs) + p.AB.bufs_for(mh * S + t * 128, mh * S + t * 128 + 128)))
            steps.append((p.ONE_ROW[0:1, 0:128], st[0:1, lo_be2 + e * 64:lo_be2 + e * 64 + 64],
                          [p.ONE_ROW_buf] + b_rows))
            c.mm(pbk[:, 0:64], pbuf, steps)
            ofap = st[:, lo_of + t * 64:lo_of + t * 64 + 64]
            gap = st[:, lo_gt + t * 8 + e:lo_gt + t * 8 + e + 1]
            if e == 0:
                def fg0(eng, pbk=pbk, ofap=ofap, gap=gap):
                    return eng.tensor_scalar(ofap, pbk[:, 0:64], gap, None, ALU.mult)
                c.emit("dve", fg0, reads=[pbuf] + b_gt, writes=b_of)
            else:
                def fga(eng, pbk=pbk, ofap=ofap, gap=gap):
                    return eng.scalar_tensor_tensor(ofap, pbk[:, 0:64], gap, ofap,
                                                    ALU.mult, ALU.add)
                c.emit("dve", fga, reads=[pbuf] + b_gt + b_of, writes=b_of)

    if dbg is not None:
        dbg("of", dict(lo_of=lo_of, b_of=b_of))
    OFT = R1
    b_oft = p.BB.bufs_for(OFT, OFT + S)
    for t in range(NT):
        pbk, pbuf = (p.PB[2], p.PBbuf[2]) if t % 2 == 0 else (p.PB[3], p.PBbuf[3])
        c.transpose(pbk[0:64, 0:128], st[:, lo_of + t * 64:lo_of + t * 64 + 64],
                    p.ID_F32[:, :], reads=b_of, writes=[pbuf], ident_buf=p.IDF_buf)
        c.copy("act", bb[0:64, OFT + t * 128:OFT + t * 128 + 128], pbk[0:64, 0:128],
               reads=[pbuf], writes=b_oft)

    GB, BBIAS = 0, 1024
    b_gbc = p.BB.bufs_for(GB, GB + 1024)
    b_bbc = p.BB.bufs_for(BBIAS, BBIAS + 1024)
    lo_grow, b_grow = p.sm_ring.alloc(2048)
    c.dma("pool", st[0:1, lo_grow:lo_grow + 1024], W["ln_g"], writes=b_grow)
    c.dma("pool", st[0:1, lo_grow + 1024:lo_grow + 2048], W["ln_b"], writes=b_grow)
    for (dstcol, dbufs, off) in ((GB, b_gbc, 0), (BBIAS, b_bbc, 1024)):
        for half in range(2):
            pbk, pbuf = p.PB[3], p.PBbuf[3]
            def fb(eng, off=off, half=half):
                return eng.matmul(pbk[:, :], p.ONE_ROW[0:1, 0:128],
                                  st[0:1, lo_grow + off + half * 512:lo_grow + off + half * 512 + 512],
                                  start=True, stop=True)
            c.emit("pe", fb, reads=[p.ONE_ROW_buf] + b_grow, writes=[pbuf])
            c.copy("dve", bb[:, dstcol + half * 512:dstcol + half * 512 + 512],
                   pbk[:, :], reads=[pbuf], writes=dbufs)

    b_decw_ab = []
    for m in range(16):
        dst = p.AB.t[:, m * 1024:(m + 1) * 1024]
        dbufs = p.AB.bufs_for(m * 1024, (m + 1) * 1024)
        c.dma("sp", dst, W["dec_W"][m * 128:(m + 1) * 128, :], writes=dbufs)
        b_decw_ab.append(dbufs)

    c2s = p.wload(W["c2s_W"], 64, 2048, persist=True)
    (c2t, c2lo, _, _, c2bufs) = c2s[0]
    for T in range(NQ):
        for m in range(16):
            pbk, pbuf = (p.PB[0], p.PBbuf[0]) if m % 2 == 0 else (p.PB[1], p.PBbuf[1])
            def fsm(eng, m=m, T=T, pbk=pbk):
                return eng.matmul(pbk[:, :], c2t[0:64, c2lo + m * 128:c2lo + m * 128 + 128],
                                  bb[0:64, OFT + T * 512:OFT + T * 512 + 512],
                                  start=True, stop=True)
            c.emit("pe", fsm, reads=list(c2bufs) + b_oft, writes=[pbuf])
            def fsg2(eng, pbk=pbk, m=m, T=T):
                return eng.activation(bb[:, SMT + m * 512:SMT + m * 512 + 512],
                                      pbk[:, :], AF.Sigmoid,
                                      bias=st[:, lo_c2sb + m:lo_c2sb + m + 1])
            c.emit("act", fsg2, reads=[pbuf] + b_c2sb, writes=b_smt)
        for t4 in range(4):
            t = T * 4 + t4
            lo_dec, b_dec = p.sm_ring.alloc(1024)
            for half in range(2):
                pbk, pbuf = (p.PB[2], p.PBbuf[2]) if half == 0 else (p.PB[3], p.PBbuf[3])
                steps = []
                for m in range(16):
                    steps.append((bb[:, SMT + m * 512 + t4 * 128:SMT + m * 512 + t4 * 128 + 128],
                                  p.AB.t[:, m * 1024 + half * 512:m * 1024 + half * 512 + 512],
                                  b_decw_ab[m] + b_smt))
                steps.append((p.ONE_ROW[0:1, 0:128],
                              st[0:1, lo_decb + half * 512:lo_decb + half * 512 + 512],
                              [p.ONE_ROW_buf] + b_rows))
                c.mm(pbk[:, :], pbuf, steps)
                def fdec(eng, pbk=pbk, half=half, lo_dec=lo_dec):
                    return eng.activation(st[:, lo_dec + half * 512:lo_dec + half * 512 + 512],
                                          pbk[:, :], AF.Sigmoid)
                c.emit("act", fdec, reads=[pbuf], writes=b_dec)
            lo_st2, b_st2 = p.sm_ring.alloc(16)
            dtile = st[:, lo_dec:lo_dec + 1024]
            def f1(eng):
                return eng.bn_stats(st[:, lo_st2:lo_st2 + 6], dtile[:, 0:512])
            def f2(eng):
                return eng.bn_stats(st[:, lo_st2 + 6:lo_st2 + 12], dtile[:, 512:1024])
            def f3(eng):
                return eng.bn_aggr(st[:, lo_st2 + 12:lo_st2 + 14], st[:, lo_st2:lo_st2 + 12])
            c.emit("dve", f1, reads=b_dec, writes=b_st2)
            c.emit("dve", f2, reads=b_dec, writes=b_st2)
            c.emit("dve", f3, reads=b_st2, writes=b_st2)
            p.rsqrt(st[:, lo_st2 + 14:lo_st2 + 15], st[:, lo_st2 + 13:lo_st2 + 14], EPS,
                    reads=b_st2, writes=b_st2)
            c.ts("dve", st[:, lo_st2 + 15:lo_st2 + 16], st[:, lo_st2 + 12:lo_st2 + 13],
                 st[:, lo_st2 + 14:lo_st2 + 15], -1.0, ALU.mult, ALU.mult,
                 reads=b_st2, writes=b_st2)
            c.ts("dve", dtile, dtile, st[:, lo_st2 + 14:lo_st2 + 15],
                 st[:, lo_st2 + 15:lo_st2 + 16], ALU.mult, ALU.add,
                 reads=b_dec + b_st2, writes=b_dec)
            rb = p.RES.bufs[t]
            rap = p.RES.t[:, t * 1024:(t + 1) * 1024]
            def fg(eng):
                return eng.tensor_tensor(dtile, dtile, bb[:, GB:GB + 1024], ALU.mult)
            c.emit("dve", fg, reads=b_dec + b_gbc, writes=b_dec)
            def fb2(eng):
                return eng.tensor_tensor(dtile, dtile, bb[:, BBIAS:BBIAS + 1024], ALU.add)
            c.emit("dve", fb2, reads=b_dec + b_bbc, writes=b_dec)
            c.tt("dve", rap, rap, dtile, ALU.add, reads=b_dec + [rb], writes=[rb])
    p.sm_ring.unscope(_scope)
    p.wr_ring.unscope(_wscope)


def _bcast_blocks(ap_3d, blk):
    import dataclasses as _dc
    new = [list(d) for d in ap_3d.ap]
    new[-1] = [0, blk]
    return _dc.replace(ap_3d, ap=new)


def emit_fp4_quant(p, out_ap, out_bufs, in_ap, in_bufs, scr_lo, scr_bufs, width,
                   pr=128):
    c = p.ctx
    st = p.SM.t
    nb = width // 16
    am = st[0:pr, scr_lo:scr_lo + nb]
    rr = st[0:pr, scr_lo + nb:scr_lo + 2 * nb]
    xa = st[0:pr, scr_lo + 2 * nb:scr_lo + 2 * nb + width]
    a05 = st[0:pr, scr_lo + 2 * nb + width:scr_lo + 2 * nb + 2 * width]
    a1 = st[0:pr, scr_lo + 2 * nb + 2 * width:scr_lo + 2 * nb + 3 * width]
    def fab(eng):
        return eng.activation(xa, in_ap, AF.Abs)
    c.emit("act", fab, reads=in_bufs, writes=scr_bufs)
    xav = xa.rearrange("p (nb b) -> p nb b", b=16)
    c.red("dve", am.rearrange("p (nb o) -> p nb o", o=1), xav, ALU.max,
          reads=scr_bufs, writes=scr_bufs)
    c.ts("dve", am, am, 1.0 / 6.0, 1e-12, ALU.mult, ALU.max, reads=scr_bufs,
         writes=scr_bufs)
    def fr(eng):
        return eng.reciprocal(rr, am)
    c.emit("dve", fr, reads=scr_bufs, writes=scr_bufs)
    rb = _bcast_blocks(rr.rearrange("p (nb o) -> p nb o", o=1), 16)
    def fxa(eng):
        return eng.tensor_tensor(xav, xav, rb, ALU.mult)
    c.emit("dve", fxa, reads=scr_bufs, writes=scr_bufs)
    c.ts("dve", a05, xa, 0.25, None, ALU.is_gt, None, reads=scr_bufs, writes=scr_bufs)
    for t in (0.75, 1.25, 1.75):
        def fs(eng, t=t):
            return eng.scalar_tensor_tensor(a05, xa, t, a05, ALU.is_gt, ALU.add)
        c.emit("dve", fs, reads=scr_bufs, writes=scr_bufs)
    c.ts("dve", a1, xa, 2.5, None, ALU.is_gt, None, reads=scr_bufs, writes=scr_bufs)
    for t in (3.5, 5.0, 5.0):
        def fs2(eng, t=t):
            return eng.scalar_tensor_tensor(a1, xa, t, a1, ALU.is_gt, ALU.add)
        c.emit("dve", fs2, reads=scr_bufs, writes=scr_bufs)
    def fqn(eng):
        return eng.scalar_tensor_tensor(a05, a05, 0.5, a1, ALU.mult, ALU.add)
    c.emit("dve", fqn, reads=scr_bufs, writes=scr_bufs)
    def fsg(eng):
        return eng.activation(out_ap, in_ap, AF.Sign)
    c.emit("act", fsg, reads=in_bufs, writes=out_bufs)
    ab_ = _bcast_blocks(am.rearrange("p (nb o) -> p nb o", o=1), 16)
    def fss(eng):
        return eng.tensor_tensor(xa.rearrange("p (nb b) -> p nb b", b=16),
                                 out_ap.rearrange("p (nb b) -> p nb b", b=16), ab_,
                                 ALU.mult)
    c.emit("dve", fss, reads=out_bufs + scr_bufs, writes=scr_bufs)
    c.tt("dve", out_ap, a05, xa, ALU.mult, reads=scr_bufs, writes=out_bufs)


def emit_fp4_weight_prep(p, src_aps, dst_aps):
    c = p.ctx
    st = p.SM.t
    bb = p.BB.t
    _scope = p.sm_ring.scope()
    for (name, rows, cols) in (("w_gate", 2816, 1024), ("w_up", 2816, 1024),
                               ("w_down", 1024, 2816)):
        src = src_aps[name]
        dst = dst_aps[name]
        nrt = (rows + 127) // 128
        for rt in range(nrt):
            pr = min(128, rows - rt * 128)
            for h0 in range(0, cols, 1024):
                w = min(1024, cols - h0)
                lo_in, b_in_ = p.sm_ring.alloc(w)
                c.dma("pool", st[0:pr, lo_in:lo_in + w], src[rt * 128:rt * 128 + pr, h0:h0 + w],
                      writes=b_in_)
                lo_s, b_s = p.sm_ring.alloc(4 * w + 2 * (w // 16))
                qb = p.BB.bufs_for(h0, h0 + w)
                emit_fp4_quant(p, bb[0:pr, h0:h0 + w], qb, st[0:pr, lo_in:lo_in + w],
                               b_in_, lo_s, b_s, w, pr=pr)
            nch = cols // 128
            dstT = bb[:, 4096:4096 + nch * 128].rearrange("p (ch t) -> p ch t", ch=nch)[:, :, 0:pr]
            dstTb = p.BB.bufs_for(4096, 4096 + nch * 128)
            c.dma("sp", dstT, bb[0:pr, 0:cols], reads=p.BB.bufs_for(0, cols),
                  writes=dstTb, transpose=True)
            ddst = dst[0:cols, rt * 128:rt * 128 + pr].rearrange("(ch pp) t -> pp ch t", pp=128)
            c.dma("sp", ddst, dstT, reads=dstTb)
    p.sm_ring.unscope(_scope)


def emit_l1_ffn(p, W):
    c = p.ctx
    S, NT = p.S, p.NT
    NQ = S // 512
    st = p.SM.t
    bb = p.BB.t
    _scope = p.sm_ring.scope()

    p.ln_xn(want_xnT=False)

    def quant_tile(t):
        xn_ap = p.AB.t[:, t * 1024:(t + 1) * 1024]
        xnb = p.AB.bufs_for(t * 1024, (t + 1) * 1024)
        lo_s, b_s = p.sm_ring.alloc(4 * 1024 + 128)
        emit_fp4_quant(p, xn_ap, xnb, xn_ap, xnb, lo_s, b_s, 1024)
        dstT = p.AA.t[:, :].rearrange("p (ch s) -> p ch s", ch=8)[:, :, t * 128:(t + 1) * 128]
        dstTb = []
        for ch in range(8):
            dstTb += p.AA.bufs_for(ch * S + t * 128, ch * S + (t + 1) * 128)
        c.dma("sp", dstT, xn_ap, reads=xnb, writes=dstTb, transpose=True)

    NB2 = S // 256
    HQB = [0, 5632]

    def emit_gu(b):
        par = b % 2
        for nci in range(6):
            n0 = nci * 512
            nw = min(512, 2816 - n0)
            wg = p.wload(W["wgT"][:, n0:n0 + nw], 1024, nw)
            wu = p.wload(W["wuT"][:, n0:n0 + nw], 1024, nw)
            for t2 in range(2):
                t = 2 * b + t2
                pg, pgb = p.PB[0], p.PBbuf[0]
                pu, pub = p.PB[1], p.PBbuf[1]
                for (wt_, ps_, psb_) in ((wg, pg, pgb), (wu, pu, pub)):
                    steps = []
                    for kc in range(8):
                        (wtt, wlo, wpr, wcols, wbufs) = wt_[kc]
                        steps.append((p.AA.t[:, kc * S + t * 128: kc * S + t * 128 + 128],
                                      wtt[0:128, wlo:wlo + nw],
                                      list(wbufs) + p.AA.bufs_for(kc * S + t * 128, kc * S + t * 128 + 128)))
                    c.mm(ps_[:, 0:nw], psb_, steps)
                lo_sg, b_sg = p.sm_ring.alloc(512)
                def fsl(eng, pg=pg, nw=nw, lo_sg=lo_sg):
                    return eng.activation(st[:, lo_sg:lo_sg + nw], pg[:, 0:nw], AF.Silu)
                c.emit("act", fsl, reads=[pgb], writes=b_sg)
                hq_ap = bb[:, HQB[par] + t2 * 2816 + n0:HQB[par] + t2 * 2816 + n0 + nw]
                hqb = p.BB.bufs_for(HQB[par] + t2 * 2816 + n0, HQB[par] + t2 * 2816 + n0 + nw)
                c.tt("dve", hq_ap, pu[:, 0:nw], st[:, lo_sg:lo_sg + nw], ALU.mult,
                     reads=[pub] + b_sg, writes=hqb)
                lo_s, b_s = p.sm_ring.alloc(4 * nw + 2 * (nw // 16))
                emit_fp4_quant(p, hq_ap, hqb, hq_ap, hqb, lo_s, b_s, nw)

    def emit_quant(b, ab0):
        par = b % 2
        for t2 in range(2):
            hq_ap = bb[:, HQB[par] + t2 * 2816:HQB[par] + t2 * 2816 + 2816]
            hqb = p.BB.bufs_for(HQB[par] + t2 * 2816, HQB[par] + t2 * 2816 + 2816)
            dstT = p.AB.t[:, ab0:ab0 + 22 * 256].rearrange("p (m tt) -> p m tt", m=22)[:, :, t2 * 128:(t2 + 1) * 128]
            dstTb = p.AB.bufs_for(ab0, ab0 + 22 * 256)
            c.dma("sp", dstT, hq_ap, reads=hqb, writes=dstTb, transpose=True)

    def emit_down(b, ab0):
        for half in range(2):
            wd = p.wload(W["wdT"][:, half * 512:half * 512 + 512], 2816, 512)
            for t2 in range(2):
                t = 2 * b + t2
                pbk, pbuf = (p.PB[2], p.PBbuf[2]) if t2 % 2 == 0 else (p.PB[3], p.PBbuf[3])
                steps = []
                for mi in range(22):
                    (wtt, wlo, wpr, wcols, wbufs) = wd[mi]
                    steps.append((p.AB.t[:, ab0 + mi * 256 + t2 * 128:ab0 + mi * 256 + t2 * 128 + 128],
                                  wtt[0:wpr, wlo:wlo + 512],
                                  list(wbufs) + p.AB.bufs_for(ab0 + mi * 256, ab0 + (mi + 1) * 256)))
                c.mm(pbk[:, :], pbuf, steps)
                rb = p.RES.bufs[t]
                rap = p.RES.t[:, t * 1024 + half * 512: t * 1024 + half * 512 + 512]
                c.tt("dve", rap, rap, pbk[:, :], ALU.add, reads=[pbuf, rb], writes=[rb])

    for t in range(min(4, NT)):
        quant_tile(t)
    emit_gu(0)
    for b in range(NB2):
        for t in (2 * b + 4, 2 * b + 5):
            if t < NT:
                quant_tile(t)
        emit_quant(b, 0)
        if b + 1 < NB2:
            emit_gu(b + 1)
        emit_down(b, 0)
    p.sm_ring.unscope(_scope)


def emit_l1_attn_b(p, W, scratch, dbg=None):
    c = p.ctx
    S, NT = p.S, p.NT
    NQ = S // 512
    NKC = S // 128
    st = p.SM.t
    bb = p.BB.t
    _scope = p.sm_ring.scope()
    xn_dram = scratch["xn_bf"]
    y_dram = scratch["y_bf"]

    lo_w1, b_w1 = p.sm_ring.persist(1024)
    lo_b1, b_b1 = p.sm_ring.persist(1024)
    lo_row, b_row = p.sm_ring.alloc(2048)
    c.dma("pool", st[0:1, lo_row:lo_row + 1024], W["ln1_w"], writes=b_row)
    c.dma("pool", st[0:1, lo_row + 1024:lo_row + 2048], W["ln1_b"], writes=b_row)
    for (lo_dst, b_dst, off) in ((lo_w1, b_w1, 0), (lo_b1, b_b1, 1024)):
        for half in range(2):
            pbk, pbuf = p.PB[3], p.PBbuf[3]
            def fb(eng, off=off, half=half):
                return eng.matmul(pbk[:, :], p.ONE_ROW[0:1, 0:128],
                                  st[0:1, lo_row + off + half * 512:lo_row + off + half * 512 + 512],
                                  start=True, stop=True)
            c.emit("pe", fb, reads=[p.ONE_ROW_buf] + b_row, writes=[pbuf])
            c.copy("dve", st[:, lo_dst + half * 512:lo_dst + half * 512 + 512],
                   pbk[:, :], reads=[pbuf], writes=b_dst)

    b_xnd = Buf("xn_dram")

    def ln_s1(t):
        r = p.RES.t[:, t * 1024:(t + 1) * 1024]
        rb = p.RES.bufs[t]
        lo_st, b_st = p.sm_ring.alloc(32)
        def f1(eng):
            return eng.bn_stats(st[:, lo_st:lo_st + 6], r[:, 0:512])
        def f2(eng):
            return eng.bn_stats(st[:, lo_st + 6:lo_st + 12], r[:, 512:1024])
        def f3(eng):
            return eng.bn_aggr(st[:, lo_st + 12:lo_st + 14], st[:, lo_st:lo_st + 12])
        c.emit("dve", f1, reads=[rb], writes=b_st)
        c.emit("dve", f2, reads=[rb], writes=b_st)
        c.emit("dve", f3, reads=b_st, writes=b_st)
        p.rsqrt(st[:, lo_st + 14:lo_st + 15], st[:, lo_st + 13:lo_st + 14], EPS,
                reads=b_st, writes=b_st)
        return lo_st, b_st

    def ln_s2(t, lo_st, b_st):
        mu = st[:, lo_st + 12:lo_st + 13]
        var = st[:, lo_st + 13:lo_st + 14]
        a1_ = st[:, lo_st + 14:lo_st + 15]
        be_ = st[:, lo_st + 15:lo_st + 16]
        u_ = st[:, lo_st + 16:lo_st + 17]
        f_ = st[:, lo_st + 17:lo_st + 18]
        c.ts("dve", be_, mu, a1_, -1.0, ALU.mult, ALU.mult, reads=b_st, writes=b_st)
        c.ts("dve", u_, var, a1_, a1_, ALU.mult, ALU.mult, reads=b_st, writes=b_st)
        p.rsqrt(f_, u_, EPS, reads=b_st, writes=b_st)

    def ln_s3(t, lo_st, b_st):
        r = p.RES.t[:, t * 1024:(t + 1) * 1024]
        rb = p.RES.bufs[t]
        a1_ = st[:, lo_st + 14:lo_st + 15]
        be_ = st[:, lo_st + 15:lo_st + 16]
        f_ = st[:, lo_st + 17:lo_st + 18]
        af_ = st[:, lo_st + 18:lo_st + 19]
        bf_ = st[:, lo_st + 19:lo_st + 20]
        c.ts("dve", af_, a1_, f_, None, ALU.mult, None, reads=b_st, writes=b_st)
        c.ts("dve", bf_, be_, f_, None, ALU.mult, None, reads=b_st, writes=b_st)
        xn = p.AB.t[:, t * 1024:(t + 1) * 1024]
        xnb = p.AB.bufs_for(t * 1024, (t + 1) * 1024)
        c.ts("dve", xn, r, a1_, be_, ALU.mult, ALU.add, reads=[rb] + b_st, writes=xnb)
        c.dma("sp", xn_dram[t * 128:(t + 1) * 128, :], xn, reads=xnb, writes=[b_xnd])
        c.ts("dve", xn, r, af_, bf_, ALU.mult, ALU.add, reads=[rb] + b_st, writes=xnb)
        def fw1(eng):
            return eng.tensor_tensor(xn, xn, st[:, lo_w1:lo_w1 + 1024], ALU.mult)
        c.emit("dve", fw1, reads=xnb + b_w1, writes=xnb)
        def fb1(eng):
            return eng.tensor_tensor(xn, xn, st[:, lo_b1:lo_b1 + 1024], ALU.add)
        c.emit("dve", fb1, reads=xnb + b_b1, writes=xnb)
        dstT = p.AA.t[:, :].rearrange("p (ch s) -> p ch s", ch=8)[:, :, t * 128:(t + 1) * 128]
        dstTb = []
        for ch in range(8):
            dstTb += p.AA.bufs_for(ch * S + t * 128, ch * S + (t + 1) * 128)
        c.dma("sp", dstT, xn, reads=xnb, writes=dstTb, transpose=True)

    ln_st = {}
    for t in range(NT + 2):
        if t < NT:
            ln_st[t] = ln_s1(t)
        if 0 <= t - 1 < NT:
            ln_s2(t - 1, *ln_st[t - 1])
        if 0 <= t - 2 < NT:
            ln_s3(t - 2, *ln_st.pop(t - 2))

    if dbg is not None:
        dbg("phaseB", {})
    lo_io, b_io = p.sm_ring.persist(512)
    def fio(eng):
        return eng.iota(st[:, lo_io:lo_io + 512], [[1, 512]], channel_multiplier=-1,
                        allow_small_or_imprecise_dtypes=True)
    c.emit("pool", fio, writes=b_io)

    QT, KT, VA, EXL, EX2 = 0, 2048, 4096, 6400, 8448

    def qkv_pair(pair):
        wq = p.wload(W["wq"][:, pair * 128:(pair + 1) * 128], 1024, 128)
        wk = p.wload(W["wk"][:, pair * 128:(pair + 1) * 128], 1024, 128)
        wv = p.wload(W["wv"][:, pair * 128:(pair + 1) * 128], 1024, 128)
        b_qT = p.BB.bufs_for(QT, QT + S)
        b_kT = p.BB.bufs_for(KT, KT + S)
        b_va = p.BB.bufs_for(VA, VA + NKC * 128)
        for (wtiles, dstcol, dstbufs) in ((wq, QT, b_qT), (wk, KT, b_kT)):
            for tc in range(NQ):
                pbk, pbuf = (p.PB[0], p.PBbuf[0]) if tc % 2 == 0 else (p.PB[1], p.PBbuf[1])
                steps = []
                for kc in range(8):
                    (wt, wlo, wpr, wcols, wbufs) = wtiles[kc]
                    steps.append((wt[0:128, wlo:wlo + 128],
                                  p.AA.t[:, kc * S + tc * 512: kc * S + tc * 512 + 512],
                                  list(wbufs) + p.AA.bufs_for(kc * S + tc * 512, kc * S + tc * 512 + 512)))
                c.mm(pbk[:, :], pbuf, steps)
                c.copy("act", bb[:, dstcol + tc * 512: dstcol + tc * 512 + 512],
                       pbk[:, :], reads=[pbuf], writes=dstbufs)
        for tt_ in range(NT):
            pbk, pbuf = (p.PB[0], p.PBbuf[0]) if tt_ % 2 == 0 else (p.PB[1], p.PBbuf[1])
            steps = []
            for kc in range(8):
                (wt, wlo, wpr, wcols, wbufs) = wv[kc]
                steps.append((p.AA.t[:, kc * S + tt_ * 128: kc * S + tt_ * 128 + 128],
                              wt[0:128, wlo:wlo + 128],
                              list(wbufs) + p.AA.bufs_for(kc * S + tt_ * 128, kc * S + tt_ * 128 + 128)))
            c.mm(pbk[:, 0:128], pbuf, steps)
            for hh in range(2):
                c.copy("act", bb[:, VA + tt_ * 128 + hh * 64: VA + tt_ * 128 + hh * 64 + 64],
                       pbk[:, hh * 64:hh * 64 + 64], reads=[pbuf], writes=b_va)
        return b_qT, b_kT, b_va

    for pair in range(8):
        _pscope = p.sm_ring.scope()
        b_qT, b_kT, b_va = qkv_pair(pair)
        pending = []
        for hh in range(2):
            _hscope = p.sm_ring.scope()
            h = 2 * pair + hh
            gamma = 1.0 - 2.0 ** (-5.0 - h)
            lg = math.log(gamma)
            lo_pat, b_pat = p.sm_ring.persist(512 * 5)
            sc_ap, sc_b = p.const_col(lg)
            c.act(st[:, lo_pat:lo_pat + 512], st[:, lo_io:lo_io + 512], AF.Exp,
                  scale=(sc_ap, sc_b), reads=b_io, writes=b_pat)
            for mi in range(4):
                dstm = st[:, lo_pat + (mi + 1) * 512:lo_pat + (mi + 2) * 512]
                def fm(eng, dstm=dstm, mi=mi):
                    return eng.affine_select(dstm, st[:, lo_pat:lo_pat + 512],
                                             [[1, 512]], ALU.is_ge, 0.0,
                                             base=-mi * 128, channel_multiplier=-1)
                c.emit("pool", fm, reads=b_pat, writes=b_pat)
            qrow0 = hh * 64
            for qc in range(NQ):
                qbase = qc * 512
                par = (qc + hh) % 2
                ob, obuf = (p.PB[2], p.PBbuf[2]) if par == 0 else (p.PB[3], p.PBbuf[3])
                alt, altbuf = (p.PB[3], p.PBbuf[3]) if par == 0 else (p.PB[2], p.PBbuf[2])
                kcs = [kc for kc in range(4 * qc + 4)
                       if (qbase - kc * 128 - 127) * lg > -25.0 or True]
                kcs = [kc for kc in kcs if (qbase - (kc + 1) * 128) * lg > -30.0]
                exi = 0
                for j, kc in enumerate(kcs):
                    sa, sabuf = (p.PA[0], p.PAbuf[0]) if (j // 2) % 2 == 0 else (p.PA[1], p.PAbuf[1])
                    soff = (j % 2) * 512
                    def fsc(eng, kc=kc, sa=sa, soff=soff):
                        return eng.matmul(
                            sa[:, soff:soff + 512],
                            bb[qrow0:qrow0 + 64, KT + kc * 128:KT + kc * 128 + 128],
                            bb[qrow0:qrow0 + 64, QT + qbase:QT + qbase + 512],
                            start=True, stop=True)
                    c.emit("pe", fsc, reads=b_kT + b_qT, writes=[sabuf])
                    if j == 0 and pending:
                        pending.pop(0)()
                    o = kc * 128 - qbase
                    cc = 0.125 * math.exp(lg * (qbase - kc * 128))
                    pat_i = 0 if o < 0 else (o // 128 + 1)
                    pat = st[:, lo_pat + pat_i * 512:lo_pat + (pat_i + 1) * 512]
                    ex = EXL + (exi % 4) * 512
                    exi += 1
                    exb = p.BB.bufs_for(ex, ex + 512)
                    def fev(eng, sa=sa, soff=soff, cc=cc, pat=pat, ex=ex):
                        return eng.scalar_tensor_tensor(bb[:, ex:ex + 512],
                                                        sa[:, soff:soff + 512],
                                                        cc, pat, ALU.mult, ALU.mult)
                    c.emit("dve", fev, reads=[sabuf] + b_pat, writes=exb)
                    def fpv(eng, kc=kc, ex=ex, ob=ob, first=(j == 0), last=(j == len(kcs) - 1)):
                        return eng.matmul(ob[0:64, :],
                                          bb[:, VA + kc * 128 + hh * 64:VA + kc * 128 + hh * 64 + 64],
                                          bb[:, ex:ex + 512],
                                          start=first, stop=last)
                    c.emit("pe", fpv, reads=b_va + exb, writes=[obuf])
                def tail(ob=ob, obuf=obuf, h=h, qbase=qbase):
                    ex2b = p.BB.bufs_for(EX2, EX2 + 512)
                    c.copy("act", bb[0:64, EX2:EX2 + 512], ob[0:64, :], reads=[obuf],
                           writes=ex2b)
                    sqb = p.BB.bufs_for(EX2 + 512, EX2 + 1024)
                    c.tt("pool", bb[0:64, EX2 + 512:EX2 + 1024], bb[0:64, EX2:EX2 + 512],
                         bb[0:64, EX2:EX2 + 512], ALU.mult, reads=ex2b, writes=sqb)
                    nb_, nbuf_ = p.PB[0], p.PBbuf[0]
                    lo_mt, b_mt = p.sm_ring.alloc(1024 + 512)
                    def fs1b(eng):
                        return eng.matmul(nb_[0:1, :], p.ONE_COL_BF[0:64, 0:1],
                                          bb[0:64, EX2:EX2 + 512], start=True, stop=True)
                    c.emit("pe", fs1b, reads=[p.ONE_COL_BF_buf] + ex2b, writes=[nbuf_])
                    c.copy("dve", st[0:1, lo_mt:lo_mt + 512], nb_[0:1, :], reads=[nbuf_],
                           writes=b_mt)
                    def fs2b(eng):
                        return eng.matmul(nb_[0:1, :], p.ONE_COL_BF[0:64, 0:1],
                                          bb[0:64, EX2 + 512:EX2 + 1024], start=True, stop=True)
                    c.emit("pe", fs2b, reads=[p.ONE_COL_BF_buf] + sqb, writes=[nbuf_])
                    mrow = st[0:1, lo_mt:lo_mt + 512]
                    vrow = st[0:1, lo_mt + 512:lo_mt + 1024]
                    c.ts("dve", mrow, mrow, 1.0 / 64.0, None, ALU.mult, None,
                         reads=b_mt, writes=b_mt)
                    def fv(eng):
                        return eng.tensor_scalar(vrow, nb_[0:1, :], 1.0 / 64.0, None, ALU.mult)
                    c.emit("dve", fv, reads=[nbuf_], writes=b_mt)
                    lo_m2, b_m2 = p.sm_ring.alloc(512)
                    c.tt("dve", st[0:1, lo_m2:lo_m2 + 512], mrow, mrow, ALU.mult,
                         reads=b_mt, writes=b_m2)
                    c.tt("dve", vrow, vrow, st[0:1, lo_m2:lo_m2 + 512], ALU.subtract,
                         reads=b_mt + b_m2, writes=b_mt)
                    p.rsqrt(vrow, vrow, 1e-6, reads=b_mt, writes=b_mt)
                    bm, bmb = p.PB[1], p.PBbuf[1]
                    def fbm(eng):
                        return eng.matmul(bm[0:64, :], p.ONE_ROW[0:1, 0:64], mrow,
                                          start=True, stop=True)
                    c.emit("pe", fbm, reads=[p.ONE_ROW_buf] + b_mt, writes=[bmb])
                    t1b = p.BB.bufs_for(EX2 + 1024, EX2 + 1536)
                    def fsub(eng):
                        return eng.tensor_tensor(bb[0:64, EX2 + 1024:EX2 + 1536],
                                                 bb[0:64, EX2:EX2 + 512], bm[0:64, :],
                                                 ALU.subtract)
                    c.emit("dve", fsub, reads=ex2b + [bmb], writes=t1b)
                    def fbr(eng):
                        return eng.matmul(bm[0:64, :], p.ONE_ROW[0:1, 0:64], vrow,
                                          start=True, stop=True)
                    c.emit("pe", fbr, reads=[p.ONE_ROW_buf] + b_mt, writes=[bmb])
                    och = h // 2
                    orow = (h % 2) * 64
                    ocol = och * S + qbase
                    ob_bufs = p.AB.bufs_for(ocol, ocol + 512)
                    def fmul(eng):
                        return eng.tensor_tensor(p.AB.t[orow:orow + 64, ocol:ocol + 512],
                                                 bb[0:64, EX2 + 1024:EX2 + 1536],
                                                 bm[0:64, :], ALU.mult)
                    c.emit("dve", fmul, reads=t1b + [bmb], writes=ob_bufs)
                pending.append(tail)
            p.sm_ring.unscope(_hscope)
        while pending:
            pending.pop(0)()
        p.sm_ring.unscope(_pscope)

    for ch in range(8):
        wg = p.wload(W["wg"][:, ch * 128:(ch + 1) * 128], 1024, 128)
        for tc in range(NQ):
            pbk, pbuf = (p.PB[0], p.PBbuf[0]) if tc % 2 == 0 else (p.PB[1], p.PBbuf[1])
            steps = []
            for kc in range(8):
                (wt, wlo, wpr, wcols, wbufs) = wg[kc]
                steps.append((wt[0:128, wlo:wlo + 128],
                              p.AA.t[:, kc * S + tc * 512: kc * S + tc * 512 + 512],
                              list(wbufs) + p.AA.bufs_for(kc * S + tc * 512, kc * S + tc * 512 + 512)))
            c.mm(pbk[:, :], pbuf, steps)
            gb = p.BB.bufs_for(EXL, EXL + 512)
            def fsl(eng, pbk=pbk):
                return eng.activation(bb[:, EXL:EXL + 512], pbk[:, :], AF.Silu)
            c.emit("act", fsl, reads=[pbuf], writes=gb)
            ocol = ch * S + tc * 512
            ob_bufs = p.AB.bufs_for(ocol, ocol + 512)
            c.tt("dve", p.AB.t[:, ocol:ocol + 512], p.AB.t[:, ocol:ocol + 512],
                 bb[:, EXL:EXL + 512], ALU.mult, reads=ob_bufs + gb, writes=ob_bufs)

    if dbg is not None:
        dbg("ret", {})
    lo_w2, b_w2 = p.sm_ring.persist(1024)
    lo_b2r, b_b2r = p.sm_ring.persist(1024)
    lo_row2, b_row2 = p.sm_ring.alloc(2048)
    c.dma("pool", st[0:1, lo_row2:lo_row2 + 1024], W["ln2_w"], writes=b_row2)
    c.dma("pool", st[0:1, lo_row2 + 1024:lo_row2 + 2048], W["ln2_b"], writes=b_row2)
    for (lo_dst, b_dst, off) in ((lo_w2, b_w2, 0), (lo_b2r, b_b2r, 1024)):
        for half in range(2):
            pbk, pbuf = p.PB[3], p.PBbuf[3]
            def fb(eng, off=off, half=half):
                return eng.matmul(pbk[:, :], p.ONE_ROW[0:1, 0:128],
                                  st[0:1, lo_row2 + off + half * 512:lo_row2 + off + half * 512 + 512],
                                  start=True, stop=True)
            c.emit("pe", fb, reads=[p.ONE_ROW_buf] + b_row2, writes=[pbuf])
            c.copy("dve", st[:, lo_dst + half * 512:lo_dst + half * 512 + 512],
                   pbk[:, :], reads=[pbuf], writes=b_dst)

    b_yd = Buf("y_dram")
    wo = p.wload(W["wo"], 1024, 1024)
    for t in range(NT):
        lo_y, b_y = p.sm_ring.alloc(1024 + 128 + 32)
        lo_xnb = lo_y + 1024
        c.dma("pool", bb[:, EXL:EXL + 1024],
              xn_dram[t * 128:(t + 1) * 128, :], reads=[b_xnd],
              writes=p.BB.bufs_for(EXL, EXL + 1024))
        for half in range(2):
            bi = (t * 2 + half) % 4
            pbk, pbuf = p.PB[bi], p.PBbuf[bi]
            steps = []
            for kc in range(8):
                (wt, wlo, wpr, wcols, wbufs) = wo[kc]
                steps.append((p.AB.t[:, kc * S + t * 128: kc * S + t * 128 + 128],
                              wt[0:128, wlo + half * 512: wlo + half * 512 + 512],
                              list(wbufs) + p.AB.bufs_for(kc * S + t * 128, kc * S + t * 128 + 128)))
            c.mm(pbk[:, :], pbuf, steps)
            def fy(eng, pbk=pbk, half=half, lo_y=lo_y):
                return eng.tensor_tensor(st[:, lo_y + half * 512:lo_y + half * 512 + 512],
                                         pbk[:, :], bb[:, EXL + half * 512:EXL + half * 512 + 512],
                                         ALU.add)
            c.emit("dve", fy, reads=[pbuf] + p.BB.bufs_for(EXL, EXL + 1024), writes=b_y)
        ytile = st[:, lo_y:lo_y + 1024]
        c.dma("pool", y_dram[t * 128:(t + 1) * 128, :], ytile, reads=b_y, writes=[b_yd])
        lo_st, b_st = p.sm_ring.alloc(32)
        def g1(eng):
            return eng.bn_stats(st[:, lo_st:lo_st + 6], ytile[:, 0:512])
        def g2(eng):
            return eng.bn_stats(st[:, lo_st + 6:lo_st + 12], ytile[:, 512:1024])
        def g3(eng):
            return eng.bn_aggr(st[:, lo_st + 12:lo_st + 14], st[:, lo_st:lo_st + 12])
        c.emit("dve", g1, reads=b_y, writes=b_st)
        c.emit("dve", g2, reads=b_y, writes=b_st)
        c.emit("dve", g3, reads=b_st, writes=b_st)
        al = st[:, lo_st + 14:lo_st + 15]
        bt_ = st[:, lo_st + 15:lo_st + 16]
        p.rsqrt(al, st[:, lo_st + 13:lo_st + 14], EPS, reads=b_st, writes=b_st)
        c.ts("dve", bt_, st[:, lo_st + 12:lo_st + 13], al, -1.0, ALU.mult, ALU.mult,
             reads=b_st, writes=b_st)
        un = bb[:, 0:1024]
        unb = p.BB.bufs_for(0, 1024)
        c.ts("dve", un, ytile, al, bt_, ALU.mult, ALU.add, reads=b_y + b_st, writes=unb)
        def gw(eng):
            return eng.tensor_tensor(un, un, st[:, lo_w2:lo_w2 + 1024], ALU.mult)
        c.emit("dve", gw, reads=unb + b_w2, writes=unb)
        def gb_(eng):
            return eng.tensor_tensor(un, un, st[:, lo_b2r:lo_b2r + 1024], ALU.add)
        c.emit("dve", gb_, reads=unb + b_b2r, writes=unb)
        dstT = p.AA.t[:, :].rearrange("p (ch s) -> p ch s", ch=8)[:, :, t * 128:(t + 1) * 128]
        dstTb = []
        for ch in range(8):
            dstTb += p.AA.bufs_for(ch * S + t * 128, ch * S + (t + 1) * 128)
        c.dma("sp", dstT, un, reads=unb, writes=dstTb, transpose=True)

    if dbg is not None:
        dbg("y", dict(b_yd=b_yd))
    _, lo_bia, _, b_bia = p.load_bias_col(W["b_in"].rearrange("a n -> (a n)")[0:2730], 2730, "bia")
    _, lo_bib, _, b_bib = p.load_bias_col(W["b_in"].rearrange("a n -> (a n)")[2730:5460], 2730, "bib")
    lo_bo, b_bo = p.sm_ring.persist(1024)
    c.dma("pool", st[0:1, lo_bo:lo_bo + 1024], W["b_out"], writes=b_bo)

    for tc in range(NQ):
        for mi in range(22):
            rows = 128 if mi < 21 else 42
            wa = p.wload(W["w_in"][:, mi * 128:mi * 128 + rows], 1024, rows)
            wb_ = p.wload(W["w_in"][:, 2730 + mi * 128:2730 + mi * 128 + rows], 1024, rows)
            if mi % 2 == 0:
                pa, pab = p.PB[0], p.PBbuf[0]
                pb_, pbb = p.PB[1], p.PBbuf[1]
            else:
                pa, pab = p.PA[0][:, 0:512], p.PAbuf[0]
                pb_, pbb = p.PA[1][:, 0:512], p.PAbuf[1]
            for (wt_, ps_, psb_) in ((wa, pa, pab), (wb_, pb_, pbb)):
                steps = []
                for kc in range(8):
                    (wtt, wlo, wpr, wcols, wbufs) = wt_[kc]
                    steps.append((wtt[0:128, wlo:wlo + rows],
                                  p.AA.t[:, kc * S + tc * 512: kc * S + tc * 512 + 512],
                                  list(wbufs) + p.AA.bufs_for(kc * S + tc * 512, kc * S + tc * 512 + 512)))
                c.mm(ps_[0:rows, :], psb_, steps)
            SLT = 11264
            sl = bb[0:rows, SLT:SLT + 512]
            slb = p.BB.bufs_for(SLT, SLT + 512)
            def fsl(eng, pb_=pb_, rows=rows, mi=mi):
                return eng.activation(sl, pb_[0:rows, :], AF.Silu,
                                      bias=st[0:rows, lo_bib + mi:lo_bib + mi + 1])
            c.emit("act", fsl, reads=[pbb] + b_bib, writes=slb)
            ht = bb[0:rows, mi * 512:(mi + 1) * 512]
            htb = p.BB.bufs_for(mi * 512, (mi + 1) * 512)
            def fh(eng, pa=pa, rows=rows, mi=mi):
                return eng.scalar_tensor_tensor(ht, pa[0:rows, :],
                                                st[0:rows, lo_bia + mi:lo_bia + mi + 1],
                                                sl, ALU.add, ALU.mult)
            c.emit("dve", fh, reads=[pab] + b_bia + slb, writes=htb)
        for half in range(2):
            wd = p.wload(W["w_out"][:, half * 512:half * 512 + 512], 2730, 512)
            for t4 in range(4):
                t = tc * 4 + t4
                pbk, pbuf = (p.PB[2], p.PBbuf[2]) if t4 % 2 == 0 else (p.PB[3], p.PBbuf[3])
                steps = []
                for mi in range(22):
                    rows = 128 if mi < 21 else 42
                    (wtt, wlo, wpr, wcols, wbufs) = wd[mi]
                    steps.append((bb[0:rows, mi * 512 + t4 * 128:mi * 512 + t4 * 128 + 128],
                                  wtt[0:rows, wlo:wlo + 512],
                                  list(wbufs) + p.BB.bufs_for(mi * 512, (mi + 1) * 512)))
                steps.append((p.ONE_ROW[0:1, 0:128],
                              st[0:1, lo_bo + half * 512:lo_bo + half * 512 + 512],
                              [p.ONE_ROW_buf] + b_bo))
                c.mm(pbk[:, :], pbuf, steps)
                YT = 11776
                yb2 = p.BB.bufs_for(YT, YT + 512)
                c.dma("pool", bb[:, YT:YT + 512],
                      y_dram[t * 128:(t + 1) * 128, half * 512:half * 512 + 512],
                      reads=[b_yd], writes=yb2)
                rb = p.RES.bufs[t]
                rap = p.RES.t[:, t * 1024 + half * 512: t * 1024 + half * 512 + 512]
                def fz1(eng, pbk=pbk):
                    return eng.tensor_tensor(rap, rap, pbk[:, :], ALU.add)
                c.emit("dve", fz1, reads=[pbuf, rb], writes=[rb])
                def fz2(eng):
                    return eng.tensor_tensor(rap, rap, bb[:, 11776:11776 + 512], ALU.add)
                c.emit("dve", fz2, reads=yb2 + [rb], writes=[rb])
    p.sm_ring.unscope(_scope)



S_FULL = 2048

_CONV = [
    ("L0_attn_a_Wq", 1024, 1024), ("L0_attn_a_Wk", 1024, 1024),
    ("L0_attn_a_Wv", 1024, 1024), ("L0_attn_a_Wo", 1024, 1024),
    ("L0_attn_b_w_q", 1024, 1024), ("L0_attn_b_w_kv", 1024, 2048),
    ("L0_attn_b_w_out", 1024, 1024),
    ("L0_ffn_enc_W", 1024, 2048), ("L0_ffn_s2c_W", 2048, 64),
    ("L0_ffn_rW1", 64, 64), ("L0_ffn_rW2", 64, 8),
    ("L0_ffn_We1", 512, 1024), ("L0_ffn_We2", 8192, 64),
    ("L0_ffn_c2s_W", 64, 2048), ("L0_ffn_dec_W", 2048, 1024),
    ("L1_attn_a_W_attn", 1024, 3072), ("L1_attn_a_W_proj", 1024, 1024),
    ("L1_attn_b_wq", 1024, 1024), ("L1_attn_b_wk", 1024, 1024),
    ("L1_attn_b_wv", 1024, 1024), ("L1_attn_b_wg", 1024, 1024),
    ("L1_attn_b_wo", 1024, 1024), ("L1_attn_b_w_in", 1024, 5460),
    ("L1_attn_b_w_out", 2730, 1024),
]


def build_prep(S=128):
    nc = bass.Bass("TRN2", target_bir_lowering=False)
    ins = {}
    outs = {}
    for (n, r, cmax) in _CONV:
        ins[n] = nc.dram_tensor(n, [r, cmax], F32, kind="ExternalInput").ap()
        outs[n] = nc.dram_tensor(n + "_bf", [r, cmax], BF16, kind="ExternalOutput").ap()
    fp4src = {n: nc.dram_tensor(n, shp, F32, kind="ExternalInput").ap()
              for n, shp in (("w_gate", [2816, 1024]), ("w_up", [2816, 1024]),
                             ("w_down", [1024, 2816]))}
    fp4dst = {"w_gate": nc.dram_tensor("wgT", [1024, 2816], BF16, kind="ExternalOutput").ap(),
              "w_up": nc.dram_tensor("wuT", [1024, 2816], BF16, kind="ExternalOutput").ap(),
              "w_down": nc.dram_tensor("wdT", [2816, 1024], BF16, kind="ExternalOutput").ap()}
    import contextlib
    es = contextlib.ExitStack()
    with es:
        p = Prog(nc, es, S=S)
        for (n, r, cw) in _CONV:
            p.to_bf16_dram(ins[n], outs[n], r, cw)
        emit_fp4_weight_prep(p, fp4src, fp4dst)
        p.finish()
    return nc


def _wmaps_l0a(i):
    return {"wq": "L0_attn_a_Wq_bf", "wk": "L0_attn_a_Wk_bf", "wv": "L0_attn_a_Wv_bf",
            "wo": "L0_attn_a_Wo_bf", "null_k": "L0_attn_a_null_k",
            "null_v": "L0_attn_a_null_v", "logit_scale": "L0_attn_a_logit_scale"}


def build_sublayer(kind, S=S_FULL):
    nc = bass.Bass("TRN2", target_bir_lowering=False)
    import contextlib
    es = contextlib.ExitStack()

    def dram(n, shp, dt=BF16):
        return nc.dram_tensor(n, shp, dt, kind="ExternalInput").ap()

    x_ap = nc.dram_tensor("x", [S, 1024], F32, kind="ExternalInput").ap()
    o_ap = nc.dram_tensor("o", [S, 1024], F32, kind="ExternalOutput").ap()
    with es:
        p = Prog(nc, es, S=S)
        p.load_res(x_ap)
        if kind == "l0a":
            W = {"wq": dram("wq", [1024, 1024]), "wk": dram("wk", [1024, 1024]),
                 "wv": dram("wv", [1024, 1024]), "wo": dram("wo", [1024, 1024]),
                 "null_k": dram("null_k", [16, 64], F32),
                 "null_v": dram("null_v", [16, 64], F32),
                 "logit_scale": dram("logit_scale", [1, 16], F32)}
            emit_l0_attn_a(p, W)
        elif kind == "l0f":
            W = {"enc_W": dram("enc_W", [1024, 2048]),
                 "enc_b": dram("enc_b", [1, 2048], F32),
                 "s2c_W": dram("s2c_W", [2048, 64]), "s2c_b": dram("s2c_b", [1, 64], F32),
                 "rW1": dram("rW1", [64, 64]), "rb1": dram("rb1", [1, 64], F32),
                 "rW2": dram("rW2", [64, 8]), "rb2": dram("rb2", [1, 8], F32),
                 "We1": dram("We1", [8, 64, 1024]), "be1": dram("be1", [8, 1024], F32),
                 "We2": dram("We2", [8, 1024, 64]), "be2": dram("be2", [8, 64], F32),
                 "c2s_W": dram("c2s_W", [64, 2048]), "c2s_b": dram("c2s_b", [1, 2048], F32),
                 "dec_W": dram("dec_W", [2048, 1024]), "dec_b": dram("dec_b", [1, 1024], F32),
                 "ln_g": dram("ln_g", [1, 1024], F32), "ln_b": dram("ln_b", [1, 1024], F32)}
            emit_l0_ffn(p, W)
        elif kind == "l0b":
            W = {"w_q": dram("w_q", [1024, 1024]), "w_kv": dram("w_kv", [1024, 2048]),
                 "w_out": dram("w_out", [1024, 1024]), "b_out": dram("b_out", [1, 1024], F32),
                 "scale": dram("scale", [1, 16], F32)}
            emit_l0_attn_b(p, W)
        elif kind == "l1a":
            W = {"w_attn": dram("w_attn", [1024, 3072]),
                 "b_attn": dram("b_attn", [1, 3072], F32),
                 "w_proj": dram("w_proj", [1024, 1024]),
                 "b_proj": dram("b_proj", [1, 1024], F32)}
            emit_l1_attn_a(p, W)
        elif kind == "l1f":
            W = {"wgT": dram("wgT", [1024, 2816]), "wuT": dram("wuT", [1024, 2816]),
                 "wdT": dram("wdT", [2816, 1024])}
            emit_l1_ffn(p, W)
        elif kind == "l1b":
            W = {"ln1_w": dram("ln1_w", [1, 1024], F32), "ln1_b": dram("ln1_b", [1, 1024], F32),
                 "wq": dram("wq", [1024, 1024]), "wk": dram("wk", [1024, 1024]),
                 "wv": dram("wv", [1024, 1024]), "wg": dram("wg", [1024, 1024]),
                 "wo": dram("wo", [1024, 1024]),
                 "ln2_w": dram("ln2_w", [1, 1024], F32), "ln2_b": dram("ln2_b", [1, 1024], F32),
                 "w_in": dram("w_in", [1024, 5460]), "b_in": dram("b_in", [1, 5460], F32),
                 "w_out": dram("w_out", [2730, 1024]), "b_out": dram("b_out", [1, 1024], F32)}
            scr = {"xn_bf": nc.dram_tensor("xn_bf", [S, 1024], BF16, kind="Internal").ap(),
                   "y_bf": nc.dram_tensor("y_bf", [S, 1024], F32, kind="Internal").ap()}
            emit_l1_attn_b(p, W, scr)
        p.store_res(o_ap)
        p.finish()
    return nc


def kernel(**inputs):
    n = NCORES
    x = np.ascontiguousarray(inputs["L0_attn_a_x"])
    shards = [x[2 * c:2 * c + 2] for c in range(n)]

    prep_in = {}
    for (nm, r, cw) in _CONV:
        prep_in[nm] = np.ascontiguousarray(inputs[nm].reshape(r, cw))
    prep_in["w_gate"] = inputs["L1_ffn_w_gate"]
    prep_in["w_up"] = inputs["L1_ffn_w_up"]
    prep_in["w_down"] = inputs["L1_ffn_w_down"]
    nc_prep = build_prep()
    res = run_bass_kernel_spmd(nc_prep, [prep_in], core_ids=[0])
    wbf = dict(res.results[0])

    def col(v, n_):
        return np.ascontiguousarray(v.reshape(1, n_).astype(np.float32))

    progs = {k: build_sublayer(k) for k in ("l0a", "l0f", "l0b", "l1a", "l1f", "l1b")}

    w_l0a = {"wq": wbf["L0_attn_a_Wq_bf"], "wk": wbf["L0_attn_a_Wk_bf"],
             "wv": wbf["L0_attn_a_Wv_bf"], "wo": wbf["L0_attn_a_Wo_bf"],
             "null_k": inputs["L0_attn_a_null_k"].reshape(16, 64),
             "null_v": inputs["L0_attn_a_null_v"].reshape(16, 64),
             "logit_scale": col(inputs["L0_attn_a_logit_scale"], 16)}
    w_l0f = {"enc_W": wbf["L0_ffn_enc_W_bf"], "enc_b": col(inputs["L0_ffn_enc_b"], 2048),
             "s2c_W": wbf["L0_ffn_s2c_W_bf"], "s2c_b": col(inputs["L0_ffn_s2c_b"], 64),
             "rW1": wbf["L0_ffn_rW1_bf"], "rb1": col(inputs["L0_ffn_rb1"], 64),
             "rW2": wbf["L0_ffn_rW2_bf"], "rb2": col(inputs["L0_ffn_rb2"], 8),
             "We1": wbf["L0_ffn_We1_bf"].reshape(8, 64, 1024),
             "be1": inputs["L0_ffn_be1"],
             "We2": wbf["L0_ffn_We2_bf"].reshape(8, 1024, 64),
             "be2": inputs["L0_ffn_be2"],
             "c2s_W": wbf["L0_ffn_c2s_W_bf"], "c2s_b": col(inputs["L0_ffn_c2s_b"], 2048),
             "dec_W": wbf["L0_ffn_dec_W_bf"], "dec_b": col(inputs["L0_ffn_dec_b"], 1024),
             "ln_g": col(inputs["L0_ffn_ln_g"], 1024), "ln_b": col(inputs["L0_ffn_ln_b"], 1024)}
    w_l0b = {"w_q": wbf["L0_attn_b_w_q_bf"], "w_kv": wbf["L0_attn_b_w_kv_bf"],
             "w_out": wbf["L0_attn_b_w_out_bf"], "b_out": col(inputs["L0_attn_b_b_out"], 1024),
             "scale": col(inputs["L0_attn_b_scale"], 16)}
    w_l1a = {"w_attn": wbf["L1_attn_a_W_attn_bf"], "b_attn": col(inputs["L1_attn_a_b_attn"], 3072),
             "w_proj": wbf["L1_attn_a_W_proj_bf"], "b_proj": col(inputs["L1_attn_a_b_proj"], 1024)}
    w_l1f = {"wgT": wbf["wgT"], "wuT": wbf["wuT"], "wdT": wbf["wdT"]}
    w_l1b = {"ln1_w": col(inputs["L1_attn_b_ln1_w"], 1024), "ln1_b": col(inputs["L1_attn_b_ln1_b"], 1024),
             "wq": wbf["L1_attn_b_wq_bf"], "wk": wbf["L1_attn_b_wk_bf"],
             "wv": wbf["L1_attn_b_wv_bf"], "wg": wbf["L1_attn_b_wg_bf"],
             "wo": wbf["L1_attn_b_wo_bf"],
             "ln2_w": col(inputs["L1_attn_b_ln2_w"], 1024), "ln2_b": col(inputs["L1_attn_b_ln2_b"], 1024),
             "w_in": wbf["L1_attn_b_w_in_bf"], "b_in": col(inputs["L1_attn_b_b_in"], 5460),
             "w_out": wbf["L1_attn_b_w_out_bf"], "b_out": col(inputs["L1_attn_b_b_out"], 1024)}

    seq = [("l0a", w_l0a), ("l0f", w_l0f), ("l0b", w_l0b), ("l0f", w_l0f),
           ("l1a", w_l1a), ("l1f", w_l1f), ("l1b", w_l1b), ("l1f", w_l1f)]

    cur = [np.ascontiguousarray(shards[c][e]) for c in range(n) for e in range(2)]
    for (kind, wm) in seq:
        nc_s = progs[kind]
        for e in range(2):
            in_maps = []
            for ci in range(n):
                m = dict(wm)
                m["x"] = cur[ci * 2 + e]
                in_maps.append(m)
            res = run_bass_kernel_spmd(nc_s, in_maps, core_ids=list(range(n)))
            for ci in range(n):
                cur[ci * 2 + e] = res.results[ci]["o"]

    out = np.empty((16, 2048, 1024), np.float32)
    for ci in range(n):
        for e in range(2):
            out[2 * ci + e] = cur[ci * 2 + e]
    return out


def build_fused():
    nc = bass.Bass("TRN2", target_bir_lowering=False)
    S = S_FULL
    ins = {}
    for (n, r, cw) in _CONV:
        ins[n] = nc.dram_tensor(n, [r, cw], F32, kind="ExternalInput").ap()
    wbf = {n: nc.dram_tensor(n + "_bf", [r, cw], BF16, kind="Internal").ap()
           for (n, r, cw) in _CONV}
    fp4src = {n: nc.dram_tensor(n, shp, F32, kind="ExternalInput").ap()
              for n, shp in (("w_gate", [2816, 1024]), ("w_up", [2816, 1024]),
                             ("w_down", [1024, 2816]))}
    fp4dst = {"w_gate": nc.dram_tensor("wgT", [1024, 2816], BF16, kind="Internal").ap(),
              "w_up": nc.dram_tensor("wuT", [1024, 2816], BF16, kind="Internal").ap(),
              "w_down": nc.dram_tensor("wdT", [2816, 1024], BF16, kind="Internal").ap()}
    sm_in = {}
    for n, shp in (("null_k", [16, 64]), ("null_v", [16, 64]), ("logit_scale", [1, 16]),
                   ("b_out_l0b", [1, 1024]), ("scale_l0b", [1, 16]),
                   ("enc_b", [1, 2048]), ("s2c_b", [1, 64]), ("rb1", [1, 64]),
                   ("rb2", [1, 8]), ("be1", [8, 1024]), ("be2", [8, 64]),
                   ("c2s_b", [1, 2048]), ("dec_b", [1, 1024]), ("ln_g", [1, 1024]),
                   ("ln_b", [1, 1024]), ("b_attn", [1, 3072]), ("b_proj", [1, 1024]),
                   ("ln1_w", [1, 1024]), ("ln1_b", [1, 1024]), ("ln2_w", [1, 1024]),
                   ("ln2_b", [1, 1024]), ("b_in", [1, 5460]), ("b_out_l1b", [1, 1024])):
        sm_in[n] = nc.dram_tensor(n, shp, F32, kind="ExternalInput").ap()
    x_ap = nc.dram_tensor("x", [2 * S, 1024], F32, kind="ExternalInput").ap()
    o_ap = nc.dram_tensor("o", [2 * S, 1024], F32, kind="ExternalOutput").ap()
    xn_bf = nc.dram_tensor("xn_bf", [S, 1024], BF16, kind="Internal").ap()
    y_bf = nc.dram_tensor("y_bf", [S, 1024], F32, kind="Internal").ap()

    import contextlib
    es = contextlib.ExitStack()
    with es:
        p = Prog(nc, es, S=S)
        c = p.ctx
        for (n, r, cw) in _CONV:
            c.dram_bufs[n + "_bf"] = Buf(n + "_bf")
        for n in ("wgT", "wuT", "wdT", "xn_bf", "y_bf"):
            c.dram_bufs[n] = Buf(n)
        p.load_res(x_ap[0:S, :])
        for (n, r, cw) in _CONV:
            p.to_bf16_dram(ins[n], wbf[n], r, cw)
        emit_fp4_weight_prep(p, fp4src, fp4dst)

        W_l0a = {"wq": wbf["L0_attn_a_Wq"], "wk": wbf["L0_attn_a_Wk"],
                 "wv": wbf["L0_attn_a_Wv"], "wo": wbf["L0_attn_a_Wo"],
                 "null_k": sm_in["null_k"], "null_v": sm_in["null_v"],
                 "logit_scale": sm_in["logit_scale"]}
        W_l0f = {"enc_W": wbf["L0_ffn_enc_W"], "enc_b": sm_in["enc_b"],
                 "s2c_W": wbf["L0_ffn_s2c_W"], "s2c_b": sm_in["s2c_b"],
                 "rW1": wbf["L0_ffn_rW1"], "rb1": sm_in["rb1"],
                 "rW2": wbf["L0_ffn_rW2"], "rb2": sm_in["rb2"],
                 "We1": wbf["L0_ffn_We1"].rearrange("(e m) h -> e m h", e=8),
                 "be1": sm_in["be1"],
                 "We2": wbf["L0_ffn_We2"].rearrange("(e h) m -> e h m", e=8),
                 "be2": sm_in["be2"],
                 "c2s_W": wbf["L0_ffn_c2s_W"], "c2s_b": sm_in["c2s_b"],
                 "dec_W": wbf["L0_ffn_dec_W"], "dec_b": sm_in["dec_b"],
                 "ln_g": sm_in["ln_g"], "ln_b": sm_in["ln_b"]}
        W_l0b = {"w_q": wbf["L0_attn_b_w_q"], "w_kv": wbf["L0_attn_b_w_kv"],
                 "w_out": wbf["L0_attn_b_w_out"], "b_out": sm_in["b_out_l0b"],
                 "scale": sm_in["scale_l0b"]}
        W_l1a = {"w_attn": wbf["L1_attn_a_W_attn"], "b_attn": sm_in["b_attn"],
                 "w_proj": wbf["L1_attn_a_W_proj"], "b_proj": sm_in["b_proj"]}
        W_l1f = {"wgT": fp4dst["w_gate"], "wuT": fp4dst["w_up"], "wdT": fp4dst["w_down"]}
        W_l1b = {"ln1_w": sm_in["ln1_w"], "ln1_b": sm_in["ln1_b"],
                 "wq": wbf["L1_attn_b_wq"], "wk": wbf["L1_attn_b_wk"],
                 "wv": wbf["L1_attn_b_wv"], "wg": wbf["L1_attn_b_wg"],
                 "wo": wbf["L1_attn_b_wo"],
                 "ln2_w": sm_in["ln2_w"], "ln2_b": sm_in["ln2_b"],
                 "w_in": wbf["L1_attn_b_w_in"], "b_in": sm_in["b_in"],
                 "w_out": wbf["L1_attn_b_w_out"], "b_out": sm_in["b_out_l1b"]}
        scr = {"xn_bf": xn_bf, "y_bf": y_bf}

        for e in range(2):
            if e > 0:
                p.load_res(x_ap[e * S:(e + 1) * S, :])
            p.WQ = "sp" if e == 0 else "pool"
            emit_l0_attn_a(p, W_l0a)
            p.WQ = "pool"
            emit_l0_ffn(p, W_l0f)
            emit_l0_attn_b(p, W_l0b)
            emit_l0_ffn(p, W_l0f)
            emit_l1_attn_a(p, W_l1a)
            emit_l1_ffn(p, W_l1f)
            emit_l1_attn_b(p, W_l1b, scr)
            emit_l1_ffn(p, W_l1f)
            p.store_res(o_ap[e * S:(e + 1) * S, :])
        p.finish()
    return nc


def kernel(**inputs):
    n = NCORES
    x = np.ascontiguousarray(inputs["L0_attn_a_x"])

    def col(v, n_):
        return np.ascontiguousarray(v.reshape(1, n_).astype(np.float32))

    base = {}
    for (nm, r, cw) in _CONV:
        base[nm] = np.ascontiguousarray(inputs[nm].reshape(r, cw))
    base["w_gate"] = inputs["L1_ffn_w_gate"]
    base["w_up"] = inputs["L1_ffn_w_up"]
    base["w_down"] = inputs["L1_ffn_w_down"]
    base["null_k"] = np.ascontiguousarray(inputs["L0_attn_a_null_k"].reshape(16, 64))
    base["null_v"] = np.ascontiguousarray(inputs["L0_attn_a_null_v"].reshape(16, 64))
    base["logit_scale"] = col(inputs["L0_attn_a_logit_scale"], 16)
    base["b_out_l0b"] = col(inputs["L0_attn_b_b_out"], 1024)
    base["scale_l0b"] = col(inputs["L0_attn_b_scale"], 16)
    base["enc_b"] = col(inputs["L0_ffn_enc_b"], 2048)
    base["s2c_b"] = col(inputs["L0_ffn_s2c_b"], 64)
    base["rb1"] = col(inputs["L0_ffn_rb1"], 64)
    base["rb2"] = col(inputs["L0_ffn_rb2"], 8)
    base["be1"] = np.ascontiguousarray(inputs["L0_ffn_be1"])
    base["be2"] = np.ascontiguousarray(inputs["L0_ffn_be2"])
    base["c2s_b"] = col(inputs["L0_ffn_c2s_b"], 2048)
    base["dec_b"] = col(inputs["L0_ffn_dec_b"], 1024)
    base["ln_g"] = col(inputs["L0_ffn_ln_g"], 1024)
    base["ln_b"] = col(inputs["L0_ffn_ln_b"], 1024)
    base["b_attn"] = col(inputs["L1_attn_a_b_attn"], 3072)
    base["b_proj"] = col(inputs["L1_attn_a_b_proj"], 1024)
    base["ln1_w"] = col(inputs["L1_attn_b_ln1_w"], 1024)
    base["ln1_b"] = col(inputs["L1_attn_b_ln1_b"], 1024)
    base["ln2_w"] = col(inputs["L1_attn_b_ln2_w"], 1024)
    base["ln2_b"] = col(inputs["L1_attn_b_ln2_b"], 1024)
    base["b_in"] = col(inputs["L1_attn_b_b_in"], 5460)
    base["b_out_l1b"] = col(inputs["L1_attn_b_b_out"], 1024)

    nc = build_fused()
    in_maps = []
    for ci in range(n):
        m = dict(base)
        m["x"] = np.ascontiguousarray(x[2 * ci:2 * ci + 2].reshape(2 * S_FULL, 1024))
        in_maps.append(m)
    res = run_bass_kernel_spmd(nc, in_maps, core_ids=list(range(n)))
    out = np.empty((16, 2048, 1024), np.float32)
    for ci in range(n):
        out[2 * ci:2 * ci + 2] = res.results[ci]["o"].reshape(2, 2048, 1024)
    return out


# version 29
# speedup vs baseline: 1.0455x; 1.0455x over previous
import math
import numpy as np

import concourse.bass as bass
import concourse.mybir as mybir
from concourse.bass_utils import run_bass_kernel_spmd

F32 = mybir.dt.float32
BF16 = mybir.dt.bfloat16
AF = mybir.ActivationFunctionType
ALU = mybir.AluOpType
AX = mybir.AxisListType

D = 1024
H = 16
HD = 64
NCORES = 8
EPS = 1e-5

ENGS = ("pe", "act", "dve", "pool", "sp")


class Buf:
    __slots__ = ("name", "last_write", "readers")

    def __init__(self, name):
        self.name = name
        self.last_write = None
        self.readers = []

    def __repr__(self):
        return f"Buf({self.name})"


class Arena:

    def __init__(self, t, ncols, slot_cols, name):
        self.t = t
        self.ncols = ncols
        self.slot_cols = slot_cols
        self.nslots = ncols // slot_cols
        self.bufs = [Buf(f"{name}[{i}]") for i in range(self.nslots)]
        self.name = name

    def bufs_for(self, lo, hi):
        s0 = lo // self.slot_cols
        s1 = (hi - 1) // self.slot_cols
        return self.bufs[s0:s1 + 1]


class RingAlloc:

    def __init__(self, arena):
        self.a = arena
        self.pos = 0
        self.top = arena.nslots

    def alloc(self, cols):
        nsl = (cols + self.a.slot_cols - 1) // self.a.slot_cols
        assert nsl <= self.top, f"ring alloc {cols} too big for {self.a.name}"
        if self.pos + nsl > self.top:
            self.pos = 0
        s0 = self.pos
        self.pos += nsl
        lo = s0 * self.a.slot_cols
        return lo, self.a.bufs[s0:s0 + nsl]

    def persist(self, cols):
        nsl = (cols + self.a.slot_cols - 1) // self.a.slot_cols
        self.top -= nsl
        assert self.top >= 0, "persist overflow"
        lo = self.top * self.a.slot_cols
        return lo, self.a.bufs[self.top:self.top + nsl]

    def scope(self):
        return self.top

    def unscope(self, saved):
        self.top = saved


class Ctx:
    def __init__(self, nc, name="mk"):
        self.nc = nc
        self.eng = {
            "pe": nc.tensor, "act": nc.scalar, "dve": nc.vector,
            "pool": nc.gpsimd, "sp": nc.sync,
        }
        self.esem = {}
        self.dram_bufs = {}
        self.ecount = {e: 0 for e in ENGS}
        self.waited = {}
        self.queues = {}
        self._closers = []
        self.uid = 0

    def init_sems(self, es):
        for e in ENGS:
            self.esem[e] = es.enter_context(self.nc.semaphore(f"S_{e}"))
        for q, n in (("sp", 12), ("pool", 8)):
            sems = [es.enter_context(self.nc.semaphore(f"Q_{q}{i}")) for i in range(n)]
            self.queues[q] = {"sems": sems, "exp": [0] * n, "i": 0}

    def nid(self):
        self.uid += 1
        return self.uid

    def _resolve(self, ek, tag):
        if tag is None:
            return None
        kind = tag[0]
        if kind == "e":
            _, pe, val = tag
            if pe == "pe" and ek == "pe":
                return None
            if pe == ek:
                assert val <= self.ecount[ek], f"wait on future own tag {tag}"
            sem = self.esem[pe]
        else:
            _, sem, val = tag
        key = (ek, id(sem))
        cur = self.waited.get(key, 0)
        if val <= cur:
            return None
        self.waited[key] = val
        return (sem, val)

    def _wait(self, ek, tag):
        r = self._resolve(ek, tag)
        if r is not None:
            self.eng[ek].wait_ge(r[0], r[1])

    def _collect(self, reads, writes):
        tags = []
        for b in reads:
            if b.last_write is not None:
                tags.append(b.last_write)
        for b in writes:
            if b.last_write is not None:
                tags.append(b.last_write)
            tags.extend(b.readers)
        best = {}
        for t in tags:
            if t is None:
                continue
            key = id(t[1]) if t[0] == "d" else t[1]
            if key not in best or best[key][2] < t[2]:
                best[key] = t
        return best.values()

    def _update(self, reads, writes, tag):
        for b in reads:
            b.readers.append(tag)
        for b in writes:
            b.last_write = tag
            b.readers = []

    def emit(self, ek, fn, reads=(), writes=(), inc=True):
        ws = [r for r in (self._resolve(ek, t) for t in self._collect(reads, writes))
              if r is not None]
        for (sem, val) in ws[1:]:
            self.eng[ek].wait_ge(sem, val)
        inst = fn(self.eng[ek])
        if ws:
            inst._wait_ge(ws[0][0], ws[0][1])
        if inc:
            self.ecount[ek] += 1
            inst.then_inc(self.esem[ek], 1)
        tag = ("e", ek, self.ecount[ek])
        self._update(reads, writes, tag)
        return tag

    def dma(self, q, dst_ap, src_ap, reads=(), writes=(), transpose=False,
            issue_ek=None, slow=False):
        qd = self.queues[q]
        i = qd["i"]
        qd["i"] = (i + 1) % len(qd["sems"])
        sem = qd["sems"][i]
        ek = issue_ek or ("sp" if q == "sp" else "pool")
        eng = self.eng[ek]
        ws = []
        if qd["exp"][i] > 0:
            key = (ek, id(sem))
            if self.waited.get(key, 0) < qd["exp"][i]:
                self.waited[key] = qd["exp"][i]
                ws.append((sem, qd["exp"][i]))
        reads = list(reads)
        writes = list(writes)
        try:
            nm = src_ap.tensor.name
            if nm in self.dram_bufs:
                reads.append(self.dram_bufs[nm])
            nm = dst_ap.tensor.name
            if nm in self.dram_bufs:
                writes.append(self.dram_bufs[nm])
        except AttributeError:
            pass
        ws += [r for r in (self._resolve(ek, t) for t in self._collect(reads, writes))
               if r is not None]
        for (s_, v_) in ws[1:]:
            eng.wait_ge(s_, v_)
        if transpose:
            inst = eng.dma_start_transpose(dst_ap, src_ap)
        elif slow:
            inst = eng.dma_start(dst_ap, src_ap, allow_slow_non_contiguous=True)
        else:
            inst = eng.dma_start(dst_ap, src_ap)
        if ws:
            inst._wait_ge(ws[0][0], ws[0][1])
        qd["exp"][i] += 16
        inst.then_inc(sem, 16)
        tag = ("d", sem, qd["exp"][i])
        self._update(reads, writes, tag)
        return tag

    def mm(self, psum_ap, psum_buf, steps, extra_reads=()):
        n = len(steps)
        for j, (lap, rap, rbufs) in enumerate(steps):
            first, last = j == 0, j == n - 1
            reads = list(rbufs) + (list(extra_reads) if last else [])
            wr = [psum_buf] if (first or last) else []
            if last:
                def fn(eng, lap=lap, rap=rap, first=first):
                    return eng.matmul(psum_ap, lap, rap, start=first, stop=True)
                self.emit("pe", fn, reads=reads,
                          writes=[psum_buf] if first else [psum_buf], inc=True)
            else:
                wb = [psum_buf] if first else []
                ws = [r for r in (self._resolve("pe", t)
                                  for t in self._collect(reads, wb))
                      if r is not None]
                for (s_, v_) in ws[1:]:
                    self.eng["pe"].wait_ge(s_, v_)
                inst = self.eng["pe"].matmul(psum_ap, lap, rap, start=first, stop=False)
                if ws:
                    inst._wait_ge(ws[0][0], ws[0][1])
                for b in reads:
                    b.readers.append(("e", "pe", self.ecount["pe"] + 1))

    def act(self, out_ap, in_ap, func, reads, writes, bias=0.0, scale=1.0,
            accum_ap=None):
        reads = list(reads)
        if isinstance(bias, tuple):
            bias, bb_ = bias
            reads += list(bb_)
        if isinstance(scale, tuple):
            scale, sb_ = scale
            reads += list(sb_)

        def fn(eng):
            return eng.activation(out_ap, in_ap, func, bias=bias, scale=scale,
                                  accum_out=accum_ap)
        return self.emit("act", fn, reads=reads, writes=writes)

    def tt(self, ek, out_ap, a_ap, b_ap, op, reads, writes):
        def fn(eng):
            return eng.tensor_tensor(out_ap, a_ap, b_ap, op)
        return self.emit(ek, fn, reads=reads, writes=writes)

    def ts(self, ek, out_ap, in_ap, s1, s2, op0, op1, reads, writes):
        def fn(eng):
            if s2 is None:
                return eng.tensor_single_scalar(out_ap, in_ap, s1, op0)
            return eng.tensor_scalar(out_ap, in_ap, s1, s2, op0, op1)
        return self.emit(ek, fn, reads=reads, writes=writes)

    def stt(self, ek, out_ap, in0_ap, scalar, in1_ap, op0, op1, reads, writes):
        def fn(eng):
            return eng.scalar_tensor_tensor(out_ap, in0_ap, scalar, in1_ap, op0, op1)
        return self.emit(ek, fn, reads=reads, writes=writes)

    def copy(self, ek, out_ap, in_ap, reads, writes):
        def fn(eng):
            if ek == "act":
                return eng.copy(out_ap, in_ap)
            return eng.tensor_copy(out_ap, in_ap)
        return self.emit(ek, fn, reads=reads, writes=writes)

    def red(self, ek, out_ap, in_ap, op, reads, writes, absval=False, negate=False):
        def fn(eng):
            return eng.tensor_reduce(out_ap, in_ap, AX.X, op,
                                     apply_absolute_value=absval, negate=negate)
        return self.emit(ek, fn, reads=reads, writes=writes)

    def memset(self, ek, ap, val, writes):
        def fn(eng):
            return eng.memset(ap, val)
        return self.emit(ek, fn, reads=(), writes=writes)

    def transpose(self, out_psum_ap, in_ap, ident_ap, reads, writes, ident_buf):
        def fn(eng):
            return eng.transpose(out_psum_ap, in_ap, ident_ap)
        return self.emit("pe", fn, reads=list(reads) + [ident_buf], writes=writes)

    def barrier(self):
        for e in ENGS:
            for f in ENGS:
                if e != f and self.ecount[f] > 0:
                    self._wait(e, ("e", f, self.ecount[f]))



class Prog:

    def __init__(self, nc, es, S=2048, n_elems=2):
        self.nc = nc
        self.es = es
        self.S = S
        self.NT = S // 128
        self.n_elems = n_elems
        self.WQ = "pool"
        self.ctx = Ctx(nc)
        self.ctx.init_sems(es)
        c = self.ctx

        def sbuf(name, shape, dt):
            return es.enter_context(nc.sbuf_tensor(name, shape, dt))

        def psum(name, shape, dt):
            return es.enter_context(nc.psum_tensor(name, shape, dt))

        self.RES = Arena(sbuf("RES", [128, self.NT * 1024], F32), self.NT * 1024, 1024, "RES")
        self.AA = Arena(sbuf("AA", [128, 8 * S], BF16), 8 * S, 512, "AA")
        abcols = max(8 * S, 16384)
        self.AB = Arena(sbuf("AB", [128, abcols], BF16), abcols, 512, "AB")
        self.WR = Arena(sbuf("WR", [128, 12288], BF16), 12288, 512, "WR")
        self.SM = Arena(sbuf("SM", [128, 7680], F32), 7680, 128, "SM")
        self.BB = Arena(sbuf("BB", [128, 12288], BF16), 12288, 256, "BB")
        self.wr_ring = RingAlloc(self.WR)
        self.sm_ring = RingAlloc(self.SM)
        self.bb_ring = RingAlloc(self.BB)

        self.PB = [psum(f"pb{i}", [128, 512], F32) for i in range(4)]
        self.PBbuf = [Buf(f"pb{i}") for i in range(4)]
        self.PA = [psum(f"pa{i}", [128, 1024], F32) for i in range(2)]
        self.PAbuf = [Buf(f"pa{i}") for i in range(2)]
        self._pb_i = 0

        self.ID_BF = sbuf("ID_BF", [128, 128], BF16)
        self.ID_F32 = sbuf("ID_F32", [128, 128], F32)
        self.ID_buf = Buf("ID_BF")
        self.IDF_buf = Buf("ID_F32")
        self.CONST = sbuf("CONST", [128, 8], F32)
        self.CONST_buf = Buf("CONST")
        self.ONE_ROW = sbuf("ONE_ROW", [1, 128], F32)
        self.ONE_ROW_buf = Buf("ONE_ROW")
        self.ONE_ROW_BF = sbuf("ONE_ROW_BF", [1, 128], BF16)
        self.ONE_ROW_BF_buf = Buf("ONE_ROW_BF")
        self.MASKHALF = sbuf("MASKHALF", [128, 2], BF16)
        self.MASKHALF_buf = Buf("MASKHALF")
        self.ONE_COL_BF = sbuf("ONE_COL_BF", [128, 1], BF16)
        self.ONE_COL_BF_buf = Buf("ONE_COL_BF")

        self._const_cache = {}
        self._const_slot = self.sm_ring.persist(128)
        self._const_used = 0
        self._emit_consts()

    def pb(self):
        i = self._pb_i
        self._pb_i = (i + 1) % len(self.PB)
        return self.PB[i], self.PBbuf[i]

    def _emit_consts(self):
        c = self.ctx
        for t, b in ((self.ID_BF, self.ID_buf), (self.ID_F32, self.IDF_buf)):
            c.memset("pool", t[:, :], 1.0, writes=[b])

            def fn(eng, t=t):
                return eng.affine_select(t[:, :], t[:, :], [[-1, 128]],
                                         ALU.is_equal, 0.0, channel_multiplier=1)
            c.emit("pool", fn, reads=[b], writes=[b])
        def fn_iota(eng):
            return eng.iota(self.CONST[:, 0:1], [[0, 1]], channel_multiplier=1,
                            allow_small_or_imprecise_dtypes=True)
        c.emit("pool", fn_iota, writes=[self.CONST_buf])
        c.memset("pool", self.CONST[:, 1:2], 1.0, writes=[self.CONST_buf])
        c.memset("pool", self.ONE_ROW[:, :], 1.0, writes=[self.ONE_ROW_buf])
        c.memset("pool", self.ONE_ROW_BF[:, :], 1.0, writes=[self.ONE_ROW_BF_buf])
        c.memset("pool", self.MASKHALF[:, :], 1.0, writes=[self.MASKHALF_buf])
        c.memset("pool", self.ONE_COL_BF[:, :], 1.0, writes=[self.ONE_COL_BF_buf])

        def fn_mh0(eng):
            return eng.affine_select(self.MASKHALF[:, 0:1], self.MASKHALF[:, 0:1],
                                     [[0, 1]], ALU.is_ge, 0.0, base=63,
                                     channel_multiplier=-1)
        c.emit("pool", fn_mh0, reads=[self.MASKHALF_buf], writes=[self.MASKHALF_buf])

        def fn_mh1(eng):
            return eng.affine_select(self.MASKHALF[:, 1:2], self.MASKHALF[:, 1:2],
                                     [[0, 1]], ALU.is_ge, 0.0, base=-64,
                                     channel_multiplier=1)
        c.emit("pool", fn_mh1, reads=[self.MASKHALF_buf], writes=[self.MASKHALF_buf])

    def rsqrt(self, out_ap, in_ap, eps, reads, writes, power=-0.5):
        c = self.ctx
        e_ap, e_b = self.const_col(float(eps))
        pw_ap, pw_b = self.const_col(float(power))
        pr = in_ap.shape[0]
        c.act(out_ap, in_ap, AF.Ln, bias=(e_ap[0:pr, :], e_b), reads=reads, writes=writes)
        c.act(out_ap, out_ap, AF.Exp, scale=(pw_ap[0:pr, :], pw_b), reads=writes, writes=writes)
        return writes

    def const_col(self, value):
        if value in self._const_cache:
            return self._const_cache[value]
        assert self._const_used < 128
        lo, bufs = self._const_slot
        col = lo + self._const_used
        self._const_used += 1
        ap = self.SM.t[:, col:col + 1]
        self.ctx.memset("pool", ap, float(value), writes=bufs)
        self._const_cache[value] = (ap, bufs)
        return ap, bufs

    def to_bf16_dram(self, src_fp32_ap, dst_bf16_ap, rows, cols, tag=""):
        c = self.ctx
        eidx = 0
        for r0 in range(0, rows, 128):
            pr = min(128, rows - r0)
            for c0 in range(0, cols, 2048):
                w = min(2048, cols - c0)
                slo, sbufs = self.sm_ring.alloc(w)
                dlo, dbufs = self.wr_ring.alloc(w)
                st = self.SM.t
                dt_ = self.WR.t
                c.dma("pool", st[0:pr, slo:slo + w], src_fp32_ap[r0:r0 + pr, c0:c0 + w],
                      writes=sbufs)
                ek = ("dve", "act", "pool")[eidx % 3]
                eidx += 1
                c.copy(ek, dt_[0:pr, dlo:dlo + w], st[0:pr, slo:slo + w],
                       reads=sbufs, writes=dbufs)
                c.dma("pool", dst_bf16_ap[r0:r0 + pr, c0:c0 + w], dt_[0:pr, dlo:dlo + w],
                      reads=dbufs)

    def load_bias_col(self, dram_1d_ap, n, name):
        c = self.ctx
        m = (n + 127) // 128
        slo, sbufs = self.sm_ring.alloc(128)
        st = self.SM.t
        if n % 128 != 0:
            c.memset("dve", st[0:m, slo:slo + 128], 0.0, writes=sbufs)
        full = (n // 128) * 128
        if full > 0:
            c.dma("pool", st[0:n // 128, slo:slo + 128].rearrange("p f -> p f"),
                  dram_1d_ap[0:full].rearrange("(p f) -> p f", f=128), reads=(), writes=sbufs)
        if n % 128 != 0:
            c.dma("pool", st[n // 128:n // 128 + 1, slo:slo + n % 128],
                  dram_1d_ap[full:n].rearrange("(a f) -> a f", a=1), reads=(), writes=sbufs)
        pbk, pbuf = self.pb()
        c.transpose(pbk[:, 0:m].rearrange("p f -> p f"), st[0:m, slo:slo + 128],
                    self.ID_F32[0:m, 0:m], reads=sbufs, writes=[pbuf], ident_buf=self.IDF_buf)
        dlo, dbufs = self.sm_ring.persist(m)
        c.copy("dve", st[:, dlo:dlo + m], pbk[:, 0:m], reads=[pbuf], writes=dbufs)
        return st, dlo, m, dbufs

    def wload(self, dram_bf16_ap, rows, cols, q=None, persist=False):
        c = self.ctx
        if q is None:
            q = self.WQ
        nch = (rows + 127) // 128
        cpad = ((cols + 511) // 512) * 512
        if persist:
            lo0, _ = self.wr_ring.persist(cpad * nch)
        else:
            lo0, _ = self.wr_ring.alloc(cpad * nch)
        out = []
        for i, r0 in enumerate(range(0, rows, 128)):
            pr = min(128, rows - r0)
            lo = lo0 + i * cpad
            bufs = self.WR.bufs_for(lo, lo + cols)
            ap = self.WR.t[0:pr, lo:lo + cols]
            c.dma(q, ap, dram_bf16_ap[r0:r0 + pr, 0:cols], writes=bufs)
            out.append((self.WR.t, lo, pr, cols, bufs))
        return out

    def ln_xn(self, elem_tag="", spill_fp32=None, spill_bufs=None, want_xnT=True,
              want_xn_tok=True):
        c = self.ctx
        S, NT = self.S, self.NT
        st = self.SM.t

        def stage1(t):
            slo, sbufs = self.sm_ring.alloc(16)
            r = self.RES.t[:, t * 1024:(t + 1) * 1024]
            rb = self.RES.bufs[t]
            def f1(eng, r=r):
                return eng.bn_stats(st[:, slo:slo + 6], r[:, 0:512])
            def f2(eng, r=r):
                return eng.bn_stats(st[:, slo + 6:slo + 12], r[:, 512:1024])
            c.emit("dve", f1, reads=[rb], writes=sbufs)
            c.emit("dve", f2, reads=[rb], writes=sbufs)
            def f3(eng):
                return eng.bn_aggr(st[:, slo + 12:slo + 14], st[:, slo:slo + 12])
            c.emit("dve", f3, reads=sbufs, writes=sbufs)
            self.rsqrt(st[:, slo + 14:slo + 15], st[:, slo + 13:slo + 14], EPS,
                       reads=sbufs, writes=sbufs)
            return slo, sbufs

        st1 = {0: stage1(0)}
        for t in range(NT):
            if t + 1 < NT:
                st1[t + 1] = stage1(t + 1)
            slo, sbufs = st1.pop(t)
            r = self.RES.t[:, t * 1024:(t + 1) * 1024]
            rb = self.RES.bufs[t]
            c.ts("dve", st[:, slo + 15:slo + 16], st[:, slo + 12:slo + 13],
                 st[:, slo + 14:slo + 15], -1.0, ALU.mult, ALU.mult,
                 reads=sbufs, writes=sbufs)
            if want_xn_tok or want_xnT:
                xn = self.AB.t[:, t * 1024:(t + 1) * 1024]
                xnb = self.AB.bufs_for(t * 1024, (t + 1) * 1024)
                c.ts("dve", xn, r, st[:, slo + 14:slo + 15], st[:, slo + 15:slo + 16],
                     ALU.mult, ALU.add, reads=[rb] + sbufs, writes=xnb)
                if spill_fp32 is not None:
                    sap, sb = spill_fp32
                    c.dma("pool", sap[t * 128:(t + 1) * 128, :], xn, reads=xnb,
                          writes=sb)
                if want_xnT:
                    dst = self.AA.t[:, t * 128:t * 128 + 128]
                    dst = self.AA.t[:, :].rearrange("p (c s) -> p c s", c=8)[:, :, t * 128:(t + 1) * 128]
                    dstb = []
                    for ch in range(8):
                        dstb += self.AA.bufs_for(ch * self.S + t * 128, ch * self.S + (t + 1) * 128)
                    c.dma("sp", dst, xn, reads=xnb, writes=dstb, transpose=True)

    def store_res(self, out_ap):
        c = self.ctx
        for t in range(self.NT):
            c.dma("sp", out_ap[t * 128:(t + 1) * 128, :],
                  self.RES.t[:, t * 1024:(t + 1) * 1024], reads=[self.RES.bufs[t]])

    def load_res(self, in_ap):
        c = self.ctx
        for t in range(self.NT):
            c.dma("sp", self.RES.t[:, t * 1024:(t + 1) * 1024],
                  in_ap[t * 128:(t + 1) * 128, :], writes=[self.RES.bufs[t]])

    def finish(self):
        c = self.ctx
        for q, qd in c.queues.items():
            ek = "sp" if q == "sp" else "pool"
            for i, sem in enumerate(qd["sems"]):
                if qd["exp"][i] > 0:
                    key = (ek, id(sem))
                    if c.waited.get(key, 0) < qd["exp"][i]:
                        c.waited[key] = qd["exp"][i]
                        c.eng[ek].wait_ge(sem, qd["exp"][i])

    def build_diag_masks(self, col):
        c = self.ctx
        self._mk_col = col
        ap = self.BB.t[:, col:col + 896]
        bufs = self.BB.bufs_for(col, col + 896)
        c.memset("pool", ap, 1.0, writes=bufs)

        def fn(eng, ap=ap):
            return eng.affine_select(ap, ap, [[1, 896]], ALU.is_ge, 0.0,
                                     base=-384, channel_multiplier=-1)
        c.emit("pool", fn, reads=bufs, writes=bufs)

    def mk(self, mi, width):
        col = self._mk_col + 384 - mi * 128
        return self.BB.t[:, col:col + width], self.BB.bufs_for(col, col + width)

    def head_tables(self, scale_row, bias_row, row_reads):
        c = self.ctx
        st = self.SM.t
        out = []
        for (rap, rbufs) in (scale_row, bias_row):
            n = rap.shape[-1]
            pbk, pbuf = self.PB[3], self.PBbuf[3]
            def fn(eng, rap=rap, n=n):
                return eng.matmul(pbk[:, 0:n], self.ONE_ROW[0:1, 0:128], rap,
                                  start=True, stop=True)
            c.emit("pe", fn, reads=[self.ONE_ROW_buf] + list(rbufs) + list(row_reads),
                   writes=[pbuf])
            lo, bufs = self.sm_ring.persist(n)
            c.copy("dve", st[:, lo:lo + n], pbk[:, 0:n], reads=[pbuf], writes=bufs)
            out.append((lo, bufs))
        return out


def emit_l0_attn_a(p, W, dbg=None):
    c = p.ctx
    S, NT = p.S, p.NT
    assert S % 512 == 0
    NQ = S // 512
    NKC = S // 128
    st = p.SM.t
    bb = p.BB.t
    _scope = p.sm_ring.scope()

    p.ln_xn()
    p.build_diag_masks(10016)

    lo_nk, b_nk = p.sm_ring.alloc(128)
    c.dma("pool", st[0:16, lo_nk:lo_nk + 64], W["null_k"], writes=b_nk)
    c.copy("dve", st[0:16, lo_nk + 64:lo_nk + 128], st[0:16, lo_nk:lo_nk + 64],
           reads=b_nk, writes=b_nk)
    pbk, pbuf = p.PB[2], p.PBbuf[2]
    c.transpose(pbk[:, 0:16], st[0:16, lo_nk:lo_nk + 128],
                p.ID_F32[0:16, 0:16], reads=b_nk, writes=[pbuf], ident_buf=p.IDF_buf)
    NKT = 8960
    b_nkT = p.BB.bufs_for(NKT, NKT + 16)
    c.copy("dve", bb[:, NKT:NKT + 16], pbk[:, 0:16], reads=[pbuf], writes=b_nkT)
    lo_n2, b_n2 = p.sm_ring.alloc(128)
    c.tt("dve", st[0:16, lo_n2:lo_n2 + 64], st[0:16, lo_nk:lo_nk + 64],
         st[0:16, lo_nk:lo_nk + 64], ALU.mult, reads=b_nk, writes=b_n2)
    c.red("dve", st[0:16, lo_n2 + 64:lo_n2 + 65], st[0:16, lo_n2:lo_n2 + 64], ALU.add,
          reads=b_n2, writes=b_n2)
    c.act(st[0:16, lo_n2 + 65:lo_n2 + 66], st[0:16, lo_n2 + 64:lo_n2 + 65], AF.Sqrt,
          reads=b_n2, writes=b_n2)
    c.transpose(pbk[0:1, 16:32], st[0:16, lo_n2 + 65:lo_n2 + 66],
                p.ID_F32[0:16, 0:16], reads=b_n2, writes=[pbuf], ident_buf=p.IDF_buf)
    lo_nn, b_nn = p.sm_ring.persist(16)
    c.copy("dve", st[0:1, lo_nn:lo_nn + 16], pbk[0:1, 16:32], reads=[pbuf], writes=b_nn)

    NV = 8976
    b_nv = p.BB.bufs_for(NV, NV + 1040)
    lo_v, b_v = p.sm_ring.alloc(1024)
    c.dma("pool", st[0:1, lo_v:lo_v + 1024], W["null_v"].rearrange("(a h) d -> a (h d)", a=1),
          writes=b_v)
    c.copy("dve", bb[0:1, NV:NV + 1040].rearrange("p (h d) -> p h d", h=16)[:, :, 0:64],
           st[0:1, lo_v:lo_v + 1024].rearrange("p (h d) -> p h d", h=16), reads=b_v,
           writes=b_nv)
    c.memset("dve", bb[0:1, NV:NV + 1040].rearrange("p (h d) -> p h d", h=16)[:, :, 64:65],
             1.0, writes=b_nv)

    lo_ls, b_ls = p.sm_ring.alloc(16)
    c.dma("pool", st[0:1, lo_ls:lo_ls + 16], W["logit_scale"], writes=b_ls)
    lo_c, b_c = p.sm_ring.persist(16)
    bconst = p.const_col(-math.log(8.0))
    c.act(st[0:1, lo_c:lo_c + 16], st[0:1, lo_ls:lo_ls + 16], AF.Exp,
          bias=(bconst[0][0:1, :], bconst[1]), reads=b_ls, writes=b_c)

    lo_qm, b_qm = p.sm_ring.persist(16)
    lo_km, b_km = p.sm_ring.persist(16)
    c.memset("dve", st[0:1, lo_qm:lo_qm + 16], 0.0, writes=b_qm)
    c.memset("dve", st[0:1, lo_km:lo_km + 16], 0.0, writes=b_km)

    QT, KT, VA, EXL, SQ = 0, 2048, 4096, 6400, 8448

    xnT_bufs = p.AA.bufs_for(0, 8 * S)

    def qkv_pair(pair):
        wq = p.wload(W["wq"][:, pair * 128:(pair + 1) * 128], 1024, 128)
        wk = p.wload(W["wk"][:, pair * 128:(pair + 1) * 128], 1024, 128)
        wv = p.wload(W["wv"][:, pair * 128:(pair + 1) * 128], 1024, 128)
        b_qT = p.BB.bufs_for(QT, QT + S)
        b_kT = p.BB.bufs_for(KT, KT + S)
        b_va = p.BB.bufs_for(VA, VA + 16 * 130)
        SQ2 = 11424
        pend_n = []
        tci = 0
        for (wtiles, dstcol, dstbufs, which) in ((wq, QT, b_qT, "q"), (wk, KT, b_kT, "k")):
            for tc in range(NQ):
                pbk, pbuf = (p.PB[0], p.PBbuf[0]) if tc % 2 == 0 else (p.PB[1], p.PBbuf[1])
                steps = []
                for kc in range(8):
                    (wt, wlo, wpr, wcols, wbufs) = wtiles[kc]
                    lap = wt[0:128, wlo:wlo + 128]
                    rap = p.AA.t[:, kc * S + tc * 512: kc * S + tc * 512 + 512]
                    steps.append((lap, rap, list(wbufs) + p.AA.bufs_for(kc * S + tc * 512, kc * S + tc * 512 + 512)))
                c.mm(pbk[:, :], pbuf, steps)
                c.copy("act", bb[:, dstcol + tc * 512: dstcol + tc * 512 + 512],
                       pbk[:, :], reads=[pbuf], writes=dstbufs)
                sqlo = SQ if tci % 2 == 0 else SQ2
                tci += 1
                sqb = p.BB.bufs_for(sqlo, sqlo + 512)
                c.tt("pool", bb[:, sqlo:sqlo + 512],
                     bb[:, dstcol + tc * 512: dstcol + tc * 512 + 512],
                     bb[:, dstcol + tc * 512: dstcol + tc * 512 + 512], ALU.mult,
                     reads=dstbufs, writes=sqb)
                if pend_n:
                    pend_n.pop(0)()

                def mk_norm(sqlo=sqlo, sqb=sqb, which=which):
                    def emit_norm():
                        pn, pnb = p.PB[2], p.PBbuf[2]
                        def fn(eng):
                            return eng.matmul(pn[0:2, 0:512], p.MASKHALF[:, 0:2],
                                              bb[:, sqlo:sqlo + 512], start=True, stop=True)
                        c.emit("pe", fn, reads=sqb + [p.MASKHALF_buf], writes=[pnb])
                        lo_t, b_t = p.sm_ring.alloc(2)
                        c.red("dve", st[0:2, lo_t:lo_t + 1], pn[0:2, 0:512], ALU.max,
                              reads=[pnb], writes=b_t)
                        c.transpose(pn[0:1, 508:510], st[0:2, lo_t:lo_t + 1],
                                    p.ID_F32[0:2, 0:2], reads=b_t, writes=[pnb],
                                    ident_buf=p.IDF_buf)
                        accap = st[0:1, (lo_qm if which == "q" else lo_km) + 2 * pair:
                                   (lo_qm if which == "q" else lo_km) + 2 * pair + 2]
                        accb = b_qm if which == "q" else b_km
                        c.tt("dve", accap, accap, pn[0:1, 508:510], ALU.max,
                             reads=[pnb] + accb, writes=accb)
                    return emit_norm

                pend_n.append(mk_norm())
        while pend_n:
            pend_n.pop(0)()
        for tt_ in range(NT):
            pbk, pbuf = (p.PB[0], p.PBbuf[0]) if tt_ % 2 == 0 else (p.PB[1], p.PBbuf[1])
            steps = []
            for kc in range(8):
                (wt, wlo, wpr, wcols, wbufs) = wv[kc]
                lap = p.AA.t[:, kc * S + tt_ * 128: kc * S + tt_ * 128 + 128]
                rap = wt[0:128, wlo:wlo + 128]
                steps.append((lap, rap,
                              list(wbufs) + p.AA.bufs_for(kc * S + tt_ * 128, kc * S + tt_ * 128 + 128)))
            c.mm(pbk[:, 0:128], pbuf, steps)
            for hh in range(2):
                c.copy("act", bb[:, VA + tt_ * 130 + hh * 65: VA + tt_ * 130 + hh * 65 + 64],
                       pbk[:, hh * 64:hh * 64 + 64], reads=[pbuf], writes=b_va)
        ones_ap = bb[:, VA:VA + NT * 130].rearrange("p (t h d) -> p t h d", t=NT, h=2)[:, :, :, 64:65]
        c.memset("dve", ones_ap, 1.0, writes=b_va)
        return b_qT, b_kT, b_va

    def pair_tables(pair, b_reads):
        lo_r, b_r = p.sm_ring.persist(8)
        sl = slice(lo_r, lo_r + 2)
        c.act(st[0:1, lo_r:lo_r + 2], st[0:1, lo_qm + 2 * pair:lo_qm + 2 * pair + 2],
              AF.Sqrt, reads=b_qm + b_reads, writes=b_r)
        c.act(st[0:1, lo_r + 2:lo_r + 4], st[0:1, lo_km + 2 * pair:lo_km + 2 * pair + 2],
              AF.Sqrt, reads=b_km + b_reads, writes=b_r)
        c.tt("dve", st[0:1, lo_r + 2:lo_r + 4], st[0:1, lo_r + 2:lo_r + 4],
             st[0:1, lo_nn + 2 * pair:lo_nn + 2 * pair + 2], ALU.max,
             reads=b_r + b_nn, writes=b_r)
        c.tt("dve", st[0:1, lo_r + 4:lo_r + 6], st[0:1, lo_r:lo_r + 2],
             st[0:1, lo_r + 2:lo_r + 4], ALU.mult, reads=b_r, writes=b_r)
        c.stt("dve", st[0:1, lo_r + 6:lo_r + 8], st[0:1, lo_r + 4:lo_r + 6], -1.0,
              st[0:1, lo_c + 2 * pair:lo_c + 2 * pair + 2], ALU.mult, ALU.mult,
              reads=b_r + b_c, writes=b_r)
        tabs = p.head_tables((st[0:1, lo_c + 2 * pair:lo_c + 2 * pair + 2], b_c),
                             (st[0:1, lo_r + 6:lo_r + 8], b_r), b_r)
        return tabs

    for pair in range(8):
        _pscope = p.sm_ring.scope()
        b_qT, b_kT, b_va = qkv_pair(pair)
        (lo_sc, b_sc), (lo_bi, b_bi) = pair_tables(pair, [])
        if dbg is not None and pair == 0:
            dbg("pair0", dict(b_qT=b_qT, b_kT=b_kT, b_va=b_va, lo_sc=lo_sc,
                              b_sc=b_sc, lo_bi=lo_bi, b_bi=b_bi, QT=QT, KT=KT,
                              VA=VA, lo_qm=lo_qm, lo_km=lo_km, b_qm=b_qm, b_km=b_km))
        pending = []
        for hh in range(2):
            h = 2 * pair + hh
            qrow0 = hh * 64
            for qc in range(NQ):
                par = (qc + hh) % 2
                ob, obuf = (p.PB[2], p.PBbuf[2]) if par == 0 else (p.PB[3], p.PBbuf[3])
                alt, altbuf = (p.PB[3], p.PBbuf[3]) if par == 0 else (p.PB[2], p.PBbuf[2])
                nb, nbuf = p.PB[0], p.PBbuf[0]
                def fns(eng, nb=nb, qrow0=qrow0, qc=qc):
                    return eng.matmul(nb[0:1, :], bb[qrow0:qrow0 + 64, NKT + h:NKT + h + 1],
                                      bb[qrow0:qrow0 + 64, QT + qc * 512:QT + qc * 512 + 512],
                                      start=True, stop=True)
                c.emit("pe", fns, reads=b_nkT + b_qT, writes=[nbuf])
                exn = 10912
                exnb = p.BB.bufs_for(exn, exn + 512)
                c.act(bb[0:1, exn:exn + 512], nb[0:1, :], AF.Exp,
                      bias=st[0:1, lo_bi:lo_bi + 2][:, hh:hh + 1],
                      scale=st[0:1, lo_sc:lo_sc + 2][:, hh:hh + 1],
                      reads=[nbuf] + b_sc + b_bi, writes=exnb)
                kc_hi = min(NKC, 4 * qc + 4)
                exi = 0
                pend_pv = None
                for j0 in range(0, kc_hi, 2):
                    pk = list(range(j0, min(j0 + 2, kc_hi)))
                    sa, sabuf = (p.PA[0], p.PAbuf[0]) if (j0 // 2) % 2 == 0 else (p.PA[1], p.PAbuf[1])
                    for i2, kc in enumerate(pk):
                        def fsc(eng, kc=kc, i2=i2, sa=sa, qrow0=qrow0, qc=qc):
                            return eng.matmul(
                                sa[:, i2 * 512:i2 * 512 + 512],
                                bb[qrow0:qrow0 + 64, KT + kc * 128:KT + kc * 128 + 128],
                                bb[qrow0:qrow0 + 64, QT + qc * 512:QT + qc * 512 + 512],
                                start=True, stop=True)
                        c.emit("pe", fsc, reads=b_kT + b_qT, writes=[sabuf])
                    if j0 == 0 and pending:
                        pending.pop(0)()
                    w2 = 512 * len(pk)
                    ex = EXL + (exi % 4) * 512
                    exi += 2
                    exb = p.BB.bufs_for(ex, ex + w2)
                    c.act(bb[:, ex:ex + w2], sa[:, 0:w2], AF.Exp,
                          bias=st[:, lo_bi:lo_bi + 2][:, hh:hh + 1],
                          scale=st[:, lo_sc:lo_sc + 2][:, hh:hh + 1],
                          reads=[sabuf] + b_sc + b_bi, writes=exb)
                    for i2, kc in enumerate(pk):
                        o = kc * 128 - qc * 512
                        if o >= 0:
                            mkap, mkb = p.mk(o // 128, 512)
                            c.tt("pool", bb[:, ex + i2 * 512:ex + i2 * 512 + 512],
                                 bb[:, ex + i2 * 512:ex + i2 * 512 + 512],
                                 mkap, ALU.mult, reads=exb + mkb, writes=exb)
                    if pend_pv is not None:
                        pend_pv()

                    def mk_pv(pk=pk, ex=ex, exb=exb, ob=ob, obuf=obuf, hh=hh):
                        def emit_pv():
                            for i2, kc in enumerate(pk):
                                def fpv(eng, kc=kc, i2=i2, first=(kc == 0)):
                                    return eng.matmul(ob[0:65, :],
                                                      bb[:, VA + kc * 130 + hh * 65:VA + kc * 130 + hh * 65 + 65],
                                                      bb[:, ex + i2 * 512:ex + i2 * 512 + 512],
                                                      start=first, stop=False)
                                c.emit("pe", fpv, reads=b_va + exb, writes=[obuf])
                        return emit_pv

                    pend_pv = mk_pv()
                pend_pv()
                def fpvn(eng, exn=exn, ob=ob, h=h):
                    return eng.matmul(ob[0:65, :], bb[0:1, NV + h * 65:NV + h * 65 + 65],
                                      bb[0:1, exn:exn + 512], start=False, stop=True)
                c.emit("pe", fpvn, reads=b_nv + exnb, writes=[obuf])

                def tail(ob=ob, obuf=obuf, h=h, qc=qc):
                    lo_rr, b_rr = p.sm_ring.alloc(512)
                    p.rsqrt(st[0:1, lo_rr:lo_rr + 512], ob[64:65, :], 0.0,
                            reads=[obuf], writes=b_rr, power=-1.0)
                    och = h // 2
                    orow = (h % 2) * 64
                    ocol = och * S + qc * 512
                    ob_bufs = p.AB.bufs_for(ocol, ocol + 512)
                    dstab = p.AB.t[orow:orow + 64, ocol:ocol + 512]
                    c.copy("act", dstab, ob[0:64, :], reads=[obuf], writes=ob_bufs)
                    bcb, bcbuf = p.PB[1], p.PBbuf[1]
                    def fbc(eng):
                        return eng.matmul(bcb[0:64, :], p.ONE_ROW[0:1, 0:64],
                                          st[0:1, lo_rr:lo_rr + 512], start=True, stop=True)
                    c.emit("pe", fbc, reads=[p.ONE_ROW_buf] + b_rr, writes=[bcbuf])
                    c.tt("dve", dstab, dstab, bcb[0:64, :], ALU.mult,
                         reads=ob_bufs + [bcbuf], writes=ob_bufs)
                pending.append(tail)
        while pending:
            pending.pop(0)()

    wo = p.wload(W["wo"], 1024, 1024)
    for t in range(NT):
        for half in range(2):
            pbk, pbuf = (p.PB[0], p.PBbuf[0]) if half == 0 else (p.PB[1], p.PBbuf[1])
            steps = []
            for kc in range(8):
                (wt, wlo, wpr, wcols, wbufs) = wo[kc]
                lap = p.AB.t[:, kc * S + t * 128: kc * S + t * 128 + 128]
                rap = wt[0:128, wlo + half * 512: wlo + half * 512 + 512]
                steps.append((lap, rap,
                              list(wbufs) + p.AB.bufs_for(kc * S + t * 128, kc * S + t * 128 + 128)))
            c.mm(pbk[:, :], pbuf, steps)
            rb = p.RES.bufs[t]
            rap = p.RES.t[:, t * 1024 + half * 512: t * 1024 + half * 512 + 512]
            c.tt("dve", rap, rap, pbk[:, :], ALU.add, reads=[pbuf, rb], writes=[rb])
    p.sm_ring.unscope(_scope)


def emit_l1_attn_a(p, W):
    c = p.ctx
    S, NT = p.S, p.NT
    NKC = S // 128
    st = p.SM.t
    bb = p.BB.t
    _scope = p.sm_ring.scope()

    p.ln_xn()
    p.build_diag_masks(9984)

    t_, lo_ba, nba, b_ba = p.load_bias_col(W["b_attn"].rearrange("a n -> (a n)"), 3072, "ba")
    lo_bp, b_bp = p.sm_ring.persist(1024)
    c.dma("pool", st[0:1, lo_bp:lo_bp + 1024], W["b_proj"], writes=b_bp)
    VBROW = 8960
    b_vbrow = p.BB.bufs_for(VBROW, VBROW + 1024)
    lo_tmp, b_tmp = p.sm_ring.alloc(1024)
    c.dma("pool", st[0:1, lo_tmp:lo_tmp + 1024], W["b_attn"][0:1, 2048:3072], writes=b_tmp)
    c.copy("dve", bb[0:1, VBROW:VBROW + 1024], st[0:1, lo_tmp:lo_tmp + 1024],
           reads=b_tmp, writes=b_vbrow)

    lo_qm, b_qm = p.sm_ring.persist(16)
    lo_km, b_km = p.sm_ring.persist(16)
    c.memset("dve", st[0:1, lo_qm:lo_qm + 16], 0.0, writes=b_qm)
    c.memset("dve", st[0:1, lo_km:lo_km + 16], 0.0, writes=b_km)

    QT, KT, VA, EXL, SQ = 0, 2048, 4096, 6400, 8448
    NTI = NT + 3
    inv8_ap, inv8_b = p.const_col(0.125)

    def qkv_pair(pair):
        wq = p.wload(W["w_attn"][:, pair * 128:(pair + 1) * 128], 1024, 128)
        wk = p.wload(W["w_attn"][:, 1024 + pair * 128:1024 + (pair + 1) * 128], 1024, 128)
        wv = p.wload(W["w_attn"][:, 2048 + pair * 128:2048 + (pair + 1) * 128], 1024, 128)
        b_qT = p.BB.bufs_for(QT, QT + S)
        b_kT = p.BB.bufs_for(KT, KT + S)
        b_va = p.BB.bufs_for(VA, VA + NT * 130)
        for (wtiles, dstcol, dstbufs, which) in ((wq, QT, b_qT, "q"), (wk, KT, b_kT, "k")):
            bcol = (0 if which == "q" else 8) + pair
            for tc in range(S // 512):
                pbk, pbuf = (p.PB[0], p.PBbuf[0]) if tc % 2 == 0 else (p.PB[1], p.PBbuf[1])
                steps = []
                for kc in range(8):
                    (wt, wlo, wpr, wcols, wbufs) = wtiles[kc]
                    steps.append((wt[0:128, wlo:wlo + 128],
                                  p.AA.t[:, kc * S + tc * 512: kc * S + tc * 512 + 512],
                                  list(wbufs) + p.AA.bufs_for(kc * S + tc * 512, kc * S + tc * 512 + 512)))
                c.mm(pbk[:, :], pbuf, steps)
                def fev(eng, pbk=pbk, tc=tc, dstcol=dstcol, bcol=bcol):
                    return eng.activation(bb[:, dstcol + tc * 512: dstcol + tc * 512 + 512],
                                          pbk[:, :], AF.Identity,
                                          bias=st[:, lo_ba + bcol:lo_ba + bcol + 1])
                c.emit("act", fev, reads=[pbuf] + b_ba, writes=dstbufs)
                sqb = p.BB.bufs_for(SQ, SQ + 512)
                c.tt("pool", bb[:, SQ:SQ + 512],
                     bb[:, dstcol + tc * 512: dstcol + tc * 512 + 512],
                     bb[:, dstcol + tc * 512: dstcol + tc * 512 + 512], ALU.mult,
                     reads=dstbufs, writes=sqb)
                pn, pnb = p.PB[2], p.PBbuf[2]
                def fn(eng):
                    return eng.matmul(pn[0:2, 0:512], p.MASKHALF[:, 0:2],
                                      bb[:, SQ:SQ + 512], start=True, stop=True)
                c.emit("pe", fn, reads=sqb + [p.MASKHALF_buf], writes=[pnb])
                lo_t, b_t = p.sm_ring.alloc(2)
                c.red("dve", st[0:2, lo_t:lo_t + 1], pn[0:2, 0:512], ALU.max,
                      reads=[pnb], writes=b_t)
                c.transpose(pn[0:1, 508:510], st[0:2, lo_t:lo_t + 1],
                            p.ID_F32[0:2, 0:2], reads=b_t, writes=[pnb],
                            ident_buf=p.IDF_buf)
                accap = st[0:1, (lo_qm if which == "q" else lo_km) + 2 * pair:
                           (lo_qm if which == "q" else lo_km) + 2 * pair + 2]
                accb = b_qm if which == "q" else b_km
                c.tt("dve", accap, accap, pn[0:1, 508:510], ALU.max,
                     reads=[pnb] + accb, writes=accb)
        for tt_ in range(NT):
            pbk, pbuf = (p.PB[0], p.PBbuf[0]) if tt_ % 2 == 0 else (p.PB[1], p.PBbuf[1])
            steps = []
            for kc in range(8):
                (wt, wlo, wpr, wcols, wbufs) = wv[kc]
                steps.append((p.AA.t[:, kc * S + tt_ * 128: kc * S + tt_ * 128 + 128],
                              wt[0:128, wlo:wlo + 128],
                              list(wbufs) + p.AA.bufs_for(kc * S + tt_ * 128, kc * S + tt_ * 128 + 128)))
            steps.append((p.ONE_ROW_BF[0:1, 0:128],
                          bb[0:1, VBROW + pair * 128:VBROW + (pair + 1) * 128],
                          [p.ONE_ROW_BF_buf] + b_vbrow))
            c.mm(pbk[:, 0:128], pbuf, steps)
            for hh in range(2):
                c.copy("act", bb[:, VA + tt_ * 130 + hh * 65: VA + tt_ * 130 + hh * 65 + 64],
                       pbk[:, hh * 64:hh * 64 + 64], reads=[pbuf], writes=b_va)
        ones_ap = bb[:, VA:VA + NT * 130].rearrange("p (t h d) -> p t h d", t=NT, h=2)[:, :, :, 64:65]
        c.memset("dve", ones_ap, 1.0, writes=b_va)
        return b_qT, b_kT, b_va

    def pair_bias_tab(pair):
        lo_r, b_r = p.sm_ring.persist(8)
        c.act(st[0:1, lo_r:lo_r + 2], st[0:1, lo_qm + 2 * pair:lo_qm + 2 * pair + 2],
              AF.Sqrt, reads=b_qm, writes=b_r)
        c.act(st[0:1, lo_r + 2:lo_r + 4], st[0:1, lo_km + 2 * pair:lo_km + 2 * pair + 2],
              AF.Sqrt, reads=b_km, writes=b_r)
        c.ts("dve", st[0:1, lo_r + 4:lo_r + 6], st[0:1, lo_r:lo_r + 2], 0.125, None,
             ALU.mult, None, reads=b_r, writes=b_r)
        c.tt("dve", st[0:1, lo_r + 4:lo_r + 6], st[0:1, lo_r + 4:lo_r + 6],
             st[0:1, lo_r + 2:lo_r + 4], ALU.mult, reads=b_r, writes=b_r)
        lo_w, b_w = p.sm_ring.alloc(4 * NTI)
        lo_s2 = lo_w + 2 * NTI
        for hh in range(2):
            h = 2 * pair + hh
            slope = 2.0 ** (-(h + 1))
            cst = st[0:1, lo_w + hh * NTI:lo_w + (hh + 1) * NTI]
            def fio(eng, cst=cst):
                return eng.iota(cst, [[1, NTI]], channel_multiplier=0,
                                allow_small_or_imprecise_dtypes=True)
            c.emit("pool", fio, writes=b_w)
            c.ts("dve", cst, cst, 128.0 * slope, -128.0 * slope * (NT - 1),
                 ALU.mult, ALU.add, reads=b_w, writes=b_w)
            def fsub(eng, cst=cst, hh=hh):
                return eng.tensor_scalar(cst, cst,
                                         st[0:1, lo_r + 4 + hh:lo_r + 5 + hh],
                                         None, ALU.subtract)
            c.emit("dve", fsub, reads=b_w + b_r, writes=b_w)
            c.memset("dve", st[0:1, lo_s2 + hh * NTI:lo_s2 + (hh + 1) * NTI], slope,
                     writes=b_w)
        lo_l, b_l = p.sm_ring.alloc(128)
        def fi2(eng):
            return eng.iota(st[0:1, lo_l:lo_l + 128], [[1, 128]], channel_multiplier=0,
                            allow_small_or_imprecise_dtypes=True)
        c.emit("pool", fi2, writes=b_l)
        pbk, pbuf = p.PB[3], p.PBbuf[3]
        def fm1(eng):
            return eng.matmul(pbk[:, 0:2 * NTI], st[0:1, lo_l:lo_l + 128],
                              st[0:1, lo_s2:lo_s2 + 2 * NTI], start=True, stop=False)
        c.emit("pe", fm1, reads=b_l + b_w, writes=[pbuf])
        def fm2(eng):
            return eng.matmul(pbk[:, 0:2 * NTI], p.ONE_ROW[0:1, 0:128],
                              st[0:1, lo_w:lo_w + 2 * NTI], start=False, stop=True)
        c.emit("pe", fm2, reads=[p.ONE_ROW_buf] + b_w, writes=[pbuf])
        lo_tab, b_tab = p.sm_ring.persist(2 * NTI)
        c.copy("dve", st[:, lo_tab:lo_tab + 2 * NTI], pbk[:, 0:2 * NTI],
               reads=[pbuf], writes=b_tab)
        return lo_tab, b_tab

    for pair in range(8):
        _pscope = p.sm_ring.scope()
        b_qT, b_kT, b_va = qkv_pair(pair)
        lo_tab, b_tab = pair_bias_tab(pair)
        pending = []

        def l1a_tail(ob, obuf, h, qbase, NQh):
            lo_rr, b_rr = p.sm_ring.alloc(512)
            def frc(eng, ob=ob):
                return eng.reciprocal(st[0:1, lo_rr:lo_rr + NQh], ob[64:65, 0:NQh])
            c.emit("dve", frc, reads=[obuf], writes=b_rr)
            och = h // 2
            orow = (h % 2) * 64
            ocol = och * S + qbase
            ob_bufs = p.AB.bufs_for(ocol, ocol + NQh)
            dstab = p.AB.t[orow:orow + 64, ocol:ocol + NQh]
            c.copy("act", dstab, ob[0:64, 0:NQh], reads=[obuf], writes=ob_bufs)
            bcb, bcbuf = p.PB[1], p.PBbuf[1]
            def fbc(eng):
                return eng.matmul(bcb[0:64, 0:NQh], p.ONE_ROW[0:1, 0:64],
                                  st[0:1, lo_rr:lo_rr + NQh], start=True, stop=True)
            c.emit("pe", fbc, reads=[p.ONE_ROW_buf] + b_rr, writes=[bcbuf])
            c.tt("dve", dstab, dstab, bcb[0:64, 0:NQh], ALU.mult,
                 reads=ob_bufs + [bcbuf], writes=ob_bufs)

        def l1a_sc(hh, kc, sa, soff, qbase, NQh, sabuf):
            qrow0 = hh * 64
            def fsc(eng):
                return eng.matmul(
                    sa[:, soff:soff + NQh],
                    bb[qrow0:qrow0 + 64, KT + kc * 128:KT + kc * 128 + 128],
                    bb[qrow0:qrow0 + 64, QT + qbase:QT + qbase + NQh],
                    start=True, stop=True)
            c.emit("pe", fsc, reads=b_kT + b_qT, writes=[sabuf])

        def l1a_exp(hh, kc, sa, soff, sabuf, ex, qbase, NQh):
            qb128 = qbase // 128
            ti = kc - qb128 + (NT - 1)
            bias_ap = st[:, lo_tab + hh * NTI + ti:lo_tab + hh * NTI + ti + 1]
            exb = p.BB.bufs_for(ex, ex + 512)
            c.act(bb[:, ex:ex + NQh], sa[:, soff:soff + NQh], AF.Exp,
                  bias=(bias_ap, b_tab), scale=(inv8_ap, inv8_b),
                  reads=[sabuf], writes=exb)
            o = kc * 128 - qbase
            if o >= 0:
                mkap, mkb = p.mk(o // 128, NQh)
                c.tt("pool", bb[:, ex:ex + NQh], bb[:, ex:ex + NQh],
                     mkap, ALU.mult, reads=exb + mkb, writes=exb)
            return exb

        def l1a_pv(hh, kc, j, nj, ex, exb, NQh, ob, obuf):
            def fpv(eng):
                return eng.matmul(ob[0:65, 0:NQh],
                                  bb[:, VA + kc * 130 + hh * 65:VA + kc * 130 + hh * 65 + 65],
                                  bb[:, ex:ex + NQh],
                                  start=(j == 0), stop=(j == nj - 1))
            c.emit("pe", fpv, reads=b_va + exb, writes=[obuf])

        def kcs_for(h, qbase, NQh):
            slope = 2.0 ** (-(h + 1))
            kc_hi = (qbase + NQh - 1) // 128 + 1
            return [kc for kc in range(kc_hi)
                    if slope * ((qbase // 128 - kc) * 128 - 127) <= 200.0]

        for hh in range(2):
            h = 2 * pair + hh
            if h >= 2:
                continue
            NQh = 128 if h == 0 else 256
            for qc in range(S // NQh):
                qbase = qc * NQh
                ob, obuf = (p.PB[2], p.PBbuf[2]) if (qc + hh) % 2 == 0 else (p.PB[3], p.PBbuf[3])
                kcs = kcs_for(h, qbase, NQh)
                pend_pv = None
                for j, kc in enumerate(kcs):
                    sa, sabuf = (p.PA[0], p.PAbuf[0]) if (j // 2) % 2 == 0 else (p.PA[1], p.PAbuf[1])
                    soff = (j % 2) * 512
                    l1a_sc(hh, kc, sa, soff, qbase, NQh, sabuf)
                    if j == 0 and pending:
                        pending.pop(0)()
                    ex = EXL + (j % 4) * 512
                    exb = l1a_exp(hh, kc, sa, soff, sabuf, ex, qbase, NQh)
                    if pend_pv is not None:
                        pend_pv()
                    pend_pv = (lambda hh=hh, kc=kc, j=j, nj=len(kcs), ex=ex, exb=exb,
                               NQh=NQh, ob=ob, obuf=obuf:
                               l1a_pv(hh, kc, j, nj, ex, exb, NQh, ob, obuf))
                pend_pv()
                pending.append(lambda ob=ob, obuf=obuf, h=h, qbase=qbase, NQh=NQh:
                               l1a_tail(ob, obuf, h, qbase, NQh))
        if pair >= 1:
            for qc in range(S // 512):
                qbase = qc * 512
                obs = [(p.PB[2], p.PBbuf[2]), (p.PB[3], p.PBbuf[3])]
                kcs2 = [kcs_for(2 * pair, qbase, 512), kcs_for(2 * pair + 1, qbase, 512)]
                nmax = max(len(kcs2[0]), len(kcs2[1]))
                pend_pv = None
                for j in range(nmax):
                    for hh in range(2):
                        if j < len(kcs2[hh]):
                            sa, sabuf = p.PA[hh], p.PAbuf[hh]
                            soff = (j % 2) * 512
                            l1a_sc(hh, kcs2[hh][j], sa, soff, qbase, 512, sabuf)
                    if j == 0:
                        while pending:
                            pending.pop(0)()
                    pvs = []
                    for hh in range(2):
                        if j < len(kcs2[hh]):
                            sa, sabuf = p.PA[hh], p.PAbuf[hh]
                            soff = (j % 2) * 512
                            ex = EXL + hh * 1024 + soff
                            exb = l1a_exp(hh, kcs2[hh][j], sa, soff, sabuf, ex, qbase, 512)
                            pvs.append((hh, kcs2[hh][j], j, len(kcs2[hh]), ex, exb,
                                        obs[hh][0], obs[hh][1]))
                    if pend_pv is not None:
                        pend_pv()
                    def mk_pv(pvs=pvs):
                        def emit_pv():
                            for (hh, kc, j_, nj, ex, exb, ob, obuf) in pvs:
                                l1a_pv(hh, kc, j_, nj, ex, exb, 512, ob, obuf)
                        return emit_pv
                    pend_pv = mk_pv()
                pend_pv()
                for hh in range(2):
                    pending.append(lambda ob=obs[hh][0], obuf=obs[hh][1],
                                   h=2 * pair + hh, qbase=qbase:
                                   l1a_tail(ob, obuf, h, qbase, 512))
        while pending:
            pending.pop(0)()
        p.sm_ring.unscope(_pscope)

    wo = p.wload(W["w_proj"], 1024, 1024)
    for t in range(NT):
        for half in range(2):
            pbk, pbuf = (p.PB[0], p.PBbuf[0]) if half == 0 else (p.PB[1], p.PBbuf[1])
            steps = []
            for kc in range(8):
                (wt, wlo, wpr, wcols, wbufs) = wo[kc]
                steps.append((p.AB.t[:, kc * S + t * 128: kc * S + t * 128 + 128],
                              wt[0:128, wlo + half * 512: wlo + half * 512 + 512],
                              list(wbufs) + p.AB.bufs_for(kc * S + t * 128, kc * S + t * 128 + 128)))
            steps.append((p.ONE_ROW[0:1, 0:128],
                          st[0:1, lo_bp + half * 512:lo_bp + half * 512 + 512],
                          [p.ONE_ROW_buf] + b_bp))
            c.mm(pbk[:, :], pbuf, steps)
            rb = p.RES.bufs[t]
            rap = p.RES.t[:, t * 1024 + half * 512: t * 1024 + half * 512 + 512]
            c.tt("dve", rap, rap, pbk[:, :], ALU.add, reads=[pbuf, rb], writes=[rb])
    p.sm_ring.unscope(_scope)


def emit_l0_attn_b(p, W):
    c = p.ctx
    S, NT = p.S, p.NT
    NKC = S // 128
    NQ = S // 512
    st = p.SM.t
    bb = p.BB.t
    _scope = p.sm_ring.scope()

    p.ln_xn()

    lo_s, b_s = p.sm_ring.persist(48)
    c.dma("pool", st[0:1, lo_s:lo_s + 16], W["scale"], writes=b_s)
    c.act(st[0:1, lo_s + 16:lo_s + 32], st[0:1, lo_s:lo_s + 16], AF.Exp,
          reads=b_s, writes=b_s)
    c.ts("dve", st[0:1, lo_s + 32:lo_s + 48], st[0:1, lo_s + 16:lo_s + 32], -1.0, None,
         ALU.mult, None, reads=b_s, writes=b_s)
    (lo_sc, b_sc), (lo_bi, b_bi) = p.head_tables(
        (st[0:1, lo_s + 16:lo_s + 32], b_s), (st[0:1, lo_s + 32:lo_s + 48], b_s), [])

    lo_bo, b_bo = p.sm_ring.persist(1024)
    c.dma("pool", st[0:1, lo_bo:lo_bo + 1024], W["b_out"], writes=b_bo)

    QT, KT, VA, EXL, SQ = 0, 2048, 4096, 6400, 10496

    def qkv_pair(pair):
        rinv = {}
        wq = p.wload(W["w_q"][:, pair * 128:(pair + 1) * 128], 1024, 128)
        wk = p.wload(W["w_kv"][:, pair * 128:(pair + 1) * 128], 1024, 128)
        wv = p.wload(W["w_kv"][:, 1024 + pair * 128:1024 + (pair + 1) * 128], 1024, 128)
        b_qT = p.BB.bufs_for(QT, QT + S)
        b_kT = p.BB.bufs_for(KT, KT + S)
        b_va = p.BB.bufs_for(VA, VA + NT * 130)
        for wi, (wtiles, dstcol, dstbufs) in enumerate(((wq, QT, b_qT), (wk, KT, b_kT))):
            for tc in range(NQ):
                pbk, pbuf = (p.PB[0], p.PBbuf[0]) if tc % 2 == 0 else (p.PB[1], p.PBbuf[1])
                steps = []
                for kc in range(8):
                    (wt, wlo, wpr, wcols, wbufs) = wtiles[kc]
                    steps.append((wt[0:128, wlo:wlo + 128],
                                  p.AA.t[:, kc * S + tc * 512: kc * S + tc * 512 + 512],
                                  list(wbufs) + p.AA.bufs_for(kc * S + tc * 512, kc * S + tc * 512 + 512)))
                c.mm(pbk[:, :], pbuf, steps)
                dst = bb[:, dstcol + tc * 512: dstcol + tc * 512 + 512]
                c.copy("act", dst, pbk[:, :], reads=[pbuf], writes=dstbufs)
                sq_col = SQ + (wi * NQ + tc) % 3 * 512
                sqb = p.BB.bufs_for(sq_col, sq_col + 512)
                c.tt("pool", bb[:, sq_col:sq_col + 512], dst, dst, ALU.mult,
                     reads=dstbufs, writes=sqb)
                for hh in range(2):
                    pn, pnb = (p.PB[2], p.PBbuf[2]) if (tc + hh) % 2 == 0 else (p.PB[3], p.PBbuf[3])
                    def fn(eng, hh=hh, pn=pn, sq_col=sq_col):
                        return eng.matmul(pn[0:1, 0:512], p.MASKHALF[:, hh:hh + 1],
                                          bb[:, sq_col:sq_col + 512], start=True, stop=True)
                    c.emit("pe", fn, reads=sqb + [p.MASKHALF_buf], writes=[pnb])
                    lo_r, b_r = p.sm_ring.alloc(512)
                    p.rsqrt(st[0:1, lo_r:lo_r + 512], pn[0:1, 0:512], 1e-24,
                            reads=[pnb], writes=b_r)
                    rinv[(wi, tc, hh)] = (lo_r, b_r)
        bi = 0
        for wi, (dstcol, dstbufs) in enumerate(((QT, b_qT), (KT, b_kT))):
            for tc in range(NQ):
                for hh in range(2):
                    lo_r, b_r = rinv[(wi, tc, hh)]
                    pa_, pab_ = (p.PA[0], p.PAbuf[0]) if bi % 2 == 0 else (p.PA[1], p.PAbuf[1])
                    boff = (bi // 2 % 2) * 512
                    bi += 1
                    def fbc(eng, lo_r=lo_r, pa_=pa_, boff=boff):
                        return eng.matmul(pa_[0:64, boff:boff + 512], p.ONE_ROW[0:1, 0:64],
                                          st[0:1, lo_r:lo_r + 512], start=True, stop=True)
                    c.emit("pe", fbc, reads=[p.ONE_ROW_buf] + b_r, writes=[pab_])
                    sub = bb[hh * 64:hh * 64 + 64, dstcol + tc * 512: dstcol + tc * 512 + 512]
                    def fmu(eng, sub=sub, pa_=pa_, boff=boff):
                        return eng.tensor_tensor(sub, sub, pa_[0:64, boff:boff + 512], ALU.mult)
                    c.emit("dve", fmu, reads=dstbufs + [pab_], writes=dstbufs)
        for tt_ in range(NT):
            pbk, pbuf = (p.PB[0], p.PBbuf[0]) if tt_ % 2 == 0 else (p.PB[1], p.PBbuf[1])
            steps = []
            for kc in range(8):
                (wt, wlo, wpr, wcols, wbufs) = wv[kc]
                steps.append((p.AA.t[:, kc * S + tt_ * 128: kc * S + tt_ * 128 + 128],
                              wt[0:128, wlo:wlo + 128],
                              list(wbufs) + p.AA.bufs_for(kc * S + tt_ * 128, kc * S + tt_ * 128 + 128)))
            c.mm(pbk[:, 0:128], pbuf, steps)
            for hh in range(2):
                c.copy("act", bb[:, VA + tt_ * 130 + hh * 65: VA + tt_ * 130 + hh * 65 + 64],
                       pbk[:, hh * 64:hh * 64 + 64], reads=[pbuf], writes=b_va)
        ones_ap = bb[:, VA:VA + NT * 130].rearrange("p (t h d) -> p t h d", t=NT, h=2)[:, :, :, 64:65]
        c.memset("dve", ones_ap, 1.0, writes=b_va)
        return b_qT, b_kT, b_va

    for pair in range(8):
        _pscope = p.sm_ring.scope()
        b_qT, b_kT, b_va = qkv_pair(pair)
        pending = []
        for qc in range(NQ):
            obs = [(p.PB[2], p.PBbuf[2]), (p.PB[3], p.PBbuf[3])]
            exi = 0
            pend_pv = None
            for j0 in range(0, NKC, 2):
                pk = list(range(j0, min(j0 + 2, NKC)))
                exoff = [EXL + (exi % 2) * 1024, EXL + 2048 + (exi % 2) * 1024]
                exi += 1
                for hh in range(2):
                    qrow0 = hh * 64
                    sa, sabuf = p.PA[hh], p.PAbuf[hh]
                    for i2, kc in enumerate(pk):
                        def fsc(eng, kc=kc, i2=i2, sa=sa, qrow0=qrow0, qc=qc):
                            return eng.matmul(
                                sa[:, i2 * 512:i2 * 512 + 512],
                                bb[qrow0:qrow0 + 64, KT + kc * 128:KT + kc * 128 + 128],
                                bb[qrow0:qrow0 + 64, QT + qc * 512:QT + qc * 512 + 512],
                                start=True, stop=True)
                        c.emit("pe", fsc, reads=b_kT + b_qT, writes=[sabuf])
                if j0 == 0:
                    while pending:
                        pending.pop(0)()
                w2 = 512 * len(pk)
                for hh in range(2):
                    h = 2 * pair + hh
                    sa, sabuf = p.PA[hh], p.PAbuf[hh]
                    ex = exoff[hh]
                    exb = p.BB.bufs_for(ex, ex + w2)
                    c.act(bb[:, ex:ex + w2], sa[:, 0:w2], AF.Exp,
                          bias=st[:, lo_bi + h:lo_bi + h + 1],
                          scale=st[:, lo_sc + h:lo_sc + h + 1],
                          reads=[sabuf] + b_sc + b_bi, writes=exb)
                if pend_pv is not None:
                    pend_pv()

                def mk_pv(pk=pk, exoff=exoff, w2=w2, obs=obs):
                    def emit_pv():
                        for hh in range(2):
                            ob, obuf = obs[hh]
                            ex = exoff[hh]
                            exb = p.BB.bufs_for(ex, ex + w2)
                            for i2, kc in enumerate(pk):
                                def fpv(eng, kc=kc, i2=i2, ex=ex, ob=ob, hh=hh,
                                        first=(kc == 0), last=(kc == NKC - 1)):
                                    return eng.matmul(ob[0:65, :],
                                                      bb[:, VA + kc * 130 + hh * 65:VA + kc * 130 + hh * 65 + 65],
                                                      bb[:, ex + i2 * 512:ex + i2 * 512 + 512],
                                                      start=first, stop=last)
                                c.emit("pe", fpv, reads=b_va + exb, writes=[obuf])
                    return emit_pv

                pend_pv = mk_pv()
            pend_pv()
            for hh in range(2):
                ob, obuf = obs[hh]
                h = 2 * pair + hh

                def tail(ob=ob, obuf=obuf, h=h, qc=qc):
                    lo_rr, b_rr = p.sm_ring.alloc(512)
                    p.rsqrt(st[0:1, lo_rr:lo_rr + 512], ob[64:65, :], 0.0,
                            reads=[obuf], writes=b_rr, power=-1.0)
                    och = h // 2
                    orow = (h % 2) * 64
                    ocol = och * S + qc * 512
                    ob_bufs = p.AB.bufs_for(ocol, ocol + 512)
                    dstab = p.AB.t[orow:orow + 64, ocol:ocol + 512]
                    c.copy("act", dstab, ob[0:64, :], reads=[obuf], writes=ob_bufs)
                    bcb, bcbuf = p.PB[1], p.PBbuf[1]
                    def fbc(eng):
                        return eng.matmul(bcb[0:64, :], p.ONE_ROW[0:1, 0:64],
                                          st[0:1, lo_rr:lo_rr + 512], start=True, stop=True)
                    c.emit("pe", fbc, reads=[p.ONE_ROW_buf] + b_rr, writes=[bcbuf])
                    c.tt("dve", dstab, dstab, bcb[0:64, :], ALU.mult,
                         reads=ob_bufs + [bcbuf], writes=ob_bufs)
                pending.append(tail)
        while pending:
            pending.pop(0)()
        p.sm_ring.unscope(_pscope)

    wo = p.wload(W["w_out"], 1024, 1024)
    for t in range(NT):
        for half in range(2):
            pbk, pbuf = (p.PB[0], p.PBbuf[0]) if half == 0 else (p.PB[1], p.PBbuf[1])
            steps = []
            for kc in range(8):
                (wt, wlo, wpr, wcols, wbufs) = wo[kc]
                steps.append((p.AB.t[:, kc * S + t * 128: kc * S + t * 128 + 128],
                              wt[0:128, wlo + half * 512: wlo + half * 512 + 512],
                              list(wbufs) + p.AB.bufs_for(kc * S + t * 128, kc * S + t * 128 + 128)))
            steps.append((p.ONE_ROW[0:1, 0:128],
                          st[0:1, lo_bo + half * 512:lo_bo + half * 512 + 512],
                          [p.ONE_ROW_buf] + b_bo))
            c.mm(pbk[:, :], pbuf, steps)
            rb = p.RES.bufs[t]
            rap = p.RES.t[:, t * 1024 + half * 512: t * 1024 + half * 512 + 512]
            c.tt("dve", rap, rap, pbk[:, :], ALU.add, reads=[pbuf, rb], writes=[rb])
    p.sm_ring.unscope(_scope)


def emit_l0_ffn(p, W, dbg=None):
    c = p.ctx
    S, NT = p.S, p.NT
    NQ = S // 512
    st = p.SM.t
    bb = p.BB.t
    _scope = p.sm_ring.scope()
    _wscope = p.wr_ring.scope()

    p.ln_xn()

    _, lo_encb, _, b_encb = p.load_bias_col(W["enc_b"].rearrange("a n -> (a n)"), 2048, "encb")
    _, lo_s2cb, _, b_s2cb = p.load_bias_col(W["s2c_b"].rearrange("a n -> (a n)"), 64, "s2cb")
    _, lo_rb1, _, b_rb1 = p.load_bias_col(W["rb1"].rearrange("a n -> (a n)"), 64, "rb1")
    _, lo_c2sb, _, b_c2sb = p.load_bias_col(W["c2s_b"].rearrange("a n -> (a n)"), 2048, "c2sb")
    be1T = []
    for e in range(8):
        _, lo_e, _, b_e = p.load_bias_col(W["be1"][e:e + 1, :].rearrange("a n -> (a n)"), 1024, f"be1{e}")
        be1T.append((lo_e, b_e))
    lo_rows, b_rows = p.sm_ring.persist(8 + 8 * 64 + 1024)
    lo_rb2 = lo_rows
    lo_be2 = lo_rows + 8
    lo_decb = lo_be2 + 8 * 64
    c.dma("pool", st[0:1, lo_rb2:lo_rb2 + 8], W["rb2"], writes=b_rows)
    c.dma("pool", st[0:1, lo_be2:lo_be2 + 8 * 64], W["be2"].rearrange("e n -> (e n)").rearrange("(a n) -> a n", a=1), writes=b_rows)
    c.dma("pool", st[0:1, lo_decb:lo_decb + 1024], W["dec_b"], writes=b_rows)

    CONT, R1, SMT = 0, 2048, 4096
    b_cont = p.BB.bufs_for(CONT, CONT + S)
    b_r1 = p.BB.bufs_for(R1, R1 + S)
    b_smt = p.BB.bufs_for(SMT, SMT + 8192)

    assert NQ <= 4
    pending_ct = []
    for m in range(16):
        par = m % 2
        SROW = SMT + par * 2048
        b_srow_m = [p.BB.bufs_for(SROW + tc * 512, SROW + tc * 512 + 512)
                    for tc in range(NQ)]
        encw = p.wload(W["enc_W"][:, m * 128:(m + 1) * 128], 1024, 128)
        s2c_m = p.wload(W["s2c_W"][m * 128:(m + 1) * 128, :], 128, 64)
        for tc in range(NQ):
            pbk, pbuf = (p.PB[0], p.PBbuf[0]) if tc % 2 == 0 else (p.PB[1], p.PBbuf[1])
            steps = []
            for kc in range(8):
                (wt, wlo, wpr, wcols, wbufs) = encw[kc]
                steps.append((wt[0:128, wlo:wlo + 128],
                              p.AA.t[:, kc * S + tc * 512: kc * S + tc * 512 + 512],
                              list(wbufs) + p.AA.bufs_for(kc * S + tc * 512, kc * S + tc * 512 + 512)))
            c.mm(pbk[:, :], pbuf, steps)
            if pending_ct:
                pending_ct.pop(0)()
            def fsg(eng, pbk=pbk, tc=tc, m=m, SROW=SROW):
                return eng.activation(bb[:, SROW + tc * 512:SROW + tc * 512 + 512],
                                      pbk[:, :], AF.Sigmoid,
                                      bias=st[:, lo_encb + m:lo_encb + m + 1])
            c.emit("act", fsg, reads=[pbuf] + b_encb, writes=b_srow_m[tc])

            def fct_tail(m=m, tc=tc, SROW=SROW, s2c_m=s2c_m, srb=b_srow_m[tc]):
                (wt, wlo, wpr, wcols, wbufs) = s2c_m[0]
                cb, cbuf = p.PA[tc // 2], p.PAbuf[tc // 2]
                coff = (tc % 2) * 512
                def fct(eng):
                    return eng.matmul(cb[0:64, coff:coff + 512], wt[0:128, wlo:wlo + 64],
                                      bb[:, SROW + tc * 512:SROW + tc * 512 + 512],
                                      start=(m == 0), stop=(m == 15))
                c.emit("pe", fct, reads=list(wbufs) + srb, writes=[cbuf])
            pending_ct.append(fct_tail)
    while pending_ct:
        pending_ct.pop(0)()
    for tc2 in range(0, NQ, 2):
        cb, cbuf = p.PA[tc2 // 2], p.PAbuf[tc2 // 2]
        w2 = min(1024, (NQ - tc2) * 512)
        def fev(eng, cb=cb, tc2=tc2, w2=w2):
            return eng.activation(bb[0:64, CONT + tc2 * 512:CONT + tc2 * 512 + w2],
                                  cb[0:64, 0:w2], AF.Identity,
                                  bias=st[0:64, lo_s2cb:lo_s2cb + 1])
        c.emit("act", fev, reads=[cbuf] + b_s2cb, writes=b_cont)

    if dbg is not None:
        dbg("cont", dict(CONT=CONT, b_cont=b_cont, SROW=SROW, b_srow=b_srow,
                         lo_encb=lo_encb, b_encb=b_encb))
    _rscope = p.wr_ring.scope()
    rw1 = p.wload(W["rW1"], 64, 64, persist=True)
    rw2 = p.wload(W["rW2"], 64, 8, persist=True)
    for tc in range(NQ):
        pbk, pbuf = (p.PB[0], p.PBbuf[0]) if tc % 2 == 0 else (p.PB[1], p.PBbuf[1])
        (wt, wlo, wpr, wcols, wbufs) = rw1[0]
        def fr1(eng, wt=wt, wlo=wlo, tc=tc, pbk=pbk):
            return eng.matmul(pbk[0:64, :], wt[0:64, wlo:wlo + 64],
                              bb[0:64, CONT + tc * 512:CONT + tc * 512 + 512],
                              start=True, stop=True)
        c.emit("pe", fr1, reads=list(wbufs) + b_cont, writes=[pbuf])
        def ft1(eng, pbk=pbk, tc=tc):
            return eng.activation(bb[0:64, R1 + tc * 512:R1 + tc * 512 + 512],
                                  pbk[0:64, :], AF.Tanh,
                                  bias=st[0:64, lo_rb1:lo_rb1 + 1])
        c.emit("act", ft1, reads=[pbuf] + b_rb1, writes=b_r1)
    lo_lg, b_lg = p.sm_ring.persist(NT * 8)
    for t in range(NT):
        pbk, pbuf = (p.PB[0], p.PBbuf[0]) if t % 2 == 0 else (p.PB[1], p.PBbuf[1])
        (wt, wlo, wpr, wcols, wbufs) = rw2[0]
        steps = [(bb[0:64, R1 + t * 128:R1 + t * 128 + 128], wt[0:64, wlo:wlo + 8],
                  list(wbufs) + b_r1),
                 (p.ONE_ROW[0:1, 0:128], st[0:1, lo_rb2:lo_rb2 + 8],
                  [p.ONE_ROW_buf] + b_rows)]
        c.mm(pbk[:, 0:8], pbuf, steps)
        c.copy("dve", st[:, lo_lg + t * 8:lo_lg + t * 8 + 8], pbk[:, 0:8],
               reads=[pbuf], writes=b_lg)

    p.wr_ring.unscope(_rscope)

    lo_gt, b_gt = p.sm_ring.persist(NT * 8)
    lo_w1, b_w1 = p.sm_ring.alloc(NT * 8 * 3)
    lo_m, b_m = p.sm_ring.alloc(NT * 4)
    L = st[:, lo_lg:lo_lg + NT * 8].rearrange("p (t e) -> p t e", e=8)
    A = st[:, lo_w1:lo_w1 + NT * 8].rearrange("p (t e) -> p t e", e=8)
    B = st[:, lo_w1 + NT * 8:lo_w1 + NT * 16].rearrange("p (t e) -> p t e", e=8)
    Z = st[:, lo_w1 + NT * 16:lo_w1 + NT * 24].rearrange("p (t e) -> p t e", e=8)
    M1 = st[:, lo_m:lo_m + NT]
    M2 = st[:, lo_m + NT:lo_m + 2 * NT]
    EM2 = st[:, lo_m + 2 * NT:lo_m + 3 * NT]
    DI = st[:, lo_m + 3 * NT:lo_m + 4 * NT]
    M1b = M1.rearrange("p (t o) -> p t o", o=1)
    M2b = M2.rearrange("p (t o) -> p t o", o=1)

    def bcast8(ap_3d):
        import dataclasses as _dc
        new = [list(d) for d in ap_3d.ap]
        new[-1] = [0, 8]
        return _dc.replace(ap_3d, ap=new)

    c.red("dve", M1b, L, ALU.max, reads=b_lg, writes=b_m)
    c.tt("dve", A, L, bcast8(M1b), ALU.is_ge, reads=b_lg + b_m, writes=b_w1)

    def top_from(selA, selbufs, outap, outbufs):
        c.tt("dve", B[:, :, 1:8], selA[:, :, 1:8], selA[:, :, 0:7], ALU.add,
             reads=selbufs, writes=b_w1)
        c.copy("dve", B[:, :, 0:1], selA[:, :, 0:1], reads=selbufs, writes=b_w1)
        c.tt("dve", Z[:, :, 2:8], B[:, :, 2:8], B[:, :, 0:6], ALU.add,
             reads=b_w1, writes=b_w1)
        c.copy("dve", Z[:, :, 0:2], B[:, :, 0:2], reads=b_w1, writes=b_w1)
        c.tt("dve", B[:, :, 4:8], Z[:, :, 4:8], Z[:, :, 0:4], ALU.add,
             reads=b_w1, writes=b_w1)
        c.copy("dve", B[:, :, 0:4], Z[:, :, 0:4], reads=b_w1, writes=b_w1)
        c.tt("dve", Z, B, selA, ALU.subtract, reads=b_w1 + selbufs, writes=b_w1)
        c.ts("dve", Z, Z, 0.0, None, ALU.is_equal, None, reads=b_w1, writes=b_w1)
        c.tt("dve", outap, Z, selA, ALU.mult, reads=b_w1 + selbufs, writes=outbufs)

    lo_f, b_f = p.sm_ring.alloc(NT * 16)
    F1 = st[:, lo_f:lo_f + NT * 8].rearrange("p (t e) -> p t e", e=8)
    F2 = st[:, lo_f + NT * 8:lo_f + NT * 16].rearrange("p (t e) -> p t e", e=8)
    top_from(A, b_w1 + b_lg, F1, b_f)
    def fmk(eng):
        return eng.scalar_tensor_tensor(A, F1, -1e30, L, ALU.mult, ALU.add)
    c.emit("dve", fmk, reads=b_f + b_lg, writes=b_w1)
    c.red("dve", M2b, A, ALU.max, reads=b_w1, writes=b_m)
    lo_a2, b_a2 = p.sm_ring.alloc(NT * 8)
    A2 = st[:, lo_a2:lo_a2 + NT * 8].rearrange("p (t e) -> p t e", e=8)
    c.tt("dve", A2, A, bcast8(M2b), ALU.is_ge, reads=b_w1 + b_m, writes=b_a2)
    top_from(A2, b_a2, F2, b_f)
    c.tt("dve", EM2, M2, M1, ALU.subtract, reads=b_m, writes=b_m)
    c.act(EM2, EM2, AF.Exp, reads=b_m, writes=b_m)
    c.ts("dve", DI, EM2, 1.0, None, ALU.add, None, reads=b_m, writes=b_m)
    def fre(eng):
        return eng.reciprocal(DI, DI)
    c.emit("dve", fre, reads=b_m, writes=b_m)
    G = st[:, lo_gt:lo_gt + NT * 8].rearrange("p (t e) -> p t e", e=8)
    EM2b = EM2.rearrange("p (t o) -> p t o", o=1)
    DIb = DI.rearrange("p (t o) -> p t o", o=1)
    c.tt("dve", G, F2, bcast8(EM2b), ALU.mult, reads=b_f + b_m, writes=b_gt)
    c.tt("dve", G, G, F1, ALU.add, reads=b_gt + b_f, writes=b_gt)
    c.tt("dve", G, G, bcast8(DIb), ALU.mult, reads=b_gt + b_m, writes=b_gt)

    if dbg is not None:
        dbg("gates", dict(lo_lg=lo_lg, b_lg=b_lg, lo_gt=lo_gt, b_gt=b_gt))
    lo_of, b_of = p.sm_ring.persist(NT * 64)
    for e in range(8):
        we1 = p.wload(W["We1"][e:e + 1].rearrange("a b c -> (a b) c"), 64, 1024)
        we2 = p.wload(W["We2"][e:e + 1].rearrange("a b c -> (a b) c"), 1024, 64)
        (w1t, w1lo, _, _, w1bufs) = we1[0]
        lo_be1, b_be1 = be1T[e]
        for mh in range(8):
            for tc in range(NQ):
                pbk, pbuf = (p.PB[0], p.PBbuf[0]) if (mh + tc) % 2 == 0 else (p.PB[1], p.PBbuf[1])
                def fh(eng, mh=mh, tc=tc, pbk=pbk):
                    return eng.matmul(pbk[:, :],
                                      w1t[0:64, w1lo + mh * 128:w1lo + mh * 128 + 128],
                                      bb[0:64, CONT + tc * 512:CONT + tc * 512 + 512],
                                      start=True, stop=True)
                c.emit("pe", fh, reads=list(w1bufs) + b_cont, writes=[pbuf])
                dst = p.AB.t[:, mh * S + tc * 512: mh * S + tc * 512 + 512]
                dstb = p.AB.bufs_for(mh * S + tc * 512, mh * S + tc * 512 + 512)
                def fre_(eng, pbk=pbk, dst=dst, mh=mh):
                    return eng.activation(dst, pbk[:, :], AF.Relu,
                                          bias=st[:, lo_be1 + mh:lo_be1 + mh + 1])
                c.emit("act", fre_, reads=[pbuf] + b_be1, writes=dstb)
        for t in range(NT):
            pbk, pbuf = (p.PB[2], p.PBbuf[2]) if t % 2 == 0 else (p.PB[3], p.PBbuf[3])
            steps = []
            for mh in range(8):
                (w2t, w2lo, _, _, w2bufs) = we2[mh]
                steps.append((p.AB.t[:, mh * S + t * 128: mh * S + t * 128 + 128],
                              w2t[0:128, w2lo:w2lo + 64],
                              list(w2bufs) + p.AB.bufs_for(mh * S + t * 128, mh * S + t * 128 + 128)))
            steps.append((p.ONE_ROW[0:1, 0:128], st[0:1, lo_be2 + e * 64:lo_be2 + e * 64 + 64],
                          [p.ONE_ROW_buf] + b_rows))
            c.mm(pbk[:, 0:64], pbuf, steps)
            ofap = st[:, lo_of + t * 64:lo_of + t * 64 + 64]
            gap = st[:, lo_gt + t * 8 + e:lo_gt + t * 8 + e + 1]
            if e == 0:
                def fg0(eng, pbk=pbk, ofap=ofap, gap=gap):
                    return eng.tensor_scalar(ofap, pbk[:, 0:64], gap, None, ALU.mult)
                c.emit("dve", fg0, reads=[pbuf] + b_gt, writes=b_of)
            else:
                def fga(eng, pbk=pbk, ofap=ofap, gap=gap):
                    return eng.scalar_tensor_tensor(ofap, pbk[:, 0:64], gap, ofap,
                                                    ALU.mult, ALU.add)
                c.emit("dve", fga, reads=[pbuf] + b_gt + b_of, writes=b_of)

    if dbg is not None:
        dbg("of", dict(lo_of=lo_of, b_of=b_of))
    OFT = R1
    b_oft = p.BB.bufs_for(OFT, OFT + S)
    for t in range(NT):
        pbk, pbuf = (p.PB[2], p.PBbuf[2]) if t % 2 == 0 else (p.PB[3], p.PBbuf[3])
        c.transpose(pbk[0:64, 0:128], st[:, lo_of + t * 64:lo_of + t * 64 + 64],
                    p.ID_F32[:, :], reads=b_of, writes=[pbuf], ident_buf=p.IDF_buf)
        c.copy("act", bb[0:64, OFT + t * 128:OFT + t * 128 + 128], pbk[0:64, 0:128],
               reads=[pbuf], writes=b_oft)

    GB, BBIAS = 0, 1024
    b_gbc = p.BB.bufs_for(GB, GB + 1024)
    b_bbc = p.BB.bufs_for(BBIAS, BBIAS + 1024)
    lo_grow, b_grow = p.sm_ring.alloc(2048)
    c.dma("pool", st[0:1, lo_grow:lo_grow + 1024], W["ln_g"], writes=b_grow)
    c.dma("pool", st[0:1, lo_grow + 1024:lo_grow + 2048], W["ln_b"], writes=b_grow)
    for (dstcol, dbufs, off) in ((GB, b_gbc, 0), (BBIAS, b_bbc, 1024)):
        for half in range(2):
            pbk, pbuf = p.PB[3], p.PBbuf[3]
            def fb(eng, off=off, half=half):
                return eng.matmul(pbk[:, :], p.ONE_ROW[0:1, 0:128],
                                  st[0:1, lo_grow + off + half * 512:lo_grow + off + half * 512 + 512],
                                  start=True, stop=True)
            c.emit("pe", fb, reads=[p.ONE_ROW_buf] + b_grow, writes=[pbuf])
            c.copy("dve", bb[:, dstcol + half * 512:dstcol + half * 512 + 512],
                   pbk[:, :], reads=[pbuf], writes=dbufs)

    b_decw_ab = []
    for m in range(16):
        dst = p.AB.t[:, m * 1024:(m + 1) * 1024]
        dbufs = p.AB.bufs_for(m * 1024, (m + 1) * 1024)
        c.dma("sp", dst, W["dec_W"][m * 128:(m + 1) * 128, :], writes=dbufs)
        b_decw_ab.append(dbufs)

    c2s = p.wload(W["c2s_W"], 64, 2048, persist=True)
    (c2t, c2lo, _, _, c2bufs) = c2s[0]
    for T in range(NQ):
        for m in range(16):
            pbk, pbuf = (p.PB[0], p.PBbuf[0]) if m % 2 == 0 else (p.PB[1], p.PBbuf[1])
            def fsm(eng, m=m, T=T, pbk=pbk):
                return eng.matmul(pbk[:, :], c2t[0:64, c2lo + m * 128:c2lo + m * 128 + 128],
                                  bb[0:64, OFT + T * 512:OFT + T * 512 + 512],
                                  start=True, stop=True)
            c.emit("pe", fsm, reads=list(c2bufs) + b_oft, writes=[pbuf])
            def fsg2(eng, pbk=pbk, m=m, T=T):
                return eng.activation(bb[:, SMT + m * 512:SMT + m * 512 + 512],
                                      pbk[:, :], AF.Sigmoid,
                                      bias=st[:, lo_c2sb + m:lo_c2sb + m + 1])
            c.emit("act", fsg2, reads=[pbuf] + b_c2sb, writes=b_smt)
        for t4 in range(4):
            t = T * 4 + t4
            lo_dec, b_dec = p.sm_ring.alloc(1024)
            for half in range(2):
                pbk, pbuf = (p.PB[2], p.PBbuf[2]) if half == 0 else (p.PB[3], p.PBbuf[3])
                steps = []
                for m in range(16):
                    steps.append((bb[:, SMT + m * 512 + t4 * 128:SMT + m * 512 + t4 * 128 + 128],
                                  p.AB.t[:, m * 1024 + half * 512:m * 1024 + half * 512 + 512],
                                  b_decw_ab[m] + b_smt))
                steps.append((p.ONE_ROW[0:1, 0:128],
                              st[0:1, lo_decb + half * 512:lo_decb + half * 512 + 512],
                              [p.ONE_ROW_buf] + b_rows))
                c.mm(pbk[:, :], pbuf, steps)
                def fdec(eng, pbk=pbk, half=half, lo_dec=lo_dec):
                    return eng.activation(st[:, lo_dec + half * 512:lo_dec + half * 512 + 512],
                                          pbk[:, :], AF.Sigmoid)
                c.emit("act", fdec, reads=[pbuf], writes=b_dec)
            lo_st2, b_st2 = p.sm_ring.alloc(16)
            dtile = st[:, lo_dec:lo_dec + 1024]
            def f1(eng):
                return eng.bn_stats(st[:, lo_st2:lo_st2 + 6], dtile[:, 0:512])
            def f2(eng):
                return eng.bn_stats(st[:, lo_st2 + 6:lo_st2 + 12], dtile[:, 512:1024])
            def f3(eng):
                return eng.bn_aggr(st[:, lo_st2 + 12:lo_st2 + 14], st[:, lo_st2:lo_st2 + 12])
            c.emit("dve", f1, reads=b_dec, writes=b_st2)
            c.emit("dve", f2, reads=b_dec, writes=b_st2)
            c.emit("dve", f3, reads=b_st2, writes=b_st2)
            p.rsqrt(st[:, lo_st2 + 14:lo_st2 + 15], st[:, lo_st2 + 13:lo_st2 + 14], EPS,
                    reads=b_st2, writes=b_st2)
            c.ts("dve", st[:, lo_st2 + 15:lo_st2 + 16], st[:, lo_st2 + 12:lo_st2 + 13],
                 st[:, lo_st2 + 14:lo_st2 + 15], -1.0, ALU.mult, ALU.mult,
                 reads=b_st2, writes=b_st2)
            c.ts("dve", dtile, dtile, st[:, lo_st2 + 14:lo_st2 + 15],
                 st[:, lo_st2 + 15:lo_st2 + 16], ALU.mult, ALU.add,
                 reads=b_dec + b_st2, writes=b_dec)
            rb = p.RES.bufs[t]
            rap = p.RES.t[:, t * 1024:(t + 1) * 1024]
            def fg(eng):
                return eng.tensor_tensor(dtile, dtile, bb[:, GB:GB + 1024], ALU.mult)
            c.emit("dve", fg, reads=b_dec + b_gbc, writes=b_dec)
            def fb2(eng):
                return eng.tensor_tensor(dtile, dtile, bb[:, BBIAS:BBIAS + 1024], ALU.add)
            c.emit("dve", fb2, reads=b_dec + b_bbc, writes=b_dec)
            c.tt("dve", rap, rap, dtile, ALU.add, reads=b_dec + [rb], writes=[rb])
    p.sm_ring.unscope(_scope)
    p.wr_ring.unscope(_wscope)


def _bcast_blocks(ap_3d, blk):
    import dataclasses as _dc
    new = [list(d) for d in ap_3d.ap]
    new[-1] = [0, blk]
    return _dc.replace(ap_3d, ap=new)


def emit_fp4_quant(p, out_ap, out_bufs, in_ap, in_bufs, scr_lo, scr_bufs, width,
                   pr=128):
    c = p.ctx
    st = p.SM.t
    nb = width // 16
    am = st[0:pr, scr_lo:scr_lo + nb]
    rr = st[0:pr, scr_lo + nb:scr_lo + 2 * nb]
    xa = st[0:pr, scr_lo + 2 * nb:scr_lo + 2 * nb + width]
    a05 = st[0:pr, scr_lo + 2 * nb + width:scr_lo + 2 * nb + 2 * width]
    a1 = st[0:pr, scr_lo + 2 * nb + 2 * width:scr_lo + 2 * nb + 3 * width]
    def fab(eng):
        return eng.activation(xa, in_ap, AF.Abs)
    c.emit("act", fab, reads=in_bufs, writes=scr_bufs)
    xav = xa.rearrange("p (nb b) -> p nb b", b=16)
    c.red("dve", am.rearrange("p (nb o) -> p nb o", o=1), xav, ALU.max,
          reads=scr_bufs, writes=scr_bufs)
    c.ts("dve", am, am, 1.0 / 6.0, 1e-12, ALU.mult, ALU.max, reads=scr_bufs,
         writes=scr_bufs)
    def fr(eng):
        return eng.reciprocal(rr, am)
    c.emit("dve", fr, reads=scr_bufs, writes=scr_bufs)
    rb = _bcast_blocks(rr.rearrange("p (nb o) -> p nb o", o=1), 16)
    def fxa(eng):
        return eng.tensor_tensor(xav, xav, rb, ALU.mult)
    c.emit("dve", fxa, reads=scr_bufs, writes=scr_bufs)
    c.ts("dve", a05, xa, 0.25, None, ALU.is_gt, None, reads=scr_bufs, writes=scr_bufs)
    for t in (0.75, 1.25, 1.75):
        def fs(eng, t=t):
            return eng.scalar_tensor_tensor(a05, xa, t, a05, ALU.is_gt, ALU.add)
        c.emit("dve", fs, reads=scr_bufs, writes=scr_bufs)
    c.ts("dve", a1, xa, 2.5, None, ALU.is_gt, None, reads=scr_bufs, writes=scr_bufs)
    for t in (3.5, 5.0, 5.0):
        def fs2(eng, t=t):
            return eng.scalar_tensor_tensor(a1, xa, t, a1, ALU.is_gt, ALU.add)
        c.emit("dve", fs2, reads=scr_bufs, writes=scr_bufs)
    def fqn(eng):
        return eng.scalar_tensor_tensor(a05, a05, 0.5, a1, ALU.mult, ALU.add)
    c.emit("dve", fqn, reads=scr_bufs, writes=scr_bufs)
    def fsg(eng):
        return eng.activation(out_ap, in_ap, AF.Sign)
    c.emit("act", fsg, reads=in_bufs, writes=out_bufs)
    ab_ = _bcast_blocks(am.rearrange("p (nb o) -> p nb o", o=1), 16)
    def fss(eng):
        return eng.tensor_tensor(xa.rearrange("p (nb b) -> p nb b", b=16),
                                 out_ap.rearrange("p (nb b) -> p nb b", b=16), ab_,
                                 ALU.mult)
    c.emit("dve", fss, reads=out_bufs + scr_bufs, writes=scr_bufs)
    c.tt("dve", out_ap, a05, xa, ALU.mult, reads=scr_bufs, writes=out_bufs)


def emit_fp4_weight_prep(p, src_aps, dst_aps):
    c = p.ctx
    st = p.SM.t
    bb = p.BB.t
    _scope = p.sm_ring.scope()
    for (name, rows, cols) in (("w_gate", 2816, 1024), ("w_up", 2816, 1024),
                               ("w_down", 1024, 2816)):
        src = src_aps[name]
        dst = dst_aps[name]
        nrt = (rows + 127) // 128
        for rt in range(nrt):
            pr = min(128, rows - rt * 128)
            for h0 in range(0, cols, 1024):
                w = min(1024, cols - h0)
                lo_in, b_in_ = p.sm_ring.alloc(w)
                c.dma("pool", st[0:pr, lo_in:lo_in + w], src[rt * 128:rt * 128 + pr, h0:h0 + w],
                      writes=b_in_)
                lo_s, b_s = p.sm_ring.alloc(4 * w + 2 * (w // 16))
                qb = p.BB.bufs_for(h0, h0 + w)
                emit_fp4_quant(p, bb[0:pr, h0:h0 + w], qb, st[0:pr, lo_in:lo_in + w],
                               b_in_, lo_s, b_s, w, pr=pr)
            nch = cols // 128
            dstT = bb[:, 4096:4096 + nch * 128].rearrange("p (ch t) -> p ch t", ch=nch)[:, :, 0:pr]
            dstTb = p.BB.bufs_for(4096, 4096 + nch * 128)
            c.dma("sp", dstT, bb[0:pr, 0:cols], reads=p.BB.bufs_for(0, cols),
                  writes=dstTb, transpose=True)
            ddst = dst[0:cols, rt * 128:rt * 128 + pr].rearrange("(ch pp) t -> pp ch t", pp=128)
            c.dma("sp", ddst, dstT, reads=dstTb)
    p.sm_ring.unscope(_scope)


def emit_l1_ffn(p, W):
    c = p.ctx
    S, NT = p.S, p.NT
    NQ = S // 512
    st = p.SM.t
    bb = p.BB.t
    _scope = p.sm_ring.scope()

    p.ln_xn(want_xnT=False)

    def quant_tile(t):
        xn_ap = p.AB.t[:, t * 1024:(t + 1) * 1024]
        xnb = p.AB.bufs_for(t * 1024, (t + 1) * 1024)
        lo_s, b_s = p.sm_ring.alloc(4 * 1024 + 128)
        emit_fp4_quant(p, xn_ap, xnb, xn_ap, xnb, lo_s, b_s, 1024)
        dstT = p.AA.t[:, :].rearrange("p (ch s) -> p ch s", ch=8)[:, :, t * 128:(t + 1) * 128]
        dstTb = []
        for ch in range(8):
            dstTb += p.AA.bufs_for(ch * S + t * 128, ch * S + (t + 1) * 128)
        c.dma("sp", dstT, xn_ap, reads=xnb, writes=dstTb, transpose=True)

    NB2 = S // 256
    HQB = [0, 5632]

    def emit_gu(b):
        par = b % 2
        for nci in range(6):
            n0 = nci * 512
            nw = min(512, 2816 - n0)
            wg = p.wload(W["wgT"][:, n0:n0 + nw], 1024, nw)
            wu = p.wload(W["wuT"][:, n0:n0 + nw], 1024, nw)
            for t2 in range(2):
                t = 2 * b + t2
                pg, pgb = p.PB[0], p.PBbuf[0]
                pu, pub = p.PB[1], p.PBbuf[1]
                for (wt_, ps_, psb_) in ((wg, pg, pgb), (wu, pu, pub)):
                    steps = []
                    for kc in range(8):
                        (wtt, wlo, wpr, wcols, wbufs) = wt_[kc]
                        steps.append((p.AA.t[:, kc * S + t * 128: kc * S + t * 128 + 128],
                                      wtt[0:128, wlo:wlo + nw],
                                      list(wbufs) + p.AA.bufs_for(kc * S + t * 128, kc * S + t * 128 + 128)))
                    c.mm(ps_[:, 0:nw], psb_, steps)
                lo_sg, b_sg = p.sm_ring.alloc(512)
                def fsl(eng, pg=pg, nw=nw, lo_sg=lo_sg):
                    return eng.activation(st[:, lo_sg:lo_sg + nw], pg[:, 0:nw], AF.Silu)
                c.emit("act", fsl, reads=[pgb], writes=b_sg)
                hq_ap = bb[:, HQB[par] + t2 * 2816 + n0:HQB[par] + t2 * 2816 + n0 + nw]
                hqb = p.BB.bufs_for(HQB[par] + t2 * 2816 + n0, HQB[par] + t2 * 2816 + n0 + nw)
                c.tt("dve", hq_ap, pu[:, 0:nw], st[:, lo_sg:lo_sg + nw], ALU.mult,
                     reads=[pub] + b_sg, writes=hqb)
                lo_s, b_s = p.sm_ring.alloc(4 * nw + 2 * (nw // 16))
                emit_fp4_quant(p, hq_ap, hqb, hq_ap, hqb, lo_s, b_s, nw)

    def emit_quant(b, ab0):
        par = b % 2
        for t2 in range(2):
            hq_ap = bb[:, HQB[par] + t2 * 2816:HQB[par] + t2 * 2816 + 2816]
            hqb = p.BB.bufs_for(HQB[par] + t2 * 2816, HQB[par] + t2 * 2816 + 2816)
            dstT = p.AB.t[:, ab0:ab0 + 22 * 256].rearrange("p (m tt) -> p m tt", m=22)[:, :, t2 * 128:(t2 + 1) * 128]
            dstTb = p.AB.bufs_for(ab0, ab0 + 22 * 256)
            c.dma("sp", dstT, hq_ap, reads=hqb, writes=dstTb, transpose=True)

    def emit_down(b, ab0):
        for half in range(2):
            wd = p.wload(W["wdT"][:, half * 512:half * 512 + 512], 2816, 512)
            for t2 in range(2):
                t = 2 * b + t2
                pbk, pbuf = (p.PB[2], p.PBbuf[2]) if t2 % 2 == 0 else (p.PB[3], p.PBbuf[3])
                steps = []
                for mi in range(22):
                    (wtt, wlo, wpr, wcols, wbufs) = wd[mi]
                    steps.append((p.AB.t[:, ab0 + mi * 256 + t2 * 128:ab0 + mi * 256 + t2 * 128 + 128],
                                  wtt[0:wpr, wlo:wlo + 512],
                                  list(wbufs) + p.AB.bufs_for(ab0 + mi * 256, ab0 + (mi + 1) * 256)))
                c.mm(pbk[:, :], pbuf, steps)
                rb = p.RES.bufs[t]
                rap = p.RES.t[:, t * 1024 + half * 512: t * 1024 + half * 512 + 512]
                c.tt("dve", rap, rap, pbk[:, :], ALU.add, reads=[pbuf, rb], writes=[rb])

    for t in range(min(4, NT)):
        quant_tile(t)
    emit_gu(0)
    for b in range(NB2):
        for t in (2 * b + 4, 2 * b + 5):
            if t < NT:
                quant_tile(t)
        emit_quant(b, 0)
        if b + 1 < NB2:
            emit_gu(b + 1)
        emit_down(b, 0)
    p.sm_ring.unscope(_scope)


def emit_l1_attn_b(p, W, scratch, dbg=None):
    c = p.ctx
    S, NT = p.S, p.NT
    NQ = S // 512
    NKC = S // 128
    st = p.SM.t
    bb = p.BB.t
    _scope = p.sm_ring.scope()
    xn_dram = scratch["xn_bf"]
    y_dram = scratch["y_bf"]

    lo_w1, b_w1 = p.sm_ring.persist(1024)
    lo_b1, b_b1 = p.sm_ring.persist(1024)
    lo_row, b_row = p.sm_ring.alloc(2048)
    c.dma("pool", st[0:1, lo_row:lo_row + 1024], W["ln1_w"], writes=b_row)
    c.dma("pool", st[0:1, lo_row + 1024:lo_row + 2048], W["ln1_b"], writes=b_row)
    for (lo_dst, b_dst, off) in ((lo_w1, b_w1, 0), (lo_b1, b_b1, 1024)):
        for half in range(2):
            pbk, pbuf = p.PB[3], p.PBbuf[3]
            def fb(eng, off=off, half=half):
                return eng.matmul(pbk[:, :], p.ONE_ROW[0:1, 0:128],
                                  st[0:1, lo_row + off + half * 512:lo_row + off + half * 512 + 512],
                                  start=True, stop=True)
            c.emit("pe", fb, reads=[p.ONE_ROW_buf] + b_row, writes=[pbuf])
            c.copy("dve", st[:, lo_dst + half * 512:lo_dst + half * 512 + 512],
                   pbk[:, :], reads=[pbuf], writes=b_dst)

    b_xnd = Buf("xn_dram")

    def ln_s1(t):
        r = p.RES.t[:, t * 1024:(t + 1) * 1024]
        rb = p.RES.bufs[t]
        lo_st, b_st = p.sm_ring.alloc(32)
        def f1(eng):
            return eng.bn_stats(st[:, lo_st:lo_st + 6], r[:, 0:512])
        def f2(eng):
            return eng.bn_stats(st[:, lo_st + 6:lo_st + 12], r[:, 512:1024])
        def f3(eng):
            return eng.bn_aggr(st[:, lo_st + 12:lo_st + 14], st[:, lo_st:lo_st + 12])
        c.emit("dve", f1, reads=[rb], writes=b_st)
        c.emit("dve", f2, reads=[rb], writes=b_st)
        c.emit("dve", f3, reads=b_st, writes=b_st)
        p.rsqrt(st[:, lo_st + 14:lo_st + 15], st[:, lo_st + 13:lo_st + 14], EPS,
                reads=b_st, writes=b_st)
        return lo_st, b_st

    def ln_s2(t, lo_st, b_st):
        mu = st[:, lo_st + 12:lo_st + 13]
        var = st[:, lo_st + 13:lo_st + 14]
        a1_ = st[:, lo_st + 14:lo_st + 15]
        be_ = st[:, lo_st + 15:lo_st + 16]
        u_ = st[:, lo_st + 16:lo_st + 17]
        f_ = st[:, lo_st + 17:lo_st + 18]
        c.ts("dve", be_, mu, a1_, -1.0, ALU.mult, ALU.mult, reads=b_st, writes=b_st)
        c.ts("dve", u_, var, a1_, a1_, ALU.mult, ALU.mult, reads=b_st, writes=b_st)
        p.rsqrt(f_, u_, EPS, reads=b_st, writes=b_st)

    def ln_s3(t, lo_st, b_st):
        r = p.RES.t[:, t * 1024:(t + 1) * 1024]
        rb = p.RES.bufs[t]
        a1_ = st[:, lo_st + 14:lo_st + 15]
        be_ = st[:, lo_st + 15:lo_st + 16]
        f_ = st[:, lo_st + 17:lo_st + 18]
        af_ = st[:, lo_st + 18:lo_st + 19]
        bf_ = st[:, lo_st + 19:lo_st + 20]
        c.ts("dve", af_, a1_, f_, None, ALU.mult, None, reads=b_st, writes=b_st)
        c.ts("dve", bf_, be_, f_, None, ALU.mult, None, reads=b_st, writes=b_st)
        xn = p.AB.t[:, t * 1024:(t + 1) * 1024]
        xnb = p.AB.bufs_for(t * 1024, (t + 1) * 1024)
        c.ts("dve", xn, r, a1_, be_, ALU.mult, ALU.add, reads=[rb] + b_st, writes=xnb)
        c.dma("sp", xn_dram[t * 128:(t + 1) * 128, :], xn, reads=xnb, writes=[b_xnd])
        c.ts("dve", xn, r, af_, bf_, ALU.mult, ALU.add, reads=[rb] + b_st, writes=xnb)
        def fw1(eng):
            return eng.tensor_tensor(xn, xn, st[:, lo_w1:lo_w1 + 1024], ALU.mult)
        c.emit("dve", fw1, reads=xnb + b_w1, writes=xnb)
        def fb1(eng):
            return eng.tensor_tensor(xn, xn, st[:, lo_b1:lo_b1 + 1024], ALU.add)
        c.emit("dve", fb1, reads=xnb + b_b1, writes=xnb)
        dstT = p.AA.t[:, :].rearrange("p (ch s) -> p ch s", ch=8)[:, :, t * 128:(t + 1) * 128]
        dstTb = []
        for ch in range(8):
            dstTb += p.AA.bufs_for(ch * S + t * 128, ch * S + (t + 1) * 128)
        c.dma("sp", dstT, xn, reads=xnb, writes=dstTb, transpose=True)

    ln_st = {}
    for t in range(NT + 2):
        if t < NT:
            ln_st[t] = ln_s1(t)
        if 0 <= t - 1 < NT:
            ln_s2(t - 1, *ln_st[t - 1])
        if 0 <= t - 2 < NT:
            ln_s3(t - 2, *ln_st.pop(t - 2))

    if dbg is not None:
        dbg("phaseB", {})
    lo_io, b_io = p.sm_ring.persist(512)
    def fio(eng):
        return eng.iota(st[:, lo_io:lo_io + 512], [[1, 512]], channel_multiplier=-1,
                        allow_small_or_imprecise_dtypes=True)
    c.emit("pool", fio, writes=b_io)

    QT, KT, VA, EXL, EX2 = 0, 2048, 4096, 6400, 8448

    def qkv_pair(pair):
        wq = p.wload(W["wq"][:, pair * 128:(pair + 1) * 128], 1024, 128)
        wk = p.wload(W["wk"][:, pair * 128:(pair + 1) * 128], 1024, 128)
        wv = p.wload(W["wv"][:, pair * 128:(pair + 1) * 128], 1024, 128)
        b_qT = p.BB.bufs_for(QT, QT + S)
        b_kT = p.BB.bufs_for(KT, KT + S)
        b_va = p.BB.bufs_for(VA, VA + NKC * 128)
        for (wtiles, dstcol, dstbufs) in ((wq, QT, b_qT), (wk, KT, b_kT)):
            for tc in range(NQ):
                pbk, pbuf = (p.PB[0], p.PBbuf[0]) if tc % 2 == 0 else (p.PB[1], p.PBbuf[1])
                steps = []
                for kc in range(8):
                    (wt, wlo, wpr, wcols, wbufs) = wtiles[kc]
                    steps.append((wt[0:128, wlo:wlo + 128],
                                  p.AA.t[:, kc * S + tc * 512: kc * S + tc * 512 + 512],
                                  list(wbufs) + p.AA.bufs_for(kc * S + tc * 512, kc * S + tc * 512 + 512)))
                c.mm(pbk[:, :], pbuf, steps)
                c.copy("act", bb[:, dstcol + tc * 512: dstcol + tc * 512 + 512],
                       pbk[:, :], reads=[pbuf], writes=dstbufs)
        for tt_ in range(NT):
            pbk, pbuf = (p.PB[0], p.PBbuf[0]) if tt_ % 2 == 0 else (p.PB[1], p.PBbuf[1])
            steps = []
            for kc in range(8):
                (wt, wlo, wpr, wcols, wbufs) = wv[kc]
                steps.append((p.AA.t[:, kc * S + tt_ * 128: kc * S + tt_ * 128 + 128],
                              wt[0:128, wlo:wlo + 128],
                              list(wbufs) + p.AA.bufs_for(kc * S + tt_ * 128, kc * S + tt_ * 128 + 128)))
            c.mm(pbk[:, 0:128], pbuf, steps)
            for hh in range(2):
                c.copy("act", bb[:, VA + tt_ * 128 + hh * 64: VA + tt_ * 128 + hh * 64 + 64],
                       pbk[:, hh * 64:hh * 64 + 64], reads=[pbuf], writes=b_va)
        return b_qT, b_kT, b_va

    for pair in range(8):
        _pscope = p.sm_ring.scope()
        b_qT, b_kT, b_va = qkv_pair(pair)
        pending = []
        for hh in range(2):
            _hscope = p.sm_ring.scope()
            h = 2 * pair + hh
            gamma = 1.0 - 2.0 ** (-5.0 - h)
            lg = math.log(gamma)
            lo_pat, b_pat = p.sm_ring.persist(512 * 5)
            sc_ap, sc_b = p.const_col(lg)
            c.act(st[:, lo_pat:lo_pat + 512], st[:, lo_io:lo_io + 512], AF.Exp,
                  scale=(sc_ap, sc_b), reads=b_io, writes=b_pat)
            for mi in range(4):
                dstm = st[:, lo_pat + (mi + 1) * 512:lo_pat + (mi + 2) * 512]
                def fm(eng, dstm=dstm, mi=mi):
                    return eng.affine_select(dstm, st[:, lo_pat:lo_pat + 512],
                                             [[1, 512]], ALU.is_ge, 0.0,
                                             base=-mi * 128, channel_multiplier=-1)
                c.emit("pool", fm, reads=b_pat, writes=b_pat)
            qrow0 = hh * 64
            for qc in range(NQ):
                qbase = qc * 512
                par = (qc + hh) % 2
                ob, obuf = (p.PB[2], p.PBbuf[2]) if par == 0 else (p.PB[3], p.PBbuf[3])
                alt, altbuf = (p.PB[3], p.PBbuf[3]) if par == 0 else (p.PB[2], p.PBbuf[2])
                kcs = [kc for kc in range(4 * qc + 4)
                       if (qbase - kc * 128 - 127) * lg > -25.0 or True]
                kcs = [kc for kc in kcs if (qbase - (kc + 1) * 128) * lg > -30.0]
                exi = 0
                pend_pv = None
                for j, kc in enumerate(kcs):
                    sa, sabuf = (p.PA[0], p.PAbuf[0]) if (j // 2) % 2 == 0 else (p.PA[1], p.PAbuf[1])
                    soff = (j % 2) * 512
                    def fsc(eng, kc=kc, sa=sa, soff=soff):
                        return eng.matmul(
                            sa[:, soff:soff + 512],
                            bb[qrow0:qrow0 + 64, KT + kc * 128:KT + kc * 128 + 128],
                            bb[qrow0:qrow0 + 64, QT + qbase:QT + qbase + 512],
                            start=True, stop=True)
                    c.emit("pe", fsc, reads=b_kT + b_qT, writes=[sabuf])
                    if j == 0 and pending:
                        pending.pop(0)()
                    o = kc * 128 - qbase
                    cc = 0.125 * math.exp(lg * (qbase - kc * 128))
                    pat_i = 0 if o < 0 else (o // 128 + 1)
                    pat = st[:, lo_pat + pat_i * 512:lo_pat + (pat_i + 1) * 512]
                    ex = EXL + (exi % 4) * 512
                    exi += 1
                    exb = p.BB.bufs_for(ex, ex + 512)
                    def fev(eng, sa=sa, soff=soff, cc=cc, pat=pat, ex=ex):
                        return eng.scalar_tensor_tensor(bb[:, ex:ex + 512],
                                                        sa[:, soff:soff + 512],
                                                        cc, pat, ALU.mult, ALU.mult)
                    c.emit("dve", fev, reads=[sabuf] + b_pat, writes=exb)
                    if pend_pv is not None:
                        pend_pv()

                    def mk_pv(kc=kc, ex=ex, exb=exb, ob=ob, obuf=obuf,
                              first=(j == 0), last=(j == len(kcs) - 1)):
                        def emit_pv():
                            def fpv(eng):
                                return eng.matmul(ob[0:64, :],
                                                  bb[:, VA + kc * 128 + hh * 64:VA + kc * 128 + hh * 64 + 64],
                                                  bb[:, ex:ex + 512],
                                                  start=first, stop=last)
                            c.emit("pe", fpv, reads=b_va + exb, writes=[obuf])
                        return emit_pv

                    pend_pv = mk_pv()
                pend_pv()
                def tail(ob=ob, obuf=obuf, h=h, qbase=qbase):
                    ex2b = p.BB.bufs_for(EX2, EX2 + 512)
                    c.copy("act", bb[0:64, EX2:EX2 + 512], ob[0:64, :], reads=[obuf],
                           writes=ex2b)
                    sqb = p.BB.bufs_for(EX2 + 512, EX2 + 1024)
                    c.tt("pool", bb[0:64, EX2 + 512:EX2 + 1024], bb[0:64, EX2:EX2 + 512],
                         bb[0:64, EX2:EX2 + 512], ALU.mult, reads=ex2b, writes=sqb)
                    nb_, nbuf_ = p.PB[0], p.PBbuf[0]
                    lo_mt, b_mt = p.sm_ring.alloc(1024 + 512)
                    def fs1b(eng):
                        return eng.matmul(nb_[0:1, :], p.ONE_COL_BF[0:64, 0:1],
                                          bb[0:64, EX2:EX2 + 512], start=True, stop=True)
                    c.emit("pe", fs1b, reads=[p.ONE_COL_BF_buf] + ex2b, writes=[nbuf_])
                    c.copy("dve", st[0:1, lo_mt:lo_mt + 512], nb_[0:1, :], reads=[nbuf_],
                           writes=b_mt)
                    def fs2b(eng):
                        return eng.matmul(nb_[0:1, :], p.ONE_COL_BF[0:64, 0:1],
                                          bb[0:64, EX2 + 512:EX2 + 1024], start=True, stop=True)
                    c.emit("pe", fs2b, reads=[p.ONE_COL_BF_buf] + sqb, writes=[nbuf_])
                    mrow = st[0:1, lo_mt:lo_mt + 512]
                    vrow = st[0:1, lo_mt + 512:lo_mt + 1024]
                    c.ts("dve", mrow, mrow, 1.0 / 64.0, None, ALU.mult, None,
                         reads=b_mt, writes=b_mt)
                    def fv(eng):
                        return eng.tensor_scalar(vrow, nb_[0:1, :], 1.0 / 64.0, None, ALU.mult)
                    c.emit("dve", fv, reads=[nbuf_], writes=b_mt)
                    lo_m2, b_m2 = p.sm_ring.alloc(512)
                    c.tt("dve", st[0:1, lo_m2:lo_m2 + 512], mrow, mrow, ALU.mult,
                         reads=b_mt, writes=b_m2)
                    c.tt("dve", vrow, vrow, st[0:1, lo_m2:lo_m2 + 512], ALU.subtract,
                         reads=b_mt + b_m2, writes=b_mt)
                    p.rsqrt(vrow, vrow, 1e-6, reads=b_mt, writes=b_mt)
                    bm, bmb = p.PB[1], p.PBbuf[1]
                    def fbm(eng):
                        return eng.matmul(bm[0:64, :], p.ONE_ROW[0:1, 0:64], mrow,
                                          start=True, stop=True)
                    c.emit("pe", fbm, reads=[p.ONE_ROW_buf] + b_mt, writes=[bmb])
                    t1b = p.BB.bufs_for(EX2 + 1024, EX2 + 1536)
                    def fsub(eng):
                        return eng.tensor_tensor(bb[0:64, EX2 + 1024:EX2 + 1536],
                                                 bb[0:64, EX2:EX2 + 512], bm[0:64, :],
                                                 ALU.subtract)
                    c.emit("dve", fsub, reads=ex2b + [bmb], writes=t1b)
                    def fbr(eng):
                        return eng.matmul(bm[0:64, :], p.ONE_ROW[0:1, 0:64], vrow,
                                          start=True, stop=True)
                    c.emit("pe", fbr, reads=[p.ONE_ROW_buf] + b_mt, writes=[bmb])
                    och = h // 2
                    orow = (h % 2) * 64
                    ocol = och * S + qbase
                    ob_bufs = p.AB.bufs_for(ocol, ocol + 512)
                    def fmul(eng):
                        return eng.tensor_tensor(p.AB.t[orow:orow + 64, ocol:ocol + 512],
                                                 bb[0:64, EX2 + 1024:EX2 + 1536],
                                                 bm[0:64, :], ALU.mult)
                    c.emit("dve", fmul, reads=t1b + [bmb], writes=ob_bufs)
                pending.append(tail)
            p.sm_ring.unscope(_hscope)
        while pending:
            pending.pop(0)()
        p.sm_ring.unscope(_pscope)

    for ch in range(8):
        wg = p.wload(W["wg"][:, ch * 128:(ch + 1) * 128], 1024, 128)
        for tc in range(NQ):
            pbk, pbuf = (p.PB[0], p.PBbuf[0]) if tc % 2 == 0 else (p.PB[1], p.PBbuf[1])
            steps = []
            for kc in range(8):
                (wt, wlo, wpr, wcols, wbufs) = wg[kc]
                steps.append((wt[0:128, wlo:wlo + 128],
                              p.AA.t[:, kc * S + tc * 512: kc * S + tc * 512 + 512],
                              list(wbufs) + p.AA.bufs_for(kc * S + tc * 512, kc * S + tc * 512 + 512)))
            c.mm(pbk[:, :], pbuf, steps)
            gb = p.BB.bufs_for(EXL, EXL + 512)
            def fsl(eng, pbk=pbk):
                return eng.activation(bb[:, EXL:EXL + 512], pbk[:, :], AF.Silu)
            c.emit("act", fsl, reads=[pbuf], writes=gb)
            ocol = ch * S + tc * 512
            ob_bufs = p.AB.bufs_for(ocol, ocol + 512)
            c.tt("dve", p.AB.t[:, ocol:ocol + 512], p.AB.t[:, ocol:ocol + 512],
                 bb[:, EXL:EXL + 512], ALU.mult, reads=ob_bufs + gb, writes=ob_bufs)

    if dbg is not None:
        dbg("ret", {})
    lo_w2, b_w2 = p.sm_ring.persist(1024)
    lo_b2r, b_b2r = p.sm_ring.persist(1024)
    lo_row2, b_row2 = p.sm_ring.alloc(2048)
    c.dma("pool", st[0:1, lo_row2:lo_row2 + 1024], W["ln2_w"], writes=b_row2)
    c.dma("pool", st[0:1, lo_row2 + 1024:lo_row2 + 2048], W["ln2_b"], writes=b_row2)
    for (lo_dst, b_dst, off) in ((lo_w2, b_w2, 0), (lo_b2r, b_b2r, 1024)):
        for half in range(2):
            pbk, pbuf = p.PB[3], p.PBbuf[3]
            def fb(eng, off=off, half=half):
                return eng.matmul(pbk[:, :], p.ONE_ROW[0:1, 0:128],
                                  st[0:1, lo_row2 + off + half * 512:lo_row2 + off + half * 512 + 512],
                                  start=True, stop=True)
            c.emit("pe", fb, reads=[p.ONE_ROW_buf] + b_row2, writes=[pbuf])
            c.copy("dve", st[:, lo_dst + half * 512:lo_dst + half * 512 + 512],
                   pbk[:, :], reads=[pbuf], writes=b_dst)

    b_yd = Buf("y_dram")
    wo = p.wload(W["wo"], 1024, 1024)
    for t in range(NT):
        lo_y, b_y = p.sm_ring.alloc(1024 + 128 + 32)
        lo_xnb = lo_y + 1024
        c.dma("pool", bb[:, EXL:EXL + 1024],
              xn_dram[t * 128:(t + 1) * 128, :], reads=[b_xnd],
              writes=p.BB.bufs_for(EXL, EXL + 1024))
        for half in range(2):
            bi = (t * 2 + half) % 4
            pbk, pbuf = p.PB[bi], p.PBbuf[bi]
            steps = []
            for kc in range(8):
                (wt, wlo, wpr, wcols, wbufs) = wo[kc]
                steps.append((p.AB.t[:, kc * S + t * 128: kc * S + t * 128 + 128],
                              wt[0:128, wlo + half * 512: wlo + half * 512 + 512],
                              list(wbufs) + p.AB.bufs_for(kc * S + t * 128, kc * S + t * 128 + 128)))
            c.mm(pbk[:, :], pbuf, steps)
            def fy(eng, pbk=pbk, half=half, lo_y=lo_y):
                return eng.tensor_tensor(st[:, lo_y + half * 512:lo_y + half * 512 + 512],
                                         pbk[:, :], bb[:, EXL + half * 512:EXL + half * 512 + 512],
                                         ALU.add)
            c.emit("dve", fy, reads=[pbuf] + p.BB.bufs_for(EXL, EXL + 1024), writes=b_y)
        ytile = st[:, lo_y:lo_y + 1024]
        c.dma("pool", y_dram[t * 128:(t + 1) * 128, :], ytile, reads=b_y, writes=[b_yd])
        lo_st, b_st = p.sm_ring.alloc(32)
        def g1(eng):
            return eng.bn_stats(st[:, lo_st:lo_st + 6], ytile[:, 0:512])
        def g2(eng):
            return eng.bn_stats(st[:, lo_st + 6:lo_st + 12], ytile[:, 512:1024])
        def g3(eng):
            return eng.bn_aggr(st[:, lo_st + 12:lo_st + 14], st[:, lo_st:lo_st + 12])
        c.emit("dve", g1, reads=b_y, writes=b_st)
        c.emit("dve", g2, reads=b_y, writes=b_st)
        c.emit("dve", g3, reads=b_st, writes=b_st)
        al = st[:, lo_st + 14:lo_st + 15]
        bt_ = st[:, lo_st + 15:lo_st + 16]
        p.rsqrt(al, st[:, lo_st + 13:lo_st + 14], EPS, reads=b_st, writes=b_st)
        c.ts("dve", bt_, st[:, lo_st + 12:lo_st + 13], al, -1.0, ALU.mult, ALU.mult,
             reads=b_st, writes=b_st)
        un = bb[:, 0:1024]
        unb = p.BB.bufs_for(0, 1024)
        c.ts("dve", un, ytile, al, bt_, ALU.mult, ALU.add, reads=b_y + b_st, writes=unb)
        def gw(eng):
            return eng.tensor_tensor(un, un, st[:, lo_w2:lo_w2 + 1024], ALU.mult)
        c.emit("dve", gw, reads=unb + b_w2, writes=unb)
        def gb_(eng):
            return eng.tensor_tensor(un, un, st[:, lo_b2r:lo_b2r + 1024], ALU.add)
        c.emit("dve", gb_, reads=unb + b_b2r, writes=unb)
        dstT = p.AA.t[:, :].rearrange("p (ch s) -> p ch s", ch=8)[:, :, t * 128:(t + 1) * 128]
        dstTb = []
        for ch in range(8):
            dstTb += p.AA.bufs_for(ch * S + t * 128, ch * S + (t + 1) * 128)
        c.dma("sp", dstT, un, reads=unb, writes=dstTb, transpose=True)

    if dbg is not None:
        dbg("y", dict(b_yd=b_yd))
    _, lo_bia, _, b_bia = p.load_bias_col(W["b_in"].rearrange("a n -> (a n)")[0:2730], 2730, "bia")
    _, lo_bib, _, b_bib = p.load_bias_col(W["b_in"].rearrange("a n -> (a n)")[2730:5460], 2730, "bib")
    lo_bo, b_bo = p.sm_ring.persist(1024)
    c.dma("pool", st[0:1, lo_bo:lo_bo + 1024], W["b_out"], writes=b_bo)

    for tc in range(NQ):
        for mi in range(22):
            rows = 128 if mi < 21 else 42
            wa = p.wload(W["w_in"][:, mi * 128:mi * 128 + rows], 1024, rows)
            wb_ = p.wload(W["w_in"][:, 2730 + mi * 128:2730 + mi * 128 + rows], 1024, rows)
            if mi % 2 == 0:
                pa, pab = p.PB[0], p.PBbuf[0]
                pb_, pbb = p.PB[1], p.PBbuf[1]
            else:
                pa, pab = p.PA[0][:, 0:512], p.PAbuf[0]
                pb_, pbb = p.PA[1][:, 0:512], p.PAbuf[1]
            for (wt_, ps_, psb_) in ((wa, pa, pab), (wb_, pb_, pbb)):
                steps = []
                for kc in range(8):
                    (wtt, wlo, wpr, wcols, wbufs) = wt_[kc]
                    steps.append((wtt[0:128, wlo:wlo + rows],
                                  p.AA.t[:, kc * S + tc * 512: kc * S + tc * 512 + 512],
                                  list(wbufs) + p.AA.bufs_for(kc * S + tc * 512, kc * S + tc * 512 + 512)))
                c.mm(ps_[0:rows, :], psb_, steps)
            SLT = 11264
            sl = bb[0:rows, SLT:SLT + 512]
            slb = p.BB.bufs_for(SLT, SLT + 512)
            def fsl(eng, pb_=pb_, rows=rows, mi=mi):
                return eng.activation(sl, pb_[0:rows, :], AF.Silu,
                                      bias=st[0:rows, lo_bib + mi:lo_bib + mi + 1])
            c.emit("act", fsl, reads=[pbb] + b_bib, writes=slb)
            ht = bb[0:rows, mi * 512:(mi + 1) * 512]
            htb = p.BB.bufs_for(mi * 512, (mi + 1) * 512)
            def fh(eng, pa=pa, rows=rows, mi=mi):
                return eng.scalar_tensor_tensor(ht, pa[0:rows, :],
                                                st[0:rows, lo_bia + mi:lo_bia + mi + 1],
                                                sl, ALU.add, ALU.mult)
            c.emit("dve", fh, reads=[pab] + b_bia + slb, writes=htb)
        for half in range(2):
            wd = p.wload(W["w_out"][:, half * 512:half * 512 + 512], 2730, 512)
            for t4 in range(4):
                t = tc * 4 + t4
                pbk, pbuf = (p.PB[2], p.PBbuf[2]) if t4 % 2 == 0 else (p.PB[3], p.PBbuf[3])
                steps = []
                for mi in range(22):
                    rows = 128 if mi < 21 else 42
                    (wtt, wlo, wpr, wcols, wbufs) = wd[mi]
                    steps.append((bb[0:rows, mi * 512 + t4 * 128:mi * 512 + t4 * 128 + 128],
                                  wtt[0:rows, wlo:wlo + 512],
                                  list(wbufs) + p.BB.bufs_for(mi * 512, (mi + 1) * 512)))
                steps.append((p.ONE_ROW[0:1, 0:128],
                              st[0:1, lo_bo + half * 512:lo_bo + half * 512 + 512],
                              [p.ONE_ROW_buf] + b_bo))
                c.mm(pbk[:, :], pbuf, steps)
                YT = 11776
                yb2 = p.BB.bufs_for(YT, YT + 512)
                c.dma("pool", bb[:, YT:YT + 512],
                      y_dram[t * 128:(t + 1) * 128, half * 512:half * 512 + 512],
                      reads=[b_yd], writes=yb2)
                rb = p.RES.bufs[t]
                rap = p.RES.t[:, t * 1024 + half * 512: t * 1024 + half * 512 + 512]
                def fz1(eng, pbk=pbk):
                    return eng.tensor_tensor(rap, rap, pbk[:, :], ALU.add)
                c.emit("dve", fz1, reads=[pbuf, rb], writes=[rb])
                def fz2(eng):
                    return eng.tensor_tensor(rap, rap, bb[:, 11776:11776 + 512], ALU.add)
                c.emit("dve", fz2, reads=yb2 + [rb], writes=[rb])
    p.sm_ring.unscope(_scope)



S_FULL = 2048

_CONV = [
    ("L0_attn_a_Wq", 1024, 1024), ("L0_attn_a_Wk", 1024, 1024),
    ("L0_attn_a_Wv", 1024, 1024), ("L0_attn_a_Wo", 1024, 1024),
    ("L0_attn_b_w_q", 1024, 1024), ("L0_attn_b_w_kv", 1024, 2048),
    ("L0_attn_b_w_out", 1024, 1024),
    ("L0_ffn_enc_W", 1024, 2048), ("L0_ffn_s2c_W", 2048, 64),
    ("L0_ffn_rW1", 64, 64), ("L0_ffn_rW2", 64, 8),
    ("L0_ffn_We1", 512, 1024), ("L0_ffn_We2", 8192, 64),
    ("L0_ffn_c2s_W", 64, 2048), ("L0_ffn_dec_W", 2048, 1024),
    ("L1_attn_a_W_attn", 1024, 3072), ("L1_attn_a_W_proj", 1024, 1024),
    ("L1_attn_b_wq", 1024, 1024), ("L1_attn_b_wk", 1024, 1024),
    ("L1_attn_b_wv", 1024, 1024), ("L1_attn_b_wg", 1024, 1024),
    ("L1_attn_b_wo", 1024, 1024), ("L1_attn_b_w_in", 1024, 5460),
    ("L1_attn_b_w_out", 2730, 1024),
]


def build_prep(S=128):
    nc = bass.Bass("TRN2", target_bir_lowering=False)
    ins = {}
    outs = {}
    for (n, r, cmax) in _CONV:
        ins[n] = nc.dram_tensor(n, [r, cmax], F32, kind="ExternalInput").ap()
        outs[n] = nc.dram_tensor(n + "_bf", [r, cmax], BF16, kind="ExternalOutput").ap()
    fp4src = {n: nc.dram_tensor(n, shp, F32, kind="ExternalInput").ap()
              for n, shp in (("w_gate", [2816, 1024]), ("w_up", [2816, 1024]),
                             ("w_down", [1024, 2816]))}
    fp4dst = {"w_gate": nc.dram_tensor("wgT", [1024, 2816], BF16, kind="ExternalOutput").ap(),
              "w_up": nc.dram_tensor("wuT", [1024, 2816], BF16, kind="ExternalOutput").ap(),
              "w_down": nc.dram_tensor("wdT", [2816, 1024], BF16, kind="ExternalOutput").ap()}
    import contextlib
    es = contextlib.ExitStack()
    with es:
        p = Prog(nc, es, S=S)
        for (n, r, cw) in _CONV:
            p.to_bf16_dram(ins[n], outs[n], r, cw)
        emit_fp4_weight_prep(p, fp4src, fp4dst)
        p.finish()
    return nc


def _wmaps_l0a(i):
    return {"wq": "L0_attn_a_Wq_bf", "wk": "L0_attn_a_Wk_bf", "wv": "L0_attn_a_Wv_bf",
            "wo": "L0_attn_a_Wo_bf", "null_k": "L0_attn_a_null_k",
            "null_v": "L0_attn_a_null_v", "logit_scale": "L0_attn_a_logit_scale"}


def build_sublayer(kind, S=S_FULL):
    nc = bass.Bass("TRN2", target_bir_lowering=False)
    import contextlib
    es = contextlib.ExitStack()

    def dram(n, shp, dt=BF16):
        return nc.dram_tensor(n, shp, dt, kind="ExternalInput").ap()

    x_ap = nc.dram_tensor("x", [S, 1024], F32, kind="ExternalInput").ap()
    o_ap = nc.dram_tensor("o", [S, 1024], F32, kind="ExternalOutput").ap()
    with es:
        p = Prog(nc, es, S=S)
        p.load_res(x_ap)
        if kind == "l0a":
            W = {"wq": dram("wq", [1024, 1024]), "wk": dram("wk", [1024, 1024]),
                 "wv": dram("wv", [1024, 1024]), "wo": dram("wo", [1024, 1024]),
                 "null_k": dram("null_k", [16, 64], F32),
                 "null_v": dram("null_v", [16, 64], F32),
                 "logit_scale": dram("logit_scale", [1, 16], F32)}
            emit_l0_attn_a(p, W)
        elif kind == "l0f":
            W = {"enc_W": dram("enc_W", [1024, 2048]),
                 "enc_b": dram("enc_b", [1, 2048], F32),
                 "s2c_W": dram("s2c_W", [2048, 64]), "s2c_b": dram("s2c_b", [1, 64], F32),
                 "rW1": dram("rW1", [64, 64]), "rb1": dram("rb1", [1, 64], F32),
                 "rW2": dram("rW2", [64, 8]), "rb2": dram("rb2", [1, 8], F32),
                 "We1": dram("We1", [8, 64, 1024]), "be1": dram("be1", [8, 1024], F32),
                 "We2": dram("We2", [8, 1024, 64]), "be2": dram("be2", [8, 64], F32),
                 "c2s_W": dram("c2s_W", [64, 2048]), "c2s_b": dram("c2s_b", [1, 2048], F32),
                 "dec_W": dram("dec_W", [2048, 1024]), "dec_b": dram("dec_b", [1, 1024], F32),
                 "ln_g": dram("ln_g", [1, 1024], F32), "ln_b": dram("ln_b", [1, 1024], F32)}
            emit_l0_ffn(p, W)
        elif kind == "l0b":
            W = {"w_q": dram("w_q", [1024, 1024]), "w_kv": dram("w_kv", [1024, 2048]),
                 "w_out": dram("w_out", [1024, 1024]), "b_out": dram("b_out", [1, 1024], F32),
                 "scale": dram("scale", [1, 16], F32)}
            emit_l0_attn_b(p, W)
        elif kind == "l1a":
            W = {"w_attn": dram("w_attn", [1024, 3072]),
                 "b_attn": dram("b_attn", [1, 3072], F32),
                 "w_proj": dram("w_proj", [1024, 1024]),
                 "b_proj": dram("b_proj", [1, 1024], F32)}
            emit_l1_attn_a(p, W)
        elif kind == "l1f":
            W = {"wgT": dram("wgT", [1024, 2816]), "wuT": dram("wuT", [1024, 2816]),
                 "wdT": dram("wdT", [2816, 1024])}
            emit_l1_ffn(p, W)
        elif kind == "l1b":
            W = {"ln1_w": dram("ln1_w", [1, 1024], F32), "ln1_b": dram("ln1_b", [1, 1024], F32),
                 "wq": dram("wq", [1024, 1024]), "wk": dram("wk", [1024, 1024]),
                 "wv": dram("wv", [1024, 1024]), "wg": dram("wg", [1024, 1024]),
                 "wo": dram("wo", [1024, 1024]),
                 "ln2_w": dram("ln2_w", [1, 1024], F32), "ln2_b": dram("ln2_b", [1, 1024], F32),
                 "w_in": dram("w_in", [1024, 5460]), "b_in": dram("b_in", [1, 5460], F32),
                 "w_out": dram("w_out", [2730, 1024]), "b_out": dram("b_out", [1, 1024], F32)}
            scr = {"xn_bf": nc.dram_tensor("xn_bf", [S, 1024], BF16, kind="Internal").ap(),
                   "y_bf": nc.dram_tensor("y_bf", [S, 1024], F32, kind="Internal").ap()}
            emit_l1_attn_b(p, W, scr)
        p.store_res(o_ap)
        p.finish()
    return nc


def kernel(**inputs):
    n = NCORES
    x = np.ascontiguousarray(inputs["L0_attn_a_x"])
    shards = [x[2 * c:2 * c + 2] for c in range(n)]

    prep_in = {}
    for (nm, r, cw) in _CONV:
        prep_in[nm] = np.ascontiguousarray(inputs[nm].reshape(r, cw))
    prep_in["w_gate"] = inputs["L1_ffn_w_gate"]
    prep_in["w_up"] = inputs["L1_ffn_w_up"]
    prep_in["w_down"] = inputs["L1_ffn_w_down"]
    nc_prep = build_prep()
    res = run_bass_kernel_spmd(nc_prep, [prep_in], core_ids=[0])
    wbf = dict(res.results[0])

    def col(v, n_):
        return np.ascontiguousarray(v.reshape(1, n_).astype(np.float32))

    progs = {k: build_sublayer(k) for k in ("l0a", "l0f", "l0b", "l1a", "l1f", "l1b")}

    w_l0a = {"wq": wbf["L0_attn_a_Wq_bf"], "wk": wbf["L0_attn_a_Wk_bf"],
             "wv": wbf["L0_attn_a_Wv_bf"], "wo": wbf["L0_attn_a_Wo_bf"],
             "null_k": inputs["L0_attn_a_null_k"].reshape(16, 64),
             "null_v": inputs["L0_attn_a_null_v"].reshape(16, 64),
             "logit_scale": col(inputs["L0_attn_a_logit_scale"], 16)}
    w_l0f = {"enc_W": wbf["L0_ffn_enc_W_bf"], "enc_b": col(inputs["L0_ffn_enc_b"], 2048),
             "s2c_W": wbf["L0_ffn_s2c_W_bf"], "s2c_b": col(inputs["L0_ffn_s2c_b"], 64),
             "rW1": wbf["L0_ffn_rW1_bf"], "rb1": col(inputs["L0_ffn_rb1"], 64),
             "rW2": wbf["L0_ffn_rW2_bf"], "rb2": col(inputs["L0_ffn_rb2"], 8),
             "We1": wbf["L0_ffn_We1_bf"].reshape(8, 64, 1024),
             "be1": inputs["L0_ffn_be1"],
             "We2": wbf["L0_ffn_We2_bf"].reshape(8, 1024, 64),
             "be2": inputs["L0_ffn_be2"],
             "c2s_W": wbf["L0_ffn_c2s_W_bf"], "c2s_b": col(inputs["L0_ffn_c2s_b"], 2048),
             "dec_W": wbf["L0_ffn_dec_W_bf"], "dec_b": col(inputs["L0_ffn_dec_b"], 1024),
             "ln_g": col(inputs["L0_ffn_ln_g"], 1024), "ln_b": col(inputs["L0_ffn_ln_b"], 1024)}
    w_l0b = {"w_q": wbf["L0_attn_b_w_q_bf"], "w_kv": wbf["L0_attn_b_w_kv_bf"],
             "w_out": wbf["L0_attn_b_w_out_bf"], "b_out": col(inputs["L0_attn_b_b_out"], 1024),
             "scale": col(inputs["L0_attn_b_scale"], 16)}
    w_l1a = {"w_attn": wbf["L1_attn_a_W_attn_bf"], "b_attn": col(inputs["L1_attn_a_b_attn"], 3072),
             "w_proj": wbf["L1_attn_a_W_proj_bf"], "b_proj": col(inputs["L1_attn_a_b_proj"], 1024)}
    w_l1f = {"wgT": wbf["wgT"], "wuT": wbf["wuT"], "wdT": wbf["wdT"]}
    w_l1b = {"ln1_w": col(inputs["L1_attn_b_ln1_w"], 1024), "ln1_b": col(inputs["L1_attn_b_ln1_b"], 1024),
             "wq": wbf["L1_attn_b_wq_bf"], "wk": wbf["L1_attn_b_wk_bf"],
             "wv": wbf["L1_attn_b_wv_bf"], "wg": wbf["L1_attn_b_wg_bf"],
             "wo": wbf["L1_attn_b_wo_bf"],
             "ln2_w": col(inputs["L1_attn_b_ln2_w"], 1024), "ln2_b": col(inputs["L1_attn_b_ln2_b"], 1024),
             "w_in": wbf["L1_attn_b_w_in_bf"], "b_in": col(inputs["L1_attn_b_b_in"], 5460),
             "w_out": wbf["L1_attn_b_w_out_bf"], "b_out": col(inputs["L1_attn_b_b_out"], 1024)}

    seq = [("l0a", w_l0a), ("l0f", w_l0f), ("l0b", w_l0b), ("l0f", w_l0f),
           ("l1a", w_l1a), ("l1f", w_l1f), ("l1b", w_l1b), ("l1f", w_l1f)]

    cur = [np.ascontiguousarray(shards[c][e]) for c in range(n) for e in range(2)]
    for (kind, wm) in seq:
        nc_s = progs[kind]
        for e in range(2):
            in_maps = []
            for ci in range(n):
                m = dict(wm)
                m["x"] = cur[ci * 2 + e]
                in_maps.append(m)
            res = run_bass_kernel_spmd(nc_s, in_maps, core_ids=list(range(n)))
            for ci in range(n):
                cur[ci * 2 + e] = res.results[ci]["o"]

    out = np.empty((16, 2048, 1024), np.float32)
    for ci in range(n):
        for e in range(2):
            out[2 * ci + e] = cur[ci * 2 + e]
    return out


def build_fused():
    nc = bass.Bass("TRN2", target_bir_lowering=False)
    S = S_FULL
    ins = {}
    for (n, r, cw) in _CONV:
        ins[n] = nc.dram_tensor(n, [r, cw], F32, kind="ExternalInput").ap()
    wbf = {n: nc.dram_tensor(n + "_bf", [r, cw], BF16, kind="Internal").ap()
           for (n, r, cw) in _CONV}
    fp4src = {n: nc.dram_tensor(n, shp, F32, kind="ExternalInput").ap()
              for n, shp in (("w_gate", [2816, 1024]), ("w_up", [2816, 1024]),
                             ("w_down", [1024, 2816]))}
    fp4dst = {"w_gate": nc.dram_tensor("wgT", [1024, 2816], BF16, kind="Internal").ap(),
              "w_up": nc.dram_tensor("wuT", [1024, 2816], BF16, kind="Internal").ap(),
              "w_down": nc.dram_tensor("wdT", [2816, 1024], BF16, kind="Internal").ap()}
    sm_in = {}
    for n, shp in (("null_k", [16, 64]), ("null_v", [16, 64]), ("logit_scale", [1, 16]),
                   ("b_out_l0b", [1, 1024]), ("scale_l0b", [1, 16]),
                   ("enc_b", [1, 2048]), ("s2c_b", [1, 64]), ("rb1", [1, 64]),
                   ("rb2", [1, 8]), ("be1", [8, 1024]), ("be2", [8, 64]),
                   ("c2s_b", [1, 2048]), ("dec_b", [1, 1024]), ("ln_g", [1, 1024]),
                   ("ln_b", [1, 1024]), ("b_attn", [1, 3072]), ("b_proj", [1, 1024]),
                   ("ln1_w", [1, 1024]), ("ln1_b", [1, 1024]), ("ln2_w", [1, 1024]),
                   ("ln2_b", [1, 1024]), ("b_in", [1, 5460]), ("b_out_l1b", [1, 1024])):
        sm_in[n] = nc.dram_tensor(n, shp, F32, kind="ExternalInput").ap()
    x_ap = nc.dram_tensor("x", [2 * S, 1024], F32, kind="ExternalInput").ap()
    o_ap = nc.dram_tensor("o", [2 * S, 1024], F32, kind="ExternalOutput").ap()
    xn_bf = nc.dram_tensor("xn_bf", [S, 1024], BF16, kind="Internal").ap()
    y_bf = nc.dram_tensor("y_bf", [S, 1024], F32, kind="Internal").ap()

    import contextlib
    es = contextlib.ExitStack()
    with es:
        p = Prog(nc, es, S=S)
        c = p.ctx
        for (n, r, cw) in _CONV:
            c.dram_bufs[n + "_bf"] = Buf(n + "_bf")
        for n in ("wgT", "wuT", "wdT", "xn_bf", "y_bf"):
            c.dram_bufs[n] = Buf(n)
        p.load_res(x_ap[0:S, :])
        for (n, r, cw) in _CONV:
            p.to_bf16_dram(ins[n], wbf[n], r, cw)
        emit_fp4_weight_prep(p, fp4src, fp4dst)

        W_l0a = {"wq": wbf["L0_attn_a_Wq"], "wk": wbf["L0_attn_a_Wk"],
                 "wv": wbf["L0_attn_a_Wv"], "wo": wbf["L0_attn_a_Wo"],
                 "null_k": sm_in["null_k"], "null_v": sm_in["null_v"],
                 "logit_scale": sm_in["logit_scale"]}
        W_l0f = {"enc_W": wbf["L0_ffn_enc_W"], "enc_b": sm_in["enc_b"],
                 "s2c_W": wbf["L0_ffn_s2c_W"], "s2c_b": sm_in["s2c_b"],
                 "rW1": wbf["L0_ffn_rW1"], "rb1": sm_in["rb1"],
                 "rW2": wbf["L0_ffn_rW2"], "rb2": sm_in["rb2"],
                 "We1": wbf["L0_ffn_We1"].rearrange("(e m) h -> e m h", e=8),
                 "be1": sm_in["be1"],
                 "We2": wbf["L0_ffn_We2"].rearrange("(e h) m -> e h m", e=8),
                 "be2": sm_in["be2"],
                 "c2s_W": wbf["L0_ffn_c2s_W"], "c2s_b": sm_in["c2s_b"],
                 "dec_W": wbf["L0_ffn_dec_W"], "dec_b": sm_in["dec_b"],
                 "ln_g": sm_in["ln_g"], "ln_b": sm_in["ln_b"]}
        W_l0b = {"w_q": wbf["L0_attn_b_w_q"], "w_kv": wbf["L0_attn_b_w_kv"],
                 "w_out": wbf["L0_attn_b_w_out"], "b_out": sm_in["b_out_l0b"],
                 "scale": sm_in["scale_l0b"]}
        W_l1a = {"w_attn": wbf["L1_attn_a_W_attn"], "b_attn": sm_in["b_attn"],
                 "w_proj": wbf["L1_attn_a_W_proj"], "b_proj": sm_in["b_proj"]}
        W_l1f = {"wgT": fp4dst["w_gate"], "wuT": fp4dst["w_up"], "wdT": fp4dst["w_down"]}
        W_l1b = {"ln1_w": sm_in["ln1_w"], "ln1_b": sm_in["ln1_b"],
                 "wq": wbf["L1_attn_b_wq"], "wk": wbf["L1_attn_b_wk"],
                 "wv": wbf["L1_attn_b_wv"], "wg": wbf["L1_attn_b_wg"],
                 "wo": wbf["L1_attn_b_wo"],
                 "ln2_w": sm_in["ln2_w"], "ln2_b": sm_in["ln2_b"],
                 "w_in": wbf["L1_attn_b_w_in"], "b_in": sm_in["b_in"],
                 "w_out": wbf["L1_attn_b_w_out"], "b_out": sm_in["b_out_l1b"]}
        scr = {"xn_bf": xn_bf, "y_bf": y_bf}

        for e in range(2):
            if e > 0:
                p.load_res(x_ap[e * S:(e + 1) * S, :])
            p.WQ = "sp" if e == 0 else "pool"
            emit_l0_attn_a(p, W_l0a)
            p.WQ = "pool"
            emit_l0_ffn(p, W_l0f)
            emit_l0_attn_b(p, W_l0b)
            emit_l0_ffn(p, W_l0f)
            emit_l1_attn_a(p, W_l1a)
            emit_l1_ffn(p, W_l1f)
            emit_l1_attn_b(p, W_l1b, scr)
            emit_l1_ffn(p, W_l1f)
            p.store_res(o_ap[e * S:(e + 1) * S, :])
        p.finish()
    return nc


def kernel(**inputs):
    n = NCORES
    x = np.ascontiguousarray(inputs["L0_attn_a_x"])

    def col(v, n_):
        return np.ascontiguousarray(v.reshape(1, n_).astype(np.float32))

    base = {}
    for (nm, r, cw) in _CONV:
        base[nm] = np.ascontiguousarray(inputs[nm].reshape(r, cw))
    base["w_gate"] = inputs["L1_ffn_w_gate"]
    base["w_up"] = inputs["L1_ffn_w_up"]
    base["w_down"] = inputs["L1_ffn_w_down"]
    base["null_k"] = np.ascontiguousarray(inputs["L0_attn_a_null_k"].reshape(16, 64))
    base["null_v"] = np.ascontiguousarray(inputs["L0_attn_a_null_v"].reshape(16, 64))
    base["logit_scale"] = col(inputs["L0_attn_a_logit_scale"], 16)
    base["b_out_l0b"] = col(inputs["L0_attn_b_b_out"], 1024)
    base["scale_l0b"] = col(inputs["L0_attn_b_scale"], 16)
    base["enc_b"] = col(inputs["L0_ffn_enc_b"], 2048)
    base["s2c_b"] = col(inputs["L0_ffn_s2c_b"], 64)
    base["rb1"] = col(inputs["L0_ffn_rb1"], 64)
    base["rb2"] = col(inputs["L0_ffn_rb2"], 8)
    base["be1"] = np.ascontiguousarray(inputs["L0_ffn_be1"])
    base["be2"] = np.ascontiguousarray(inputs["L0_ffn_be2"])
    base["c2s_b"] = col(inputs["L0_ffn_c2s_b"], 2048)
    base["dec_b"] = col(inputs["L0_ffn_dec_b"], 1024)
    base["ln_g"] = col(inputs["L0_ffn_ln_g"], 1024)
    base["ln_b"] = col(inputs["L0_ffn_ln_b"], 1024)
    base["b_attn"] = col(inputs["L1_attn_a_b_attn"], 3072)
    base["b_proj"] = col(inputs["L1_attn_a_b_proj"], 1024)
    base["ln1_w"] = col(inputs["L1_attn_b_ln1_w"], 1024)
    base["ln1_b"] = col(inputs["L1_attn_b_ln1_b"], 1024)
    base["ln2_w"] = col(inputs["L1_attn_b_ln2_w"], 1024)
    base["ln2_b"] = col(inputs["L1_attn_b_ln2_b"], 1024)
    base["b_in"] = col(inputs["L1_attn_b_b_in"], 5460)
    base["b_out_l1b"] = col(inputs["L1_attn_b_b_out"], 1024)

    nc = build_fused()
    in_maps = []
    for ci in range(n):
        m = dict(base)
        m["x"] = np.ascontiguousarray(x[2 * ci:2 * ci + 2].reshape(2 * S_FULL, 1024))
        in_maps.append(m)
    res = run_bass_kernel_spmd(nc, in_maps, core_ids=list(range(n)))
    out = np.empty((16, 2048, 1024), np.float32)
    for ci in range(n):
        out[2 * ci:2 * ci + 2] = res.results[ci]["o"].reshape(2, 2048, 1024)
    return out


# version 30 (final)
# speedup vs baseline: 1.0505x; 1.0048x over previous
import math
import numpy as np

import concourse.bass as bass
import concourse.mybir as mybir
from concourse.bass_utils import run_bass_kernel_spmd

F32 = mybir.dt.float32
BF16 = mybir.dt.bfloat16
AF = mybir.ActivationFunctionType
ALU = mybir.AluOpType
AX = mybir.AxisListType

D = 1024
H = 16
HD = 64
NCORES = 8
EPS = 1e-5

ENGS = ("pe", "act", "dve", "pool", "sp")


class Buf:
    __slots__ = ("name", "last_write", "readers")

    def __init__(self, name):
        self.name = name
        self.last_write = None
        self.readers = []

    def __repr__(self):
        return f"Buf({self.name})"


class Arena:

    def __init__(self, t, ncols, slot_cols, name):
        self.t = t
        self.ncols = ncols
        self.slot_cols = slot_cols
        self.nslots = ncols // slot_cols
        self.bufs = [Buf(f"{name}[{i}]") for i in range(self.nslots)]
        self.name = name

    def bufs_for(self, lo, hi):
        s0 = lo // self.slot_cols
        s1 = (hi - 1) // self.slot_cols
        return self.bufs[s0:s1 + 1]


class RingAlloc:

    def __init__(self, arena):
        self.a = arena
        self.pos = 0
        self.top = arena.nslots

    def alloc(self, cols):
        nsl = (cols + self.a.slot_cols - 1) // self.a.slot_cols
        assert nsl <= self.top, f"ring alloc {cols} too big for {self.a.name}"
        if self.pos + nsl > self.top:
            self.pos = 0
        s0 = self.pos
        self.pos += nsl
        lo = s0 * self.a.slot_cols
        return lo, self.a.bufs[s0:s0 + nsl]

    def persist(self, cols):
        nsl = (cols + self.a.slot_cols - 1) // self.a.slot_cols
        self.top -= nsl
        assert self.top >= 0, "persist overflow"
        lo = self.top * self.a.slot_cols
        return lo, self.a.bufs[self.top:self.top + nsl]

    def scope(self):
        return self.top

    def unscope(self, saved):
        self.top = saved


class Ctx:
    def __init__(self, nc, name="mk"):
        self.nc = nc
        self.eng = {
            "pe": nc.tensor, "act": nc.scalar, "dve": nc.vector,
            "pool": nc.gpsimd, "sp": nc.sync,
        }
        self.esem = {}
        self.dram_bufs = {}
        self.ecount = {e: 0 for e in ENGS}
        self.waited = {}
        self.queues = {}
        self._closers = []
        self.uid = 0

    def init_sems(self, es):
        for e in ENGS:
            self.esem[e] = es.enter_context(self.nc.semaphore(f"S_{e}"))
        for q, n in (("sp", 12), ("pool", 8)):
            sems = [es.enter_context(self.nc.semaphore(f"Q_{q}{i}")) for i in range(n)]
            self.queues[q] = {"sems": sems, "exp": [0] * n, "i": 0}

    def nid(self):
        self.uid += 1
        return self.uid

    def _resolve(self, ek, tag):
        if tag is None:
            return None
        kind = tag[0]
        if kind == "e":
            _, pe, val = tag
            if pe == "pe" and ek == "pe":
                return None
            if pe == ek:
                assert val <= self.ecount[ek], f"wait on future own tag {tag}"
            sem = self.esem[pe]
        else:
            _, sem, val = tag
        key = (ek, id(sem))
        cur = self.waited.get(key, 0)
        if val <= cur:
            return None
        self.waited[key] = val
        return (sem, val)

    def _wait(self, ek, tag):
        r = self._resolve(ek, tag)
        if r is not None:
            self.eng[ek].wait_ge(r[0], r[1])

    def _collect(self, reads, writes):
        tags = []
        for b in reads:
            if b.last_write is not None:
                tags.append(b.last_write)
        for b in writes:
            if b.last_write is not None:
                tags.append(b.last_write)
            tags.extend(b.readers)
        best = {}
        for t in tags:
            if t is None:
                continue
            key = id(t[1]) if t[0] == "d" else t[1]
            if key not in best or best[key][2] < t[2]:
                best[key] = t
        return best.values()

    def _update(self, reads, writes, tag):
        for b in reads:
            b.readers.append(tag)
        for b in writes:
            b.last_write = tag
            b.readers = []

    def emit(self, ek, fn, reads=(), writes=(), inc=True):
        ws = [r for r in (self._resolve(ek, t) for t in self._collect(reads, writes))
              if r is not None]
        for (sem, val) in ws[1:]:
            self.eng[ek].wait_ge(sem, val)
        inst = fn(self.eng[ek])
        if ws:
            inst._wait_ge(ws[0][0], ws[0][1])
        if inc:
            self.ecount[ek] += 1
            inst.then_inc(self.esem[ek], 1)
        tag = ("e", ek, self.ecount[ek])
        self._update(reads, writes, tag)
        return tag

    def dma(self, q, dst_ap, src_ap, reads=(), writes=(), transpose=False,
            issue_ek=None, slow=False):
        qd = self.queues[q]
        i = qd["i"]
        qd["i"] = (i + 1) % len(qd["sems"])
        sem = qd["sems"][i]
        ek = issue_ek or ("sp" if q == "sp" else "pool")
        eng = self.eng[ek]
        ws = []
        if qd["exp"][i] > 0:
            key = (ek, id(sem))
            if self.waited.get(key, 0) < qd["exp"][i]:
                self.waited[key] = qd["exp"][i]
                ws.append((sem, qd["exp"][i]))
        reads = list(reads)
        writes = list(writes)
        try:
            nm = src_ap.tensor.name
            if nm in self.dram_bufs:
                reads.append(self.dram_bufs[nm])
            nm = dst_ap.tensor.name
            if nm in self.dram_bufs:
                writes.append(self.dram_bufs[nm])
        except AttributeError:
            pass
        ws += [r for r in (self._resolve(ek, t) for t in self._collect(reads, writes))
               if r is not None]
        for (s_, v_) in ws[1:]:
            eng.wait_ge(s_, v_)
        if transpose:
            inst = eng.dma_start_transpose(dst_ap, src_ap)
        elif slow:
            inst = eng.dma_start(dst_ap, src_ap, allow_slow_non_contiguous=True)
        else:
            inst = eng.dma_start(dst_ap, src_ap)
        if ws:
            inst._wait_ge(ws[0][0], ws[0][1])
        qd["exp"][i] += 16
        inst.then_inc(sem, 16)
        tag = ("d", sem, qd["exp"][i])
        self._update(reads, writes, tag)
        return tag

    def mm(self, psum_ap, psum_buf, steps, extra_reads=()):
        n = len(steps)
        for j, (lap, rap, rbufs) in enumerate(steps):
            first, last = j == 0, j == n - 1
            reads = list(rbufs) + (list(extra_reads) if last else [])
            wr = [psum_buf] if (first or last) else []
            if last:
                def fn(eng, lap=lap, rap=rap, first=first):
                    return eng.matmul(psum_ap, lap, rap, start=first, stop=True)
                self.emit("pe", fn, reads=reads,
                          writes=[psum_buf] if first else [psum_buf], inc=True)
            else:
                wb = [psum_buf] if first else []
                ws = [r for r in (self._resolve("pe", t)
                                  for t in self._collect(reads, wb))
                      if r is not None]
                for (s_, v_) in ws[1:]:
                    self.eng["pe"].wait_ge(s_, v_)
                inst = self.eng["pe"].matmul(psum_ap, lap, rap, start=first, stop=False)
                if ws:
                    inst._wait_ge(ws[0][0], ws[0][1])
                for b in reads:
                    b.readers.append(("e", "pe", self.ecount["pe"] + 1))

    def act(self, out_ap, in_ap, func, reads, writes, bias=0.0, scale=1.0,
            accum_ap=None):
        reads = list(reads)
        if isinstance(bias, tuple):
            bias, bb_ = bias
            reads += list(bb_)
        if isinstance(scale, tuple):
            scale, sb_ = scale
            reads += list(sb_)

        def fn(eng):
            return eng.activation(out_ap, in_ap, func, bias=bias, scale=scale,
                                  accum_out=accum_ap)
        return self.emit("act", fn, reads=reads, writes=writes)

    def tt(self, ek, out_ap, a_ap, b_ap, op, reads, writes):
        def fn(eng):
            return eng.tensor_tensor(out_ap, a_ap, b_ap, op)
        return self.emit(ek, fn, reads=reads, writes=writes)

    def ts(self, ek, out_ap, in_ap, s1, s2, op0, op1, reads, writes):
        def fn(eng):
            if s2 is None:
                return eng.tensor_single_scalar(out_ap, in_ap, s1, op0)
            return eng.tensor_scalar(out_ap, in_ap, s1, s2, op0, op1)
        return self.emit(ek, fn, reads=reads, writes=writes)

    def stt(self, ek, out_ap, in0_ap, scalar, in1_ap, op0, op1, reads, writes):
        def fn(eng):
            return eng.scalar_tensor_tensor(out_ap, in0_ap, scalar, in1_ap, op0, op1)
        return self.emit(ek, fn, reads=reads, writes=writes)

    def copy(self, ek, out_ap, in_ap, reads, writes):
        def fn(eng):
            if ek == "act":
                return eng.copy(out_ap, in_ap)
            return eng.tensor_copy(out_ap, in_ap)
        return self.emit(ek, fn, reads=reads, writes=writes)

    def red(self, ek, out_ap, in_ap, op, reads, writes, absval=False, negate=False):
        def fn(eng):
            return eng.tensor_reduce(out_ap, in_ap, AX.X, op,
                                     apply_absolute_value=absval, negate=negate)
        return self.emit(ek, fn, reads=reads, writes=writes)

    def memset(self, ek, ap, val, writes):
        def fn(eng):
            return eng.memset(ap, val)
        return self.emit(ek, fn, reads=(), writes=writes)

    def transpose(self, out_psum_ap, in_ap, ident_ap, reads, writes, ident_buf):
        def fn(eng):
            return eng.transpose(out_psum_ap, in_ap, ident_ap)
        return self.emit("pe", fn, reads=list(reads) + [ident_buf], writes=writes)

    def barrier(self):
        for e in ENGS:
            for f in ENGS:
                if e != f and self.ecount[f] > 0:
                    self._wait(e, ("e", f, self.ecount[f]))



class Prog:

    def __init__(self, nc, es, S=2048, n_elems=2):
        self.nc = nc
        self.es = es
        self.S = S
        self.NT = S // 128
        self.n_elems = n_elems
        self.WQ = "pool"
        self.ctx = Ctx(nc)
        self.ctx.init_sems(es)
        c = self.ctx

        def sbuf(name, shape, dt):
            return es.enter_context(nc.sbuf_tensor(name, shape, dt))

        def psum(name, shape, dt):
            return es.enter_context(nc.psum_tensor(name, shape, dt))

        self.RES = Arena(sbuf("RES", [128, self.NT * 1024], F32), self.NT * 1024, 1024, "RES")
        self.AA = Arena(sbuf("AA", [128, 8 * S], BF16), 8 * S, 512, "AA")
        abcols = max(8 * S, 16384)
        self.AB = Arena(sbuf("AB", [128, abcols], BF16), abcols, 512, "AB")
        self.WR = Arena(sbuf("WR", [128, 12288], BF16), 12288, 512, "WR")
        self.SM = Arena(sbuf("SM", [128, 7680], F32), 7680, 128, "SM")
        self.BB = Arena(sbuf("BB", [128, 12288], BF16), 12288, 256, "BB")
        self.wr_ring = RingAlloc(self.WR)
        self.sm_ring = RingAlloc(self.SM)
        self.bb_ring = RingAlloc(self.BB)

        self.PB = [psum(f"pb{i}", [128, 512], F32) for i in range(4)]
        self.PBbuf = [Buf(f"pb{i}") for i in range(4)]
        self.PA = [psum(f"pa{i}", [128, 1024], F32) for i in range(2)]
        self.PAbuf = [Buf(f"pa{i}") for i in range(2)]
        self._pb_i = 0

        self.ID_BF = sbuf("ID_BF", [128, 128], BF16)
        self.ID_F32 = sbuf("ID_F32", [128, 128], F32)
        self.ID_buf = Buf("ID_BF")
        self.IDF_buf = Buf("ID_F32")
        self.CONST = sbuf("CONST", [128, 8], F32)
        self.CONST_buf = Buf("CONST")
        self.ONE_ROW = sbuf("ONE_ROW", [1, 128], F32)
        self.ONE_ROW_buf = Buf("ONE_ROW")
        self.ONE_ROW_BF = sbuf("ONE_ROW_BF", [1, 128], BF16)
        self.ONE_ROW_BF_buf = Buf("ONE_ROW_BF")
        self.MASKHALF = sbuf("MASKHALF", [128, 2], BF16)
        self.MASKHALF_buf = Buf("MASKHALF")
        self.ONE_COL_BF = sbuf("ONE_COL_BF", [128, 1], BF16)
        self.ONE_COL_BF_buf = Buf("ONE_COL_BF")

        self._const_cache = {}
        self._const_slot = self.sm_ring.persist(128)
        self._const_used = 0
        self._emit_consts()

    def pb(self):
        i = self._pb_i
        self._pb_i = (i + 1) % len(self.PB)
        return self.PB[i], self.PBbuf[i]

    def _emit_consts(self):
        c = self.ctx
        for t, b in ((self.ID_BF, self.ID_buf), (self.ID_F32, self.IDF_buf)):
            c.memset("pool", t[:, :], 1.0, writes=[b])

            def fn(eng, t=t):
                return eng.affine_select(t[:, :], t[:, :], [[-1, 128]],
                                         ALU.is_equal, 0.0, channel_multiplier=1)
            c.emit("pool", fn, reads=[b], writes=[b])
        def fn_iota(eng):
            return eng.iota(self.CONST[:, 0:1], [[0, 1]], channel_multiplier=1,
                            allow_small_or_imprecise_dtypes=True)
        c.emit("pool", fn_iota, writes=[self.CONST_buf])
        c.memset("pool", self.CONST[:, 1:2], 1.0, writes=[self.CONST_buf])
        c.memset("pool", self.ONE_ROW[:, :], 1.0, writes=[self.ONE_ROW_buf])
        c.memset("pool", self.ONE_ROW_BF[:, :], 1.0, writes=[self.ONE_ROW_BF_buf])
        c.memset("pool", self.MASKHALF[:, :], 1.0, writes=[self.MASKHALF_buf])
        c.memset("pool", self.ONE_COL_BF[:, :], 1.0, writes=[self.ONE_COL_BF_buf])

        def fn_mh0(eng):
            return eng.affine_select(self.MASKHALF[:, 0:1], self.MASKHALF[:, 0:1],
                                     [[0, 1]], ALU.is_ge, 0.0, base=63,
                                     channel_multiplier=-1)
        c.emit("pool", fn_mh0, reads=[self.MASKHALF_buf], writes=[self.MASKHALF_buf])

        def fn_mh1(eng):
            return eng.affine_select(self.MASKHALF[:, 1:2], self.MASKHALF[:, 1:2],
                                     [[0, 1]], ALU.is_ge, 0.0, base=-64,
                                     channel_multiplier=1)
        c.emit("pool", fn_mh1, reads=[self.MASKHALF_buf], writes=[self.MASKHALF_buf])

    def rsqrt(self, out_ap, in_ap, eps, reads, writes, power=-0.5):
        c = self.ctx
        e_ap, e_b = self.const_col(float(eps))
        pw_ap, pw_b = self.const_col(float(power))
        pr = in_ap.shape[0]
        c.act(out_ap, in_ap, AF.Ln, bias=(e_ap[0:pr, :], e_b), reads=reads, writes=writes)
        c.act(out_ap, out_ap, AF.Exp, scale=(pw_ap[0:pr, :], pw_b), reads=writes, writes=writes)
        return writes

    def const_col(self, value):
        if value in self._const_cache:
            return self._const_cache[value]
        assert self._const_used < 128
        lo, bufs = self._const_slot
        col = lo + self._const_used
        self._const_used += 1
        ap = self.SM.t[:, col:col + 1]
        self.ctx.memset("pool", ap, float(value), writes=bufs)
        self._const_cache[value] = (ap, bufs)
        return ap, bufs

    def to_bf16_dram(self, src_fp32_ap, dst_bf16_ap, rows, cols, tag=""):
        c = self.ctx
        eidx = 0
        for r0 in range(0, rows, 128):
            pr = min(128, rows - r0)
            for c0 in range(0, cols, 2048):
                w = min(2048, cols - c0)
                slo, sbufs = self.sm_ring.alloc(w)
                dlo, dbufs = self.wr_ring.alloc(w)
                st = self.SM.t
                dt_ = self.WR.t
                c.dma("pool", st[0:pr, slo:slo + w], src_fp32_ap[r0:r0 + pr, c0:c0 + w],
                      writes=sbufs)
                ek = ("dve", "act", "pool")[eidx % 3]
                eidx += 1
                c.copy(ek, dt_[0:pr, dlo:dlo + w], st[0:pr, slo:slo + w],
                       reads=sbufs, writes=dbufs)
                c.dma("pool", dst_bf16_ap[r0:r0 + pr, c0:c0 + w], dt_[0:pr, dlo:dlo + w],
                      reads=dbufs)

    def load_bias_col(self, dram_1d_ap, n, name):
        c = self.ctx
        m = (n + 127) // 128
        slo, sbufs = self.sm_ring.alloc(128)
        st = self.SM.t
        if n % 128 != 0:
            c.memset("dve", st[0:m, slo:slo + 128], 0.0, writes=sbufs)
        full = (n // 128) * 128
        if full > 0:
            c.dma("pool", st[0:n // 128, slo:slo + 128].rearrange("p f -> p f"),
                  dram_1d_ap[0:full].rearrange("(p f) -> p f", f=128), reads=(), writes=sbufs)
        if n % 128 != 0:
            c.dma("pool", st[n // 128:n // 128 + 1, slo:slo + n % 128],
                  dram_1d_ap[full:n].rearrange("(a f) -> a f", a=1), reads=(), writes=sbufs)
        pbk, pbuf = self.pb()
        c.transpose(pbk[:, 0:m].rearrange("p f -> p f"), st[0:m, slo:slo + 128],
                    self.ID_F32[0:m, 0:m], reads=sbufs, writes=[pbuf], ident_buf=self.IDF_buf)
        dlo, dbufs = self.sm_ring.persist(m)
        c.copy("dve", st[:, dlo:dlo + m], pbk[:, 0:m], reads=[pbuf], writes=dbufs)
        return st, dlo, m, dbufs

    def wload(self, dram_bf16_ap, rows, cols, q=None, persist=False):
        c = self.ctx
        if q is None:
            q = self.WQ
        nch = (rows + 127) // 128
        cpad = ((cols + 511) // 512) * 512
        if persist:
            lo0, _ = self.wr_ring.persist(cpad * nch)
        else:
            lo0, _ = self.wr_ring.alloc(cpad * nch)
        out = []
        for i, r0 in enumerate(range(0, rows, 128)):
            pr = min(128, rows - r0)
            lo = lo0 + i * cpad
            bufs = self.WR.bufs_for(lo, lo + cols)
            ap = self.WR.t[0:pr, lo:lo + cols]
            c.dma(q, ap, dram_bf16_ap[r0:r0 + pr, 0:cols], writes=bufs)
            out.append((self.WR.t, lo, pr, cols, bufs))
        return out

    def ln_xn(self, elem_tag="", spill_fp32=None, spill_bufs=None, want_xnT=True,
              want_xn_tok=True):
        c = self.ctx
        S, NT = self.S, self.NT
        st = self.SM.t

        def stage1(t):
            slo, sbufs = self.sm_ring.alloc(16)
            r = self.RES.t[:, t * 1024:(t + 1) * 1024]
            rb = self.RES.bufs[t]
            def f1(eng, r=r):
                return eng.bn_stats(st[:, slo:slo + 6], r[:, 0:512])
            def f2(eng, r=r):
                return eng.bn_stats(st[:, slo + 6:slo + 12], r[:, 512:1024])
            c.emit("dve", f1, reads=[rb], writes=sbufs)
            c.emit("dve", f2, reads=[rb], writes=sbufs)
            def f3(eng):
                return eng.bn_aggr(st[:, slo + 12:slo + 14], st[:, slo:slo + 12])
            c.emit("dve", f3, reads=sbufs, writes=sbufs)
            self.rsqrt(st[:, slo + 14:slo + 15], st[:, slo + 13:slo + 14], EPS,
                       reads=sbufs, writes=sbufs)
            return slo, sbufs

        st1 = {0: stage1(0)}
        for t in range(NT):
            if t + 1 < NT:
                st1[t + 1] = stage1(t + 1)
            slo, sbufs = st1.pop(t)
            r = self.RES.t[:, t * 1024:(t + 1) * 1024]
            rb = self.RES.bufs[t]
            c.ts("dve", st[:, slo + 15:slo + 16], st[:, slo + 12:slo + 13],
                 st[:, slo + 14:slo + 15], -1.0, ALU.mult, ALU.mult,
                 reads=sbufs, writes=sbufs)
            if want_xn_tok or want_xnT:
                xn = self.AB.t[:, t * 1024:(t + 1) * 1024]
                xnb = self.AB.bufs_for(t * 1024, (t + 1) * 1024)
                c.ts("dve", xn, r, st[:, slo + 14:slo + 15], st[:, slo + 15:slo + 16],
                     ALU.mult, ALU.add, reads=[rb] + sbufs, writes=xnb)
                if spill_fp32 is not None:
                    sap, sb = spill_fp32
                    c.dma("pool", sap[t * 128:(t + 1) * 128, :], xn, reads=xnb,
                          writes=sb)
                if want_xnT:
                    dst = self.AA.t[:, t * 128:t * 128 + 128]
                    dst = self.AA.t[:, :].rearrange("p (c s) -> p c s", c=8)[:, :, t * 128:(t + 1) * 128]
                    dstb = []
                    for ch in range(8):
                        dstb += self.AA.bufs_for(ch * self.S + t * 128, ch * self.S + (t + 1) * 128)
                    c.dma("sp", dst, xn, reads=xnb, writes=dstb, transpose=True)

    def store_res(self, out_ap):
        c = self.ctx
        for t in range(self.NT):
            c.dma("sp", out_ap[t * 128:(t + 1) * 128, :],
                  self.RES.t[:, t * 1024:(t + 1) * 1024], reads=[self.RES.bufs[t]])

    def load_res(self, in_ap):
        c = self.ctx
        for t in range(self.NT):
            c.dma("sp", self.RES.t[:, t * 1024:(t + 1) * 1024],
                  in_ap[t * 128:(t + 1) * 128, :], writes=[self.RES.bufs[t]])

    def finish(self):
        c = self.ctx
        for q, qd in c.queues.items():
            ek = "sp" if q == "sp" else "pool"
            for i, sem in enumerate(qd["sems"]):
                if qd["exp"][i] > 0:
                    key = (ek, id(sem))
                    if c.waited.get(key, 0) < qd["exp"][i]:
                        c.waited[key] = qd["exp"][i]
                        c.eng[ek].wait_ge(sem, qd["exp"][i])

    def build_diag_masks(self, col):
        c = self.ctx
        self._mk_col = col
        ap = self.BB.t[:, col:col + 896]
        bufs = self.BB.bufs_for(col, col + 896)
        c.memset("pool", ap, 1.0, writes=bufs)

        def fn(eng, ap=ap):
            return eng.affine_select(ap, ap, [[1, 896]], ALU.is_ge, 0.0,
                                     base=-384, channel_multiplier=-1)
        c.emit("pool", fn, reads=bufs, writes=bufs)

    def mk(self, mi, width):
        col = self._mk_col + 384 - mi * 128
        return self.BB.t[:, col:col + width], self.BB.bufs_for(col, col + width)

    def head_tables(self, scale_row, bias_row, row_reads):
        c = self.ctx
        st = self.SM.t
        out = []
        for (rap, rbufs) in (scale_row, bias_row):
            n = rap.shape[-1]
            pbk, pbuf = self.PB[3], self.PBbuf[3]
            def fn(eng, rap=rap, n=n):
                return eng.matmul(pbk[:, 0:n], self.ONE_ROW[0:1, 0:128], rap,
                                  start=True, stop=True)
            c.emit("pe", fn, reads=[self.ONE_ROW_buf] + list(rbufs) + list(row_reads),
                   writes=[pbuf])
            lo, bufs = self.sm_ring.persist(n)
            c.copy("dve", st[:, lo:lo + n], pbk[:, 0:n], reads=[pbuf], writes=bufs)
            out.append((lo, bufs))
        return out


def emit_l0_attn_a(p, W, dbg=None):
    c = p.ctx
    S, NT = p.S, p.NT
    assert S % 512 == 0
    NQ = S // 512
    NKC = S // 128
    st = p.SM.t
    bb = p.BB.t
    _scope = p.sm_ring.scope()

    p.ln_xn()
    p.build_diag_masks(10016)

    lo_nk, b_nk = p.sm_ring.alloc(128)
    c.dma("pool", st[0:16, lo_nk:lo_nk + 64], W["null_k"], writes=b_nk)
    c.copy("dve", st[0:16, lo_nk + 64:lo_nk + 128], st[0:16, lo_nk:lo_nk + 64],
           reads=b_nk, writes=b_nk)
    pbk, pbuf = p.PB[2], p.PBbuf[2]
    c.transpose(pbk[:, 0:16], st[0:16, lo_nk:lo_nk + 128],
                p.ID_F32[0:16, 0:16], reads=b_nk, writes=[pbuf], ident_buf=p.IDF_buf)
    NKT = 8960
    b_nkT = p.BB.bufs_for(NKT, NKT + 16)
    c.copy("dve", bb[:, NKT:NKT + 16], pbk[:, 0:16], reads=[pbuf], writes=b_nkT)
    lo_n2, b_n2 = p.sm_ring.alloc(128)
    c.tt("dve", st[0:16, lo_n2:lo_n2 + 64], st[0:16, lo_nk:lo_nk + 64],
         st[0:16, lo_nk:lo_nk + 64], ALU.mult, reads=b_nk, writes=b_n2)
    c.red("dve", st[0:16, lo_n2 + 64:lo_n2 + 65], st[0:16, lo_n2:lo_n2 + 64], ALU.add,
          reads=b_n2, writes=b_n2)
    c.act(st[0:16, lo_n2 + 65:lo_n2 + 66], st[0:16, lo_n2 + 64:lo_n2 + 65], AF.Sqrt,
          reads=b_n2, writes=b_n2)
    c.transpose(pbk[0:1, 16:32], st[0:16, lo_n2 + 65:lo_n2 + 66],
                p.ID_F32[0:16, 0:16], reads=b_n2, writes=[pbuf], ident_buf=p.IDF_buf)
    lo_nn, b_nn = p.sm_ring.persist(16)
    c.copy("dve", st[0:1, lo_nn:lo_nn + 16], pbk[0:1, 16:32], reads=[pbuf], writes=b_nn)

    NV = 8976
    b_nv = p.BB.bufs_for(NV, NV + 1040)
    lo_v, b_v = p.sm_ring.alloc(1024)
    c.dma("pool", st[0:1, lo_v:lo_v + 1024], W["null_v"].rearrange("(a h) d -> a (h d)", a=1),
          writes=b_v)
    c.copy("dve", bb[0:1, NV:NV + 1040].rearrange("p (h d) -> p h d", h=16)[:, :, 0:64],
           st[0:1, lo_v:lo_v + 1024].rearrange("p (h d) -> p h d", h=16), reads=b_v,
           writes=b_nv)
    c.memset("dve", bb[0:1, NV:NV + 1040].rearrange("p (h d) -> p h d", h=16)[:, :, 64:65],
             1.0, writes=b_nv)

    lo_ls, b_ls = p.sm_ring.alloc(16)
    c.dma("pool", st[0:1, lo_ls:lo_ls + 16], W["logit_scale"], writes=b_ls)
    lo_c, b_c = p.sm_ring.persist(16)
    bconst = p.const_col(-math.log(8.0))
    c.act(st[0:1, lo_c:lo_c + 16], st[0:1, lo_ls:lo_ls + 16], AF.Exp,
          bias=(bconst[0][0:1, :], bconst[1]), reads=b_ls, writes=b_c)

    lo_qm, b_qm = p.sm_ring.persist(16)
    lo_km, b_km = p.sm_ring.persist(16)
    c.memset("dve", st[0:1, lo_qm:lo_qm + 16], 0.0, writes=b_qm)
    c.memset("dve", st[0:1, lo_km:lo_km + 16], 0.0, writes=b_km)

    QT, KT, VA, EXL, SQ = 0, 2048, 4096, 6400, 8448

    xnT_bufs = p.AA.bufs_for(0, 8 * S)

    def qkv_pair(pair):
        wq = p.wload(W["wq"][:, pair * 128:(pair + 1) * 128], 1024, 128)
        wk = p.wload(W["wk"][:, pair * 128:(pair + 1) * 128], 1024, 128)
        wv = p.wload(W["wv"][:, pair * 128:(pair + 1) * 128], 1024, 128)
        b_qT = p.BB.bufs_for(QT, QT + S)
        b_kT = p.BB.bufs_for(KT, KT + S)
        b_va = p.BB.bufs_for(VA, VA + 16 * 130)
        SQ2 = 11424
        pend_n = []
        tci = 0
        for (wtiles, dstcol, dstbufs, which) in ((wq, QT, b_qT, "q"), (wk, KT, b_kT, "k")):
            for tc in range(NQ):
                pbk, pbuf = (p.PB[0], p.PBbuf[0]) if tc % 2 == 0 else (p.PB[1], p.PBbuf[1])
                steps = []
                for kc in range(8):
                    (wt, wlo, wpr, wcols, wbufs) = wtiles[kc]
                    lap = wt[0:128, wlo:wlo + 128]
                    rap = p.AA.t[:, kc * S + tc * 512: kc * S + tc * 512 + 512]
                    steps.append((lap, rap, list(wbufs) + p.AA.bufs_for(kc * S + tc * 512, kc * S + tc * 512 + 512)))
                c.mm(pbk[:, :], pbuf, steps)
                c.copy("act", bb[:, dstcol + tc * 512: dstcol + tc * 512 + 512],
                       pbk[:, :], reads=[pbuf], writes=dstbufs)
                sqlo = SQ if tci % 2 == 0 else SQ2
                tci += 1
                sqb = p.BB.bufs_for(sqlo, sqlo + 512)
                c.tt("pool", bb[:, sqlo:sqlo + 512],
                     bb[:, dstcol + tc * 512: dstcol + tc * 512 + 512],
                     bb[:, dstcol + tc * 512: dstcol + tc * 512 + 512], ALU.mult,
                     reads=dstbufs, writes=sqb)
                if pend_n:
                    pend_n.pop(0)()

                def mk_norm(sqlo=sqlo, sqb=sqb, which=which):
                    def emit_norm():
                        pn, pnb = p.PB[2], p.PBbuf[2]
                        def fn(eng):
                            return eng.matmul(pn[0:2, 0:512], p.MASKHALF[:, 0:2],
                                              bb[:, sqlo:sqlo + 512], start=True, stop=True)
                        c.emit("pe", fn, reads=sqb + [p.MASKHALF_buf], writes=[pnb])
                        lo_t, b_t = p.sm_ring.alloc(2)
                        c.red("dve", st[0:2, lo_t:lo_t + 1], pn[0:2, 0:512], ALU.max,
                              reads=[pnb], writes=b_t)
                        c.transpose(pn[0:1, 508:510], st[0:2, lo_t:lo_t + 1],
                                    p.ID_F32[0:2, 0:2], reads=b_t, writes=[pnb],
                                    ident_buf=p.IDF_buf)
                        accap = st[0:1, (lo_qm if which == "q" else lo_km) + 2 * pair:
                                   (lo_qm if which == "q" else lo_km) + 2 * pair + 2]
                        accb = b_qm if which == "q" else b_km
                        c.tt("dve", accap, accap, pn[0:1, 508:510], ALU.max,
                             reads=[pnb] + accb, writes=accb)
                    return emit_norm

                pend_n.append(mk_norm())
        while pend_n:
            pend_n.pop(0)()
        for tt_ in range(NT):
            pbk, pbuf = (p.PB[0], p.PBbuf[0]) if tt_ % 2 == 0 else (p.PB[1], p.PBbuf[1])
            steps = []
            for kc in range(8):
                (wt, wlo, wpr, wcols, wbufs) = wv[kc]
                lap = p.AA.t[:, kc * S + tt_ * 128: kc * S + tt_ * 128 + 128]
                rap = wt[0:128, wlo:wlo + 128]
                steps.append((lap, rap,
                              list(wbufs) + p.AA.bufs_for(kc * S + tt_ * 128, kc * S + tt_ * 128 + 128)))
            c.mm(pbk[:, 0:128], pbuf, steps)
            for hh in range(2):
                c.copy("act", bb[:, VA + tt_ * 130 + hh * 65: VA + tt_ * 130 + hh * 65 + 64],
                       pbk[:, hh * 64:hh * 64 + 64], reads=[pbuf], writes=b_va)
        ones_ap = bb[:, VA:VA + NT * 130].rearrange("p (t h d) -> p t h d", t=NT, h=2)[:, :, :, 64:65]
        c.memset("dve", ones_ap, 1.0, writes=b_va)
        return b_qT, b_kT, b_va

    def pair_tables(pair, b_reads):
        lo_r, b_r = p.sm_ring.persist(8)
        sl = slice(lo_r, lo_r + 2)
        c.act(st[0:1, lo_r:lo_r + 2], st[0:1, lo_qm + 2 * pair:lo_qm + 2 * pair + 2],
              AF.Sqrt, reads=b_qm + b_reads, writes=b_r)
        c.act(st[0:1, lo_r + 2:lo_r + 4], st[0:1, lo_km + 2 * pair:lo_km + 2 * pair + 2],
              AF.Sqrt, reads=b_km + b_reads, writes=b_r)
        c.tt("dve", st[0:1, lo_r + 2:lo_r + 4], st[0:1, lo_r + 2:lo_r + 4],
             st[0:1, lo_nn + 2 * pair:lo_nn + 2 * pair + 2], ALU.max,
             reads=b_r + b_nn, writes=b_r)
        c.tt("dve", st[0:1, lo_r + 4:lo_r + 6], st[0:1, lo_r:lo_r + 2],
             st[0:1, lo_r + 2:lo_r + 4], ALU.mult, reads=b_r, writes=b_r)
        c.stt("dve", st[0:1, lo_r + 6:lo_r + 8], st[0:1, lo_r + 4:lo_r + 6], -1.0,
              st[0:1, lo_c + 2 * pair:lo_c + 2 * pair + 2], ALU.mult, ALU.mult,
              reads=b_r + b_c, writes=b_r)
        tabs = p.head_tables((st[0:1, lo_c + 2 * pair:lo_c + 2 * pair + 2], b_c),
                             (st[0:1, lo_r + 6:lo_r + 8], b_r), b_r)
        return tabs

    for pair in range(8):
        _pscope = p.sm_ring.scope()
        b_qT, b_kT, b_va = qkv_pair(pair)
        (lo_sc, b_sc), (lo_bi, b_bi) = pair_tables(pair, [])
        if dbg is not None and pair == 0:
            dbg("pair0", dict(b_qT=b_qT, b_kT=b_kT, b_va=b_va, lo_sc=lo_sc,
                              b_sc=b_sc, lo_bi=lo_bi, b_bi=b_bi, QT=QT, KT=KT,
                              VA=VA, lo_qm=lo_qm, lo_km=lo_km, b_qm=b_qm, b_km=b_km))
        pending = []
        for hh in range(2):
            h = 2 * pair + hh
            qrow0 = hh * 64
            for qc in range(NQ):
                par = (qc + hh) % 2
                ob, obuf = (p.PB[2], p.PBbuf[2]) if par == 0 else (p.PB[3], p.PBbuf[3])
                alt, altbuf = (p.PB[3], p.PBbuf[3]) if par == 0 else (p.PB[2], p.PBbuf[2])
                nb, nbuf = p.PB[0], p.PBbuf[0]
                def fns(eng, nb=nb, qrow0=qrow0, qc=qc):
                    return eng.matmul(nb[0:1, :], bb[qrow0:qrow0 + 64, NKT + h:NKT + h + 1],
                                      bb[qrow0:qrow0 + 64, QT + qc * 512:QT + qc * 512 + 512],
                                      start=True, stop=True)
                c.emit("pe", fns, reads=b_nkT + b_qT, writes=[nbuf])
                exn = 10912
                exnb = p.BB.bufs_for(exn, exn + 512)
                c.act(bb[0:1, exn:exn + 512], nb[0:1, :], AF.Exp,
                      bias=st[0:1, lo_bi:lo_bi + 2][:, hh:hh + 1],
                      scale=st[0:1, lo_sc:lo_sc + 2][:, hh:hh + 1],
                      reads=[nbuf] + b_sc + b_bi, writes=exnb)
                kc_hi = min(NKC, 4 * qc + 4)
                exi = 0
                pend_pv = None
                for j0 in range(0, kc_hi, 2):
                    pk = list(range(j0, min(j0 + 2, kc_hi)))
                    sa, sabuf = (p.PA[0], p.PAbuf[0]) if (j0 // 2) % 2 == 0 else (p.PA[1], p.PAbuf[1])
                    for i2, kc in enumerate(pk):
                        def fsc(eng, kc=kc, i2=i2, sa=sa, qrow0=qrow0, qc=qc):
                            return eng.matmul(
                                sa[:, i2 * 512:i2 * 512 + 512],
                                bb[qrow0:qrow0 + 64, KT + kc * 128:KT + kc * 128 + 128],
                                bb[qrow0:qrow0 + 64, QT + qc * 512:QT + qc * 512 + 512],
                                start=True, stop=True)
                        c.emit("pe", fsc, reads=b_kT + b_qT, writes=[sabuf])
                    if j0 == 0 and pending:
                        pending.pop(0)()
                    w2 = 512 * len(pk)
                    ex = EXL + (exi % 4) * 512
                    exi += 2
                    exb = p.BB.bufs_for(ex, ex + w2)
                    c.act(bb[:, ex:ex + w2], sa[:, 0:w2], AF.Exp,
                          bias=st[:, lo_bi:lo_bi + 2][:, hh:hh + 1],
                          scale=st[:, lo_sc:lo_sc + 2][:, hh:hh + 1],
                          reads=[sabuf] + b_sc + b_bi, writes=exb)
                    for i2, kc in enumerate(pk):
                        o = kc * 128 - qc * 512
                        if o >= 0:
                            mkap, mkb = p.mk(o // 128, 512)
                            c.tt("pool", bb[:, ex + i2 * 512:ex + i2 * 512 + 512],
                                 bb[:, ex + i2 * 512:ex + i2 * 512 + 512],
                                 mkap, ALU.mult, reads=exb + mkb, writes=exb)
                    if pend_pv is not None:
                        pend_pv()

                    def mk_pv(pk=pk, ex=ex, exb=exb, ob=ob, obuf=obuf, hh=hh):
                        def emit_pv():
                            for i2, kc in enumerate(pk):
                                def fpv(eng, kc=kc, i2=i2, first=(kc == 0)):
                                    return eng.matmul(ob[0:65, :],
                                                      bb[:, VA + kc * 130 + hh * 65:VA + kc * 130 + hh * 65 + 65],
                                                      bb[:, ex + i2 * 512:ex + i2 * 512 + 512],
                                                      start=first, stop=False)
                                c.emit("pe", fpv, reads=b_va + exb, writes=[obuf])
                        return emit_pv

                    pend_pv = mk_pv()
                pend_pv()
                def fpvn(eng, exn=exn, ob=ob, h=h):
                    return eng.matmul(ob[0:65, :], bb[0:1, NV + h * 65:NV + h * 65 + 65],
                                      bb[0:1, exn:exn + 512], start=False, stop=True)
                c.emit("pe", fpvn, reads=b_nv + exnb, writes=[obuf])

                def tail(ob=ob, obuf=obuf, h=h, qc=qc):
                    lo_rr, b_rr = p.sm_ring.alloc(512)
                    p.rsqrt(st[0:1, lo_rr:lo_rr + 512], ob[64:65, :], 0.0,
                            reads=[obuf], writes=b_rr, power=-1.0)
                    och = h // 2
                    orow = (h % 2) * 64
                    ocol = och * S + qc * 512
                    ob_bufs = p.AB.bufs_for(ocol, ocol + 512)
                    dstab = p.AB.t[orow:orow + 64, ocol:ocol + 512]
                    c.copy("act", dstab, ob[0:64, :], reads=[obuf], writes=ob_bufs)
                    bcb, bcbuf = p.PB[1], p.PBbuf[1]
                    def fbc(eng):
                        return eng.matmul(bcb[0:64, :], p.ONE_ROW[0:1, 0:64],
                                          st[0:1, lo_rr:lo_rr + 512], start=True, stop=True)
                    c.emit("pe", fbc, reads=[p.ONE_ROW_buf] + b_rr, writes=[bcbuf])
                    c.tt("dve", dstab, dstab, bcb[0:64, :], ALU.mult,
                         reads=ob_bufs + [bcbuf], writes=ob_bufs)
                pending.append(tail)
        while pending:
            pending.pop(0)()

    wo = p.wload(W["wo"], 1024, 1024)
    for t in range(NT):
        for half in range(2):
            pbk, pbuf = (p.PB[0], p.PBbuf[0]) if half == 0 else (p.PB[1], p.PBbuf[1])
            steps = []
            for kc in range(8):
                (wt, wlo, wpr, wcols, wbufs) = wo[kc]
                lap = p.AB.t[:, kc * S + t * 128: kc * S + t * 128 + 128]
                rap = wt[0:128, wlo + half * 512: wlo + half * 512 + 512]
                steps.append((lap, rap,
                              list(wbufs) + p.AB.bufs_for(kc * S + t * 128, kc * S + t * 128 + 128)))
            c.mm(pbk[:, :], pbuf, steps)
            rb = p.RES.bufs[t]
            rap = p.RES.t[:, t * 1024 + half * 512: t * 1024 + half * 512 + 512]
            c.tt("dve", rap, rap, pbk[:, :], ALU.add, reads=[pbuf, rb], writes=[rb])
    p.sm_ring.unscope(_scope)


def emit_l1_attn_a(p, W):
    c = p.ctx
    S, NT = p.S, p.NT
    NKC = S // 128
    st = p.SM.t
    bb = p.BB.t
    _scope = p.sm_ring.scope()

    p.ln_xn()
    p.build_diag_masks(9984)

    t_, lo_ba, nba, b_ba = p.load_bias_col(W["b_attn"].rearrange("a n -> (a n)"), 3072, "ba")
    lo_bp, b_bp = p.sm_ring.persist(1024)
    c.dma("pool", st[0:1, lo_bp:lo_bp + 1024], W["b_proj"], writes=b_bp)
    VBROW = 8960
    b_vbrow = p.BB.bufs_for(VBROW, VBROW + 1024)
    lo_tmp, b_tmp = p.sm_ring.alloc(1024)
    c.dma("pool", st[0:1, lo_tmp:lo_tmp + 1024], W["b_attn"][0:1, 2048:3072], writes=b_tmp)
    c.copy("dve", bb[0:1, VBROW:VBROW + 1024], st[0:1, lo_tmp:lo_tmp + 1024],
           reads=b_tmp, writes=b_vbrow)

    lo_qm, b_qm = p.sm_ring.persist(16)
    lo_km, b_km = p.sm_ring.persist(16)
    c.memset("dve", st[0:1, lo_qm:lo_qm + 16], 0.0, writes=b_qm)
    c.memset("dve", st[0:1, lo_km:lo_km + 16], 0.0, writes=b_km)

    QT, KT, VA, EXL, SQ = 0, 2048, 4096, 6400, 8448
    NTI = NT + 3
    inv8_ap, inv8_b = p.const_col(0.125)

    def qkv_pair(pair):
        wq = p.wload(W["w_attn"][:, pair * 128:(pair + 1) * 128], 1024, 128)
        wk = p.wload(W["w_attn"][:, 1024 + pair * 128:1024 + (pair + 1) * 128], 1024, 128)
        wv = p.wload(W["w_attn"][:, 2048 + pair * 128:2048 + (pair + 1) * 128], 1024, 128)
        b_qT = p.BB.bufs_for(QT, QT + S)
        b_kT = p.BB.bufs_for(KT, KT + S)
        b_va = p.BB.bufs_for(VA, VA + NT * 130)
        for (wtiles, dstcol, dstbufs, which) in ((wq, QT, b_qT, "q"), (wk, KT, b_kT, "k")):
            bcol = (0 if which == "q" else 8) + pair
            for tc in range(S // 512):
                pbk, pbuf = (p.PB[0], p.PBbuf[0]) if tc % 2 == 0 else (p.PB[1], p.PBbuf[1])
                steps = []
                for kc in range(8):
                    (wt, wlo, wpr, wcols, wbufs) = wtiles[kc]
                    steps.append((wt[0:128, wlo:wlo + 128],
                                  p.AA.t[:, kc * S + tc * 512: kc * S + tc * 512 + 512],
                                  list(wbufs) + p.AA.bufs_for(kc * S + tc * 512, kc * S + tc * 512 + 512)))
                c.mm(pbk[:, :], pbuf, steps)
                def fev(eng, pbk=pbk, tc=tc, dstcol=dstcol, bcol=bcol):
                    return eng.activation(bb[:, dstcol + tc * 512: dstcol + tc * 512 + 512],
                                          pbk[:, :], AF.Identity,
                                          bias=st[:, lo_ba + bcol:lo_ba + bcol + 1])
                c.emit("act", fev, reads=[pbuf] + b_ba, writes=dstbufs)
                sqb = p.BB.bufs_for(SQ, SQ + 512)
                c.tt("pool", bb[:, SQ:SQ + 512],
                     bb[:, dstcol + tc * 512: dstcol + tc * 512 + 512],
                     bb[:, dstcol + tc * 512: dstcol + tc * 512 + 512], ALU.mult,
                     reads=dstbufs, writes=sqb)
                pn, pnb = p.PB[2], p.PBbuf[2]
                def fn(eng):
                    return eng.matmul(pn[0:2, 0:512], p.MASKHALF[:, 0:2],
                                      bb[:, SQ:SQ + 512], start=True, stop=True)
                c.emit("pe", fn, reads=sqb + [p.MASKHALF_buf], writes=[pnb])
                lo_t, b_t = p.sm_ring.alloc(2)
                c.red("dve", st[0:2, lo_t:lo_t + 1], pn[0:2, 0:512], ALU.max,
                      reads=[pnb], writes=b_t)
                c.transpose(pn[0:1, 508:510], st[0:2, lo_t:lo_t + 1],
                            p.ID_F32[0:2, 0:2], reads=b_t, writes=[pnb],
                            ident_buf=p.IDF_buf)
                accap = st[0:1, (lo_qm if which == "q" else lo_km) + 2 * pair:
                           (lo_qm if which == "q" else lo_km) + 2 * pair + 2]
                accb = b_qm if which == "q" else b_km
                c.tt("dve", accap, accap, pn[0:1, 508:510], ALU.max,
                     reads=[pnb] + accb, writes=accb)
        for tt_ in range(NT):
            pbk, pbuf = (p.PB[0], p.PBbuf[0]) if tt_ % 2 == 0 else (p.PB[1], p.PBbuf[1])
            steps = []
            for kc in range(8):
                (wt, wlo, wpr, wcols, wbufs) = wv[kc]
                steps.append((p.AA.t[:, kc * S + tt_ * 128: kc * S + tt_ * 128 + 128],
                              wt[0:128, wlo:wlo + 128],
                              list(wbufs) + p.AA.bufs_for(kc * S + tt_ * 128, kc * S + tt_ * 128 + 128)))
            steps.append((p.ONE_ROW_BF[0:1, 0:128],
                          bb[0:1, VBROW + pair * 128:VBROW + (pair + 1) * 128],
                          [p.ONE_ROW_BF_buf] + b_vbrow))
            c.mm(pbk[:, 0:128], pbuf, steps)
            for hh in range(2):
                c.copy("act", bb[:, VA + tt_ * 130 + hh * 65: VA + tt_ * 130 + hh * 65 + 64],
                       pbk[:, hh * 64:hh * 64 + 64], reads=[pbuf], writes=b_va)
        ones_ap = bb[:, VA:VA + NT * 130].rearrange("p (t h d) -> p t h d", t=NT, h=2)[:, :, :, 64:65]
        c.memset("dve", ones_ap, 1.0, writes=b_va)
        return b_qT, b_kT, b_va

    def pair_bias_tab(pair):
        lo_r, b_r = p.sm_ring.persist(8)
        c.act(st[0:1, lo_r:lo_r + 2], st[0:1, lo_qm + 2 * pair:lo_qm + 2 * pair + 2],
              AF.Sqrt, reads=b_qm, writes=b_r)
        c.act(st[0:1, lo_r + 2:lo_r + 4], st[0:1, lo_km + 2 * pair:lo_km + 2 * pair + 2],
              AF.Sqrt, reads=b_km, writes=b_r)
        c.ts("dve", st[0:1, lo_r + 4:lo_r + 6], st[0:1, lo_r:lo_r + 2], 0.125, None,
             ALU.mult, None, reads=b_r, writes=b_r)
        c.tt("dve", st[0:1, lo_r + 4:lo_r + 6], st[0:1, lo_r + 4:lo_r + 6],
             st[0:1, lo_r + 2:lo_r + 4], ALU.mult, reads=b_r, writes=b_r)
        lo_w, b_w = p.sm_ring.alloc(4 * NTI)
        lo_s2 = lo_w + 2 * NTI
        for hh in range(2):
            h = 2 * pair + hh
            slope = 2.0 ** (-(h + 1))
            cst = st[0:1, lo_w + hh * NTI:lo_w + (hh + 1) * NTI]
            def fio(eng, cst=cst):
                return eng.iota(cst, [[1, NTI]], channel_multiplier=0,
                                allow_small_or_imprecise_dtypes=True)
            c.emit("pool", fio, writes=b_w)
            c.ts("dve", cst, cst, 128.0 * slope, -128.0 * slope * (NT - 1),
                 ALU.mult, ALU.add, reads=b_w, writes=b_w)
            def fsub(eng, cst=cst, hh=hh):
                return eng.tensor_scalar(cst, cst,
                                         st[0:1, lo_r + 4 + hh:lo_r + 5 + hh],
                                         None, ALU.subtract)
            c.emit("dve", fsub, reads=b_w + b_r, writes=b_w)
            c.memset("dve", st[0:1, lo_s2 + hh * NTI:lo_s2 + (hh + 1) * NTI], slope,
                     writes=b_w)
        lo_l, b_l = p.sm_ring.alloc(128)
        def fi2(eng):
            return eng.iota(st[0:1, lo_l:lo_l + 128], [[1, 128]], channel_multiplier=0,
                            allow_small_or_imprecise_dtypes=True)
        c.emit("pool", fi2, writes=b_l)
        pbk, pbuf = p.PB[3], p.PBbuf[3]
        def fm1(eng):
            return eng.matmul(pbk[:, 0:2 * NTI], st[0:1, lo_l:lo_l + 128],
                              st[0:1, lo_s2:lo_s2 + 2 * NTI], start=True, stop=False)
        c.emit("pe", fm1, reads=b_l + b_w, writes=[pbuf])
        def fm2(eng):
            return eng.matmul(pbk[:, 0:2 * NTI], p.ONE_ROW[0:1, 0:128],
                              st[0:1, lo_w:lo_w + 2 * NTI], start=False, stop=True)
        c.emit("pe", fm2, reads=[p.ONE_ROW_buf] + b_w, writes=[pbuf])
        lo_tab, b_tab = p.sm_ring.persist(2 * NTI)
        c.copy("dve", st[:, lo_tab:lo_tab + 2 * NTI], pbk[:, 0:2 * NTI],
               reads=[pbuf], writes=b_tab)
        return lo_tab, b_tab

    for pair in range(8):
        _pscope = p.sm_ring.scope()
        b_qT, b_kT, b_va = qkv_pair(pair)
        lo_tab, b_tab = pair_bias_tab(pair)
        pending = []

        def l1a_tail(ob, obuf, h, qbase, NQh):
            lo_rr, b_rr = p.sm_ring.alloc(512)
            def frc(eng, ob=ob):
                return eng.reciprocal(st[0:1, lo_rr:lo_rr + NQh], ob[64:65, 0:NQh])
            c.emit("dve", frc, reads=[obuf], writes=b_rr)
            och = h // 2
            orow = (h % 2) * 64
            ocol = och * S + qbase
            ob_bufs = p.AB.bufs_for(ocol, ocol + NQh)
            dstab = p.AB.t[orow:orow + 64, ocol:ocol + NQh]
            c.copy("act", dstab, ob[0:64, 0:NQh], reads=[obuf], writes=ob_bufs)
            bcb, bcbuf = p.PB[1], p.PBbuf[1]
            def fbc(eng):
                return eng.matmul(bcb[0:64, 0:NQh], p.ONE_ROW[0:1, 0:64],
                                  st[0:1, lo_rr:lo_rr + NQh], start=True, stop=True)
            c.emit("pe", fbc, reads=[p.ONE_ROW_buf] + b_rr, writes=[bcbuf])
            c.tt("dve", dstab, dstab, bcb[0:64, 0:NQh], ALU.mult,
                 reads=ob_bufs + [bcbuf], writes=ob_bufs)

        def l1a_sc(hh, kc, sa, soff, qbase, NQh, sabuf):
            qrow0 = hh * 64
            def fsc(eng):
                return eng.matmul(
                    sa[:, soff:soff + NQh],
                    bb[qrow0:qrow0 + 64, KT + kc * 128:KT + kc * 128 + 128],
                    bb[qrow0:qrow0 + 64, QT + qbase:QT + qbase + NQh],
                    start=True, stop=True)
            c.emit("pe", fsc, reads=b_kT + b_qT, writes=[sabuf])

        def l1a_exp(hh, kc, sa, soff, sabuf, ex, qbase, NQh):
            qb128 = qbase // 128
            ti = kc - qb128 + (NT - 1)
            bias_ap = st[:, lo_tab + hh * NTI + ti:lo_tab + hh * NTI + ti + 1]
            exb = p.BB.bufs_for(ex, ex + 512)
            c.act(bb[:, ex:ex + NQh], sa[:, soff:soff + NQh], AF.Exp,
                  bias=(bias_ap, b_tab), scale=(inv8_ap, inv8_b),
                  reads=[sabuf], writes=exb)
            o = kc * 128 - qbase
            if o >= 0:
                mkap, mkb = p.mk(o // 128, NQh)
                c.tt("pool", bb[:, ex:ex + NQh], bb[:, ex:ex + NQh],
                     mkap, ALU.mult, reads=exb + mkb, writes=exb)
            return exb

        def l1a_pv(hh, kc, j, nj, ex, exb, NQh, ob, obuf):
            def fpv(eng):
                return eng.matmul(ob[0:65, 0:NQh],
                                  bb[:, VA + kc * 130 + hh * 65:VA + kc * 130 + hh * 65 + 65],
                                  bb[:, ex:ex + NQh],
                                  start=(j == 0), stop=(j == nj - 1))
            c.emit("pe", fpv, reads=b_va + exb, writes=[obuf])

        def kcs_for(h, qbase, NQh):
            slope = 2.0 ** (-(h + 1))
            kc_hi = (qbase + NQh - 1) // 128 + 1
            return [kc for kc in range(kc_hi)
                    if slope * ((qbase // 128 - kc) * 128 - 127) <= 200.0]

        for hh in range(2):
            h = 2 * pair + hh
            if h >= 2:
                continue
            NQh = 128 if h == 0 else 256
            for qc in range(S // NQh):
                qbase = qc * NQh
                ob, obuf = (p.PB[2], p.PBbuf[2]) if (qc + hh) % 2 == 0 else (p.PB[3], p.PBbuf[3])
                kcs = kcs_for(h, qbase, NQh)
                pend_pv = None
                for j, kc in enumerate(kcs):
                    sa, sabuf = (p.PA[0], p.PAbuf[0]) if (j // 2) % 2 == 0 else (p.PA[1], p.PAbuf[1])
                    soff = (j % 2) * 512
                    l1a_sc(hh, kc, sa, soff, qbase, NQh, sabuf)
                    if j == 0 and pending:
                        pending.pop(0)()
                    ex = EXL + (j % 4) * 512
                    exb = l1a_exp(hh, kc, sa, soff, sabuf, ex, qbase, NQh)
                    if pend_pv is not None:
                        pend_pv()
                    pend_pv = (lambda hh=hh, kc=kc, j=j, nj=len(kcs), ex=ex, exb=exb,
                               NQh=NQh, ob=ob, obuf=obuf:
                               l1a_pv(hh, kc, j, nj, ex, exb, NQh, ob, obuf))
                pend_pv()
                pending.append(lambda ob=ob, obuf=obuf, h=h, qbase=qbase, NQh=NQh:
                               l1a_tail(ob, obuf, h, qbase, NQh))
        if pair >= 1:
            for qc in range(S // 512):
                qbase = qc * 512
                obs = [(p.PB[2], p.PBbuf[2]), (p.PB[3], p.PBbuf[3])]
                kcs2 = [kcs_for(2 * pair, qbase, 512), kcs_for(2 * pair + 1, qbase, 512)]
                nmax = max(len(kcs2[0]), len(kcs2[1]))
                pend_pv = None
                for j in range(nmax):
                    for hh in range(2):
                        if j < len(kcs2[hh]):
                            sa, sabuf = p.PA[hh], p.PAbuf[hh]
                            soff = (j % 2) * 512
                            l1a_sc(hh, kcs2[hh][j], sa, soff, qbase, 512, sabuf)
                    if j == 0:
                        while pending:
                            pending.pop(0)()
                    pvs = []
                    for hh in range(2):
                        if j < len(kcs2[hh]):
                            sa, sabuf = p.PA[hh], p.PAbuf[hh]
                            soff = (j % 2) * 512
                            ex = EXL + hh * 1024 + soff
                            exb = l1a_exp(hh, kcs2[hh][j], sa, soff, sabuf, ex, qbase, 512)
                            pvs.append((hh, kcs2[hh][j], j, len(kcs2[hh]), ex, exb,
                                        obs[hh][0], obs[hh][1]))
                    if pend_pv is not None:
                        pend_pv()
                    def mk_pv(pvs=pvs):
                        def emit_pv():
                            for (hh, kc, j_, nj, ex, exb, ob, obuf) in pvs:
                                l1a_pv(hh, kc, j_, nj, ex, exb, 512, ob, obuf)
                        return emit_pv
                    pend_pv = mk_pv()
                pend_pv()
                for hh in range(2):
                    pending.append(lambda ob=obs[hh][0], obuf=obs[hh][1],
                                   h=2 * pair + hh, qbase=qbase:
                                   l1a_tail(ob, obuf, h, qbase, 512))
        while pending:
            pending.pop(0)()
        p.sm_ring.unscope(_pscope)

    wo = p.wload(W["w_proj"], 1024, 1024)
    for t in range(NT):
        for half in range(2):
            pbk, pbuf = (p.PB[0], p.PBbuf[0]) if half == 0 else (p.PB[1], p.PBbuf[1])
            steps = []
            for kc in range(8):
                (wt, wlo, wpr, wcols, wbufs) = wo[kc]
                steps.append((p.AB.t[:, kc * S + t * 128: kc * S + t * 128 + 128],
                              wt[0:128, wlo + half * 512: wlo + half * 512 + 512],
                              list(wbufs) + p.AB.bufs_for(kc * S + t * 128, kc * S + t * 128 + 128)))
            steps.append((p.ONE_ROW[0:1, 0:128],
                          st[0:1, lo_bp + half * 512:lo_bp + half * 512 + 512],
                          [p.ONE_ROW_buf] + b_bp))
            c.mm(pbk[:, :], pbuf, steps)
            rb = p.RES.bufs[t]
            rap = p.RES.t[:, t * 1024 + half * 512: t * 1024 + half * 512 + 512]
            c.tt("dve", rap, rap, pbk[:, :], ALU.add, reads=[pbuf, rb], writes=[rb])
    p.sm_ring.unscope(_scope)


def emit_l0_attn_b(p, W):
    c = p.ctx
    S, NT = p.S, p.NT
    NKC = S // 128
    NQ = S // 512
    st = p.SM.t
    bb = p.BB.t
    _scope = p.sm_ring.scope()

    p.ln_xn()

    lo_s, b_s = p.sm_ring.persist(48)
    c.dma("pool", st[0:1, lo_s:lo_s + 16], W["scale"], writes=b_s)
    c.act(st[0:1, lo_s + 16:lo_s + 32], st[0:1, lo_s:lo_s + 16], AF.Exp,
          reads=b_s, writes=b_s)
    c.ts("dve", st[0:1, lo_s + 32:lo_s + 48], st[0:1, lo_s + 16:lo_s + 32], -1.0, None,
         ALU.mult, None, reads=b_s, writes=b_s)
    (lo_sc, b_sc), (lo_bi, b_bi) = p.head_tables(
        (st[0:1, lo_s + 16:lo_s + 32], b_s), (st[0:1, lo_s + 32:lo_s + 48], b_s), [])

    lo_bo, b_bo = p.sm_ring.persist(1024)
    c.dma("pool", st[0:1, lo_bo:lo_bo + 1024], W["b_out"], writes=b_bo)

    QT, KT, VA, EXL, SQ = 0, 2048, 4096, 6400, 10496

    def qkv_pair(pair):
        rinv = {}
        wq = p.wload(W["w_q"][:, pair * 128:(pair + 1) * 128], 1024, 128)
        wk = p.wload(W["w_kv"][:, pair * 128:(pair + 1) * 128], 1024, 128)
        wv = p.wload(W["w_kv"][:, 1024 + pair * 128:1024 + (pair + 1) * 128], 1024, 128)
        b_qT = p.BB.bufs_for(QT, QT + S)
        b_kT = p.BB.bufs_for(KT, KT + S)
        b_va = p.BB.bufs_for(VA, VA + NT * 130)
        pend_n = []
        for wi, (wtiles, dstcol, dstbufs) in enumerate(((wq, QT, b_qT), (wk, KT, b_kT))):
            for tc in range(NQ):
                pbk, pbuf = (p.PB[0], p.PBbuf[0]) if tc % 2 == 0 else (p.PB[1], p.PBbuf[1])
                steps = []
                for kc in range(8):
                    (wt, wlo, wpr, wcols, wbufs) = wtiles[kc]
                    steps.append((wt[0:128, wlo:wlo + 128],
                                  p.AA.t[:, kc * S + tc * 512: kc * S + tc * 512 + 512],
                                  list(wbufs) + p.AA.bufs_for(kc * S + tc * 512, kc * S + tc * 512 + 512)))
                c.mm(pbk[:, :], pbuf, steps)
                dst = bb[:, dstcol + tc * 512: dstcol + tc * 512 + 512]
                c.copy("act", dst, pbk[:, :], reads=[pbuf], writes=dstbufs)
                sq_col = SQ + (wi * NQ + tc) % 3 * 512
                sqb = p.BB.bufs_for(sq_col, sq_col + 512)
                c.tt("pool", bb[:, sq_col:sq_col + 512], dst, dst, ALU.mult,
                     reads=dstbufs, writes=sqb)
                if pend_n:
                    pend_n.pop(0)()

                def mk_norm(wi=wi, tc=tc, sq_col=sq_col, sqb=sqb):
                    def emit_norm():
                        for hh in range(2):
                            pn, pnb = (p.PB[2], p.PBbuf[2]) if (tc + hh) % 2 == 0 else (p.PB[3], p.PBbuf[3])
                            def fn(eng, hh=hh, pn=pn, sq_col=sq_col):
                                return eng.matmul(pn[0:1, 0:512], p.MASKHALF[:, hh:hh + 1],
                                                  bb[:, sq_col:sq_col + 512], start=True, stop=True)
                            c.emit("pe", fn, reads=sqb + [p.MASKHALF_buf], writes=[pnb])
                            lo_r, b_r = p.sm_ring.alloc(512)
                            p.rsqrt(st[0:1, lo_r:lo_r + 512], pn[0:1, 0:512], 1e-24,
                                    reads=[pnb], writes=b_r)
                            rinv[(wi, tc, hh)] = (lo_r, b_r)
                    return emit_norm

                pend_n.append(mk_norm())
        while pend_n:
            pend_n.pop(0)()
        bi = 0
        for wi, (dstcol, dstbufs) in enumerate(((QT, b_qT), (KT, b_kT))):
            for tc in range(NQ):
                for hh in range(2):
                    lo_r, b_r = rinv[(wi, tc, hh)]
                    pa_, pab_ = (p.PA[0], p.PAbuf[0]) if bi % 2 == 0 else (p.PA[1], p.PAbuf[1])
                    boff = (bi // 2 % 2) * 512
                    bi += 1
                    def fbc(eng, lo_r=lo_r, pa_=pa_, boff=boff):
                        return eng.matmul(pa_[0:64, boff:boff + 512], p.ONE_ROW[0:1, 0:64],
                                          st[0:1, lo_r:lo_r + 512], start=True, stop=True)
                    c.emit("pe", fbc, reads=[p.ONE_ROW_buf] + b_r, writes=[pab_])
                    sub = bb[hh * 64:hh * 64 + 64, dstcol + tc * 512: dstcol + tc * 512 + 512]
                    def fmu(eng, sub=sub, pa_=pa_, boff=boff):
                        return eng.tensor_tensor(sub, sub, pa_[0:64, boff:boff + 512], ALU.mult)
                    c.emit("dve", fmu, reads=dstbufs + [pab_], writes=dstbufs)
        for tt_ in range(NT):
            pbk, pbuf = (p.PB[0], p.PBbuf[0]) if tt_ % 2 == 0 else (p.PB[1], p.PBbuf[1])
            steps = []
            for kc in range(8):
                (wt, wlo, wpr, wcols, wbufs) = wv[kc]
                steps.append((p.AA.t[:, kc * S + tt_ * 128: kc * S + tt_ * 128 + 128],
                              wt[0:128, wlo:wlo + 128],
                              list(wbufs) + p.AA.bufs_for(kc * S + tt_ * 128, kc * S + tt_ * 128 + 128)))
            c.mm(pbk[:, 0:128], pbuf, steps)
            for hh in range(2):
                c.copy("act", bb[:, VA + tt_ * 130 + hh * 65: VA + tt_ * 130 + hh * 65 + 64],
                       pbk[:, hh * 64:hh * 64 + 64], reads=[pbuf], writes=b_va)
        ones_ap = bb[:, VA:VA + NT * 130].rearrange("p (t h d) -> p t h d", t=NT, h=2)[:, :, :, 64:65]
        c.memset("dve", ones_ap, 1.0, writes=b_va)
        return b_qT, b_kT, b_va

    for pair in range(8):
        _pscope = p.sm_ring.scope()
        b_qT, b_kT, b_va = qkv_pair(pair)
        pending = []
        for qc in range(NQ):
            obs = [(p.PB[2], p.PBbuf[2]), (p.PB[3], p.PBbuf[3])]
            exi = 0
            pend_pv = None
            for j0 in range(0, NKC, 2):
                pk = list(range(j0, min(j0 + 2, NKC)))
                exoff = [EXL + (exi % 2) * 1024, EXL + 2048 + (exi % 2) * 1024]
                exi += 1
                for hh in range(2):
                    qrow0 = hh * 64
                    sa, sabuf = p.PA[hh], p.PAbuf[hh]
                    for i2, kc in enumerate(pk):
                        def fsc(eng, kc=kc, i2=i2, sa=sa, qrow0=qrow0, qc=qc):
                            return eng.matmul(
                                sa[:, i2 * 512:i2 * 512 + 512],
                                bb[qrow0:qrow0 + 64, KT + kc * 128:KT + kc * 128 + 128],
                                bb[qrow0:qrow0 + 64, QT + qc * 512:QT + qc * 512 + 512],
                                start=True, stop=True)
                        c.emit("pe", fsc, reads=b_kT + b_qT, writes=[sabuf])
                if j0 == 0:
                    while pending:
                        pending.pop(0)()
                w2 = 512 * len(pk)
                for hh in range(2):
                    h = 2 * pair + hh
                    sa, sabuf = p.PA[hh], p.PAbuf[hh]
                    ex = exoff[hh]
                    exb = p.BB.bufs_for(ex, ex + w2)
                    c.act(bb[:, ex:ex + w2], sa[:, 0:w2], AF.Exp,
                          bias=st[:, lo_bi + h:lo_bi + h + 1],
                          scale=st[:, lo_sc + h:lo_sc + h + 1],
                          reads=[sabuf] + b_sc + b_bi, writes=exb)
                if pend_pv is not None:
                    pend_pv()

                def mk_pv(pk=pk, exoff=exoff, w2=w2, obs=obs):
                    def emit_pv():
                        for hh in range(2):
                            ob, obuf = obs[hh]
                            ex = exoff[hh]
                            exb = p.BB.bufs_for(ex, ex + w2)
                            for i2, kc in enumerate(pk):
                                def fpv(eng, kc=kc, i2=i2, ex=ex, ob=ob, hh=hh,
                                        first=(kc == 0), last=(kc == NKC - 1)):
                                    return eng.matmul(ob[0:65, :],
                                                      bb[:, VA + kc * 130 + hh * 65:VA + kc * 130 + hh * 65 + 65],
                                                      bb[:, ex + i2 * 512:ex + i2 * 512 + 512],
                                                      start=first, stop=last)
                                c.emit("pe", fpv, reads=b_va + exb, writes=[obuf])
                    return emit_pv

                pend_pv = mk_pv()
            pend_pv()
            for hh in range(2):
                ob, obuf = obs[hh]
                h = 2 * pair + hh

                def tail(ob=ob, obuf=obuf, h=h, qc=qc):
                    lo_rr, b_rr = p.sm_ring.alloc(512)
                    p.rsqrt(st[0:1, lo_rr:lo_rr + 512], ob[64:65, :], 0.0,
                            reads=[obuf], writes=b_rr, power=-1.0)
                    och = h // 2
                    orow = (h % 2) * 64
                    ocol = och * S + qc * 512
                    ob_bufs = p.AB.bufs_for(ocol, ocol + 512)
                    dstab = p.AB.t[orow:orow + 64, ocol:ocol + 512]
                    c.copy("act", dstab, ob[0:64, :], reads=[obuf], writes=ob_bufs)
                    bcb, bcbuf = p.PB[1], p.PBbuf[1]
                    def fbc(eng):
                        return eng.matmul(bcb[0:64, :], p.ONE_ROW[0:1, 0:64],
                                          st[0:1, lo_rr:lo_rr + 512], start=True, stop=True)
                    c.emit("pe", fbc, reads=[p.ONE_ROW_buf] + b_rr, writes=[bcbuf])
                    c.tt("dve", dstab, dstab, bcb[0:64, :], ALU.mult,
                         reads=ob_bufs + [bcbuf], writes=ob_bufs)
                pending.append(tail)
        while pending:
            pending.pop(0)()
        p.sm_ring.unscope(_pscope)

    wo = p.wload(W["w_out"], 1024, 1024)
    for t in range(NT):
        for half in range(2):
            pbk, pbuf = (p.PB[0], p.PBbuf[0]) if half == 0 else (p.PB[1], p.PBbuf[1])
            steps = []
            for kc in range(8):
                (wt, wlo, wpr, wcols, wbufs) = wo[kc]
                steps.append((p.AB.t[:, kc * S + t * 128: kc * S + t * 128 + 128],
                              wt[0:128, wlo + half * 512: wlo + half * 512 + 512],
                              list(wbufs) + p.AB.bufs_for(kc * S + t * 128, kc * S + t * 128 + 128)))
            steps.append((p.ONE_ROW[0:1, 0:128],
                          st[0:1, lo_bo + half * 512:lo_bo + half * 512 + 512],
                          [p.ONE_ROW_buf] + b_bo))
            c.mm(pbk[:, :], pbuf, steps)
            rb = p.RES.bufs[t]
            rap = p.RES.t[:, t * 1024 + half * 512: t * 1024 + half * 512 + 512]
            c.tt("dve", rap, rap, pbk[:, :], ALU.add, reads=[pbuf, rb], writes=[rb])
    p.sm_ring.unscope(_scope)


def emit_l0_ffn(p, W, dbg=None):
    c = p.ctx
    S, NT = p.S, p.NT
    NQ = S // 512
    st = p.SM.t
    bb = p.BB.t
    _scope = p.sm_ring.scope()
    _wscope = p.wr_ring.scope()

    p.ln_xn()

    _, lo_encb, _, b_encb = p.load_bias_col(W["enc_b"].rearrange("a n -> (a n)"), 2048, "encb")
    _, lo_s2cb, _, b_s2cb = p.load_bias_col(W["s2c_b"].rearrange("a n -> (a n)"), 64, "s2cb")
    _, lo_rb1, _, b_rb1 = p.load_bias_col(W["rb1"].rearrange("a n -> (a n)"), 64, "rb1")
    _, lo_c2sb, _, b_c2sb = p.load_bias_col(W["c2s_b"].rearrange("a n -> (a n)"), 2048, "c2sb")
    be1T = []
    for e in range(8):
        _, lo_e, _, b_e = p.load_bias_col(W["be1"][e:e + 1, :].rearrange("a n -> (a n)"), 1024, f"be1{e}")
        be1T.append((lo_e, b_e))
    lo_rows, b_rows = p.sm_ring.persist(8 + 8 * 64 + 1024)
    lo_rb2 = lo_rows
    lo_be2 = lo_rows + 8
    lo_decb = lo_be2 + 8 * 64
    c.dma("pool", st[0:1, lo_rb2:lo_rb2 + 8], W["rb2"], writes=b_rows)
    c.dma("pool", st[0:1, lo_be2:lo_be2 + 8 * 64], W["be2"].rearrange("e n -> (e n)").rearrange("(a n) -> a n", a=1), writes=b_rows)
    c.dma("pool", st[0:1, lo_decb:lo_decb + 1024], W["dec_b"], writes=b_rows)

    CONT, R1, SMT = 0, 2048, 4096
    b_cont = p.BB.bufs_for(CONT, CONT + S)
    b_r1 = p.BB.bufs_for(R1, R1 + S)
    b_smt = p.BB.bufs_for(SMT, SMT + 8192)

    assert NQ <= 4
    pending_ct = []
    for m in range(16):
        par = m % 2
        SROW = SMT + par * 2048
        b_srow_m = [p.BB.bufs_for(SROW + tc * 512, SROW + tc * 512 + 512)
                    for tc in range(NQ)]
        encw = p.wload(W["enc_W"][:, m * 128:(m + 1) * 128], 1024, 128)
        s2c_m = p.wload(W["s2c_W"][m * 128:(m + 1) * 128, :], 128, 64)
        for tc in range(NQ):
            pbk, pbuf = (p.PB[0], p.PBbuf[0]) if tc % 2 == 0 else (p.PB[1], p.PBbuf[1])
            steps = []
            for kc in range(8):
                (wt, wlo, wpr, wcols, wbufs) = encw[kc]
                steps.append((wt[0:128, wlo:wlo + 128],
                              p.AA.t[:, kc * S + tc * 512: kc * S + tc * 512 + 512],
                              list(wbufs) + p.AA.bufs_for(kc * S + tc * 512, kc * S + tc * 512 + 512)))
            c.mm(pbk[:, :], pbuf, steps)
            if pending_ct:
                pending_ct.pop(0)()
            def fsg(eng, pbk=pbk, tc=tc, m=m, SROW=SROW):
                return eng.activation(bb[:, SROW + tc * 512:SROW + tc * 512 + 512],
                                      pbk[:, :], AF.Sigmoid,
                                      bias=st[:, lo_encb + m:lo_encb + m + 1])
            c.emit("act", fsg, reads=[pbuf] + b_encb, writes=b_srow_m[tc])

            def fct_tail(m=m, tc=tc, SROW=SROW, s2c_m=s2c_m, srb=b_srow_m[tc]):
                (wt, wlo, wpr, wcols, wbufs) = s2c_m[0]
                cb, cbuf = p.PA[tc // 2], p.PAbuf[tc // 2]
                coff = (tc % 2) * 512
                def fct(eng):
                    return eng.matmul(cb[0:64, coff:coff + 512], wt[0:128, wlo:wlo + 64],
                                      bb[:, SROW + tc * 512:SROW + tc * 512 + 512],
                                      start=(m == 0), stop=(m == 15))
                c.emit("pe", fct, reads=list(wbufs) + srb, writes=[cbuf])
            pending_ct.append(fct_tail)
    while pending_ct:
        pending_ct.pop(0)()
    for tc2 in range(0, NQ, 2):
        cb, cbuf = p.PA[tc2 // 2], p.PAbuf[tc2 // 2]
        w2 = min(1024, (NQ - tc2) * 512)
        def fev(eng, cb=cb, tc2=tc2, w2=w2):
            return eng.activation(bb[0:64, CONT + tc2 * 512:CONT + tc2 * 512 + w2],
                                  cb[0:64, 0:w2], AF.Identity,
                                  bias=st[0:64, lo_s2cb:lo_s2cb + 1])
        c.emit("act", fev, reads=[cbuf] + b_s2cb, writes=b_cont)

    if dbg is not None:
        dbg("cont", dict(CONT=CONT, b_cont=b_cont, SROW=SROW, b_srow=b_srow,
                         lo_encb=lo_encb, b_encb=b_encb))
    _rscope = p.wr_ring.scope()
    rw1 = p.wload(W["rW1"], 64, 64, persist=True)
    rw2 = p.wload(W["rW2"], 64, 8, persist=True)
    for tc in range(NQ):
        pbk, pbuf = (p.PB[0], p.PBbuf[0]) if tc % 2 == 0 else (p.PB[1], p.PBbuf[1])
        (wt, wlo, wpr, wcols, wbufs) = rw1[0]
        def fr1(eng, wt=wt, wlo=wlo, tc=tc, pbk=pbk):
            return eng.matmul(pbk[0:64, :], wt[0:64, wlo:wlo + 64],
                              bb[0:64, CONT + tc * 512:CONT + tc * 512 + 512],
                              start=True, stop=True)
        c.emit("pe", fr1, reads=list(wbufs) + b_cont, writes=[pbuf])
        def ft1(eng, pbk=pbk, tc=tc):
            return eng.activation(bb[0:64, R1 + tc * 512:R1 + tc * 512 + 512],
                                  pbk[0:64, :], AF.Tanh,
                                  bias=st[0:64, lo_rb1:lo_rb1 + 1])
        c.emit("act", ft1, reads=[pbuf] + b_rb1, writes=b_r1)
    lo_lg, b_lg = p.sm_ring.persist(NT * 8)
    for t in range(NT):
        pbk, pbuf = (p.PB[0], p.PBbuf[0]) if t % 2 == 0 else (p.PB[1], p.PBbuf[1])
        (wt, wlo, wpr, wcols, wbufs) = rw2[0]
        steps = [(bb[0:64, R1 + t * 128:R1 + t * 128 + 128], wt[0:64, wlo:wlo + 8],
                  list(wbufs) + b_r1),
                 (p.ONE_ROW[0:1, 0:128], st[0:1, lo_rb2:lo_rb2 + 8],
                  [p.ONE_ROW_buf] + b_rows)]
        c.mm(pbk[:, 0:8], pbuf, steps)
        c.copy("dve", st[:, lo_lg + t * 8:lo_lg + t * 8 + 8], pbk[:, 0:8],
               reads=[pbuf], writes=b_lg)

    p.wr_ring.unscope(_rscope)

    lo_gt, b_gt = p.sm_ring.persist(NT * 8)
    lo_w1, b_w1 = p.sm_ring.alloc(NT * 8 * 3)
    lo_m, b_m = p.sm_ring.alloc(NT * 4)
    L = st[:, lo_lg:lo_lg + NT * 8].rearrange("p (t e) -> p t e", e=8)
    A = st[:, lo_w1:lo_w1 + NT * 8].rearrange("p (t e) -> p t e", e=8)
    B = st[:, lo_w1 + NT * 8:lo_w1 + NT * 16].rearrange("p (t e) -> p t e", e=8)
    Z = st[:, lo_w1 + NT * 16:lo_w1 + NT * 24].rearrange("p (t e) -> p t e", e=8)
    M1 = st[:, lo_m:lo_m + NT]
    M2 = st[:, lo_m + NT:lo_m + 2 * NT]
    EM2 = st[:, lo_m + 2 * NT:lo_m + 3 * NT]
    DI = st[:, lo_m + 3 * NT:lo_m + 4 * NT]
    M1b = M1.rearrange("p (t o) -> p t o", o=1)
    M2b = M2.rearrange("p (t o) -> p t o", o=1)

    def bcast8(ap_3d):
        import dataclasses as _dc
        new = [list(d) for d in ap_3d.ap]
        new[-1] = [0, 8]
        return _dc.replace(ap_3d, ap=new)

    c.red("dve", M1b, L, ALU.max, reads=b_lg, writes=b_m)
    c.tt("dve", A, L, bcast8(M1b), ALU.is_ge, reads=b_lg + b_m, writes=b_w1)

    def top_from(selA, selbufs, outap, outbufs):
        c.tt("dve", B[:, :, 1:8], selA[:, :, 1:8], selA[:, :, 0:7], ALU.add,
             reads=selbufs, writes=b_w1)
        c.copy("dve", B[:, :, 0:1], selA[:, :, 0:1], reads=selbufs, writes=b_w1)
        c.tt("dve", Z[:, :, 2:8], B[:, :, 2:8], B[:, :, 0:6], ALU.add,
             reads=b_w1, writes=b_w1)
        c.copy("dve", Z[:, :, 0:2], B[:, :, 0:2], reads=b_w1, writes=b_w1)
        c.tt("dve", B[:, :, 4:8], Z[:, :, 4:8], Z[:, :, 0:4], ALU.add,
             reads=b_w1, writes=b_w1)
        c.copy("dve", B[:, :, 0:4], Z[:, :, 0:4], reads=b_w1, writes=b_w1)
        c.tt("dve", Z, B, selA, ALU.subtract, reads=b_w1 + selbufs, writes=b_w1)
        c.ts("dve", Z, Z, 0.0, None, ALU.is_equal, None, reads=b_w1, writes=b_w1)
        c.tt("dve", outap, Z, selA, ALU.mult, reads=b_w1 + selbufs, writes=outbufs)

    lo_f, b_f = p.sm_ring.alloc(NT * 16)
    F1 = st[:, lo_f:lo_f + NT * 8].rearrange("p (t e) -> p t e", e=8)
    F2 = st[:, lo_f + NT * 8:lo_f + NT * 16].rearrange("p (t e) -> p t e", e=8)
    top_from(A, b_w1 + b_lg, F1, b_f)
    def fmk(eng):
        return eng.scalar_tensor_tensor(A, F1, -1e30, L, ALU.mult, ALU.add)
    c.emit("dve", fmk, reads=b_f + b_lg, writes=b_w1)
    c.red("dve", M2b, A, ALU.max, reads=b_w1, writes=b_m)
    lo_a2, b_a2 = p.sm_ring.alloc(NT * 8)
    A2 = st[:, lo_a2:lo_a2 + NT * 8].rearrange("p (t e) -> p t e", e=8)
    c.tt("dve", A2, A, bcast8(M2b), ALU.is_ge, reads=b_w1 + b_m, writes=b_a2)
    top_from(A2, b_a2, F2, b_f)
    c.tt("dve", EM2, M2, M1, ALU.subtract, reads=b_m, writes=b_m)
    c.act(EM2, EM2, AF.Exp, reads=b_m, writes=b_m)
    c.ts("dve", DI, EM2, 1.0, None, ALU.add, None, reads=b_m, writes=b_m)
    def fre(eng):
        return eng.reciprocal(DI, DI)
    c.emit("dve", fre, reads=b_m, writes=b_m)
    G = st[:, lo_gt:lo_gt + NT * 8].rearrange("p (t e) -> p t e", e=8)
    EM2b = EM2.rearrange("p (t o) -> p t o", o=1)
    DIb = DI.rearrange("p (t o) -> p t o", o=1)
    c.tt("dve", G, F2, bcast8(EM2b), ALU.mult, reads=b_f + b_m, writes=b_gt)
    c.tt("dve", G, G, F1, ALU.add, reads=b_gt + b_f, writes=b_gt)
    c.tt("dve", G, G, bcast8(DIb), ALU.mult, reads=b_gt + b_m, writes=b_gt)

    if dbg is not None:
        dbg("gates", dict(lo_lg=lo_lg, b_lg=b_lg, lo_gt=lo_gt, b_gt=b_gt))
    lo_of, b_of = p.sm_ring.persist(NT * 64)
    for e in range(8):
        we1 = p.wload(W["We1"][e:e + 1].rearrange("a b c -> (a b) c"), 64, 1024)
        we2 = p.wload(W["We2"][e:e + 1].rearrange("a b c -> (a b) c"), 1024, 64)
        (w1t, w1lo, _, _, w1bufs) = we1[0]
        lo_be1, b_be1 = be1T[e]
        for mh in range(8):
            for tc in range(NQ):
                pbk, pbuf = (p.PB[0], p.PBbuf[0]) if (mh + tc) % 2 == 0 else (p.PB[1], p.PBbuf[1])
                def fh(eng, mh=mh, tc=tc, pbk=pbk):
                    return eng.matmul(pbk[:, :],
                                      w1t[0:64, w1lo + mh * 128:w1lo + mh * 128 + 128],
                                      bb[0:64, CONT + tc * 512:CONT + tc * 512 + 512],
                                      start=True, stop=True)
                c.emit("pe", fh, reads=list(w1bufs) + b_cont, writes=[pbuf])
                dst = p.AB.t[:, mh * S + tc * 512: mh * S + tc * 512 + 512]
                dstb = p.AB.bufs_for(mh * S + tc * 512, mh * S + tc * 512 + 512)
                def fre_(eng, pbk=pbk, dst=dst, mh=mh):
                    return eng.activation(dst, pbk[:, :], AF.Relu,
                                          bias=st[:, lo_be1 + mh:lo_be1 + mh + 1])
                c.emit("act", fre_, reads=[pbuf] + b_be1, writes=dstb)
        for t in range(NT):
            pbk, pbuf = (p.PB[2], p.PBbuf[2]) if t % 2 == 0 else (p.PB[3], p.PBbuf[3])
            steps = []
            for mh in range(8):
                (w2t, w2lo, _, _, w2bufs) = we2[mh]
                steps.append((p.AB.t[:, mh * S + t * 128: mh * S + t * 128 + 128],
                              w2t[0:128, w2lo:w2lo + 64],
                              list(w2bufs) + p.AB.bufs_for(mh * S + t * 128, mh * S + t * 128 + 128)))
            steps.append((p.ONE_ROW[0:1, 0:128], st[0:1, lo_be2 + e * 64:lo_be2 + e * 64 + 64],
                          [p.ONE_ROW_buf] + b_rows))
            c.mm(pbk[:, 0:64], pbuf, steps)
            ofap = st[:, lo_of + t * 64:lo_of + t * 64 + 64]
            gap = st[:, lo_gt + t * 8 + e:lo_gt + t * 8 + e + 1]
            if e == 0:
                def fg0(eng, pbk=pbk, ofap=ofap, gap=gap):
                    return eng.tensor_scalar(ofap, pbk[:, 0:64], gap, None, ALU.mult)
                c.emit("dve", fg0, reads=[pbuf] + b_gt, writes=b_of)
            else:
                def fga(eng, pbk=pbk, ofap=ofap, gap=gap):
                    return eng.scalar_tensor_tensor(ofap, pbk[:, 0:64], gap, ofap,
                                                    ALU.mult, ALU.add)
                c.emit("dve", fga, reads=[pbuf] + b_gt + b_of, writes=b_of)

    if dbg is not None:
        dbg("of", dict(lo_of=lo_of, b_of=b_of))
    OFT = R1
    b_oft = p.BB.bufs_for(OFT, OFT + S)
    for t in range(NT):
        pbk, pbuf = (p.PB[2], p.PBbuf[2]) if t % 2 == 0 else (p.PB[3], p.PBbuf[3])
        c.transpose(pbk[0:64, 0:128], st[:, lo_of + t * 64:lo_of + t * 64 + 64],
                    p.ID_F32[:, :], reads=b_of, writes=[pbuf], ident_buf=p.IDF_buf)
        c.copy("act", bb[0:64, OFT + t * 128:OFT + t * 128 + 128], pbk[0:64, 0:128],
               reads=[pbuf], writes=b_oft)

    GB, BBIAS = 0, 1024
    b_gbc = p.BB.bufs_for(GB, GB + 1024)
    b_bbc = p.BB.bufs_for(BBIAS, BBIAS + 1024)
    lo_grow, b_grow = p.sm_ring.alloc(2048)
    c.dma("pool", st[0:1, lo_grow:lo_grow + 1024], W["ln_g"], writes=b_grow)
    c.dma("pool", st[0:1, lo_grow + 1024:lo_grow + 2048], W["ln_b"], writes=b_grow)
    for (dstcol, dbufs, off) in ((GB, b_gbc, 0), (BBIAS, b_bbc, 1024)):
        for half in range(2):
            pbk, pbuf = p.PB[3], p.PBbuf[3]
            def fb(eng, off=off, half=half):
                return eng.matmul(pbk[:, :], p.ONE_ROW[0:1, 0:128],
                                  st[0:1, lo_grow + off + half * 512:lo_grow + off + half * 512 + 512],
                                  start=True, stop=True)
            c.emit("pe", fb, reads=[p.ONE_ROW_buf] + b_grow, writes=[pbuf])
            c.copy("dve", bb[:, dstcol + half * 512:dstcol + half * 512 + 512],
                   pbk[:, :], reads=[pbuf], writes=dbufs)

    b_decw_ab = []
    for m in range(16):
        dst = p.AB.t[:, m * 1024:(m + 1) * 1024]
        dbufs = p.AB.bufs_for(m * 1024, (m + 1) * 1024)
        c.dma("sp", dst, W["dec_W"][m * 128:(m + 1) * 128, :], writes=dbufs)
        b_decw_ab.append(dbufs)

    c2s = p.wload(W["c2s_W"], 64, 2048, persist=True)
    (c2t, c2lo, _, _, c2bufs) = c2s[0]
    for T in range(NQ):
        for m in range(16):
            pbk, pbuf = (p.PB[0], p.PBbuf[0]) if m % 2 == 0 else (p.PB[1], p.PBbuf[1])
            def fsm(eng, m=m, T=T, pbk=pbk):
                return eng.matmul(pbk[:, :], c2t[0:64, c2lo + m * 128:c2lo + m * 128 + 128],
                                  bb[0:64, OFT + T * 512:OFT + T * 512 + 512],
                                  start=True, stop=True)
            c.emit("pe", fsm, reads=list(c2bufs) + b_oft, writes=[pbuf])
            def fsg2(eng, pbk=pbk, m=m, T=T):
                return eng.activation(bb[:, SMT + m * 512:SMT + m * 512 + 512],
                                      pbk[:, :], AF.Sigmoid,
                                      bias=st[:, lo_c2sb + m:lo_c2sb + m + 1])
            c.emit("act", fsg2, reads=[pbuf] + b_c2sb, writes=b_smt)
        for t4 in range(4):
            t = T * 4 + t4
            lo_dec, b_dec = p.sm_ring.alloc(1024)
            for half in range(2):
                pbk, pbuf = (p.PB[2], p.PBbuf[2]) if half == 0 else (p.PB[3], p.PBbuf[3])
                steps = []
                for m in range(16):
                    steps.append((bb[:, SMT + m * 512 + t4 * 128:SMT + m * 512 + t4 * 128 + 128],
                                  p.AB.t[:, m * 1024 + half * 512:m * 1024 + half * 512 + 512],
                                  b_decw_ab[m] + b_smt))
                steps.append((p.ONE_ROW[0:1, 0:128],
                              st[0:1, lo_decb + half * 512:lo_decb + half * 512 + 512],
                              [p.ONE_ROW_buf] + b_rows))
                c.mm(pbk[:, :], pbuf, steps)
                def fdec(eng, pbk=pbk, half=half, lo_dec=lo_dec):
                    return eng.activation(st[:, lo_dec + half * 512:lo_dec + half * 512 + 512],
                                          pbk[:, :], AF.Sigmoid)
                c.emit("act", fdec, reads=[pbuf], writes=b_dec)
            lo_st2, b_st2 = p.sm_ring.alloc(16)
            dtile = st[:, lo_dec:lo_dec + 1024]
            def f1(eng):
                return eng.bn_stats(st[:, lo_st2:lo_st2 + 6], dtile[:, 0:512])
            def f2(eng):
                return eng.bn_stats(st[:, lo_st2 + 6:lo_st2 + 12], dtile[:, 512:1024])
            def f3(eng):
                return eng.bn_aggr(st[:, lo_st2 + 12:lo_st2 + 14], st[:, lo_st2:lo_st2 + 12])
            c.emit("dve", f1, reads=b_dec, writes=b_st2)
            c.emit("dve", f2, reads=b_dec, writes=b_st2)
            c.emit("dve", f3, reads=b_st2, writes=b_st2)
            p.rsqrt(st[:, lo_st2 + 14:lo_st2 + 15], st[:, lo_st2 + 13:lo_st2 + 14], EPS,
                    reads=b_st2, writes=b_st2)
            c.ts("dve", st[:, lo_st2 + 15:lo_st2 + 16], st[:, lo_st2 + 12:lo_st2 + 13],
                 st[:, lo_st2 + 14:lo_st2 + 15], -1.0, ALU.mult, ALU.mult,
                 reads=b_st2, writes=b_st2)
            c.ts("dve", dtile, dtile, st[:, lo_st2 + 14:lo_st2 + 15],
                 st[:, lo_st2 + 15:lo_st2 + 16], ALU.mult, ALU.add,
                 reads=b_dec + b_st2, writes=b_dec)
            rb = p.RES.bufs[t]
            rap = p.RES.t[:, t * 1024:(t + 1) * 1024]
            def fg(eng):
                return eng.tensor_tensor(dtile, dtile, bb[:, GB:GB + 1024], ALU.mult)
            c.emit("dve", fg, reads=b_dec + b_gbc, writes=b_dec)
            def fb2(eng):
                return eng.tensor_tensor(dtile, dtile, bb[:, BBIAS:BBIAS + 1024], ALU.add)
            c.emit("dve", fb2, reads=b_dec + b_bbc, writes=b_dec)
            c.tt("dve", rap, rap, dtile, ALU.add, reads=b_dec + [rb], writes=[rb])
    p.sm_ring.unscope(_scope)
    p.wr_ring.unscope(_wscope)


def _bcast_blocks(ap_3d, blk):
    import dataclasses as _dc
    new = [list(d) for d in ap_3d.ap]
    new[-1] = [0, blk]
    return _dc.replace(ap_3d, ap=new)


def emit_fp4_quant(p, out_ap, out_bufs, in_ap, in_bufs, scr_lo, scr_bufs, width,
                   pr=128):
    c = p.ctx
    st = p.SM.t
    nb = width // 16
    am = st[0:pr, scr_lo:scr_lo + nb]
    rr = st[0:pr, scr_lo + nb:scr_lo + 2 * nb]
    xa = st[0:pr, scr_lo + 2 * nb:scr_lo + 2 * nb + width]
    a05 = st[0:pr, scr_lo + 2 * nb + width:scr_lo + 2 * nb + 2 * width]
    a1 = st[0:pr, scr_lo + 2 * nb + 2 * width:scr_lo + 2 * nb + 3 * width]
    def fab(eng):
        return eng.activation(xa, in_ap, AF.Abs)
    c.emit("act", fab, reads=in_bufs, writes=scr_bufs)
    xav = xa.rearrange("p (nb b) -> p nb b", b=16)
    c.red("dve", am.rearrange("p (nb o) -> p nb o", o=1), xav, ALU.max,
          reads=scr_bufs, writes=scr_bufs)
    c.ts("dve", am, am, 1.0 / 6.0, 1e-12, ALU.mult, ALU.max, reads=scr_bufs,
         writes=scr_bufs)
    def fr(eng):
        return eng.reciprocal(rr, am)
    c.emit("dve", fr, reads=scr_bufs, writes=scr_bufs)
    rb = _bcast_blocks(rr.rearrange("p (nb o) -> p nb o", o=1), 16)
    def fxa(eng):
        return eng.tensor_tensor(xav, xav, rb, ALU.mult)
    c.emit("dve", fxa, reads=scr_bufs, writes=scr_bufs)
    c.ts("dve", a05, xa, 0.25, None, ALU.is_gt, None, reads=scr_bufs, writes=scr_bufs)
    for t in (0.75, 1.25, 1.75):
        def fs(eng, t=t):
            return eng.scalar_tensor_tensor(a05, xa, t, a05, ALU.is_gt, ALU.add)
        c.emit("dve", fs, reads=scr_bufs, writes=scr_bufs)
    c.ts("dve", a1, xa, 2.5, None, ALU.is_gt, None, reads=scr_bufs, writes=scr_bufs)
    for t in (3.5, 5.0, 5.0):
        def fs2(eng, t=t):
            return eng.scalar_tensor_tensor(a1, xa, t, a1, ALU.is_gt, ALU.add)
        c.emit("dve", fs2, reads=scr_bufs, writes=scr_bufs)
    def fqn(eng):
        return eng.scalar_tensor_tensor(a05, a05, 0.5, a1, ALU.mult, ALU.add)
    c.emit("dve", fqn, reads=scr_bufs, writes=scr_bufs)
    def fsg(eng):
        return eng.activation(out_ap, in_ap, AF.Sign)
    c.emit("act", fsg, reads=in_bufs, writes=out_bufs)
    ab_ = _bcast_blocks(am.rearrange("p (nb o) -> p nb o", o=1), 16)
    def fss(eng):
        return eng.tensor_tensor(xa.rearrange("p (nb b) -> p nb b", b=16),
                                 out_ap.rearrange("p (nb b) -> p nb b", b=16), ab_,
                                 ALU.mult)
    c.emit("dve", fss, reads=out_bufs + scr_bufs, writes=scr_bufs)
    c.tt("dve", out_ap, a05, xa, ALU.mult, reads=scr_bufs, writes=out_bufs)


def emit_fp4_weight_prep(p, src_aps, dst_aps):
    c = p.ctx
    st = p.SM.t
    bb = p.BB.t
    _scope = p.sm_ring.scope()
    for (name, rows, cols) in (("w_gate", 2816, 1024), ("w_up", 2816, 1024),
                               ("w_down", 1024, 2816)):
        src = src_aps[name]
        dst = dst_aps[name]
        nrt = (rows + 127) // 128
        for rt in range(nrt):
            pr = min(128, rows - rt * 128)
            for h0 in range(0, cols, 1024):
                w = min(1024, cols - h0)
                lo_in, b_in_ = p.sm_ring.alloc(w)
                c.dma("pool", st[0:pr, lo_in:lo_in + w], src[rt * 128:rt * 128 + pr, h0:h0 + w],
                      writes=b_in_)
                lo_s, b_s = p.sm_ring.alloc(4 * w + 2 * (w // 16))
                qb = p.BB.bufs_for(h0, h0 + w)
                emit_fp4_quant(p, bb[0:pr, h0:h0 + w], qb, st[0:pr, lo_in:lo_in + w],
                               b_in_, lo_s, b_s, w, pr=pr)
            nch = cols // 128
            dstT = bb[:, 4096:4096 + nch * 128].rearrange("p (ch t) -> p ch t", ch=nch)[:, :, 0:pr]
            dstTb = p.BB.bufs_for(4096, 4096 + nch * 128)
            c.dma("sp", dstT, bb[0:pr, 0:cols], reads=p.BB.bufs_for(0, cols),
                  writes=dstTb, transpose=True)
            ddst = dst[0:cols, rt * 128:rt * 128 + pr].rearrange("(ch pp) t -> pp ch t", pp=128)
            c.dma("sp", ddst, dstT, reads=dstTb)
    p.sm_ring.unscope(_scope)


def emit_l1_ffn(p, W):
    c = p.ctx
    S, NT = p.S, p.NT
    NQ = S // 512
    st = p.SM.t
    bb = p.BB.t
    _scope = p.sm_ring.scope()

    p.ln_xn(want_xnT=False)

    def quant_tile(t):
        xn_ap = p.AB.t[:, t * 1024:(t + 1) * 1024]
        xnb = p.AB.bufs_for(t * 1024, (t + 1) * 1024)
        lo_s, b_s = p.sm_ring.alloc(4 * 1024 + 128)
        emit_fp4_quant(p, xn_ap, xnb, xn_ap, xnb, lo_s, b_s, 1024)
        dstT = p.AA.t[:, :].rearrange("p (ch s) -> p ch s", ch=8)[:, :, t * 128:(t + 1) * 128]
        dstTb = []
        for ch in range(8):
            dstTb += p.AA.bufs_for(ch * S + t * 128, ch * S + (t + 1) * 128)
        c.dma("sp", dstT, xn_ap, reads=xnb, writes=dstTb, transpose=True)

    NB2 = S // 256
    HQB = [0, 5632]

    def emit_gu(b):
        par = b % 2
        for nci in range(6):
            n0 = nci * 512
            nw = min(512, 2816 - n0)
            wg = p.wload(W["wgT"][:, n0:n0 + nw], 1024, nw)
            wu = p.wload(W["wuT"][:, n0:n0 + nw], 1024, nw)
            for t2 in range(2):
                t = 2 * b + t2
                pg, pgb = p.PB[0], p.PBbuf[0]
                pu, pub = p.PB[1], p.PBbuf[1]
                for (wt_, ps_, psb_) in ((wg, pg, pgb), (wu, pu, pub)):
                    steps = []
                    for kc in range(8):
                        (wtt, wlo, wpr, wcols, wbufs) = wt_[kc]
                        steps.append((p.AA.t[:, kc * S + t * 128: kc * S + t * 128 + 128],
                                      wtt[0:128, wlo:wlo + nw],
                                      list(wbufs) + p.AA.bufs_for(kc * S + t * 128, kc * S + t * 128 + 128)))
                    c.mm(ps_[:, 0:nw], psb_, steps)
                lo_sg, b_sg = p.sm_ring.alloc(512)
                def fsl(eng, pg=pg, nw=nw, lo_sg=lo_sg):
                    return eng.activation(st[:, lo_sg:lo_sg + nw], pg[:, 0:nw], AF.Silu)
                c.emit("act", fsl, reads=[pgb], writes=b_sg)
                hq_ap = bb[:, HQB[par] + t2 * 2816 + n0:HQB[par] + t2 * 2816 + n0 + nw]
                hqb = p.BB.bufs_for(HQB[par] + t2 * 2816 + n0, HQB[par] + t2 * 2816 + n0 + nw)
                c.tt("dve", hq_ap, pu[:, 0:nw], st[:, lo_sg:lo_sg + nw], ALU.mult,
                     reads=[pub] + b_sg, writes=hqb)
                lo_s, b_s = p.sm_ring.alloc(4 * nw + 2 * (nw // 16))
                emit_fp4_quant(p, hq_ap, hqb, hq_ap, hqb, lo_s, b_s, nw)

    def emit_quant(b, ab0):
        par = b % 2
        for t2 in range(2):
            hq_ap = bb[:, HQB[par] + t2 * 2816:HQB[par] + t2 * 2816 + 2816]
            hqb = p.BB.bufs_for(HQB[par] + t2 * 2816, HQB[par] + t2 * 2816 + 2816)
            dstT = p.AB.t[:, ab0:ab0 + 22 * 256].rearrange("p (m tt) -> p m tt", m=22)[:, :, t2 * 128:(t2 + 1) * 128]
            dstTb = p.AB.bufs_for(ab0, ab0 + 22 * 256)
            c.dma("sp", dstT, hq_ap, reads=hqb, writes=dstTb, transpose=True)

    def emit_down(b, ab0):
        for half in range(2):
            wd = p.wload(W["wdT"][:, half * 512:half * 512 + 512], 2816, 512)
            for t2 in range(2):
                t = 2 * b + t2
                pbk, pbuf = (p.PB[2], p.PBbuf[2]) if t2 % 2 == 0 else (p.PB[3], p.PBbuf[3])
                steps = []
                for mi in range(22):
                    (wtt, wlo, wpr, wcols, wbufs) = wd[mi]
                    steps.append((p.AB.t[:, ab0 + mi * 256 + t2 * 128:ab0 + mi * 256 + t2 * 128 + 128],
                                  wtt[0:wpr, wlo:wlo + 512],
                                  list(wbufs) + p.AB.bufs_for(ab0 + mi * 256, ab0 + (mi + 1) * 256)))
                c.mm(pbk[:, :], pbuf, steps)
                rb = p.RES.bufs[t]
                rap = p.RES.t[:, t * 1024 + half * 512: t * 1024 + half * 512 + 512]
                c.tt("dve", rap, rap, pbk[:, :], ALU.add, reads=[pbuf, rb], writes=[rb])

    for t in range(min(4, NT)):
        quant_tile(t)
    emit_gu(0)
    for b in range(NB2):
        for t in (2 * b + 4, 2 * b + 5):
            if t < NT:
                quant_tile(t)
        emit_quant(b, 0)
        if b + 1 < NB2:
            emit_gu(b + 1)
        emit_down(b, 0)
    p.sm_ring.unscope(_scope)


def emit_l1_attn_b(p, W, scratch, dbg=None):
    c = p.ctx
    S, NT = p.S, p.NT
    NQ = S // 512
    NKC = S // 128
    st = p.SM.t
    bb = p.BB.t
    _scope = p.sm_ring.scope()
    xn_dram = scratch["xn_bf"]
    y_dram = scratch["y_bf"]

    lo_w1, b_w1 = p.sm_ring.persist(1024)
    lo_b1, b_b1 = p.sm_ring.persist(1024)
    lo_row, b_row = p.sm_ring.alloc(2048)
    c.dma("pool", st[0:1, lo_row:lo_row + 1024], W["ln1_w"], writes=b_row)
    c.dma("pool", st[0:1, lo_row + 1024:lo_row + 2048], W["ln1_b"], writes=b_row)
    for (lo_dst, b_dst, off) in ((lo_w1, b_w1, 0), (lo_b1, b_b1, 1024)):
        for half in range(2):
            pbk, pbuf = p.PB[3], p.PBbuf[3]
            def fb(eng, off=off, half=half):
                return eng.matmul(pbk[:, :], p.ONE_ROW[0:1, 0:128],
                                  st[0:1, lo_row + off + half * 512:lo_row + off + half * 512 + 512],
                                  start=True, stop=True)
            c.emit("pe", fb, reads=[p.ONE_ROW_buf] + b_row, writes=[pbuf])
            c.copy("dve", st[:, lo_dst + half * 512:lo_dst + half * 512 + 512],
                   pbk[:, :], reads=[pbuf], writes=b_dst)

    b_xnd = Buf("xn_dram")

    def ln_s1(t):
        r = p.RES.t[:, t * 1024:(t + 1) * 1024]
        rb = p.RES.bufs[t]
        lo_st, b_st = p.sm_ring.alloc(32)
        def f1(eng):
            return eng.bn_stats(st[:, lo_st:lo_st + 6], r[:, 0:512])
        def f2(eng):
            return eng.bn_stats(st[:, lo_st + 6:lo_st + 12], r[:, 512:1024])
        def f3(eng):
            return eng.bn_aggr(st[:, lo_st + 12:lo_st + 14], st[:, lo_st:lo_st + 12])
        c.emit("dve", f1, reads=[rb], writes=b_st)
        c.emit("dve", f2, reads=[rb], writes=b_st)
        c.emit("dve", f3, reads=b_st, writes=b_st)
        p.rsqrt(st[:, lo_st + 14:lo_st + 15], st[:, lo_st + 13:lo_st + 14], EPS,
                reads=b_st, writes=b_st)
        return lo_st, b_st

    def ln_s2(t, lo_st, b_st):
        mu = st[:, lo_st + 12:lo_st + 13]
        var = st[:, lo_st + 13:lo_st + 14]
        a1_ = st[:, lo_st + 14:lo_st + 15]
        be_ = st[:, lo_st + 15:lo_st + 16]
        u_ = st[:, lo_st + 16:lo_st + 17]
        f_ = st[:, lo_st + 17:lo_st + 18]
        c.ts("dve", be_, mu, a1_, -1.0, ALU.mult, ALU.mult, reads=b_st, writes=b_st)
        c.ts("dve", u_, var, a1_, a1_, ALU.mult, ALU.mult, reads=b_st, writes=b_st)
        p.rsqrt(f_, u_, EPS, reads=b_st, writes=b_st)

    def ln_s3(t, lo_st, b_st):
        r = p.RES.t[:, t * 1024:(t + 1) * 1024]
        rb = p.RES.bufs[t]
        a1_ = st[:, lo_st + 14:lo_st + 15]
        be_ = st[:, lo_st + 15:lo_st + 16]
        f_ = st[:, lo_st + 17:lo_st + 18]
        af_ = st[:, lo_st + 18:lo_st + 19]
        bf_ = st[:, lo_st + 19:lo_st + 20]
        c.ts("dve", af_, a1_, f_, None, ALU.mult, None, reads=b_st, writes=b_st)
        c.ts("dve", bf_, be_, f_, None, ALU.mult, None, reads=b_st, writes=b_st)
        xn = p.AB.t[:, t * 1024:(t + 1) * 1024]
        xnb = p.AB.bufs_for(t * 1024, (t + 1) * 1024)
        c.ts("dve", xn, r, a1_, be_, ALU.mult, ALU.add, reads=[rb] + b_st, writes=xnb)
        c.dma("sp", xn_dram[t * 128:(t + 1) * 128, :], xn, reads=xnb, writes=[b_xnd])
        c.ts("dve", xn, r, af_, bf_, ALU.mult, ALU.add, reads=[rb] + b_st, writes=xnb)
        def fw1(eng):
            return eng.tensor_tensor(xn, xn, st[:, lo_w1:lo_w1 + 1024], ALU.mult)
        c.emit("dve", fw1, reads=xnb + b_w1, writes=xnb)
        def fb1(eng):
            return eng.tensor_tensor(xn, xn, st[:, lo_b1:lo_b1 + 1024], ALU.add)
        c.emit("dve", fb1, reads=xnb + b_b1, writes=xnb)
        dstT = p.AA.t[:, :].rearrange("p (ch s) -> p ch s", ch=8)[:, :, t * 128:(t + 1) * 128]
        dstTb = []
        for ch in range(8):
            dstTb += p.AA.bufs_for(ch * S + t * 128, ch * S + (t + 1) * 128)
        c.dma("sp", dstT, xn, reads=xnb, writes=dstTb, transpose=True)

    ln_st = {}
    for t in range(NT + 2):
        if t < NT:
            ln_st[t] = ln_s1(t)
        if 0 <= t - 1 < NT:
            ln_s2(t - 1, *ln_st[t - 1])
        if 0 <= t - 2 < NT:
            ln_s3(t - 2, *ln_st.pop(t - 2))

    if dbg is not None:
        dbg("phaseB", {})
    lo_io, b_io = p.sm_ring.persist(512)
    def fio(eng):
        return eng.iota(st[:, lo_io:lo_io + 512], [[1, 512]], channel_multiplier=-1,
                        allow_small_or_imprecise_dtypes=True)
    c.emit("pool", fio, writes=b_io)

    QT, KT, VA, EXL, EX2 = 0, 2048, 4096, 6400, 8448

    def qkv_pair(pair):
        wq = p.wload(W["wq"][:, pair * 128:(pair + 1) * 128], 1024, 128)
        wk = p.wload(W["wk"][:, pair * 128:(pair + 1) * 128], 1024, 128)
        wv = p.wload(W["wv"][:, pair * 128:(pair + 1) * 128], 1024, 128)
        b_qT = p.BB.bufs_for(QT, QT + S)
        b_kT = p.BB.bufs_for(KT, KT + S)
        b_va = p.BB.bufs_for(VA, VA + NKC * 128)
        for (wtiles, dstcol, dstbufs) in ((wq, QT, b_qT), (wk, KT, b_kT)):
            for tc in range(NQ):
                pbk, pbuf = (p.PB[0], p.PBbuf[0]) if tc % 2 == 0 else (p.PB[1], p.PBbuf[1])
                steps = []
                for kc in range(8):
                    (wt, wlo, wpr, wcols, wbufs) = wtiles[kc]
                    steps.append((wt[0:128, wlo:wlo + 128],
                                  p.AA.t[:, kc * S + tc * 512: kc * S + tc * 512 + 512],
                                  list(wbufs) + p.AA.bufs_for(kc * S + tc * 512, kc * S + tc * 512 + 512)))
                c.mm(pbk[:, :], pbuf, steps)
                c.copy("act", bb[:, dstcol + tc * 512: dstcol + tc * 512 + 512],
                       pbk[:, :], reads=[pbuf], writes=dstbufs)
        for tt_ in range(NT):
            pbk, pbuf = (p.PB[0], p.PBbuf[0]) if tt_ % 2 == 0 else (p.PB[1], p.PBbuf[1])
            steps = []
            for kc in range(8):
                (wt, wlo, wpr, wcols, wbufs) = wv[kc]
                steps.append((p.AA.t[:, kc * S + tt_ * 128: kc * S + tt_ * 128 + 128],
                              wt[0:128, wlo:wlo + 128],
                              list(wbufs) + p.AA.bufs_for(kc * S + tt_ * 128, kc * S + tt_ * 128 + 128)))
            c.mm(pbk[:, 0:128], pbuf, steps)
            for hh in range(2):
                c.copy("act", bb[:, VA + tt_ * 128 + hh * 64: VA + tt_ * 128 + hh * 64 + 64],
                       pbk[:, hh * 64:hh * 64 + 64], reads=[pbuf], writes=b_va)
        return b_qT, b_kT, b_va

    for pair in range(8):
        _pscope = p.sm_ring.scope()
        b_qT, b_kT, b_va = qkv_pair(pair)
        pending = []
        for hh in range(2):
            _hscope = p.sm_ring.scope()
            h = 2 * pair + hh
            gamma = 1.0 - 2.0 ** (-5.0 - h)
            lg = math.log(gamma)
            lo_pat, b_pat = p.sm_ring.persist(512 * 5)
            sc_ap, sc_b = p.const_col(lg)
            c.act(st[:, lo_pat:lo_pat + 512], st[:, lo_io:lo_io + 512], AF.Exp,
                  scale=(sc_ap, sc_b), reads=b_io, writes=b_pat)
            for mi in range(4):
                dstm = st[:, lo_pat + (mi + 1) * 512:lo_pat + (mi + 2) * 512]
                def fm(eng, dstm=dstm, mi=mi):
                    return eng.affine_select(dstm, st[:, lo_pat:lo_pat + 512],
                                             [[1, 512]], ALU.is_ge, 0.0,
                                             base=-mi * 128, channel_multiplier=-1)
                c.emit("pool", fm, reads=b_pat, writes=b_pat)
            qrow0 = hh * 64
            for qc in range(NQ):
                qbase = qc * 512
                par = (qc + hh) % 2
                ob, obuf = (p.PB[2], p.PBbuf[2]) if par == 0 else (p.PB[3], p.PBbuf[3])
                alt, altbuf = (p.PB[3], p.PBbuf[3]) if par == 0 else (p.PB[2], p.PBbuf[2])
                kcs = [kc for kc in range(4 * qc + 4)
                       if (qbase - kc * 128 - 127) * lg > -25.0 or True]
                kcs = [kc for kc in kcs if (qbase - (kc + 1) * 128) * lg > -30.0]
                exi = 0
                pend_pv = None
                for j, kc in enumerate(kcs):
                    sa, sabuf = (p.PA[0], p.PAbuf[0]) if (j // 2) % 2 == 0 else (p.PA[1], p.PAbuf[1])
                    soff = (j % 2) * 512
                    def fsc(eng, kc=kc, sa=sa, soff=soff):
                        return eng.matmul(
                            sa[:, soff:soff + 512],
                            bb[qrow0:qrow0 + 64, KT + kc * 128:KT + kc * 128 + 128],
                            bb[qrow0:qrow0 + 64, QT + qbase:QT + qbase + 512],
                            start=True, stop=True)
                    c.emit("pe", fsc, reads=b_kT + b_qT, writes=[sabuf])
                    if j == 0 and pending:
                        pending.pop(0)()
                    o = kc * 128 - qbase
                    cc = 0.125 * math.exp(lg * (qbase - kc * 128))
                    pat_i = 0 if o < 0 else (o // 128 + 1)
                    pat = st[:, lo_pat + pat_i * 512:lo_pat + (pat_i + 1) * 512]
                    ex = EXL + (exi % 4) * 512
                    exi += 1
                    exb = p.BB.bufs_for(ex, ex + 512)
                    def fev(eng, sa=sa, soff=soff, cc=cc, pat=pat, ex=ex):
                        return eng.scalar_tensor_tensor(bb[:, ex:ex + 512],
                                                        sa[:, soff:soff + 512],
                                                        cc, pat, ALU.mult, ALU.mult)
                    c.emit("dve", fev, reads=[sabuf] + b_pat, writes=exb)
                    if pend_pv is not None:
                        pend_pv()

                    def mk_pv(kc=kc, ex=ex, exb=exb, ob=ob, obuf=obuf,
                              first=(j == 0), last=(j == len(kcs) - 1)):
                        def emit_pv():
                            def fpv(eng):
                                return eng.matmul(ob[0:64, :],
                                                  bb[:, VA + kc * 128 + hh * 64:VA + kc * 128 + hh * 64 + 64],
                                                  bb[:, ex:ex + 512],
                                                  start=first, stop=last)
                            c.emit("pe", fpv, reads=b_va + exb, writes=[obuf])
                        return emit_pv

                    pend_pv = mk_pv()
                pend_pv()
                def tail(ob=ob, obuf=obuf, h=h, qbase=qbase):
                    ex2b = p.BB.bufs_for(EX2, EX2 + 512)
                    c.copy("act", bb[0:64, EX2:EX2 + 512], ob[0:64, :], reads=[obuf],
                           writes=ex2b)
                    sqb = p.BB.bufs_for(EX2 + 512, EX2 + 1024)
                    c.tt("pool", bb[0:64, EX2 + 512:EX2 + 1024], bb[0:64, EX2:EX2 + 512],
                         bb[0:64, EX2:EX2 + 512], ALU.mult, reads=ex2b, writes=sqb)
                    nb_, nbuf_ = p.PB[0], p.PBbuf[0]
                    lo_mt, b_mt = p.sm_ring.alloc(1024 + 512)
                    def fs1b(eng):
                        return eng.matmul(nb_[0:1, :], p.ONE_COL_BF[0:64, 0:1],
                                          bb[0:64, EX2:EX2 + 512], start=True, stop=True)
                    c.emit("pe", fs1b, reads=[p.ONE_COL_BF_buf] + ex2b, writes=[nbuf_])
                    c.copy("dve", st[0:1, lo_mt:lo_mt + 512], nb_[0:1, :], reads=[nbuf_],
                           writes=b_mt)
                    def fs2b(eng):
                        return eng.matmul(nb_[0:1, :], p.ONE_COL_BF[0:64, 0:1],
                                          bb[0:64, EX2 + 512:EX2 + 1024], start=True, stop=True)
                    c.emit("pe", fs2b, reads=[p.ONE_COL_BF_buf] + sqb, writes=[nbuf_])
                    mrow = st[0:1, lo_mt:lo_mt + 512]
                    vrow = st[0:1, lo_mt + 512:lo_mt + 1024]
                    c.ts("dve", mrow, mrow, 1.0 / 64.0, None, ALU.mult, None,
                         reads=b_mt, writes=b_mt)
                    def fv(eng):
                        return eng.tensor_scalar(vrow, nb_[0:1, :], 1.0 / 64.0, None, ALU.mult)
                    c.emit("dve", fv, reads=[nbuf_], writes=b_mt)
                    lo_m2, b_m2 = p.sm_ring.alloc(512)
                    c.tt("dve", st[0:1, lo_m2:lo_m2 + 512], mrow, mrow, ALU.mult,
                         reads=b_mt, writes=b_m2)
                    c.tt("dve", vrow, vrow, st[0:1, lo_m2:lo_m2 + 512], ALU.subtract,
                         reads=b_mt + b_m2, writes=b_mt)
                    p.rsqrt(vrow, vrow, 1e-6, reads=b_mt, writes=b_mt)
                    bm, bmb = p.PB[1], p.PBbuf[1]
                    def fbm(eng):
                        return eng.matmul(bm[0:64, :], p.ONE_ROW[0:1, 0:64], mrow,
                                          start=True, stop=True)
                    c.emit("pe", fbm, reads=[p.ONE_ROW_buf] + b_mt, writes=[bmb])
                    t1b = p.BB.bufs_for(EX2 + 1024, EX2 + 1536)
                    def fsub(eng):
                        return eng.tensor_tensor(bb[0:64, EX2 + 1024:EX2 + 1536],
                                                 bb[0:64, EX2:EX2 + 512], bm[0:64, :],
                                                 ALU.subtract)
                    c.emit("dve", fsub, reads=ex2b + [bmb], writes=t1b)
                    def fbr(eng):
                        return eng.matmul(bm[0:64, :], p.ONE_ROW[0:1, 0:64], vrow,
                                          start=True, stop=True)
                    c.emit("pe", fbr, reads=[p.ONE_ROW_buf] + b_mt, writes=[bmb])
                    och = h // 2
                    orow = (h % 2) * 64
                    ocol = och * S + qbase
                    ob_bufs = p.AB.bufs_for(ocol, ocol + 512)
                    def fmul(eng):
                        return eng.tensor_tensor(p.AB.t[orow:orow + 64, ocol:ocol + 512],
                                                 bb[0:64, EX2 + 1024:EX2 + 1536],
                                                 bm[0:64, :], ALU.mult)
                    c.emit("dve", fmul, reads=t1b + [bmb], writes=ob_bufs)
                pending.append(tail)
            p.sm_ring.unscope(_hscope)
        while pending:
            pending.pop(0)()
        p.sm_ring.unscope(_pscope)

    for ch in range(8):
        wg = p.wload(W["wg"][:, ch * 128:(ch + 1) * 128], 1024, 128)
        for tc in range(NQ):
            pbk, pbuf = (p.PB[0], p.PBbuf[0]) if tc % 2 == 0 else (p.PB[1], p.PBbuf[1])
            steps = []
            for kc in range(8):
                (wt, wlo, wpr, wcols, wbufs) = wg[kc]
                steps.append((wt[0:128, wlo:wlo + 128],
                              p.AA.t[:, kc * S + tc * 512: kc * S + tc * 512 + 512],
                              list(wbufs) + p.AA.bufs_for(kc * S + tc * 512, kc * S + tc * 512 + 512)))
            c.mm(pbk[:, :], pbuf, steps)
            gb = p.BB.bufs_for(EXL, EXL + 512)
            def fsl(eng, pbk=pbk):
                return eng.activation(bb[:, EXL:EXL + 512], pbk[:, :], AF.Silu)
            c.emit("act", fsl, reads=[pbuf], writes=gb)
            ocol = ch * S + tc * 512
            ob_bufs = p.AB.bufs_for(ocol, ocol + 512)
            c.tt("dve", p.AB.t[:, ocol:ocol + 512], p.AB.t[:, ocol:ocol + 512],
                 bb[:, EXL:EXL + 512], ALU.mult, reads=ob_bufs + gb, writes=ob_bufs)

    if dbg is not None:
        dbg("ret", {})
    lo_w2, b_w2 = p.sm_ring.persist(1024)
    lo_b2r, b_b2r = p.sm_ring.persist(1024)
    lo_row2, b_row2 = p.sm_ring.alloc(2048)
    c.dma("pool", st[0:1, lo_row2:lo_row2 + 1024], W["ln2_w"], writes=b_row2)
    c.dma("pool", st[0:1, lo_row2 + 1024:lo_row2 + 2048], W["ln2_b"], writes=b_row2)
    for (lo_dst, b_dst, off) in ((lo_w2, b_w2, 0), (lo_b2r, b_b2r, 1024)):
        for half in range(2):
            pbk, pbuf = p.PB[3], p.PBbuf[3]
            def fb(eng, off=off, half=half):
                return eng.matmul(pbk[:, :], p.ONE_ROW[0:1, 0:128],
                                  st[0:1, lo_row2 + off + half * 512:lo_row2 + off + half * 512 + 512],
                                  start=True, stop=True)
            c.emit("pe", fb, reads=[p.ONE_ROW_buf] + b_row2, writes=[pbuf])
            c.copy("dve", st[:, lo_dst + half * 512:lo_dst + half * 512 + 512],
                   pbk[:, :], reads=[pbuf], writes=b_dst)

    b_yd = Buf("y_dram")
    wo = p.wload(W["wo"], 1024, 1024)
    for t in range(NT):
        lo_y, b_y = p.sm_ring.alloc(1024 + 128 + 32)
        lo_xnb = lo_y + 1024
        c.dma("pool", bb[:, EXL:EXL + 1024],
              xn_dram[t * 128:(t + 1) * 128, :], reads=[b_xnd],
              writes=p.BB.bufs_for(EXL, EXL + 1024))
        for half in range(2):
            bi = (t * 2 + half) % 4
            pbk, pbuf = p.PB[bi], p.PBbuf[bi]
            steps = []
            for kc in range(8):
                (wt, wlo, wpr, wcols, wbufs) = wo[kc]
                steps.append((p.AB.t[:, kc * S + t * 128: kc * S + t * 128 + 128],
                              wt[0:128, wlo + half * 512: wlo + half * 512 + 512],
                              list(wbufs) + p.AB.bufs_for(kc * S + t * 128, kc * S + t * 128 + 128)))
            c.mm(pbk[:, :], pbuf, steps)
            def fy(eng, pbk=pbk, half=half, lo_y=lo_y):
                return eng.tensor_tensor(st[:, lo_y + half * 512:lo_y + half * 512 + 512],
                                         pbk[:, :], bb[:, EXL + half * 512:EXL + half * 512 + 512],
                                         ALU.add)
            c.emit("dve", fy, reads=[pbuf] + p.BB.bufs_for(EXL, EXL + 1024), writes=b_y)
        ytile = st[:, lo_y:lo_y + 1024]
        c.dma("pool", y_dram[t * 128:(t + 1) * 128, :], ytile, reads=b_y, writes=[b_yd])
        lo_st, b_st = p.sm_ring.alloc(32)
        def g1(eng):
            return eng.bn_stats(st[:, lo_st:lo_st + 6], ytile[:, 0:512])
        def g2(eng):
            return eng.bn_stats(st[:, lo_st + 6:lo_st + 12], ytile[:, 512:1024])
        def g3(eng):
            return eng.bn_aggr(st[:, lo_st + 12:lo_st + 14], st[:, lo_st:lo_st + 12])
        c.emit("dve", g1, reads=b_y, writes=b_st)
        c.emit("dve", g2, reads=b_y, writes=b_st)
        c.emit("dve", g3, reads=b_st, writes=b_st)
        al = st[:, lo_st + 14:lo_st + 15]
        bt_ = st[:, lo_st + 15:lo_st + 16]
        p.rsqrt(al, st[:, lo_st + 13:lo_st + 14], EPS, reads=b_st, writes=b_st)
        c.ts("dve", bt_, st[:, lo_st + 12:lo_st + 13], al, -1.0, ALU.mult, ALU.mult,
             reads=b_st, writes=b_st)
        un = bb[:, 0:1024]
        unb = p.BB.bufs_for(0, 1024)
        c.ts("dve", un, ytile, al, bt_, ALU.mult, ALU.add, reads=b_y + b_st, writes=unb)
        def gw(eng):
            return eng.tensor_tensor(un, un, st[:, lo_w2:lo_w2 + 1024], ALU.mult)
        c.emit("dve", gw, reads=unb + b_w2, writes=unb)
        def gb_(eng):
            return eng.tensor_tensor(un, un, st[:, lo_b2r:lo_b2r + 1024], ALU.add)
        c.emit("dve", gb_, reads=unb + b_b2r, writes=unb)
        dstT = p.AA.t[:, :].rearrange("p (ch s) -> p ch s", ch=8)[:, :, t * 128:(t + 1) * 128]
        dstTb = []
        for ch in range(8):
            dstTb += p.AA.bufs_for(ch * S + t * 128, ch * S + (t + 1) * 128)
        c.dma("sp", dstT, un, reads=unb, writes=dstTb, transpose=True)

    if dbg is not None:
        dbg("y", dict(b_yd=b_yd))
    _, lo_bia, _, b_bia = p.load_bias_col(W["b_in"].rearrange("a n -> (a n)")[0:2730], 2730, "bia")
    _, lo_bib, _, b_bib = p.load_bias_col(W["b_in"].rearrange("a n -> (a n)")[2730:5460], 2730, "bib")
    lo_bo, b_bo = p.sm_ring.persist(1024)
    c.dma("pool", st[0:1, lo_bo:lo_bo + 1024], W["b_out"], writes=b_bo)

    for tc in range(NQ):
        for mi in range(22):
            rows = 128 if mi < 21 else 42
            wa = p.wload(W["w_in"][:, mi * 128:mi * 128 + rows], 1024, rows)
            wb_ = p.wload(W["w_in"][:, 2730 + mi * 128:2730 + mi * 128 + rows], 1024, rows)
            if mi % 2 == 0:
                pa, pab = p.PB[0], p.PBbuf[0]
                pb_, pbb = p.PB[1], p.PBbuf[1]
            else:
                pa, pab = p.PA[0][:, 0:512], p.PAbuf[0]
                pb_, pbb = p.PA[1][:, 0:512], p.PAbuf[1]
            for (wt_, ps_, psb_) in ((wa, pa, pab), (wb_, pb_, pbb)):
                steps = []
                for kc in range(8):
                    (wtt, wlo, wpr, wcols, wbufs) = wt_[kc]
                    steps.append((wtt[0:128, wlo:wlo + rows],
                                  p.AA.t[:, kc * S + tc * 512: kc * S + tc * 512 + 512],
                                  list(wbufs) + p.AA.bufs_for(kc * S + tc * 512, kc * S + tc * 512 + 512)))
                c.mm(ps_[0:rows, :], psb_, steps)
            SLT = 11264
            sl = bb[0:rows, SLT:SLT + 512]
            slb = p.BB.bufs_for(SLT, SLT + 512)
            def fsl(eng, pb_=pb_, rows=rows, mi=mi):
                return eng.activation(sl, pb_[0:rows, :], AF.Silu,
                                      bias=st[0:rows, lo_bib + mi:lo_bib + mi + 1])
            c.emit("act", fsl, reads=[pbb] + b_bib, writes=slb)
            ht = bb[0:rows, mi * 512:(mi + 1) * 512]
            htb = p.BB.bufs_for(mi * 512, (mi + 1) * 512)
            def fh(eng, pa=pa, rows=rows, mi=mi):
                return eng.scalar_tensor_tensor(ht, pa[0:rows, :],
                                                st[0:rows, lo_bia + mi:lo_bia + mi + 1],
                                                sl, ALU.add, ALU.mult)
            c.emit("dve", fh, reads=[pab] + b_bia + slb, writes=htb)
        for half in range(2):
            wd = p.wload(W["w_out"][:, half * 512:half * 512 + 512], 2730, 512)
            for t4 in range(4):
                t = tc * 4 + t4
                pbk, pbuf = (p.PB[2], p.PBbuf[2]) if t4 % 2 == 0 else (p.PB[3], p.PBbuf[3])
                steps = []
                for mi in range(22):
                    rows = 128 if mi < 21 else 42
                    (wtt, wlo, wpr, wcols, wbufs) = wd[mi]
                    steps.append((bb[0:rows, mi * 512 + t4 * 128:mi * 512 + t4 * 128 + 128],
                                  wtt[0:rows, wlo:wlo + 512],
                                  list(wbufs) + p.BB.bufs_for(mi * 512, (mi + 1) * 512)))
                steps.append((p.ONE_ROW[0:1, 0:128],
                              st[0:1, lo_bo + half * 512:lo_bo + half * 512 + 512],
                              [p.ONE_ROW_buf] + b_bo))
                c.mm(pbk[:, :], pbuf, steps)
                YT = 11776
                yb2 = p.BB.bufs_for(YT, YT + 512)
                c.dma("pool", bb[:, YT:YT + 512],
                      y_dram[t * 128:(t + 1) * 128, half * 512:half * 512 + 512],
                      reads=[b_yd], writes=yb2)
                rb = p.RES.bufs[t]
                rap = p.RES.t[:, t * 1024 + half * 512: t * 1024 + half * 512 + 512]
                def fz1(eng, pbk=pbk):
                    return eng.tensor_tensor(rap, rap, pbk[:, :], ALU.add)
                c.emit("dve", fz1, reads=[pbuf, rb], writes=[rb])
                def fz2(eng):
                    return eng.tensor_tensor(rap, rap, bb[:, 11776:11776 + 512], ALU.add)
                c.emit("dve", fz2, reads=yb2 + [rb], writes=[rb])
    p.sm_ring.unscope(_scope)



S_FULL = 2048

_CONV = [
    ("L0_attn_a_Wq", 1024, 1024), ("L0_attn_a_Wk", 1024, 1024),
    ("L0_attn_a_Wv", 1024, 1024), ("L0_attn_a_Wo", 1024, 1024),
    ("L0_attn_b_w_q", 1024, 1024), ("L0_attn_b_w_kv", 1024, 2048),
    ("L0_attn_b_w_out", 1024, 1024),
    ("L0_ffn_enc_W", 1024, 2048), ("L0_ffn_s2c_W", 2048, 64),
    ("L0_ffn_rW1", 64, 64), ("L0_ffn_rW2", 64, 8),
    ("L0_ffn_We1", 512, 1024), ("L0_ffn_We2", 8192, 64),
    ("L0_ffn_c2s_W", 64, 2048), ("L0_ffn_dec_W", 2048, 1024),
    ("L1_attn_a_W_attn", 1024, 3072), ("L1_attn_a_W_proj", 1024, 1024),
    ("L1_attn_b_wq", 1024, 1024), ("L1_attn_b_wk", 1024, 1024),
    ("L1_attn_b_wv", 1024, 1024), ("L1_attn_b_wg", 1024, 1024),
    ("L1_attn_b_wo", 1024, 1024), ("L1_attn_b_w_in", 1024, 5460),
    ("L1_attn_b_w_out", 2730, 1024),
]


def build_prep(S=128):
    nc = bass.Bass("TRN2", target_bir_lowering=False)
    ins = {}
    outs = {}
    for (n, r, cmax) in _CONV:
        ins[n] = nc.dram_tensor(n, [r, cmax], F32, kind="ExternalInput").ap()
        outs[n] = nc.dram_tensor(n + "_bf", [r, cmax], BF16, kind="ExternalOutput").ap()
    fp4src = {n: nc.dram_tensor(n, shp, F32, kind="ExternalInput").ap()
              for n, shp in (("w_gate", [2816, 1024]), ("w_up", [2816, 1024]),
                             ("w_down", [1024, 2816]))}
    fp4dst = {"w_gate": nc.dram_tensor("wgT", [1024, 2816], BF16, kind="ExternalOutput").ap(),
              "w_up": nc.dram_tensor("wuT", [1024, 2816], BF16, kind="ExternalOutput").ap(),
              "w_down": nc.dram_tensor("wdT", [2816, 1024], BF16, kind="ExternalOutput").ap()}
    import contextlib
    es = contextlib.ExitStack()
    with es:
        p = Prog(nc, es, S=S)
        for (n, r, cw) in _CONV:
            p.to_bf16_dram(ins[n], outs[n], r, cw)
        emit_fp4_weight_prep(p, fp4src, fp4dst)
        p.finish()
    return nc


def _wmaps_l0a(i):
    return {"wq": "L0_attn_a_Wq_bf", "wk": "L0_attn_a_Wk_bf", "wv": "L0_attn_a_Wv_bf",
            "wo": "L0_attn_a_Wo_bf", "null_k": "L0_attn_a_null_k",
            "null_v": "L0_attn_a_null_v", "logit_scale": "L0_attn_a_logit_scale"}


def build_sublayer(kind, S=S_FULL):
    nc = bass.Bass("TRN2", target_bir_lowering=False)
    import contextlib
    es = contextlib.ExitStack()

    def dram(n, shp, dt=BF16):
        return nc.dram_tensor(n, shp, dt, kind="ExternalInput").ap()

    x_ap = nc.dram_tensor("x", [S, 1024], F32, kind="ExternalInput").ap()
    o_ap = nc.dram_tensor("o", [S, 1024], F32, kind="ExternalOutput").ap()
    with es:
        p = Prog(nc, es, S=S)
        p.load_res(x_ap)
        if kind == "l0a":
            W = {"wq": dram("wq", [1024, 1024]), "wk": dram("wk", [1024, 1024]),
                 "wv": dram("wv", [1024, 1024]), "wo": dram("wo", [1024, 1024]),
                 "null_k": dram("null_k", [16, 64], F32),
                 "null_v": dram("null_v", [16, 64], F32),
                 "logit_scale": dram("logit_scale", [1, 16], F32)}
            emit_l0_attn_a(p, W)
        elif kind == "l0f":
            W = {"enc_W": dram("enc_W", [1024, 2048]),
                 "enc_b": dram("enc_b", [1, 2048], F32),
                 "s2c_W": dram("s2c_W", [2048, 64]), "s2c_b": dram("s2c_b", [1, 64], F32),
                 "rW1": dram("rW1", [64, 64]), "rb1": dram("rb1", [1, 64], F32),
                 "rW2": dram("rW2", [64, 8]), "rb2": dram("rb2", [1, 8], F32),
                 "We1": dram("We1", [8, 64, 1024]), "be1": dram("be1", [8, 1024], F32),
                 "We2": dram("We2", [8, 1024, 64]), "be2": dram("be2", [8, 64], F32),
                 "c2s_W": dram("c2s_W", [64, 2048]), "c2s_b": dram("c2s_b", [1, 2048], F32),
                 "dec_W": dram("dec_W", [2048, 1024]), "dec_b": dram("dec_b", [1, 1024], F32),
                 "ln_g": dram("ln_g", [1, 1024], F32), "ln_b": dram("ln_b", [1, 1024], F32)}
            emit_l0_ffn(p, W)
        elif kind == "l0b":
            W = {"w_q": dram("w_q", [1024, 1024]), "w_kv": dram("w_kv", [1024, 2048]),
                 "w_out": dram("w_out", [1024, 1024]), "b_out": dram("b_out", [1, 1024], F32),
                 "scale": dram("scale", [1, 16], F32)}
            emit_l0_attn_b(p, W)
        elif kind == "l1a":
            W = {"w_attn": dram("w_attn", [1024, 3072]),
                 "b_attn": dram("b_attn", [1, 3072], F32),
                 "w_proj": dram("w_proj", [1024, 1024]),
                 "b_proj": dram("b_proj", [1, 1024], F32)}
            emit_l1_attn_a(p, W)
        elif kind == "l1f":
            W = {"wgT": dram("wgT", [1024, 2816]), "wuT": dram("wuT", [1024, 2816]),
                 "wdT": dram("wdT", [2816, 1024])}
            emit_l1_ffn(p, W)
        elif kind == "l1b":
            W = {"ln1_w": dram("ln1_w", [1, 1024], F32), "ln1_b": dram("ln1_b", [1, 1024], F32),
                 "wq": dram("wq", [1024, 1024]), "wk": dram("wk", [1024, 1024]),
                 "wv": dram("wv", [1024, 1024]), "wg": dram("wg", [1024, 1024]),
                 "wo": dram("wo", [1024, 1024]),
                 "ln2_w": dram("ln2_w", [1, 1024], F32), "ln2_b": dram("ln2_b", [1, 1024], F32),
                 "w_in": dram("w_in", [1024, 5460]), "b_in": dram("b_in", [1, 5460], F32),
                 "w_out": dram("w_out", [2730, 1024]), "b_out": dram("b_out", [1, 1024], F32)}
            scr = {"xn_bf": nc.dram_tensor("xn_bf", [S, 1024], BF16, kind="Internal").ap(),
                   "y_bf": nc.dram_tensor("y_bf", [S, 1024], F32, kind="Internal").ap()}
            emit_l1_attn_b(p, W, scr)
        p.store_res(o_ap)
        p.finish()
    return nc


def kernel(**inputs):
    n = NCORES
    x = np.ascontiguousarray(inputs["L0_attn_a_x"])
    shards = [x[2 * c:2 * c + 2] for c in range(n)]

    prep_in = {}
    for (nm, r, cw) in _CONV:
        prep_in[nm] = np.ascontiguousarray(inputs[nm].reshape(r, cw))
    prep_in["w_gate"] = inputs["L1_ffn_w_gate"]
    prep_in["w_up"] = inputs["L1_ffn_w_up"]
    prep_in["w_down"] = inputs["L1_ffn_w_down"]
    nc_prep = build_prep()
    res = run_bass_kernel_spmd(nc_prep, [prep_in], core_ids=[0])
    wbf = dict(res.results[0])

    def col(v, n_):
        return np.ascontiguousarray(v.reshape(1, n_).astype(np.float32))

    progs = {k: build_sublayer(k) for k in ("l0a", "l0f", "l0b", "l1a", "l1f", "l1b")}

    w_l0a = {"wq": wbf["L0_attn_a_Wq_bf"], "wk": wbf["L0_attn_a_Wk_bf"],
             "wv": wbf["L0_attn_a_Wv_bf"], "wo": wbf["L0_attn_a_Wo_bf"],
             "null_k": inputs["L0_attn_a_null_k"].reshape(16, 64),
             "null_v": inputs["L0_attn_a_null_v"].reshape(16, 64),
             "logit_scale": col(inputs["L0_attn_a_logit_scale"], 16)}
    w_l0f = {"enc_W": wbf["L0_ffn_enc_W_bf"], "enc_b": col(inputs["L0_ffn_enc_b"], 2048),
             "s2c_W": wbf["L0_ffn_s2c_W_bf"], "s2c_b": col(inputs["L0_ffn_s2c_b"], 64),
             "rW1": wbf["L0_ffn_rW1_bf"], "rb1": col(inputs["L0_ffn_rb1"], 64),
             "rW2": wbf["L0_ffn_rW2_bf"], "rb2": col(inputs["L0_ffn_rb2"], 8),
             "We1": wbf["L0_ffn_We1_bf"].reshape(8, 64, 1024),
             "be1": inputs["L0_ffn_be1"],
             "We2": wbf["L0_ffn_We2_bf"].reshape(8, 1024, 64),
             "be2": inputs["L0_ffn_be2"],
             "c2s_W": wbf["L0_ffn_c2s_W_bf"], "c2s_b": col(inputs["L0_ffn_c2s_b"], 2048),
             "dec_W": wbf["L0_ffn_dec_W_bf"], "dec_b": col(inputs["L0_ffn_dec_b"], 1024),
             "ln_g": col(inputs["L0_ffn_ln_g"], 1024), "ln_b": col(inputs["L0_ffn_ln_b"], 1024)}
    w_l0b = {"w_q": wbf["L0_attn_b_w_q_bf"], "w_kv": wbf["L0_attn_b_w_kv_bf"],
             "w_out": wbf["L0_attn_b_w_out_bf"], "b_out": col(inputs["L0_attn_b_b_out"], 1024),
             "scale": col(inputs["L0_attn_b_scale"], 16)}
    w_l1a = {"w_attn": wbf["L1_attn_a_W_attn_bf"], "b_attn": col(inputs["L1_attn_a_b_attn"], 3072),
             "w_proj": wbf["L1_attn_a_W_proj_bf"], "b_proj": col(inputs["L1_attn_a_b_proj"], 1024)}
    w_l1f = {"wgT": wbf["wgT"], "wuT": wbf["wuT"], "wdT": wbf["wdT"]}
    w_l1b = {"ln1_w": col(inputs["L1_attn_b_ln1_w"], 1024), "ln1_b": col(inputs["L1_attn_b_ln1_b"], 1024),
             "wq": wbf["L1_attn_b_wq_bf"], "wk": wbf["L1_attn_b_wk_bf"],
             "wv": wbf["L1_attn_b_wv_bf"], "wg": wbf["L1_attn_b_wg_bf"],
             "wo": wbf["L1_attn_b_wo_bf"],
             "ln2_w": col(inputs["L1_attn_b_ln2_w"], 1024), "ln2_b": col(inputs["L1_attn_b_ln2_b"], 1024),
             "w_in": wbf["L1_attn_b_w_in_bf"], "b_in": col(inputs["L1_attn_b_b_in"], 5460),
             "w_out": wbf["L1_attn_b_w_out_bf"], "b_out": col(inputs["L1_attn_b_b_out"], 1024)}

    seq = [("l0a", w_l0a), ("l0f", w_l0f), ("l0b", w_l0b), ("l0f", w_l0f),
           ("l1a", w_l1a), ("l1f", w_l1f), ("l1b", w_l1b), ("l1f", w_l1f)]

    cur = [np.ascontiguousarray(shards[c][e]) for c in range(n) for e in range(2)]
    for (kind, wm) in seq:
        nc_s = progs[kind]
        for e in range(2):
            in_maps = []
            for ci in range(n):
                m = dict(wm)
                m["x"] = cur[ci * 2 + e]
                in_maps.append(m)
            res = run_bass_kernel_spmd(nc_s, in_maps, core_ids=list(range(n)))
            for ci in range(n):
                cur[ci * 2 + e] = res.results[ci]["o"]

    out = np.empty((16, 2048, 1024), np.float32)
    for ci in range(n):
        for e in range(2):
            out[2 * ci + e] = cur[ci * 2 + e]
    return out


def build_fused():
    nc = bass.Bass("TRN2", target_bir_lowering=False)
    S = S_FULL
    ins = {}
    for (n, r, cw) in _CONV:
        ins[n] = nc.dram_tensor(n, [r, cw], F32, kind="ExternalInput").ap()
    wbf = {n: nc.dram_tensor(n + "_bf", [r, cw], BF16, kind="Internal").ap()
           for (n, r, cw) in _CONV}
    fp4src = {n: nc.dram_tensor(n, shp, F32, kind="ExternalInput").ap()
              for n, shp in (("w_gate", [2816, 1024]), ("w_up", [2816, 1024]),
                             ("w_down", [1024, 2816]))}
    fp4dst = {"w_gate": nc.dram_tensor("wgT", [1024, 2816], BF16, kind="Internal").ap(),
              "w_up": nc.dram_tensor("wuT", [1024, 2816], BF16, kind="Internal").ap(),
              "w_down": nc.dram_tensor("wdT", [2816, 1024], BF16, kind="Internal").ap()}
    sm_in = {}
    for n, shp in (("null_k", [16, 64]), ("null_v", [16, 64]), ("logit_scale", [1, 16]),
                   ("b_out_l0b", [1, 1024]), ("scale_l0b", [1, 16]),
                   ("enc_b", [1, 2048]), ("s2c_b", [1, 64]), ("rb1", [1, 64]),
                   ("rb2", [1, 8]), ("be1", [8, 1024]), ("be2", [8, 64]),
                   ("c2s_b", [1, 2048]), ("dec_b", [1, 1024]), ("ln_g", [1, 1024]),
                   ("ln_b", [1, 1024]), ("b_attn", [1, 3072]), ("b_proj", [1, 1024]),
                   ("ln1_w", [1, 1024]), ("ln1_b", [1, 1024]), ("ln2_w", [1, 1024]),
                   ("ln2_b", [1, 1024]), ("b_in", [1, 5460]), ("b_out_l1b", [1, 1024])):
        sm_in[n] = nc.dram_tensor(n, shp, F32, kind="ExternalInput").ap()
    x_ap = nc.dram_tensor("x", [2 * S, 1024], F32, kind="ExternalInput").ap()
    o_ap = nc.dram_tensor("o", [2 * S, 1024], F32, kind="ExternalOutput").ap()
    xn_bf = nc.dram_tensor("xn_bf", [S, 1024], BF16, kind="Internal").ap()
    y_bf = nc.dram_tensor("y_bf", [S, 1024], F32, kind="Internal").ap()

    import contextlib
    es = contextlib.ExitStack()
    with es:
        p = Prog(nc, es, S=S)
        c = p.ctx
        for (n, r, cw) in _CONV:
            c.dram_bufs[n + "_bf"] = Buf(n + "_bf")
        for n in ("wgT", "wuT", "wdT", "xn_bf", "y_bf"):
            c.dram_bufs[n] = Buf(n)
        p.load_res(x_ap[0:S, :])
        for (n, r, cw) in _CONV:
            p.to_bf16_dram(ins[n], wbf[n], r, cw)
        emit_fp4_weight_prep(p, fp4src, fp4dst)

        W_l0a = {"wq": wbf["L0_attn_a_Wq"], "wk": wbf["L0_attn_a_Wk"],
                 "wv": wbf["L0_attn_a_Wv"], "wo": wbf["L0_attn_a_Wo"],
                 "null_k": sm_in["null_k"], "null_v": sm_in["null_v"],
                 "logit_scale": sm_in["logit_scale"]}
        W_l0f = {"enc_W": wbf["L0_ffn_enc_W"], "enc_b": sm_in["enc_b"],
                 "s2c_W": wbf["L0_ffn_s2c_W"], "s2c_b": sm_in["s2c_b"],
                 "rW1": wbf["L0_ffn_rW1"], "rb1": sm_in["rb1"],
                 "rW2": wbf["L0_ffn_rW2"], "rb2": sm_in["rb2"],
                 "We1": wbf["L0_ffn_We1"].rearrange("(e m) h -> e m h", e=8),
                 "be1": sm_in["be1"],
                 "We2": wbf["L0_ffn_We2"].rearrange("(e h) m -> e h m", e=8),
                 "be2": sm_in["be2"],
                 "c2s_W": wbf["L0_ffn_c2s_W"], "c2s_b": sm_in["c2s_b"],
                 "dec_W": wbf["L0_ffn_dec_W"], "dec_b": sm_in["dec_b"],
                 "ln_g": sm_in["ln_g"], "ln_b": sm_in["ln_b"]}
        W_l0b = {"w_q": wbf["L0_attn_b_w_q"], "w_kv": wbf["L0_attn_b_w_kv"],
                 "w_out": wbf["L0_attn_b_w_out"], "b_out": sm_in["b_out_l0b"],
                 "scale": sm_in["scale_l0b"]}
        W_l1a = {"w_attn": wbf["L1_attn_a_W_attn"], "b_attn": sm_in["b_attn"],
                 "w_proj": wbf["L1_attn_a_W_proj"], "b_proj": sm_in["b_proj"]}
        W_l1f = {"wgT": fp4dst["w_gate"], "wuT": fp4dst["w_up"], "wdT": fp4dst["w_down"]}
        W_l1b = {"ln1_w": sm_in["ln1_w"], "ln1_b": sm_in["ln1_b"],
                 "wq": wbf["L1_attn_b_wq"], "wk": wbf["L1_attn_b_wk"],
                 "wv": wbf["L1_attn_b_wv"], "wg": wbf["L1_attn_b_wg"],
                 "wo": wbf["L1_attn_b_wo"],
                 "ln2_w": sm_in["ln2_w"], "ln2_b": sm_in["ln2_b"],
                 "w_in": wbf["L1_attn_b_w_in"], "b_in": sm_in["b_in"],
                 "w_out": wbf["L1_attn_b_w_out"], "b_out": sm_in["b_out_l1b"]}
        scr = {"xn_bf": xn_bf, "y_bf": y_bf}

        for e in range(2):
            if e > 0:
                p.load_res(x_ap[e * S:(e + 1) * S, :])
            p.WQ = "sp" if e == 0 else "pool"
            emit_l0_attn_a(p, W_l0a)
            p.WQ = "pool"
            emit_l0_ffn(p, W_l0f)
            emit_l0_attn_b(p, W_l0b)
            emit_l0_ffn(p, W_l0f)
            emit_l1_attn_a(p, W_l1a)
            emit_l1_ffn(p, W_l1f)
            emit_l1_attn_b(p, W_l1b, scr)
            emit_l1_ffn(p, W_l1f)
            p.store_res(o_ap[e * S:(e + 1) * S, :])
        p.finish()
    return nc


def kernel(**inputs):
    n = NCORES
    x = np.ascontiguousarray(inputs["L0_attn_a_x"])

    def col(v, n_):
        return np.ascontiguousarray(v.reshape(1, n_).astype(np.float32))

    base = {}
    for (nm, r, cw) in _CONV:
        base[nm] = np.ascontiguousarray(inputs[nm].reshape(r, cw))
    base["w_gate"] = inputs["L1_ffn_w_gate"]
    base["w_up"] = inputs["L1_ffn_w_up"]
    base["w_down"] = inputs["L1_ffn_w_down"]
    base["null_k"] = np.ascontiguousarray(inputs["L0_attn_a_null_k"].reshape(16, 64))
    base["null_v"] = np.ascontiguousarray(inputs["L0_attn_a_null_v"].reshape(16, 64))
    base["logit_scale"] = col(inputs["L0_attn_a_logit_scale"], 16)
    base["b_out_l0b"] = col(inputs["L0_attn_b_b_out"], 1024)
    base["scale_l0b"] = col(inputs["L0_attn_b_scale"], 16)
    base["enc_b"] = col(inputs["L0_ffn_enc_b"], 2048)
    base["s2c_b"] = col(inputs["L0_ffn_s2c_b"], 64)
    base["rb1"] = col(inputs["L0_ffn_rb1"], 64)
    base["rb2"] = col(inputs["L0_ffn_rb2"], 8)
    base["be1"] = np.ascontiguousarray(inputs["L0_ffn_be1"])
    base["be2"] = np.ascontiguousarray(inputs["L0_ffn_be2"])
    base["c2s_b"] = col(inputs["L0_ffn_c2s_b"], 2048)
    base["dec_b"] = col(inputs["L0_ffn_dec_b"], 1024)
    base["ln_g"] = col(inputs["L0_ffn_ln_g"], 1024)
    base["ln_b"] = col(inputs["L0_ffn_ln_b"], 1024)
    base["b_attn"] = col(inputs["L1_attn_a_b_attn"], 3072)
    base["b_proj"] = col(inputs["L1_attn_a_b_proj"], 1024)
    base["ln1_w"] = col(inputs["L1_attn_b_ln1_w"], 1024)
    base["ln1_b"] = col(inputs["L1_attn_b_ln1_b"], 1024)
    base["ln2_w"] = col(inputs["L1_attn_b_ln2_w"], 1024)
    base["ln2_b"] = col(inputs["L1_attn_b_ln2_b"], 1024)
    base["b_in"] = col(inputs["L1_attn_b_b_in"], 5460)
    base["b_out_l1b"] = col(inputs["L1_attn_b_b_out"], 1024)

    nc = build_fused()
    in_maps = []
    for ci in range(n):
        m = dict(base)
        m["x"] = np.ascontiguousarray(x[2 * ci:2 * ci + 2].reshape(2 * S_FULL, 1024))
        in_maps.append(m)
    res = run_bass_kernel_spmd(nc, in_maps, core_ids=list(range(n)))
    out = np.empty((16, 2048, 1024), np.float32)
    for ci in range(n):
        out[2 * ci:2 * ci + 2] = res.results[ci]["o"].reshape(2, 2048, 1024)
    return out
